# version 03
# baseline (speedup 1.0000x reference)
import numpy as np
import ml_dtypes
from contextlib import ExitStack

import concourse.bass as bass
import concourse.mybir as mybir
from concourse.bass_utils import run_bass_kernel_spmd

F32 = mybir.dt.float32
BF16 = mybir.dt.bfloat16
U32 = mybir.dt.uint32
U8 = mybir.dt.uint8
AF = mybir.ActivationFunctionType
ALU = mybir.AluOpType

D = 1024
NT = 2048
TT = NT // 128
NHEAD = 16
DH = 64
NE = 8
HFF = 4096
CAP = 5120
DI = 2048
NCORE = 8

ENGINES = ("sync", "scalar", "vector", "tensor", "gpsimd")
NO_CC = False


class Sem:
    def __init__(self, ctx, nc, name):
        self.h = ctx.enter_context(nc.semaphore(name))
        self.n = 0

    def bump(self, k=1):
        self.n += k
        return self.n


class Rec:
    def __init__(self):
        self.steps = {e: [] for e in ENGINES}

    def on(self, e):
        def deco(fn):
            self.steps[e].append(fn)
            return None
        return deco

    def register(self, block):
        for e in ENGINES:
            fns = self.steps[e]
            getattr(block, e)(lambda eng, fns=fns: [f(eng) for f in fns])


class Ctx:
    pass



def build_l0(dbg=(), moe=True, no_attn=False):
    nc = bass.Bass("TRN2", target_bir_lowering=False)
    ctx = ExitStack()
    c = Ctx()
    c.nc = nc
    c.rec = Rec()
    c.dbg = set(dbg)
    c.ctx = ctx

    A = lambda name, shape, dt: nc.dram_tensor(name, shape, dt, kind="ExternalInput").ap()
    c.x_own = A("x_own", [NT, D], F32)
    c.x_ext = A("x_ext", [NT, D], F32)
    c.Wq = A("L0_attn_Wq", [D, D], F32)
    c.Wk = A("L0_attn_Wk", [D, D], F32)
    c.Wv = A("L0_attn_Wv", [D, D], F32)
    c.Wo = A("L0_attn_Wo", [D, D], F32)
    c.wg = A("L0_ffn_wg", [D, NE], F32)
    c.fc1_w = A("L0_ffn_fc1_w", [NE, HFF, D], F32)
    c.fc1_b = A("L0_ffn_fc1_b", [NE, HFF], F32)
    c.fc2_w = A("L0_ffn_fc2_w", [NE, HFF, D], F32)
    c.fc2_b = A("L0_ffn_fc2_b", [NE, D], F32)
    c.ln_g = A("L0_ffn_ln_gamma", [D], F32)
    c.ln_b = A("L0_ffn_ln_beta", [D], F32)
    c.ident_bf = A("ident_bf", [128, 128], BF16)
    c.ident_f32 = A("ident_f32", [128, 128], F32)
    c.triu_f32 = A("triu_f32", [128, 128], F32)
    c.iota8 = A("iota8", [128, 8], F32)
    c.cmask = A("cmask", [128, 4, 512], BF16)
    c.vmask = A("vmask", [128, 32], F32)
    c.onespfx = A("ones_prefix", [8, 1], F32)

    c.x1_out = nc.dram_tensor("x1_out", [NT, D], F32, kind="ExternalOutput").ap()

    c.fc1bf = nc.dram_tensor("fc1bf", [NE * HFF, D], BF16)
    c.acc_spill = nc.dram_tensor("acc_spill", [NT, D], F32)
    c.ag_cnt_in = nc.dram_tensor("ag_cnt_in", [1, 512], F32)
    c.ag_cnt_out = nc.dram_tensor("ag_cnt_out", [8, 512], F32)

    _alloc_common(c)
    _emit_phase0(c)
    _emit_keepw(c)
    c._p0_ev_done = c.v_ev.n
    if not no_attn:
        _emit_attn(c)
    else:
        c._spill_all = None
        c._attn_end_mm = None
    if moe:
        _emit_moe0(c, no_attn=no_attn)
    _emit_dbg(c)
    _emit_final_wait(c)

    with nc.Block() as block:
        c.rec.register(block)
    ctx.close()
    return nc


def _alloc_common(c):
    nc, ctx = c.nc, c.ctx
    E = ctx.enter_context

    c.arA = E(nc.sbuf_tensor("arA", [128, 32768], U8))
    c.arB = E(nc.sbuf_tensor("arB", [128, 32768], U8))
    c.arC = E(nc.sbuf_tensor("arC", [128, 32768], U8))
    c.arD = E(nc.sbuf_tensor("arD", [128, 65536], U8))

    c.xs = E(nc.sbuf_tensor("xs", [128, 2, 1024], F32))
    c.xnb = E(nc.sbuf_tensor("xnb", [128, 2, 1024], BF16))
    c.stats = E(nc.sbuf_tensor("stats", [128, 2, 12], F32))
    c.mv = E(nc.sbuf_tensor("mv", [128, 2, 2], F32))
    c.rstd = E(nc.sbuf_tensor("rstd", [128, 2, 1], F32))
    c.eps_t = E(nc.sbuf_tensor("eps_t", [128, 1], F32))
    c.m20 = E(nc.sbuf_tensor("m20", [128, 1], F32))
    c.ones128_f = E(nc.sbuf_tensor("ones128_f", [128, 1], F32))
    c.ones1_f = E(nc.sbuf_tensor("ones1_f", [1, 128], F32))
    c.ident_bf_sb = E(nc.sbuf_tensor("ident_bf_sb", [128, 128], BF16))
    c.ident_f_sb = E(nc.sbuf_tensor("ident_f_sb", [128, 128], F32))
    c.triu_sb = E(nc.sbuf_tensor("triu_sb", [128, 128], F32))
    c.iota8_sb = E(nc.sbuf_tensor("iota8_sb", [128, 8], F32))
    c.cmask_sb = E(nc.sbuf_tensor("cmask_sb", [128, 4, 512], BF16))
    c.vmask_sb = E(nc.sbuf_tensor("vmask_sb", [128, 32], F32))
    c.onespfx_sb = E(nc.sbuf_tensor("onespfx_sb", [8, 1], F32))
    c.ones8_sb = E(nc.sbuf_tensor("ones8_sb", [8, 1], F32))
    c.wg_sb = E(nc.sbuf_tensor("wg_sb", [128, 8, 8], BF16))
    c.p1_all = E(nc.sbuf_tensor("p1_all", [128, 16, 16], F32))
    c.mask16f = E(nc.sbuf_tensor("mask16f", [128, 16, 16], F32))
    c.g_all = E(nc.sbuf_tensor("g_all", [128, 16, 2], F32))
    c.bases_all = E(nc.sbuf_tensor("bases_all", [1, 16, 16], F32))
    c.base_acc = E(nc.sbuf_tensor("base_acc", [1, 16], F32))
    c.cnt8 = E(nc.sbuf_tensor("cnt8", [8, 16], F32))
    c.allbase = E(nc.sbuf_tensor("allbase", [1, 16], F32))
    c.w8_all = E(nc.sbuf_tensor("w8_all", [128, 16, 8], F32))
    c.wT_all = E(nc.sbuf_tensor("wT_all", [8, NT], BF16))
    c.scr16 = E(nc.sbuf_tensor("scr16", [128, 16], F32))
    c.scr8 = E(nc.sbuf_tensor("scr8", [128, 8], F32))
    c.scrrow = E(nc.sbuf_tensor("scrrow", [1, 16], F32))
    c.zrow = E(nc.sbuf_tensor("zrow", [1, 512], F32))
    c.selk = E(nc.sbuf_tensor("selk", [128, 2], F32))
    c.if2 = E(nc.sbuf_tensor("if2", [128, 2], F32))
    c.topv = E(nc.sbuf_tensor("topv", [128, 8], F32))
    c.topi = E(nc.sbuf_tensor("topi", [128, 8], U32))
    c.gam_t = E(nc.sbuf_tensor("gam_t", [128, 1024], F32))
    c.bet_t = E(nc.sbuf_tensor("bet_t", [128, 1024], F32))

    c.b1col = E(nc.sbuf_tensor("b1col", [128, 2, 8], F32))
    c.b1row = E(nc.sbuf_tensor("b1row", [8, 2, 128], F32))
    c.fc2b_sb = E(nc.sbuf_tensor("fc2b_sb", [8, 1024], BF16))
    c.D_all = E(nc.sbuf_tensor("D_all", [16, 2048], F32))
    c.ginv_sb = E(nc.sbuf_tensor("ginv_sb", [128, 16], F32))

    c.ps = [E(nc.psum_tensor(f"ps{i}", [128, 512], F32)) for i in range(8)]

    for name in ("sq", "gq", "tq", "lq0", "lq1", "t_mm", "t_tr", "v_mv", "v_xnb", "v_ev",
                 "s_rstd", "s_act", "v_r", "cc", "st", "v_aux", "t_aux", "g_aux",
                 "tw0", "tw1", "gw0", "gw1", "bw0", "bw1", "aq0", "aq1", "s_silu", "v_cmb", "x1q", "agq", "fbq", "x1s0", "x1s1",
                 "wsl0", "wsl1", "a_mm", "a_exp", "a_msk", "a_pv", "dq", "dq1", "spq", "spq1", "tl0", "tl1"):
        setattr(c, name, Sem(c.ctx, c.nc, name))


def vA(c):
    return c.arA[:, :].bitcast(BF16).rearrange("p (k t) -> p k t", k=8)


def vB(c):
    return c.arB[:, :].bitcast(BF16).rearrange("p (k t) -> p k t", k=8)


def vC(c):
    return c.arC[:, :].bitcast(BF16).rearrange("p (k t) -> p k t", k=8)


def vD_acc(c):
    return c.arD[:, :].bitcast(F32).rearrange("p (t d) -> p t d", t=16)


def vD_comb(c):
    return c.arD[:, 0:32768].bitcast(BF16).rearrange("p (t d) -> p t d", t=16)


def vD_hTq(c):
    return c.arD[:, 32768:65536].bitcast(BF16).rearrange("p (k t) -> p k t", k=8)


def vW1(c):
    return c.arB[:, :].bitcast(BF16).rearrange("p (q k h) -> p q k h", q=2, k=8)


def vW2(c):
    return c.arC[:, :].bitcast(BF16).rearrange("p (q k d) -> p q k d", q=2, k=8)


def vEP_xn0n(c):
    return c.arD[:, 32768:36864].bitcast(BF16).rearrange("p (s d) -> p s d", s=2)


def vEP_r2(c):
    return c.arD[:, 36864:45056].bitcast(F32).rearrange("p (s d) -> p s d", s=2)


def vEP_acc(c):
    return c.arD[:, 45056:53248].bitcast(F32).rearrange("p (s d) -> p s d", s=2)


def vEP_x1t(c):
    return c.arD[:, 53248:61440].bitcast(F32).rearrange("p (s d) -> p s d", s=2)




def _emit_consts(c):
    on = c.rec.on

    @on("vector")
    def _(v):
        v.memset(c.eps_t[:, :], 1e-5)
        v.memset(c.m20[:, :], -20.0)
        v.memset(c.ones128_f[:, :], 1.0)
        v.memset(c.ones1_f[:, :], 1.0)
        v.memset(c.ones8_sb[:, :], 1.0)
        v.memset(c.base_acc[:, :], 0.0)
        v.memset(c.zrow[:, :], 0.0)
        v.engine_nop().then_inc(c.v_aux.h, 1)

    t_c = c.sq.bump(16 * 9)
    ms_gate = c.v_aux.bump()
    c._ms_gate = ms_gate

    @c.rec.on("scalar")
    def _(s_, ms_gate=ms_gate):
        s_.wait_ge(c.v_aux.h, ms_gate)

    @on("sync")
    def _(s):
        s.dma_start(out=c.ident_bf_sb[:, :], in_=c.ident_bf).then_inc(c.sq.h, 16)
        s.dma_start(out=c.ident_f_sb[:, :], in_=c.ident_f32).then_inc(c.sq.h, 16)
        s.dma_start(out=c.triu_sb[:, :], in_=c.triu_f32).then_inc(c.sq.h, 16)
        s.dma_start(out=c.iota8_sb[:, :], in_=c.iota8).then_inc(c.sq.h, 16)
        s.dma_start(out=c.cmask_sb[:, :, :], in_=c.cmask).then_inc(c.sq.h, 16)
        s.dma_start(out=c.vmask_sb[:, :], in_=c.vmask).then_inc(c.sq.h, 16)
        s.dma_start(out=c.onespfx_sb[:, :], in_=c.onespfx).then_inc(c.sq.h, 16)
        s.dma_start(out=vGBROW(c)[0:1, 0:1024], in_=c.ln_g[None, :]).then_inc(c.sq.h, 16)
        s.dma_start(out=vGBROW(c)[0:1, 1024:2048], in_=c.ln_b[None, :]).then_inc(c.sq.h, 16)

    c.wg_ld = c.gq.bump(16)

    @on("gpsimd")
    def _(g):
        g.dma_start(out=c.wg_sb[:, :, :], in_=c.wg.rearrange("(k p) e -> p k e", p=128)).then_inc(c.gq.h, 16)

    mm_a = c.t_mm.bump()
    ev_a = c.v_ev.bump(2)
    mm_b = c.t_mm.bump()
    ev_b = c.v_ev.bump(2)

    @on("tensor")
    def _(t, t_c=t_c, mm_a=mm_a, ev_a=ev_a, mm_b=mm_b, ms_gate=ms_gate):
        t.wait_ge(c.v_aux.h, ms_gate)
        t.wait_ge(c.sq.h, t_c)
        t.matmul(c.ps[6][:, :], c.ones1_f[:, :], vGBROW(c)[0:1, 0:512], start=True, stop=True)
        t.matmul(c.ps[7][:, :], c.ones1_f[:, :], vGBROW(c)[0:1, 1024:1536], start=True, stop=True).then_inc(c.t_mm.h, 1)
        t.wait_ge(c.v_ev.h, ev_a)
        t.matmul(c.ps[6][:, :], c.ones1_f[:, :], vGBROW(c)[0:1, 512:1024], start=True, stop=True)
        t.matmul(c.ps[7][:, :], c.ones1_f[:, :], vGBROW(c)[0:1, 1536:2048], start=True, stop=True).then_inc(c.t_mm.h, 1)

    @on("vector")
    def _(v, mm_a=mm_a, mm_b=mm_b):
        v.wait_ge(c.t_mm.h, mm_a)
        v.tensor_copy(out=c.gam_t[:, 0:512], in_=c.ps[6][:, :]).then_inc(c.v_ev.h, 1)
        v.tensor_copy(out=c.bet_t[:, 0:512], in_=c.ps[7][:, :]).then_inc(c.v_ev.h, 1)
        v.wait_ge(c.t_mm.h, mm_b)
        v.tensor_copy(out=c.gam_t[:, 512:1024], in_=c.ps[6][:, :]).then_inc(c.v_ev.h, 1)
        v.tensor_copy(out=c.bet_t[:, 512:1024], in_=c.ps[7][:, :]).then_inc(c.v_ev.h, 1)

    c.gb_done = ev_b


def _emit_ln_tile(c, src_ap, tt, dstT, routing, extra_gates=None):
    on = c.rec.on
    p = c._lnidx % 2
    c._lnidx += 1

    need_xs_free = c._xnb_cons[p]
    need_xnb_free = c._xnb_tr[p]
    lq = c.lq0 if p == 0 else c.lq1
    ld_t = lq.bump(16)

    @on("sync")
    def _(s, need_xs_free=need_xs_free, lq=lq, extra_gates=extra_gates):
        if extra_gates:
            for sem_h, val in extra_gates:
                s.wait_ge(sem_h, val)
        if need_xs_free is not None:
            s.wait_ge(c.v_xnb.h, need_xs_free)
        s.dma_start(out=c.xs[:, p, :], in_=src_ap[tt * 128:(tt + 1) * 128, :]).then_inc(lq.h, 16)

    mv_t = c.v_mv.bump()

    @on("vector")
    def _(v, ld_t=ld_t, lq=lq):
        v.wait_ge(lq.h, ld_t)
        v.bn_stats(c.stats[:, p, 0:6], c.xs[:, p, 0:512])
        v.bn_stats(c.stats[:, p, 6:12], c.xs[:, p, 512:1024])
        v.drain()
        v.bn_aggr(c.mv[:, p, :], c.stats[:, p, :]).then_inc(c.v_mv.h, 1)

    rs_t = c.s_rstd.bump()

    @on("scalar")
    def _(s, mv_t=mv_t):
        s.wait_ge(c.v_mv.h, mv_t)
        s.activation(out=c.rstd[:, p, :], in_=c.mv[:, p, 1:2], func=AF.Ln, bias=c.eps_t[:, 0:1], scale=1.0)
        s.drain()
        s.activation(out=c.rstd[:, p, :], in_=c.rstd[:, p, :], func=AF.Exp, bias=0.0, scale=-0.5).then_inc(c.s_rstd.h, 1)

    xnb_t = c.v_xnb.bump()
    c._xnb_cons[p] = xnb_t

    @on("vector")
    def _(v, rs_t=rs_t, need_xnb_free=need_xnb_free):
        v.wait_ge(c.s_rstd.h, rs_t)
        if need_xnb_free is not None:
            v.wait_ge(c.t_tr.h, need_xnb_free)
        v.tensor_scalar(out=c.xnb[:, p, :], in0=c.xs[:, p, :], scalar1=c.mv[:, p, 0:1],
                        scalar2=c.rstd[:, p, 0:1], op0=ALU.subtract, op1=ALU.mult).then_inc(c.v_xnb.h, 1)

    trs = [c.t_tr.bump() for _ in range(8)]
    c._xnb_tr[p] = trs[-1]
    evs = [c.v_ev.bump() for _ in range(8)]
    pre = (c._tr_ev_pre[0], c._tr_ev_pre[1])
    c._tr_ev_pre = [evs[6], evs[7]]

    @on("tensor")
    def _(t, xnb_t=xnb_t, pre=pre, evs=evs):
        t.wait_ge(c.v_xnb.h, xnb_t)
        for j in range(8):
            bank = 4 + (j % 2)
            gate = pre[j % 2] if j < 2 else evs[j - 2]
            if gate is not None:
                t.wait_ge(c.v_ev.h, gate)
            pv = c.ps[bank][:, :].bitcast(BF16)
            t.transpose(pv[:, 0:128], c.xnb[:, p, j * 128:(j + 1) * 128], c.ident_bf_sb[:, :]).then_inc(c.t_tr.h, 1)

    @on("vector")
    def _(v, trs=trs, evs=evs):
        for j in range(8):
            bank = 4 + (j % 2)
            pv = c.ps[bank][:, :].bitcast(BF16)
            v.wait_ge(c.t_tr.h, trs[j])
            v.tensor_copy(out=dstT(c)[:, j, tt * 128:(tt + 1) * 128], in_=pv[:, 0:128]).then_inc(c.v_ev.h, 1)

    if not routing:
        return

    lastev = evs[-1]
    rt_gate = c._rt_prev
    mm_t = c.t_mm.bump()

    @on("tensor")
    def _(t, lastev=lastev, rt_gate=rt_gate, wgld=c.wg_ld):
        t.wait_ge(c.v_ev.h, lastev)
        t.wait_ge(c.gq.h, wgld)
        if rt_gate is not None:
            t.wait_ge(c.v_r.h, rt_gate)
        for k in range(8):
            mm = t.matmul(c.ps[3][:, 0:8], vA(c)[:, k, tt * 128:(tt + 1) * 128], c.wg_sb[:, k, :],
                          start=(k == 0), stop=(k == 7))
        mm.then_inc(c.t_mm.h, 1)

    vr1 = c.v_r.bump()

    @on("vector")
    def _(v, mm_t=mm_t):
        v.wait_ge(c.t_mm.h, mm_t)
        v.max(c.topv[:, :], c.ps[3][:, 0:8])
        v.drain()
        v.max_index(c.topi[:, :], c.topv[:, :], c.ps[3][:, 0:8])
        v.tensor_tensor(out=c.if2[:, 0:1], in0=c.topv[:, 0:1], in1=c.topv[:, 1:2], op=ALU.subtract)
        v.drain()
        v.tensor_copy(out=c.if2[:, 1:2], in_=c.topi[:, 1:2])
        v.tensor_copy(out=c.scr16[:, 0:1], in_=c.topi[:, 0:1])
        v.drain()
        v.tensor_single_scalar(out=c.mask16f[:, tt, 0:8], in_=c.iota8_sb[:, :], scalar=c.scr16[:, 0:1], op=ALU.is_equal)
        v.tensor_single_scalar(out=c.mask16f[:, tt, 8:16], in_=c.iota8_sb[:, :], scalar=c.if2[:, 1:2], op=ALU.is_equal).then_inc(c.v_r.h, 1)

    sg_t = c.s_act.bump()

    @on("scalar")
    def _(s, vr1=vr1):
        s.wait_ge(c.v_r.h, vr1)
        s.activation(out=c.g_all[:, tt, 0:1], in_=c.if2[:, 0:1], func=AF.Sigmoid, bias=0.0, scale=1.0)
        s.drain()
        s.activation(out=c.g_all[:, tt, 1:2], in_=c.if2[:, 0:1], func=AF.Sigmoid, bias=0.0, scale=-1.0).then_inc(c.s_act.h, 1)

    if routing == "l1":
        vr2 = c.v_r.bump()
        c._rt_prev = vr2

        @on("vector")
        def _(v, sg_t=sg_t):
            v.wait_ge(c.s_act.h, sg_t)
            v.engine_nop().then_inc(c.v_r.h, 1)
        return

    mm2 = c.t_mm.bump()

    @on("tensor")
    def _(t, vr1=vr1):
        t.wait_ge(c.v_r.h, vr1)
        t.matmul(c.ps[2][:, 0:16], c.triu_sb[:, :], c.mask16f[:, tt, :], start=True, stop=True)
        t.matmul(c.ps[1][0:1, 0:16], c.ones128_f[:, :], c.mask16f[:, tt, :], start=True, stop=True).then_inc(c.t_mm.h, 1)

    vr2 = c.v_r.bump()
    c._rt_prev = vr2

    @on("vector")
    def _(v, mm2=mm2, sg_t=sg_t):
        v.wait_ge(c.t_mm.h, mm2)
        v.wait_ge(c.s_act.h, sg_t)
        v.tensor_copy(out=c.p1_all[:, tt, :], in_=c.ps[2][:, 0:16])
        v.tensor_copy(out=c.bases_all[:, tt, :], in_=c.base_acc[:, :])
        v.drain()
        v.tensor_tensor(out=c.base_acc[:, :], in0=c.base_acc[:, :], in1=c.ps[1][0:1, 0:16], op=ALU.add).then_inc(c.v_r.h, 1)


def _emit_phase0(c):
    on = c.rec.on
    c._lnidx = 0
    c._xnb_cons = [None, None]
    c._xnb_tr = [None, None]
    c._tr_ev_pre = [None, None]
    c._rt_prev = None

    _emit_consts(c)

    for tt in range(TT):
        _emit_ln_tile(c, c.x_own, tt, vA, routing='l0')
    for tt in range(TT):
        _emit_ln_tile(c, c.x_ext, tt, vB, routing=False)

    last_r = c._rt_prev
    g1 = c.agq.bump(16)
    c.agq.bump(16)
    cc1 = c.cc.bump(16 * NCORE if NO_CC else 1)
    g2 = c.agq.bump(16)

    @on("gpsimd")
    def _(g, last_r=last_r, g1=g1, cc1=cc1):
        g.wait_ge(c.v_r.h, last_r)
        g.dma_start(out=c.ag_cnt_in.ap(), in_=c.zrow[:, :]).then_inc(c.agq.h, 16)
        g.wait_ge(c.agq.h, g1)
        g.dma_start(out=c.ag_cnt_in.ap()[0:1, 0:16], in_=c.base_acc[:, :]).then_inc(c.agq.h, 16)
        g.wait_ge(c.agq.h, g1 + 16)
        if NO_CC:
            for r in range(NCORE):
                g.dma_start(out=c.ag_cnt_out.ap()[r:r + 1, :], in_=c.ag_cnt_in.ap()).then_inc(c.cc.h, 16)
        else:
            g.collective_compute(
                "AllGather", ALU.bypass,
                replica_groups=[list(range(NCORE))],
                ins=[c.ag_cnt_in.ap().opt()],
                outs=[c.ag_cnt_out.ap().opt()],
            ).then_inc(c.cc.h, 1)
        g.wait_ge(c.cc.h, cc1)
        g.dma_start(out=c.cnt8[:, :], in_=c.ag_cnt_out.ap()[:, 0:16]).then_inc(c.agq.h, 16)

    mm3 = c.t_mm.bump()

    @on("tensor")
    def _(t, g2=g2):
        t.wait_ge(c.agq.h, g2)
        t.matmul(c.ps[2][0:1, 0:16], c.onespfx_sb[:, :], c.cnt8[:, :], start=True, stop=True)
        t.matmul(c.ps[1][0:1, 0:16], c.ones8_sb[:, :], c.cnt8[:, :], start=True, stop=True).then_inc(c.t_mm.h, 1)

    ab_t = c.v_r.bump()
    c._allbase_done = ab_t

    @on("vector")
    def _(v, mm3=mm3):
        v.wait_ge(c.t_mm.h, mm3)
        v.memset(c.scrrow[0:1, 0:16], 0.0)
        v.drain()
        v.tensor_copy(out=c.scrrow[0:1, 8:16], in_=c.ps[1][0:1, 0:8])
        v.drain()
        v.tensor_tensor(out=c.allbase[:, :], in0=c.ps[2][0:1, 0:16], in1=c.scrrow[0:1, 0:16], op=ALU.add).then_inc(c.v_r.h, 1)



def _emit_keepw(c):
    on = c.rec.on
    kw_prev = None
    ev_prev = None
    for tt in range(TT):
        va = c.v_aux.bump()

        @on("vector")
        def _(v, tt=tt, first=(tt == 0), kw_prev=kw_prev):
            if first:
                v.wait_ge(c.v_r.h, c._allbase_done)
            v.tensor_tensor(out=c.scrrow[0:1, 0:16], in0=c.allbase[0:1, :], in1=c.bases_all[0:1, tt, :], op=ALU.add).then_inc(c.v_aux.h, 1)

        mm = c.t_mm.bump()

        @on("tensor")
        def _(t, va=va, kw_prev=kw_prev):
            t.wait_ge(c.v_aux.h, va)
            if kw_prev is not None:
                t.wait_ge(c.v_r.h, kw_prev)
            t.matmul(c.ps[3][:, 0:16], c.ones1_f[:, :], c.scrrow[0:1, 0:16], start=True, stop=True).then_inc(c.t_mm.h, 1)

        kw = c.v_r.bump()
        kw_prev = kw

        @on("vector")
        def _(v, tt=tt, mm=mm):
            v.wait_ge(c.t_mm.h, mm)
            v.tensor_tensor(out=c.scr16[:, 0:16], in0=c.p1_all[:, tt, :], in1=c.ps[3][:, 0:16], op=ALU.add)
            v.drain()
            for k in range(2):
                v.tensor_tensor(out=c.scr8[:, 0:8], in0=c.scr16[:, k * 8:(k + 1) * 8],
                                in1=c.mask16f[:, tt, k * 8:(k + 1) * 8], op=ALU.mult)
                v.drain()
                v.reduce_sum(c.selk[:, k:k + 1], c.scr8[:, 0:8], axis=mybir.AxisListType.X)
                v.drain()
            v.tensor_single_scalar(out=c.selk[:, 0:2], in_=c.selk[:, 0:2], scalar=float(CAP) + 0.5, op=ALU.is_lt)
            v.drain()
            v.tensor_tensor(out=c.selk[:, 0:2], in0=c.selk[:, 0:2], in1=c.g_all[:, tt, :], op=ALU.mult)
            v.drain()
            v.tensor_single_scalar(out=c.scr16[:, 0:8], in_=c.mask16f[:, tt, 8:16], scalar=c.selk[:, 1:2], op=ALU.mult)
            v.drain()
            v.scalar_tensor_tensor(out=c.w8_all[:, tt, :], in0=c.mask16f[:, tt, 0:8], scalar=c.selk[:, 0:1],
                                   in1=c.scr16[:, 0:8], op0=ALU.mult, op1=ALU.add).then_inc(c.v_r.h, 1)

        trw = c.t_tr.bump()

        @on("tensor")
        def _(t, tt=tt, kw=kw, ev_prev=ev_prev):
            t.wait_ge(c.v_r.h, kw)
            if ev_prev is not None:
                t.wait_ge(c.v_ev.h, ev_prev)
            t.transpose(c.ps[2][0:8, 0:128], c.w8_all[:, tt, :], c.ident_f_sb[:, :]).then_inc(c.t_tr.h, 1)

        evw = c.v_ev.bump()
        ev_prev = evw

        @on("vector")
        def _(v, tt=tt, trw=trw):
            v.wait_ge(c.t_tr.h, trw)
            v.tensor_copy(out=c.wT_all[0:8, tt * 128:(tt + 1) * 128], in_=c.ps[2][0:8, 0:128]).then_inc(c.v_ev.h, 1)

    c._w8_done = kw_prev
    c._wT_done = ev_prev






def vAT_qT2(c):
    return c.arD[:, 0:4096].bitcast(BF16)


def vAT_kT2(c):
    return c.arD[:, 4096:12288].bitcast(BF16)


def vAT_vT2(c):
    return c.arD[:, 12288:20480].bitcast(BF16)


def vAT_vaug(c):
    return c.arD[:, 20480:28928].bitcast(BF16).rearrange("p (k a) -> p k a", k=32)

def vAT_pt(c):
    return c.arD[:, 28928:31040].bitcast(BF16).rearrange("p (s a) -> p s a", s=2)

def vWSL(c):
    return c.arD[:, 32768:45056].bitcast(BF16).rearrange("p (q w k a) -> p q w k a", q=2, w=3, k=8)


def vWROW(c):
    return c.arD[:, 45056:57344].bitcast(BF16).rearrange("p (q w a) -> p q w a", q=2, w=3)


def vGBROW(c):
    return c.arD[0:1, 57344:65536].bitcast(F32)


def vDROW(c):
    return c.arD[0:1, 57344:61440].bitcast(F32).rearrange("p (s a) -> p s a", s=2)


def vWoT(c):
    return c.arB[:, 0:16384].bitcast(BF16).rearrange("p (k a) -> p k a", k=8)





def _emit_attn_pair(c, hp):
    on = c.rec.on
    par = hp % 2
    wsl = c.wsl0 if par == 0 else c.wsl1

    wload = [wsl.bump(16) for _ in range(3)]
    slot_gate = c._wsl_gate[par]

    @on("gpsimd")
    def _(g, hp=hp, par=par, wsl=wsl, slot_gate=slot_gate):
        if slot_gate is not None:
            g.wait_ge(c.t_mm.h, slot_gate)
        for w, W in enumerate((c.Wq, c.Wk, c.Wv)):
            g.dma_start(out=vWROW(c)[:, par, w, :], in_=W[hp * 128:(hp + 1) * 128, :]).then_inc(wsl.h, 16)

    trs = [c.t_tr.bump() for _ in range(24)]
    evs = [c.v_ev.bump() for _ in range(24)]
    pre = tuple(c._tb_pre)
    c._tb_pre = [evs[22], evs[23]]

    @on("tensor")
    def _(t, par=par, wsl=wsl, wload=wload, evs=evs, pre=pre,
          exp_gate=(c.a_exp.n if hp > 0 else None)):
        t.wait_ge(wsl.h, wload[2])
        if exp_gate is not None:
            t.wait_ge(c.a_exp.h, exp_gate)
        n = 0
        for w in range(3):
            for dblk in range(8):
                bank = 4 + (n % 2)
                gate = pre[n % 2] if n < 2 else evs[n - 2]
                if gate is not None:
                    t.wait_ge(c.v_ev.h, gate)
                pv = c.ps[bank][:, :].bitcast(BF16)
                t.transpose(pv[:, 0:128], vWROW(c)[:, par, w, dblk * 128:(dblk + 1) * 128],
                            c.ident_bf_sb[:, :]).then_inc(c.t_tr.h, 1)
                n += 1

    @on("vector")
    def _(v, par=par, trs=trs):
        n = 0
        for w in range(3):
            for dblk in range(8):
                bank = 4 + (n % 2)
                pv = c.ps[bank][:, :].bitcast(BF16)
                v.wait_ge(c.t_tr.h, trs[n])
                v.tensor_copy(out=vWSL(c)[:, par, w, dblk, :], in_=pv[:, 0:128]).then_inc(c.v_ev.h, 1)
                n += 1

    wsl_ev = evs[-1]

    specs = []
    for w, nchunk in ((0, 4), (1, 8), (2, 8)):
        for ch in range(nchunk):
            specs.append((w, ch))
    gen_sv0 = c.s_act.n
    for n, (w, ch) in enumerate(specs):
        mm = c.t_mm.bump()
        sv = c.s_act.bump()

        @on("tensor")
        def _(t, w=w, ch=ch, n=n, par=par, wsl_ev=wsl_ev,
              pg=(c._gen_ps_gate[n % 2]), bank=(n % 2),
              eg=(c.a_exp.n if (n < 2 and hp > 0) else None)):
            if n == 0:
                t.wait_ge(c.v_ev.h, wsl_ev)
            if eg is not None:
                t.wait_ge(c.a_exp.h, eg)
            if pg is not None:
                t.wait_ge(c.s_act.h, pg)
            src = vA if (w == 0 or ch < 4) else vB
            cc_ = ch if (w == 0 or ch < 4) else ch - 4
            for k in range(8):
                mmh = t.matmul(c.ps[bank][:, :], vWSL(c)[:, par, w, k, :],
                               src(c)[:, k, cc_ * 512:(cc_ + 1) * 512], start=(k == 0), stop=(k == 7))
            mmh.then_inc(c.t_mm.h, 1)

        @on("scalar")
        def _(s_, w=w, ch=ch, mm=mm, bank=(n % 2)):
            s_.wait_ge(c.t_mm.h, mm)
            dst = (vAT_qT2, vAT_kT2, vAT_vT2)[w]
            s_.activation(out=dst(c)[:, ch * 512:(ch + 1) * 512], in_=c.ps[bank][:, :],
                          func=AF.Copy, bias=0.0, scale=(0.125 if w == 0 else 1.0)).then_inc(c.s_act.h, 1)

        c._gen_ps_gate[n % 2] = sv

    gen_end_sv = c.s_act.n
    c._wsl_gate[par] = c.t_mm.n

    vtr = [c.t_tr.bump() for _ in range(32)]
    vev = [c.v_aux.bump() for _ in range(32)]
    pre2 = tuple(c._tb_pre)

    @on("tensor")
    def _(t, vtr=vtr, vev=vev, gen_end_sv=gen_end_sv, pre2=pre2):
        t.wait_ge(c.s_act.h, gen_end_sv)
        for kt in range(32):
            bank = 4 + (kt % 2)
            if kt < 2:
                gate = pre2[kt % 2]
                if gate is not None:
                    t.wait_ge(c.v_ev.h, gate)
            else:
                t.wait_ge(c.v_aux.h, vev[kt - 2])
            pv = c.ps[bank][:, :].bitcast(BF16)
            t.transpose(pv[:, 0:128], vAT_vT2(c)[:, kt * 128:(kt + 1) * 128],
                        c.ident_bf_sb[:, :]).then_inc(c.t_tr.h, 1)

    c._tb_pre = [None, None]
    c._tb_pre_aux = [vev[30], vev[31]]

    @on("vector")
    def _(v, vtr=vtr):
        for kt in range(32):
            bank = 4 + (kt % 2)
            pv = c.ps[bank][:, :].bitcast(BF16)
            v.wait_ge(c.t_tr.h, vtr[kt])
            v.tensor_single_scalar(out=vAT_vaug(c)[:, kt, 0:64], in_=pv[:, 0:64],
                                   scalar=c.vmask_sb[:, kt:kt + 1], op=ALU.mult)
            v.tensor_single_scalar(out=vAT_vaug(c)[:, kt, 66:130], in_=pv[:, 64:128],
                                   scalar=c.vmask_sb[:, kt:kt + 1], op=ALU.mult)
            v.tensor_copy(out=vAT_vaug(c)[:, kt, 64:65], in_=c.vmask_sb[:, kt:kt + 1])
            v.tensor_copy(out=vAT_vaug(c)[:, kt, 130:131], in_=c.vmask_sb[:, kt:kt + 1]).then_inc(c.v_aux.h, 1)

    vaug_ev = c.v_aux.n

    for h01 in range(2):
        for i in range(4):
            kts = list(range(0, 4 * i + 4)) + list(range(16, 32))
            nk = len(kts)
            pv_gate = c._pv_evac_gate[h01]
            mms, exs, mks, ready = [], [], [], []
            for idx, kt in enumerate(kts):
                diag = (4 * i <= kt <= 4 * i + 3)
                mm = c.a_mm.bump()
                mms.append(mm)

                @on("tensor")
                def _(t, kt=kt, idx=idx, i=i, h01=h01, b=(idx % 2), diag=diag,
                      vg=(vaug_ev if idx == 0 else None),
                      sg=(gen_end_sv if idx < 2 else None),
                      eg=(c.a_exp.n - 1 if idx >= 2 else (c.a_exp.n if c.a_exp.n else None)),
                      pveg=(pv_gate if idx == 0 else None)):
                    if vg is not None:
                        t.wait_ge(c.v_aux.h, vg)
                    if sg is not None:
                        t.wait_ge(c.s_act.h, sg)
                    if pveg is not None:
                        t.wait_ge(c.s_act.h, pveg)
                    if eg is not None:
                        t.wait_ge(c.a_exp.h, eg)
                    t.matmul(c.ps[b][:, :], vAT_kT2(c)[h01 * 64:(h01 + 1) * 64, kt * 128:(kt + 1) * 128],
                             vAT_qT2(c)[h01 * 64:(h01 + 1) * 64, i * 512:(i + 1) * 512],
                             start=True, stop=True).then_inc(c.a_mm.h, 1)

                ex = c.a_exp.bump()
                exs.append(ex)

                @on("scalar")
                def _(s_, idx=idx, b=(idx % 2), mm=mm,
                      pvg=(c.a_pv.n - 1 if idx >= 2 else (c._pt_gate[idx % 2]))):
                    s_.wait_ge(c.a_mm.h, mm)
                    if pvg is not None:
                        s_.wait_ge(c.a_pv.h, pvg)
                    s_.activation(out=vAT_pt(c)[:, idx % 2, 0:512], in_=c.ps[idx % 2][:, :],
                                  func=AF.Exp, bias=c.m20[:, 0:1], scale=1.0).then_inc(c.a_exp.h, 1)

                if diag:
                    mk = c.a_msk.bump()
                    mks.append((idx, mk))

                    @on("vector")
                    def _(v, kt=kt, i=i, idx=idx, ex=ex):
                        v.wait_ge(c.a_exp.h, ex)
                        v.tensor_tensor(out=vAT_pt(c)[:, idx % 2, 0:512], in0=vAT_pt(c)[:, idx % 2, 0:512],
                                        in1=c.cmask_sb[:, kt - 4 * i, :], op=ALU.mult).then_inc(c.a_msk.h, 1)
                    ready.append(("m", mk))
                else:
                    ready.append(("e", ex))

                pvn = c.a_pv.bump()

                @on("tensor")
                def _(t, kt=kt, idx=idx, h01=h01, b=(idx % 2), nk=nk, r=ready[idx]):
                    t.wait_ge((c.a_msk.h if r[0] == "m" else c.a_exp.h), r[1])
                    t.matmul(c.ps[2 + h01][0:65, :], vAT_vaug(c)[:, kt, h01 * 66:h01 * 66 + 65],
                             vAT_pt(c)[:, b, 0:512], start=(idx == 0), stop=(idx == nk - 1)).then_inc(c.a_pv.h, 1)

            pvend = c.a_pv.n
            c._pt_gate = [pvend, pvend]

            dslot = c._dslot % 2
            dqs = c.dq if dslot == 0 else c.dq1
            dgate = c._drow_gate[dslot]
            sa1 = c.s_act.bump()
            sa2 = c.s_act.bump()
            dq_t = dqs.bump(16)
            c._drow_gate[dslot] = dq_t
            c._dslot += 1

            @on("scalar")
            def _(s_, hp=hp, h01=h01, i=i, pvend=pvend, dslot=dslot, dgate=dgate, dqs=dqs):
                s_.wait_ge(c.a_pv.h, pvend)
                if dgate is not None:
                    s_.wait_ge(dqs.h, dgate)
                s_.activation(out=vC(c)[h01 * 64:(h01 + 1) * 64, hp, i * 512:(i + 1) * 512],
                              in_=c.ps[2 + h01][0:64, :], func=AF.Copy, bias=0.0, scale=1.0).then_inc(c.s_act.h, 1)
                s_.activation(out=vDROW(c)[0:1, dslot, :], in_=c.ps[2 + h01][64:65, :],
                              func=AF.Copy, bias=0.0, scale=1.0).then_inc(c.s_act.h, 1)

            c._pv_evac_gate[h01] = c.s_act.n

            @on("gpsimd")
            def _(g, hp=hp, h01=h01, i=i, sa2=sa2, dslot=dslot, dqs=dqs):
                g.wait_ge(c.s_act.h, sa2)
                g.dma_start(out=c.D_all[2 * hp + h01:2 * hp + h01 + 1, i * 512:(i + 1) * 512],
                            in_=vDROW(c)[0:1, dslot, :]).then_inc(dqs.h, 16)


def _emit_attn(c):
    on = c.rec.on
    c._wsl_gate = [None, None]
    c._tb_pre = [c._tr_ev_pre[0], c._tr_ev_pre[1]]
    c._tb_pre_aux = [None, None]
    c._gen_ps_gate = [None, None]
    c._pv_evac_gate = [None, None]
    c._pt_gate = [None, None]
    c._drow_gate = [None, None]
    c._dslot = 0

    for hp in range(8):
        _emit_attn_pair(c, hp)

    dq_all = c.dq.n
    dq1_all = c.dq1.n
    rec_v = c.v_aux.bump()

    @on("vector")
    def _(v, dq_all=dq_all, dq1_all=dq1_all):
        v.wait_ge(c.dq.h, dq_all)
        v.wait_ge(c.dq1.h, dq1_all)
        v.reciprocal(out=c.D_all[0:16, :], in_=c.D_all[0:16, :])
        v.drain()
        v.engine_nop().then_inc(c.v_aux.h, 1)

    attn_exp_all = c.a_exp.n
    attn_sact_all = c.s_act.n
    prev_ev = None
    for ob in range(8):
        ld = c.wsl0.bump(16)

        @on("gpsimd")
        def _(g, ob=ob, prev_ev=prev_ev, gate=(c._wsl_gate[1] if ob == 0 else None)):
            if gate is not None:
                g.wait_ge(c.t_mm.h, gate)
            if prev_ev is not None:
                g.wait_ge(c.v_ev.h, prev_ev)
            g.dma_start(out=vWROW(c)[:, ob % 2, 0, :], in_=c.Wo[ob * 128:(ob + 1) * 128, :]).then_inc(c.wsl0.h, 16)

        trs = [c.t_tr.bump() for _ in range(8)]
        evs = [c.v_ev.bump() for _ in range(8)]
        pre = tuple(c._tb_pre_aux) if ob == 0 else (None, None)

        @on("tensor")
        def _(t, ob=ob, ld=ld, trs=trs, evs=evs, pre=pre, prev_ev=prev_ev):
            t.wait_ge(c.wsl0.h, ld)
            for ab in range(8):
                bank = 4 + (ab % 2)
                if ab < 2:
                    if ob == 0:
                        if pre[ab] is not None:
                            t.wait_ge(c.v_aux.h, pre[ab])
                    elif prev_ev is not None:
                        t.wait_ge(c.v_ev.h, prev_ev)
                else:
                    t.wait_ge(c.v_ev.h, evs[ab - 2])
                pv = c.ps[bank][:, :].bitcast(BF16)
                t.transpose(pv[:, 0:128], vWROW(c)[:, ob % 2, 0, ab * 128:(ab + 1) * 128],
                            c.ident_bf_sb[:, :]).then_inc(c.t_tr.h, 1)

        @on("vector")
        def _(v, ob=ob, trs=trs):
            for ab in range(8):
                bank = 4 + (ab % 2)
                pv = c.ps[bank][:, :].bitcast(BF16)
                v.wait_ge(c.t_tr.h, trs[ab])
                v.tensor_copy(out=vWoT(c)[:, ab, ob * 128:(ob + 1) * 128], in_=pv[:, 0:128]).then_inc(c.v_ev.h, 1)

        prev_ev = evs[-1]

    wo_ev_all = c.v_ev.n

    c._ginv_pre = None
    c._wo_ps_gate = [None, None]
    c._xres_gate = [None, None]
    c._spill_gate = [None, None]
    xnb_all = c.v_xnb.n
    for tt in range(TT):
        p = tt % 2
        gtr = c.t_tr.bump()
        gev = c.v_aux.bump()

        @on("tensor")
        def _(t, tt=tt, gtr=gtr, rec_v=rec_v, first=(tt == 0), gpre=c._ginv_pre, wo_ev_all=wo_ev_all):
            if first:
                t.wait_ge(c.v_aux.h, rec_v)
                t.wait_ge(c.v_ev.h, wo_ev_all)
            if gpre is not None:
                t.wait_ge(c.v_aux.h, gpre)
            t.transpose(c.ps[5][0:128, 0:16], c.D_all[0:16, tt * 128:(tt + 1) * 128],
                        c.ident_f_sb[0:16, 0:16]).then_inc(c.t_tr.h, 1)

        c._ginv_pre = gev

        @on("vector")
        def _(v, gtr=gtr):
            v.wait_ge(c.t_tr.h, gtr)
            v.tensor_copy(out=c.ginv_sb[:, :], in_=c.ps[5][0:128, 0:16]).then_inc(c.v_aux.h, 1)

        lq = c.lq0 if p == 0 else c.lq1
        xld = lq.bump(16)

        @on("sync")
        def _(s_, tt=tt, p=p, lq=lq, gate=c._xres_gate[p], xg=(xnb_all if tt < 2 else None)):
            if xg is not None:
                s_.wait_ge(c.v_xnb.h, xg)
            if gate is not None:
                s_.wait_ge(c.v_aux.h, gate)
            s_.dma_start(out=c.xs[:, p, :], in_=c.x_own[tt * 128:(tt + 1) * 128, :]).then_inc(lq.h, 16)

        for oc in range(2):
            for hd in range(16):
                b = hd % 2
                mm = c.t_mm.bump()

                @on("tensor")
                def _(t, tt=tt, oc=oc, hd=hd, b=b, first=(tt == 0 and oc == 0 and hd < 2),
                      wo_ev_all=wo_ev_all, sact=attn_sact_all, aexp=attn_exp_all,
                      wog=c._wo_ps_gate[b]):
                    if first:
                        t.wait_ge(c.v_ev.h, wo_ev_all)
                        t.wait_ge(c.s_act.h, sact)
                        t.wait_ge(c.a_exp.h, aexp)
                    if wog is not None:
                        t.wait_ge(c.v_cmb.h, wog)
                    t.matmul(c.ps[b][:, :], vC(c)[(hd % 2) * 64:(hd % 2) * 64 + 64, hd // 2, tt * 128:(tt + 1) * 128],
                             vWoT(c)[(hd % 2) * 64:(hd % 2) * 64 + 64, hd // 2, oc * 512:(oc + 1) * 512],
                             start=True, stop=True).then_inc(c.t_mm.h, 1)

                cm = c.v_cmb.bump()
                c._wo_ps_gate[b] = cm

                @on("vector")
                def _(v, tt=tt, oc=oc, hd=hd, b=b, mm=mm, gev=gev):
                    v.wait_ge(c.t_mm.h, mm)
                    if hd == 0:
                        v.wait_ge(c.v_aux.h, gev)
                        v.tensor_single_scalar(out=vD_acc(c)[:, tt, oc * 512:(oc + 1) * 512], in_=c.ps[b][:, :],
                                               scalar=c.ginv_sb[:, 0:1], op=ALU.mult)
                        v.drain()
                        v.engine_nop().then_inc(c.v_cmb.h, 1)
                    else:
                        v.scalar_tensor_tensor(out=vD_acc(c)[:, tt, oc * 512:(oc + 1) * 512], in0=c.ps[b][:, :],
                                               scalar=c.ginv_sb[:, hd:hd + 1],
                                               in1=vD_acc(c)[:, tt, oc * 512:(oc + 1) * 512],
                                               op0=ALU.mult, op1=ALU.add)
                        v.drain()
                        v.engine_nop().then_inc(c.v_cmb.h, 1)

        va = c.v_aux.bump()
        c._xres_gate[p] = va
        spqs = c.spq if p == 0 else c.spq1
        sp = spqs.bump(16)

        @on("vector")
        def _(v, tt=tt, p=p, xld=xld, lq=lq, spg=c._spill_gate[p], spqs=spqs):
            v.wait_ge(lq.h, xld)
            if spg is not None:
                v.wait_ge(spqs.h, spg)
            v.tensor_tensor(out=vD_acc(c)[:, tt, :], in0=vD_acc(c)[:, tt, :], in1=c.xs[:, p, :],
                            op=ALU.add).then_inc(c.v_aux.h, 1)

        c._spill_gate[p] = sp

        @on("gpsimd")
        def _(g, tt=tt, va=va, spqs=spqs):
            g.wait_ge(c.v_aux.h, va)
            g.dma_start(out=c.acc_spill.ap()[tt * 128:(tt + 1) * 128, :], in_=vD_acc(c)[:, tt, :]).then_inc(spqs.h, 16)

    c._spill_all = (c.spq.n, c.spq1.n)
    c._attn_end_mm = c.t_mm.n



def _emit_moe0_prep(c):
    on = c.rec.on
    c._gcast = Sem(c.ctx, c.nc, "gcast")
    casts = [c._gcast.bump(16) for _ in range(NE)]
    fb = c.fbq.bump(16)
    c._fc2b_ld = fb

    @on("gpsimd")
    def _(g, casts=casts, mb2=c.m_b2):
        for e in range(NE):
            if e > 0:
                g.wait_ge(c._gcast.h, casts[e - 1])
            g.dma_start(out=c.fc1bf.ap()[e * HFF:(e + 1) * HFF, :], in_=c.fc1_w[e]).then_inc(c._gcast.h, 16)
        g.dma_start(out=c.fc2b_sb[:, :], in_=mb2).then_inc(c.fbq.h, 16)


def _emit_moe0_combinit(c):
    on = c.rec.on
    comb = vD_comb
    prev_ci = None
    wT_done = c._wT_done
    fc2b_ld = c._fc2b_ld
    for tt in range(TT):
        mm = c.t_mm.bump()

        @on("tensor")
        def _(t, tt=tt, first=(tt == 0), prev_ci=prev_ci, wT_done=wT_done, fc2b_ld=fc2b_ld,
              spill=(c._spill_all if tt == 0 else None)):
            if first:
                t.wait_ge(c.v_ev.h, wT_done)
                t.wait_ge(c.fbq.h, fc2b_ld)
                if spill is not None:
                    t.wait_ge(c.spq.h, spill[0])
                    t.wait_ge(c.spq1.h, spill[1])
            else:
                t.wait_ge(c.v_cmb.h, prev_ci)
            t.matmul(c.ps[4][:, :], c.wT_all[0:8, tt * 128:(tt + 1) * 128], c.fc2b_sb[0:8, 0:512],
                     start=True, stop=True)
            t.matmul(c.ps[5][:, :], c.wT_all[0:8, tt * 128:(tt + 1) * 128], c.fc2b_sb[0:8, 512:1024],
                     start=True, stop=True).then_inc(c.t_mm.h, 1)

        ci = c.v_cmb.bump()
        prev_ci = ci

        @on("vector")
        def _(v, tt=tt, mm=mm):
            v.wait_ge(c.t_mm.h, mm)
            v.tensor_copy(out=comb(c)[:, tt, 0:512], in_=c.ps[4][:, :])
            v.tensor_copy(out=comb(c)[:, tt, 512:1024], in_=c.ps[5][:, :]).then_inc(c.v_cmb.h, 1)

    c._comb_init_done = c.v_cmb.n


def _emit_moe0_quarter(c, e, q, qi):
    on = c.rec.on
    qp = qi % 2
    if c.mcfg["l1"]:
        tw = c.tl0 if qp == 0 else c.tl1
    else:
        tw = c.tw0 if qp == 0 else c.tw1
    gw = c.gw0 if qp == 0 else c.gw1
    bw = c.bw0 if qp == 0 else c.bw1

    slot_free = c._q_slot_free[qp]

    l1 = c.mcfg["l1"]
    if l1:
        tw_t = tw.bump(16 * 8)

    if not l1:
        wev_last = None
        for hch in range(8):
            sp = hch % 2
            tws = c.tw0 if sp == 0 else c.tw1
            tw_t = tws.bump(16)
            stg_gate = c._fstg_gate[sp]

            @on("gpsimd")
            def _(g, e=e, q=q, hch=hch, sp=sp, tws=tws, slot_free=slot_free, gate=c._gcast_gate[e],
                  ag=(c._arena_gate_ev if qi == 0 and hch == 0 else None),
                  agm=(c._attn_end_mm if qi == 0 and hch == 0 else None),
                  stg_gate=stg_gate, first=(hch == 0)):
                if first:
                    g.wait_ge(c._gcast.h, gate)
                    if ag is not None:
                        g.wait_ge(c.v_ev.h, ag)
                    if agm is not None:
                        g.wait_ge(c.t_mm.h, agm)
                    if slot_free is not None:
                        g.wait_ge(c.t_mm.h, slot_free)
                if stg_gate is not None:
                    g.wait_ge(c.v_ev.h, stg_gate)
                base = e * HFF + q * 1024 + hch * 128
                g.dma_start(out=c.xnb[:, sp, :], in_=c.fc1bf.ap()[base:base + 128, :]).then_inc(tws.h, 16)

            trs = [c.t_tr.bump() for _ in range(8)]
            evs = [c.v_ev.bump() for _ in range(8)]
            c._fstg_gate[sp] = evs[-1]
            wev_last = evs[-1]
            pre = tuple(c._mtr_pre)
            c._mtr_pre = [evs[6], evs[7]]

            @on("tensor")
            def _(t, sp=sp, tw_t=tw_t, tws=tws, evs=evs, pre=pre,
                  civ=(getattr(c, "_comb_init_done", None) if (qi == 0 and hch == 0) else None)):
                t.wait_ge(tws.h, tw_t)
                if civ is not None:
                    t.wait_ge(c.v_cmb.h, civ)
                for dblk in range(8):
                    bank = 4 + (dblk % 2)
                    gate = pre[dblk % 2] if dblk < 2 else evs[dblk - 2]
                    if gate is not None:
                        t.wait_ge(c.v_ev.h, gate)
                    pv = c.ps[bank][:, :].bitcast(BF16)
                    t.transpose(pv[:, 0:128], c.xnb[:, sp, dblk * 128:(dblk + 1) * 128],
                                c.ident_bf_sb[:, :]).then_inc(c.t_tr.h, 1)

            @on("vector")
            def _(v, sp=sp, hch=hch, trs=trs):
                for dblk in range(8):
                    bank = 4 + (dblk % 2)
                    pv = c.ps[bank][:, :].bitcast(BF16)
                    v.wait_ge(c.t_tr.h, trs[dblk])
                    v.tensor_copy(out=vW1(c)[:, qp, dblk, hch * 128:(hch + 1) * 128],
                                  in_=pv[:, 0:128]).then_inc(c.v_ev.h, 1)

        c._w1tr_done = wev_last
        tw_t = None
    else:
        @on("gpsimd")
        def _(g, e=e, q=q, slot_free=slot_free,
              ag=(c._arena_gate_ev if qi == 0 else None),
              agm=(c._attn_end_mm if qi == 0 else None), mw1=c.m_w1):
            if ag is not None:
                g.wait_ge(c.v_ev.h, ag)
            if agm is not None:
                g.wait_ge(c.t_mm.h, agm)
            if slot_free is not None:
                g.wait_ge(c.t_mm.h, slot_free)
            for dblk in range(8):
                g.dma_start(out=vW1(c)[:, qp, dblk, :],
                            in_=mw1[e, dblk * 128:(dblk + 1) * 128, q * 1024:(q + 1) * 1024]).then_inc(tw.h, 16)

    gw_t = gw.bump(16 * 8)
    bw_t = bw.bump(16)

    @on("gpsimd")
    def _(g, e=e, q=q, slot_free=slot_free, ag=(c._arena_gate_ev if qi == 0 else None),
          agm=(c._attn_end_mm if qi == 0 else None), mw2=c.m_w2, mb1=c.m_b1):
        if ag is not None:
            g.wait_ge(c.v_ev.h, ag)
        if agm is not None:
            g.wait_ge(c.t_mm.h, agm)
        if slot_free is not None:
            g.wait_ge(c.t_mm.h, slot_free)
        for hb in range(8):
            g.dma_start(out=vW2(c)[:, qp, hb, :],
                        in_=mw2[e, q * 1024 + hb * 128:q * 1024 + (hb + 1) * 128, :]).then_inc(gw.h, 16)
        g.dma_start(out=c.b1row[0:8, qp, :],
                    in_=mb1[e, q * 1024:(q + 1) * 1024].rearrange("(c p) -> c p", p=128)).then_inc(bw.h, 16)

    btr = c.t_tr.bump()
    bev = c.v_ev.bump()

    @on("tensor")
    def _(t, bw_t=bw_t, gate=c._b1_ev_prev, ci_done=(c._comb_init_done if c._b1_ev_prev is None else None),
          mpre=c._mtr_pre[0]):
        t.wait_ge(bw.h, bw_t)
        if ci_done is not None:
            t.wait_ge(c.v_cmb.h, ci_done)
        if gate is not None:
            t.wait_ge(c.v_ev.h, gate)
        if mpre is not None:
            t.wait_ge(c.v_ev.h, mpre)
        t.transpose(c.ps[4][0:128, 0:8], c.b1row[0:8, qp, :], c.ident_f_sb[0:8, 0:8]).then_inc(c.t_tr.h, 1)

    c._b1_ev_prev = bev
    c._mtr_pre = [bev, c._mtr_pre[1]]

    @on("vector")
    def _(v, btr=btr):
        v.wait_ge(c.t_tr.h, btr)
        v.tensor_copy(out=c.b1col[:, qp, :], in_=c.ps[4][0:128, 0:8]).then_inc(c.v_ev.h, 1)

    hT = vD_hTq
    silu_start = c.s_silu.n
    mm_groups = []
    for gidx in range(32):
        ht, tc = gidx // 4, gidx % 4
        mm = c.t_mm.bump()
        mm_groups.append(mm)

        @on("tensor")
        def _(t, ht=ht, tc=tc, gidx=gidx, tw_t=tw_t, bev=bev,
              first=(gidx == 0), silu_start=silu_start, fg=(c._moe_first_gate if qi == 0 else None),
              wtr=(c._w1tr_done if (not l1 and gidx == 0) else None)):
            bank = gidx % 2
            if first:
                if fg is not None:
                    t.wait_ge(c.v_r.h, fg)
                if tw_t is not None:
                    t.wait_ge(tw.h, tw_t)
                if wtr is not None:
                    t.wait_ge(c.v_ev.h, wtr)
                t.wait_ge(c.v_ev.h, bev)
            if gidx >= 2:
                t.wait_ge(c.s_silu.h, silu_start + gidx - 1)
            for k in range(8):
                mmh = t.matmul(c.ps[bank][:, :], vW1(c)[:, qp, k, ht * 128:(ht + 1) * 128],
                               vA(c)[:, k, tc * 512:(tc + 1) * 512], start=(k == 0), stop=(k == 7))
            mmh.then_inc(c.t_mm.h, 1)

        sl = c.s_silu.bump()

        @on("scalar")
        def _(s_, ht=ht, tc=tc, gidx=gidx, mm=mm, act=c.mcfg["act"]):
            bank = gidx % 2
            s_.wait_ge(c.t_mm.h, mm)
            s_.activation(out=hT(c)[:, ht, tc * 512:(tc + 1) * 512], in_=c.ps[bank][:, :],
                          func=act, bias=c.b1col[:, qp, ht:ht + 1], scale=1.0).then_inc(c.s_silu.h, 1)

    silu_all = c.s_silu.n

    for tt in range(TT):
        pair = (2, 3) if tt % 2 == 0 else (6, 7)
        mm = c.t_mm.bump()

        @on("tensor")
        def _(t, tt=tt, pair=pair, first=(tt == 0), silu_all=silu_all, gw_t=gw_t,
              ygate=c._y_gate[tt % 2], wtd=(c._wT_done if qi == 0 and tt <= 1 else None)):
            if first:
                t.wait_ge(c.s_silu.h, silu_all)
                t.wait_ge(gw.h, gw_t)
            if wtd is not None:
                t.wait_ge(c.v_ev.h, wtd)
            if ygate is not None:
                t.wait_ge(c.v_cmb.h, ygate)
            for half in range(2):
                for k in range(8):
                    mmh = t.matmul(c.ps[pair[half]][:, :], hT(c)[:, k, tt * 128:(tt + 1) * 128],
                                   vW2(c)[:, qp, k, half * 512:(half + 1) * 512],
                                   start=(k == 0), stop=(k == 7))
            mmh.then_inc(c.t_mm.h, 1)

        cm = c.v_cmb.bump()
        c._y_gate[tt % 2] = cm

        @on("vector")
        def _(v, tt=tt, pair=pair, mm=mm, e=e):
            v.wait_ge(c.t_mm.h, mm)
            v.scalar_tensor_tensor(out=vD_comb(c)[:, tt, 0:512], in0=c.ps[pair[0]][:, :],
                                   scalar=c.w8_all[:, tt, e:e + 1], in1=vD_comb(c)[:, tt, 0:512],
                                   op0=ALU.mult, op1=ALU.add)
            v.scalar_tensor_tensor(out=vD_comb(c)[:, tt, 512:1024], in0=c.ps[pair[1]][:, :],
                                   scalar=c.w8_all[:, tt, e:e + 1], in1=vD_comb(c)[:, tt, 512:1024],
                                   op0=ALU.mult, op1=ALU.add).then_inc(c.v_cmb.h, 1)

    c._q_slot_free[qp] = c.t_mm.n


def _emit_moe0_epilogue(c):
    on = c.rec.on
    comb_all = c.v_cmb.n
    for tt in range(TT):
        p = tt % 2
        trs = [c.t_tr.bump() for _ in range(8)]
        evs = [c.v_ev.bump() for _ in range(8)]
        pre = (c._ep_tr_pre[0], c._ep_tr_pre[1])
        c._ep_tr_pre = [evs[6], evs[7]]

        @on("tensor")
        def _(t, tt=tt, p=p, trs=trs, evs=evs, pre=pre, first=(tt == 0), comb_all=comb_all,
              xgate=c._xn0n_gate[p]):
            if first:
                t.wait_ge(c.v_cmb.h, comb_all)
            if xgate is not None:
                t.wait_ge(c.v_aux.h, xgate)
            for j in range(8):
                bank = 4 + (j % 2)
                gate = pre[j % 2] if j < 2 else evs[j - 2]
                if gate is not None:
                    t.wait_ge(c.v_ev.h, gate)
                pv = c.ps[bank][:, :].bitcast(BF16)
                t.transpose(pv[:, 0:128], vA(c)[:, j, tt * 128:(tt + 1) * 128],
                            c.ident_bf_sb[:, :]).then_inc(c.t_tr.h, 1)

        @on("vector")
        def _(v, tt=tt, p=p, trs=trs):
            for j in range(8):
                bank = 4 + (j % 2)
                pv = c.ps[bank][:, :].bitcast(BF16)
                v.wait_ge(c.t_tr.h, trs[j])
                v.tensor_copy(out=vEP_xn0n(c)[:, p, j * 128:(j + 1) * 128], in_=pv[:, 0:128]).then_inc(c.v_ev.h, 1)

        aq = c.aq0 if p == 0 else c.aq1
        aq_t = aq.bump(16)

        @on("sync")
        def _(s_, tt=tt, p=p, aq=aq, agate=c._acc_gate[p], ca=(comb_all if tt < 2 else None),
              spa=(c._spill_all if tt < 2 else None)):
            if c._acc_pre is not None:
                s_.wait_ge(c.g_aux.h, c._acc_pre)
                c._acc_pre = None
            if ca is not None:
                s_.wait_ge(c.v_cmb.h, ca)
            if spa is not None:
                s_.wait_ge(c.spq.h, spa[0])
                s_.wait_ge(c.spq1.h, spa[1])
            if agate is not None:
                s_.wait_ge(c.v_aux.h, agate)
            s_.dma_start(out=vEP_acc(c)[:, p, :], in_=c.acc_spill.ap()[tt * 128:(tt + 1) * 128, :]).then_inc(aq.h, 16)

        lastev = evs[-1]
        x1sg = c._x1_store_gate[p]
        mvt = c.v_mv.bump()

        @on("vector")
        def _(v, tt=tt, p=p, lastev=lastev, xgate=x1sg):
            v.wait_ge(c.v_ev.h, lastev)
            if xgate is not None:
                v.wait_ge((c.x1s0 if p == 0 else c.x1s1).h, xgate)
            v.tensor_tensor(out=vEP_r2(c)[:, p, :], in0=vEP_xn0n(c)[:, p, :], in1=vD_comb(c)[:, tt, :], op=ALU.add)
            v.drain()
            v.bn_stats(c.stats[:, p, 0:6], vEP_r2(c)[:, p, 0:512])
            v.bn_stats(c.stats[:, p, 6:12], vEP_r2(c)[:, p, 512:1024])
            v.drain()
            v.bn_aggr(c.mv[:, p, :], c.stats[:, p, :]).then_inc(c.v_mv.h, 1)

        rs = c.s_rstd.bump()

        @on("scalar")
        def _(s_, p=p, mvt=mvt):
            s_.wait_ge(c.v_mv.h, mvt)
            s_.activation(out=c.rstd[:, p, :], in_=c.mv[:, p, 1:2], func=AF.Ln, bias=c.eps_t[:, 0:1], scale=1.0)
            s_.drain()
            s_.activation(out=c.rstd[:, p, :], in_=c.rstd[:, p, :], func=AF.Exp, bias=0.0, scale=-0.5).then_inc(c.s_rstd.h, 1)

        xa = c.v_aux.bump()
        c._xn0n_gate[p] = xa
        c._acc_gate[p] = xa
        x1s = c.x1s0 if p == 0 else c.x1s1
        st_t = x1s.bump(16)

        @on("vector")
        def _(v, tt=tt, p=p, rs=rs, aq_t=aq_t, aq=aq):
            v.wait_ge(c.s_rstd.h, rs)
            v.tensor_scalar(out=vEP_x1t(c)[:, p, :], in0=vEP_r2(c)[:, p, :], scalar1=c.mv[:, p, 0:1],
                            scalar2=c.rstd[:, p, 0:1], op0=ALU.subtract, op1=ALU.mult)
            v.drain()
            v.tensor_tensor(out=vEP_r2(c)[:, p, :], in0=vEP_x1t(c)[:, p, :], in1=c.gam_t[:, :], op=ALU.mult)
            v.drain()
            v.tensor_tensor(out=vEP_x1t(c)[:, p, :], in0=vEP_r2(c)[:, p, :], in1=c.bet_t[:, :], op=ALU.add)
            v.drain()
            v.wait_ge(aq.h, aq_t)
            v.tensor_tensor(out=vEP_r2(c)[:, p, :], in0=vEP_x1t(c)[:, p, :], in1=vEP_acc(c)[:, p, :], op=ALU.add).then_inc(c.v_aux.h, 1)

        c._x1_store_gate[p] = st_t

        @on("gpsimd")
        def _(g, tt=tt, p=p, xa=xa, x1s=x1s):
            g.wait_ge(c.v_aux.h, xa)
            g.dma_start(out=c.x1_out[tt * 128:(tt + 1) * 128, :], in_=vEP_r2(c)[:, p, :]).then_inc(x1s.h, 16)


def _emit_moe0(c, no_attn=False):
    on = c.rec.on
    if no_attn:
        ga = c.g_aux.bump(16)
        c._acc_pre = ga

        @on("gpsimd")
        def _(g):
            g.dma_start(out=c.acc_spill.ap(), in_=c.x_own).then_inc(c.g_aux.h, 16)
    else:
        c._acc_pre = None

    c.mcfg = {"l1": False, "act": AF.Silu}
    c.m_w1 = None
    c.m_w2 = c.fc2_w
    c.m_b1 = c.fc1_b
    c.m_b2 = c.fc2_b
    _emit_moe0_prep(c)
    c._fstg_gate = [None, None]
    c._mtr_pre = [None, None]
    c._mtr_init = c.v_aux.n
    c._arena_gate_ev = c._p0_ev_done
    c._gcast_gate = [16 * (e + 1) for e in range(NE)]
    c._moe_first_gate = c._w8_done
    c._q_slot_free = [None, None]
    c._b1_ev_prev = None
    c._y_gate = [None, None]
    _emit_moe0_combinit(c)
    for e in range(NE):
        for q in range(4):
            _emit_moe0_quarter(c, e, q, e * 4 + q)
    c._ep_tr_pre = [None, None]
    c._xn0n_gate = [None, None]
    c._acc_gate = [None, None]
    c._x1_store_gate = [None, None]
    _emit_moe0_epilogue(c)





def vL1_hid(c):
    return c.arD[:, 0:8192].bitcast(F32)


def vL1_gate(c):
    return c.arD[:, 8192:16384].bitcast(F32)


def vL1_c(c):
    return c.arD[:, 16384:24576].bitcast(F32)


def vL1_v(c):
    return c.arD[:, 24576:32768].bitcast(F32)


def vL1_scan(c):
    return c.arD[:, 57344:65536].bitcast(F32)


def vMK8(c):
    return c.arD[:, 24576:26624]


def vACC2ST(c):
    return c.arD[:, 32768:40960].bitcast(F32).rearrange("p (s d) -> p s d", s=2)


def vL1_wosl(c):
    return c.arD[:, 0:32768].bitcast(BF16).rearrange("p (cb d) -> p cb d", cb=16)


def vL1_car8(c):
    return c.arD[0:8, 32768:40960].bitcast(F32)


def vL1_carrow(c):
    return c.arD[0:1, 0:8192].bitcast(F32)


def vL1_acc2st(c):
    return c.arD[:, 49152:57344].bitcast(F32).rearrange("p (s d) -> p s d", s=2)


def hT_sc(c, sc):
    if sc < 8:
        return vB(c)[:, sc, :]
    return vC(c)[:, sc - 8, :]


def _emit_scan_pass(c, sc, passB):
    on = c.rec.on
    par = sc % 2
    wsl = c.wsl0 if par == 0 else c.wsl1
    wld = wsl.bump(32)
    slot_gate = c._wsl_gate[par]

    @on("gpsimd")
    def _(g, sc=sc, par=par, wsl=wsl, slot_gate=slot_gate, passB=passB,
          wg=(getattr(c, "_scan_wrow_gate", None) if (sc < 2 and not passB) else None)):
        if wg is not None:
            g.wait_ge(wg[0], wg[1])
        if slot_gate is not None:
            g.wait_ge(c.t_mm.h, slot_gate)
        g.dma_start(out=vWROW(c)[:, par, 0, :], in_=c.W_hg[sc * 128:(sc + 1) * 128, :]).then_inc(wsl.h, 16)
        g.dma_start(out=vWROW(c)[:, par, 1, :], in_=c.W_hg[DI + sc * 128:DI + (sc + 1) * 128, :]).then_inc(wsl.h, 16)

    trs = [c.t_tr.bump() for _ in range(16)]
    evs = [c.v_ev.bump() for _ in range(16)]
    pre = tuple(c._tb_pre)
    c._tb_pre = [evs[14], evs[15]]

    @on("tensor")
    def _(t, par=par, wsl=wsl, wld=wld, evs=evs, pre=pre):
        t.wait_ge(wsl.h, wld)
        n = 0
        for w in range(2):
            for dblk in range(8):
                bank = 4 + (n % 2)
                gate = pre[n % 2] if n < 2 else evs[n - 2]
                if gate is not None:
                    t.wait_ge(c.v_ev.h, gate)
                pv = c.ps[bank][:, :].bitcast(BF16)
                t.transpose(pv[:, 0:128], vWROW(c)[:, par, w, dblk * 128:(dblk + 1) * 128],
                            c.ident_bf_sb[:, :]).then_inc(c.t_tr.h, 1)
                n += 1

    @on("vector")
    def _(v, par=par, trs=trs):
        n = 0
        for w in range(2):
            for dblk in range(8):
                bank = 4 + (n % 2)
                pv = c.ps[bank][:, :].bitcast(BF16)
                v.wait_ge(c.t_tr.h, trs[n])
                v.tensor_copy(out=vWSL(c)[:, par, w, dblk, :], in_=pv[:, 0:128]).then_inc(c.v_ev.h, 1)
                n += 1

    wsl_ev = evs[-1]

    hg_sv = []
    for w in range(2):
        for ch in range(4):
            n = w * 4 + ch
            mm = c.t_mm.bump()
            sv = c.s_act.bump()
            hg_sv.append(sv)

            @on("tensor")
            def _(t, w=w, ch=ch, n=n, par=par, mm=mm, wsl_ev=wsl_ev,
                  pg=c._gen_ps_gate[n % 2], bank=(n % 2)):
                if n == 0:
                    t.wait_ge(c.v_ev.h, wsl_ev)
                if pg is not None:
                    t.wait_ge(c.s_act.h, pg)
                for k in range(8):
                    mmh = t.matmul(c.ps[bank][:, :], vWSL(c)[:, par, w, k, :],
                                   vA(c)[:, k, ch * 512:(ch + 1) * 512], start=(k == 0), stop=(k == 7))
                mmh.then_inc(c.t_mm.h, 1)

            @on("scalar")
            def _(s_, w=w, ch=ch, mm=mm, bank=(n % 2),
                  scg=(c._scan_consumed if n == 0 else None),
                  crg=(getattr(c, "_carrow_ev", None) if (n == 0 and passB and sc == 0) else None)):
                s_.wait_ge(c.t_mm.h, mm)
                if scg is not None:
                    s_.wait_ge(c.v_aux.h, scg)
                if crg is not None:
                    s_.wait_ge(c.v_ev.h, crg)
                dst = vL1_hid if w == 0 else vL1_gate
                s_.activation(out=dst(c)[:, ch * 512:(ch + 1) * 512], in_=c.ps[bank][:, :],
                              func=AF.Copy, bias=0.0, scale=1.0).then_inc(c.s_act.h, 1)

            c._gen_ps_gate[n % 2] = sv

    hg_done = c.s_act.n

    s1 = c.s_act.bump()
    s2 = c.s_act.bump()
    v1 = c.v_aux.bump()
    s3 = c.s_act.bump()
    v5 = c.v_aux.bump()

    @on("scalar")
    def _(s_, hg_done=hg_done, scg=c._scan_consumed):
        s_.wait_ge(c.s_act.h, hg_done)
        if scg is not None:
            s_.wait_ge(c.v_aux.h, scg)
        s_.activation(out=vL1_c(c)[:, :], in_=vL1_gate(c)[:, :], func=AF.Sigmoid, bias=0.0, scale=-1.0).then_inc(c.s_act.h, 1)
        s_.drain()
        s_.activation(out=vL1_gate(c)[:, :], in_=vL1_gate(c)[:, :], func=AF.Sigmoid, bias=0.0, scale=1.0).then_inc(c.s_act.h, 1)

    @on("vector")
    def _(v, hg_done=hg_done, scg=c._scan_consumed):
        v.wait_ge(c.s_act.h, hg_done)
        if scg is not None:
            v.wait_ge(c.v_aux.h, scg)
        v.tensor_single_scalar(out=vMK8(c)[:, :], in_=vL1_hid(c)[:, :], scalar=0.0, op=ALU.is_ge).then_inc(c.v_aux.h, 1)

    @on("scalar")
    def _(s_, v1=v1):
        s_.activation(out=vL1_scan(c)[:, :], in_=vL1_hid(c)[:, :], func=AF.Sigmoid, bias=0.0, scale=1.0)
        s_.drain()
        s_.wait_ge(c.v_aux.h, v1)
        s_.activation(out=vL1_hid(c)[:, :], in_=vL1_hid(c)[:, :], func=AF.Identity, bias=c.half_t[:, 0:1], scale=1.0).then_inc(c.s_act.h, 1)

    @on("vector")
    def _(v, sc=sc, s3=s3, s2=s2, passB=passB,
          cd=(c._carry_done_ev if (passB and sc == 0) else None)):
        v.wait_ge(c.s_act.h, s3)
        if cd is not None:
            v.wait_ge(c.v_ev.h, cd)
        v.select(out=vL1_scan(c)[:, :], mask=vMK8(c)[:, :], on_true=vL1_hid(c)[:, :], on_false=vL1_scan(c)[:, :], add_drain=True)
        v.drain()
        v.tensor_tensor(out=vL1_v(c)[:, :], in0=vL1_gate(c)[:, :], in1=vL1_scan(c)[:, :], op=ALU.mult)
        v.drain()
        init = c.carry_sb[:, sc:sc + 1] if passB else 0.0
        v.tensor_tensor_scan(out=vL1_scan(c)[:, :], data0=vL1_c(c)[:, :], data1=vL1_v(c)[:, :],
                             initial=init, op0=ALU.mult, op1=ALU.add)
        v.drain()
        v.tensor_copy(out=hT_sc(c, sc), in_=vL1_scan(c)[:, :])
        if not passB:
            v.tensor_copy(out=c.fin_sb[:, sc:sc + 1], in_=vL1_scan(c)[:, 2047:2048])
        v.drain()
        v.engine_nop().then_inc(c.v_aux.h, 1)

    c._scan_consumed = v5
    c._wsl_gate[par] = c.t_mm.n


def _emit_l1_scan(c):
    on = c.rec.on
    c._scan_consumed = None
    c._gen_ps_gate = [c._gen_ps_gate[0], c._gen_ps_gate[1]] if hasattr(c, "_gen_ps_gate") else [None, None]

    for sc in range(16):
        _emit_scan_pass(c, sc, passB=False)

    last = c._scan_consumed
    ev_all = c.v_ev.n
    ftr = c.t_tr.bump()
    fev = c.v_ev.bump()

    @on("tensor")
    def _(t, last=last, ev_all=ev_all):
        t.wait_ge(c.v_aux.h, last)
        t.wait_ge(c.v_ev.h, ev_all)
        t.transpose(c.ps[4][0:16, 0:128], c.fin_sb[:, :], c.ident_f_sb[:, :]).then_inc(c.t_tr.h, 1)

    @on("vector")
    def _(v, ftr=ftr):
        v.wait_ge(c.t_tr.h, ftr)
        v.tensor_copy(out=c.D_all[0:16, 0:128], in_=c.ps[4][0:16, 0:128]).then_inc(c.v_ev.h, 1)

    g1 = c.agq.bump(16)
    cc1 = c.cc.bump(16 * NCORE if NO_CC else 1)
    g2 = c.agq.bump(16)

    @on("gpsimd")
    def _(g, fev=fev, g1=g1, cc1=cc1):
        g.wait_ge(c.v_ev.h, fev)
        g.dma_start(out=c.ag_car_in.ap(), in_=c.D_all[0:16, 0:128]).then_inc(c.agq.h, 16)
        g.wait_ge(c.agq.h, g1)
        if NO_CC:
            for r in range(NCORE):
                g.dma_start(out=c.ag_car_out.ap()[r * 16:(r + 1) * 16, :], in_=c.ag_car_in.ap()).then_inc(c.cc.h, 16)
        else:
            g.collective_compute(
                "AllGather", ALU.bypass,
                replica_groups=[list(range(NCORE))],
                ins=[c.ag_car_in.ap().opt()],
                outs=[c.ag_car_out.ap().opt()],
            ).then_inc(c.cc.h, 1)
        g.wait_ge(c.cc.h, cc1)
        g.dma_start(out=vL1_car8(c)[0:8, :], in_=c.ag_car_out.ap().rearrange("(r i) p -> r (i p)", r=8)).then_inc(c.agq.h, 16)

    sel_evs = []
    prev_se = None
    for j in range(4):
        smm = c.t_mm.bump()
        se = c.v_ev.bump()
        sel_evs.append(se)

        @on("tensor")
        def _(t, j=j, g2=g2, fev=fev, prev_se=prev_se):
            if j == 0:
                t.wait_ge(c.agq.h, g2)
                t.wait_ge(c.v_ev.h, fev)
            else:
                t.wait_ge(c.v_ev.h, prev_se)
            t.matmul(c.ps[4][0:1, 0:512], c.selcar_sb[0:8, :], vL1_car8(c)[0:8, j * 512:(j + 1) * 512],
                     start=True, stop=True).then_inc(c.t_mm.h, 1)

        smm_t = c.t_mm.n

        @on("vector")
        def _(v, j=j, smm_t=smm_t):
            v.wait_ge(c.t_mm.h, smm_t)
            v.tensor_copy(out=vL1_carrow(c)[0:1, j * 512:(j + 1) * 512], in_=c.ps[4][0:1, 0:512]).then_inc(c.v_ev.h, 1)

        prev_se = se

    ctr = [c.t_tr.bump() for _ in range(16)]
    cev = [c.v_ev.bump() for _ in range(16)]

    @on("tensor")
    def _(t, ctr=ctr, cev=cev, prev_se=prev_se):
        t.wait_ge(c.v_ev.h, prev_se)
        for sc in range(16):
            if sc >= 2:
                t.wait_ge(c.v_ev.h, cev[sc - 2])
            t.transpose(c.ps[4 + sc % 2][0:128, 0:1], vL1_carrow(c)[0:1, sc * 128:(sc + 1) * 128],
                        c.ident_f_sb[0:1, 0:1]).then_inc(c.t_tr.h, 1)

    @on("vector")
    def _(v, ctr=ctr):
        for sc in range(16):
            v.wait_ge(c.t_tr.h, ctr[sc])
            v.tensor_copy(out=c.carry_sb[:, sc:sc + 1], in_=c.ps[4 + sc % 2][0:128, 0:1]).then_inc(c.v_ev.h, 1)

    c._carry_done_ev = cev[-1]
    c._carrow_ev = cev[-1]
    c._carry_done = None
    c._tb_pre = [cev[14], cev[15]]

    for sc in range(16):
        _emit_scan_pass(c, sc, passB=True)


def _emit_l1_outproj(c):
    on = c.rec.on
    scan_done = c._scan_consumed
    c._wosl_tr = [None, None]
    for cb in range(16):
        par = cb % 2
        wsl = c.wsl0 if par == 0 else c.wsl1
        wld = wsl.bump(16 * 8)
        slot_gate = c._wsl_gate[par]
        tr_gate = c._wosl_tr[par]

        @on("gpsimd")
        def _(g, cb=cb, par=par, wsl=wsl, slot_gate=slot_gate, sd=(scan_done if cb < 2 else None),
              tr_gate=tr_gate):
            if sd is not None:
                g.wait_ge(c.v_aux.h, sd)
            if slot_gate is not None:
                g.wait_ge(c.t_mm.h, slot_gate)
            if tr_gate is not None:
                g.wait_ge(c.t_tr.h, tr_gate)
            for dblk in range(8):
                g.dma_start(out=vWROW(c)[:, par, 0, dblk * 128:(dblk + 1) * 128],
                            in_=c.W_out[dblk * 128:(dblk + 1) * 128, cb * 128:(cb + 1) * 128]).then_inc(wsl.h, 16)

        trs = [c.t_tr.bump() for _ in range(8)]
        evs = [c.v_ev.bump() for _ in range(8)]
        pre = tuple(c._tb_pre)
        c._tb_pre = [evs[6], evs[7]]

        @on("tensor")
        def _(t, cb=cb, par=par, wsl=wsl, wld=wld, evs=evs, pre=pre, sd=(scan_done if cb == 0 else None)):
            t.wait_ge(wsl.h, wld)
            if sd is not None:
                t.wait_ge(c.v_aux.h, sd)
            for dblk in range(8):
                bank = 4 + (dblk % 2)
                gate = pre[dblk % 2] if dblk < 2 else evs[dblk - 2]
                if gate is not None:
                    t.wait_ge(c.v_ev.h, gate)
                pv = c.ps[bank][:, :].bitcast(BF16)
                t.transpose(pv[:, 0:128], vWROW(c)[:, par, 0, dblk * 128:(dblk + 1) * 128],
                            c.ident_bf_sb[:, :]).then_inc(c.t_tr.h, 1)

        @on("vector")
        def _(v, cb=cb, trs=trs, sd=(scan_done if cb == 0 else None)):
            if sd is not None:
                v.wait_ge(c.v_aux.h, sd)
            for dblk in range(8):
                bank = 4 + (dblk % 2)
                pv = c.ps[bank][:, :].bitcast(BF16)
                v.wait_ge(c.t_tr.h, trs[dblk])
                v.tensor_copy(out=vL1_wosl(c)[:, cb, dblk * 128:(dblk + 1) * 128], in_=pv[:, 0:128]).then_inc(c.v_ev.h, 1)

        c._wsl_gate[par] = c.t_mm.n
        c._wosl_tr[par] = trs[-1]

    wosl_ev = c.v_ev.n

    c._op_ps_gate = [None, None]
    c._x1ld_gate = [None, None]
    c._a2spill_gate = [None, None]
    xnb_all = c.v_xnb.n
    for tt in range(TT):
        p = tt % 2
        lq = c.lq0 if p == 0 else c.lq1
        xld = lq.bump(16)

        @on("sync")
        def _(s_, tt=tt, p=p, lq=lq, gate=c._x1ld_gate[p], xg=(xnb_all if tt < 2 else None)):
            if xg is not None:
                s_.wait_ge(c.v_xnb.h, xg)
            if gate is not None:
                s_.wait_ge(c.v_aux.h, gate)
            s_.dma_start(out=c.xs[:, p, :], in_=c.x1_src[tt * 128:(tt + 1) * 128, :]).then_inc(lq.h, 16)

        va_parts = []
        for half in range(2):
            b = half
            mm = c.t_mm.bump()

            @on("tensor")
            def _(t, tt=tt, half=half, b=b, first=(tt == 0 and half == 0), wosl_ev=wosl_ev,
                  og=c._op_ps_gate[b]):
                if first:
                    t.wait_ge(c.v_ev.h, wosl_ev)
                if og is not None:
                    t.wait_ge(c.v_aux.h, og)
                for cb in range(16):
                    mmh = t.matmul(c.ps[b][:, :], hT_sc(c, cb)[:, tt * 128:(tt + 1) * 128],
                                   vL1_wosl(c)[:, cb, half * 512:(half + 1) * 512],
                                   start=(cb == 0), stop=(cb == 15))
                mmh.then_inc(c.t_mm.h, 1)

            va = c.v_aux.bump()
            va_parts.append(va)
            c._op_ps_gate[b] = va

            @on("vector")
            def _(v, tt=tt, p=p, half=half, b=b, mm=mm, xld=xld, lq=lq, spg=c._a2spill_gate[p]):
                v.wait_ge(c.t_mm.h, mm)
                v.wait_ge(lq.h, xld)
                if spg is not None and half == 0:
                    v.wait_ge((c.spq if p == 0 else c.spq1).h, spg)
                v.tensor_tensor(out=vL1_acc2st(c)[:, p, half * 512:(half + 1) * 512], in0=c.ps[b][:, :],
                                in1=c.xs[:, p, half * 512:(half + 1) * 512], op=ALU.add).then_inc(c.v_aux.h, 1)

        spqs = c.spq if p == 0 else c.spq1
        sp = spqs.bump(16)
        c._a2spill_gate[p] = sp
        va2 = va_parts[-1]
        c._x1ld_gate[p] = va2

        @on("gpsimd")
        def _(g, tt=tt, p=p, va2=va2, spqs=spqs):
            g.wait_ge(c.v_aux.h, va2)
            g.dma_start(out=c.acc2_spill.ap()[tt * 128:(tt + 1) * 128, :], in_=vL1_acc2st(c)[:, p, :]).then_inc(spqs.h, 16)

    c._spill_all = (c.spq.n, c.spq1.n)
    c._attn_end_mm = c.t_mm.n


def _emit_l1_w8(c):
    on = c.rec.on
    kw_prev = None
    ev_prev = getattr(c, "_l1w8_ev_init", None)
    w8gate_cmb = getattr(c, "_l1w8_cmb_gate", None)
    last_r = c._rt_prev
    for tt in range(TT):
        kw = c.v_r.bump()

        @on("vector")
        def _(v, tt=tt, first=(tt == 0), last_r=last_r, wg8=(w8gate_cmb if tt == 0 else None)):
            if first:
                v.wait_ge(c.v_r.h, last_r)
            if wg8 is not None:
                v.wait_ge(c.v_cmb.h, wg8)
            v.tensor_single_scalar(out=c.scr16[:, 0:8], in_=c.mask16f[:, tt, 8:16], scalar=c.g_all[:, tt, 1:2], op=ALU.mult)
            v.drain()
            v.scalar_tensor_tensor(out=c.w8_all[:, tt, :], in0=c.mask16f[:, tt, 0:8], scalar=c.g_all[:, tt, 0:1],
                                   in1=c.scr16[:, 0:8], op0=ALU.mult, op1=ALU.add).then_inc(c.v_r.h, 1)

        kw_prev = kw
        trw = c.t_tr.bump()

        @on("tensor")
        def _(t, tt=tt, kw=kw, ev_prev=ev_prev):
            t.wait_ge(c.v_r.h, kw)
            if ev_prev is not None:
                t.wait_ge(c.v_ev.h, ev_prev)
            t.transpose(c.ps[2][0:8, 0:128], c.w8_all[:, tt, :], c.ident_f_sb[:, :]).then_inc(c.t_tr.h, 1)

        evw = c.v_ev.bump()
        ev_prev = evw

        @on("vector")
        def _(v, tt=tt, trw=trw):
            v.wait_ge(c.t_tr.h, trw)
            v.tensor_copy(out=c.wT_all[0:8, tt * 128:(tt + 1) * 128], in_=c.ps[2][0:8, 0:128]).then_inc(c.v_ev.h, 1)

    c._w8_done = kw_prev
    c._wT_done = ev_prev
    c._moe_first_gate = kw_prev


def _emit_l1_final(c):
    on = c.rec.on
    comb_all = c.v_cmb.n
    g_acc2 = [None, None]
    x2s = [c.x1s0, c.x1s1]
    x2gate = [None, None]
    for tt in range(TT):
        p = tt % 2
        aq = c.aq0 if p == 0 else c.aq1
        aq_t = aq.bump(16)

        @on("sync")
        def _(s_, tt=tt, p=p, aq=aq, agate=g_acc2[p], ca=(comb_all if tt < 2 else None),
              spa=(c._spill_all if tt < 2 else None), xg=x2gate[p], x1s=x2s[p]):
            if ca is not None:
                s_.wait_ge(c.v_cmb.h, ca)
            if spa is not None:
                s_.wait_ge(c.spq.h, spa[0])
                s_.wait_ge(c.spq1.h, spa[1])
            if agate is not None:
                s_.wait_ge(c.v_aux.h, agate)
            if xg is not None:
                s_.wait_ge(x1s.h, xg)
            s_.dma_start(out=vACC2ST(c)[:, p, :], in_=c.acc2_spill.ap()[tt * 128:(tt + 1) * 128, :]).then_inc(aq.h, 16)

        va = c.v_aux.bump()
        g_acc2[p] = va
        x1s = x2s[p]
        st_t = x1s.bump(16)

        @on("vector")
        def _(v, tt=tt, p=p, aq_t=aq_t, aq=aq):
            v.wait_ge(aq.h, aq_t)
            v.tensor_tensor(out=vACC2ST(c)[:, p, :], in0=vACC2ST(c)[:, p, :], in1=vD_comb(c)[:, tt, :],
                            op=ALU.add).then_inc(c.v_aux.h, 1)

        x2gate[p] = st_t

        @on("gpsimd")
        def _(g, tt=tt, p=p, va=va, x1s=x1s):
            g.wait_ge(c.v_aux.h, va)
            g.dma_start(out=c.x2_out[tt * 128:(tt + 1) * 128, :], in_=vACC2ST(c)[:, p, :]).then_inc(x1s.h, 16)


def build_l1(dbg=(), moe=True):
    nc = bass.Bass("TRN2", target_bir_lowering=False)
    ctx = ExitStack()
    c = Ctx()
    c.nc = nc
    c.rec = Rec()
    c.dbg = set(dbg)
    c.ctx = ctx

    A = lambda name, shape, dt: nc.dram_tensor(name, shape, dt, kind="ExternalInput").ap()
    c.x1_src = A("x1_own", [NT, D], F32)
    c.W_hg = A("L1_attn_W_hg", [2 * DI, D], F32)
    c.W_out = A("L1_attn_W_out", [D, DI], F32)
    c.wg = A("L1_ffn_gate_w", [D, NE], F32)
    c.w1_in = A("L1_ffn_w1", [NE, D, HFF], F32)
    c.b1_in = A("L1_ffn_b1", [NE, HFF], F32)
    c.w2_in = A("L1_ffn_w2", [NE, HFF, D], F32)
    c.b2_in = A("L1_ffn_b2", [NE, D], F32)
    c.ident_bf = A("ident_bf", [128, 128], BF16)
    c.ident_f32 = A("ident_f32", [128, 128], F32)
    c.iota8 = A("iota8", [128, 8], F32)
    c.selcar = A("sel_carry", [8, 1], F32)
    c.triu_f32 = A("triu_f32", [128, 128], F32)
    c.cmask = A("cmask", [128, 4, 512], BF16)
    c.vmask = A("vmask", [128, 32], F32)
    c.onespfx = A("ones_prefix", [8, 1], F32)
    c.ln_g = A("L0_ffn_ln_gamma", [D], F32)
    c.ln_b = A("L0_ffn_ln_beta", [D], F32)

    c.x2_out = nc.dram_tensor("x2_out", [NT, D], F32, kind="ExternalOutput").ap()

    c.acc2_spill = nc.dram_tensor("acc2_spill", [NT, D], F32)
    c.ag_car_in = nc.dram_tensor("ag_car_in", [16, 128], F32)
    c.ag_car_out = nc.dram_tensor("ag_car_out", [128, 128], F32)

    _alloc_common(c)
    _alloc_l1(c)
    _emit_phase0_l1(c)
    _emit_l1_w8(c)
    _emit_l1_scan(c)
    if moe != "scanonly":
        _emit_l1_outproj(c)
    if moe is True:
        _emit_moe1(c)
        _emit_l1_final(c)
    _emit_dbg(c)
    _emit_final_wait(c)

    with nc.Block() as block:
        c.rec.register(block)
    ctx.close()
    return nc


def _alloc_l1(c):
    nc, ctx = c.nc, c.ctx
    E = ctx.enter_context
    c.half_t = E(nc.sbuf_tensor("half_t", [128, 1], F32))
    c.fin_sb = E(nc.sbuf_tensor("fin_sb", [128, 16], F32))
    c.carry_sb = E(nc.sbuf_tensor("carry_sb", [128, 16], F32))
    c.selcar_sb = E(nc.sbuf_tensor("selcar_sb", [8, 1], F32))


def _emit_phase0_l1(c):
    on = c.rec.on
    c._lnidx = 0
    c._xnb_cons = [None, None]
    c._xnb_tr = [None, None]
    c._tr_ev_pre = [None, None]
    c._rt_prev = None

    _emit_consts(c)

    @on("vector")
    def _(v):
        v.memset(c.half_t[:, :], 0.5)

    sld = c.agq.bump(16)

    @on("gpsimd")
    def _(s_):
        s_.dma_start(out=c.selcar_sb[0:8, :], in_=c.selcar).then_inc(c.agq.h, 16)

    c._selcar_ld = sld

    for tt in range(TT):
        _emit_ln_tile(c, c.x1_src, tt, vA, routing="l1")

    c._tb_pre = [c._tr_ev_pre[0], c._tr_ev_pre[1]]
    c._gen_ps_gate = [None, None]
    c._wsl_gate = [None, None]


def _emit_moe1(c):
    on = c.rec.on
    c.mcfg = {"l1": True, "act": AF.Gelu}
    c.m_w1 = c.w1_in
    c.m_w2 = c.w2_in
    c.m_b1 = c.b1_in
    c.m_b2 = c.b2_in
    c._arena_gate_ev = c.v_ev.n
    c._fstg_gate = [None, None]
    c._mtr_pre = [None, None]
    c._mtr_init = None
    c._gcast_gate = [None] * NE
    c._moe_first_gate = c._w8_done
    c._q_slot_free = [None, None]
    c._b1_ev_prev = None
    c._y_gate = [None, None]
    c._acc_pre = None
    fb = c.fbq.bump(16)
    c._fc2b_ld = fb

    @on("gpsimd")
    def _(g, fbg=getattr(c, "_l1fc2b_gate", None), mb2=c.m_b2):
        if fbg is not None:
            g.wait_ge(c.v_cmb.h, fbg)
        g.dma_start(out=c.fc2b_sb[:, :], in_=mb2).then_inc(c.fbq.h, 16)

    _emit_moe0_combinit(c)
    for e in range(NE):
        for q in range(4):
            _emit_moe0_quarter(c, e, q, e * 4 + q)



def _emit_dbg(c):
    on = c.rec.on
    nc = c.nc
    if "l1scan" in c.dbg:
        dh = nc.dram_tensor("dbg_hT", [16, 128, NT], BF16, kind="ExternalOutput").ap()
        dx = nc.dram_tensor("dbg_xn1T", [8, 128, NT], BF16, kind="ExternalOutput").ap()
        dc = nc.dram_tensor("dbg_carry", [128, 16], F32, kind="ExternalOutput").ap()
        dfin = nc.dram_tensor("dbg_fin", [128, 16], F32, kind="ExternalOutput").ap()
        dmid = nc.dram_tensor("dbg_mid", [4, 128, NT], F32, kind="ExternalOutput").ap()
        c.st.bump(16 * 30)
        va_all = c.v_aux.n
        ev_all = c.v_ev.n

        @on("gpsimd")
        def _(g, va_all=va_all, ev_all=ev_all):
            g.wait_ge(c.v_aux.h, va_all)
            g.wait_ge(c.v_ev.h, ev_all)
            for sc in range(16):
                g.dma_start(out=dh[sc], in_=hT_sc(c, sc)).then_inc(c.st.h, 16)
            for k in range(8):
                g.dma_start(out=dx[k], in_=vA(c)[:, k, :]).then_inc(c.st.h, 16)
            g.dma_start(out=dc, in_=c.carry_sb[:, :]).then_inc(c.st.h, 16)
            g.dma_start(out=dfin, in_=c.fin_sb[:, :]).then_inc(c.st.h, 16)
            g.dma_start(out=dmid[0], in_=vL1_hid(c)[:, :]).then_inc(c.st.h, 16)
            g.dma_start(out=dmid[1], in_=vL1_gate(c)[:, :]).then_inc(c.st.h, 16)
            g.dma_start(out=dmid[2], in_=vL1_c(c)[:, :]).then_inc(c.st.h, 16)
            g.dma_start(out=dmid[3], in_=vL1_v(c)[:, :]).then_inc(c.st.h, 16)

    if "xnT" in c.dbg:
        d = nc.dram_tensor("dbg_xnT", [8, 128, NT], BF16, kind="ExternalOutput").ap()
        d2 = nc.dram_tensor("dbg_xnTe", [8, 128, NT], BF16, kind="ExternalOutput").ap()
        c.st.bump(16 * 16)
        ev_all = c.v_ev.n

        @on("gpsimd")
        def _(g, ev_all=ev_all):
            g.wait_ge(c.v_ev.h, ev_all)
            for k in range(8):
                g.dma_start(out=d[k], in_=vA(c)[:, k, :]).then_inc(c.st.h, 16)
                g.dma_start(out=d2[k], in_=vB(c)[:, k, :]).then_inc(c.st.h, 16)

    if "route" in c.dbg:
        dw = nc.dram_tensor("dbg_w8", [128, 16, 8], F32, kind="ExternalOutput").ap()
        dm = nc.dram_tensor("dbg_mask", [128, 16, 16], F32, kind="ExternalOutput").ap()
        dp = nc.dram_tensor("dbg_p1", [128, 16, 16], F32, kind="ExternalOutput").ap()
        db = nc.dram_tensor("dbg_allbase", [1, 16], F32, kind="ExternalOutput").ap()
        dbs = nc.dram_tensor("dbg_bases", [1, 16, 16], F32, kind="ExternalOutput").ap()
        dwt = nc.dram_tensor("dbg_wT", [8, NT], BF16, kind="ExternalOutput").ap()
        c.st.bump(16 * 6)
        vr_all = c.v_r.n
        ev_all = c.v_ev.n

        @on("gpsimd")
        def _(g, vr_all=vr_all, ev_all=ev_all):
            g.wait_ge(c.v_r.h, vr_all)
            g.wait_ge(c.v_ev.h, ev_all)
            g.dma_start(out=dw, in_=c.w8_all[:, :, :]).then_inc(c.st.h, 16)
            g.dma_start(out=dm, in_=c.mask16f[:, :, :]).then_inc(c.st.h, 16)
            g.dma_start(out=dp, in_=c.p1_all[:, :, :]).then_inc(c.st.h, 16)
            g.dma_start(out=db, in_=c.allbase[:, :]).then_inc(c.st.h, 16)
            g.dma_start(out=dbs, in_=c.bases_all[:, :, :]).then_inc(c.st.h, 16)
            g.dma_start(out=dwt, in_=c.wT_all[:, :]).then_inc(c.st.h, 16)


def _emit_final_wait(c):
    on = c.rec.on
    target = c.st.n
    t0, t1 = c.x1s0.n, c.x1s1.n

    @on("gpsimd")
    def _(g, target=target, t0=t0, t1=t1):
        if target:
            g.wait_ge(c.st.h, target)
        if t0:
            g.wait_ge(c.x1s0.h, t0)
        if t1:
            g.wait_ge(c.x1s1.h, t1)



def host_constants(core_id):
    b, h = core_id // 2, core_id % 2
    ident = np.eye(128, dtype=np.float32)
    cm = np.zeros((128, 4, 512), dtype=np.float32)
    for j in range(4):
        for jj in range(4):
            blk = cm[:, j, jj * 128:(jj + 1) * 128]
            if jj > j:
                blk[:] = 1.0
            elif jj == j:
                blk[:] = np.tril(np.ones((128, 128), np.float32)).T
    vm = np.zeros((128, 32), dtype=np.float32)
    vm[:, 0:16] = 1.0
    if h == 1:
        vm[:, 16:32] = 1.0
    onespfx = np.zeros((8, 1), dtype=np.float32)
    onespfx[:core_id, 0] = 1.0
    sel = np.zeros((8, 1), dtype=np.float32)
    if h == 1:
        sel[core_id - 1, 0] = 1.0
    return {
        "ident_bf": ident.astype(ml_dtypes.bfloat16),
        "ident_f32": ident,
        "triu_f32": np.triu(np.ones((128, 128), np.float32)),
        "iota8": np.tile(np.arange(8, dtype=np.float32), (128, 1)),
        "cmask": cm.astype(ml_dtypes.bfloat16),
        "vmask": vm,
        "ones_prefix": onespfx,
        "sel_carry": sel,
    }


def shard_inputs(inputs):
    x = np.ascontiguousarray(inputs["L0_attn_in_features"])
    maps = []
    for cid in range(NCORE):
        b, h = cid // 2, cid % 2
        m = dict(host_constants(cid))
        m.pop("sel_carry")
        m["x_own"] = np.ascontiguousarray(x[b, h * NT:(h + 1) * NT])
        m["x_ext"] = np.ascontiguousarray(x[b, (1 - h) * NT:(2 - h) * NT])
        for k in ("L0_attn_Wq", "L0_attn_Wk", "L0_attn_Wv", "L0_attn_Wo", "L0_ffn_wg",
                  "L0_ffn_fc1_w", "L0_ffn_fc1_b", "L0_ffn_fc2_w", "L0_ffn_fc2_b",
                  "L0_ffn_ln_gamma", "L0_ffn_ln_beta"):
            m[k] = inputs[k]
        maps.append(m)
    return maps





def build_fused(dbg=()):
    nc = bass.Bass("TRN2", target_bir_lowering=False)
    ctx = ExitStack()
    c = Ctx()
    c.nc = nc
    c.rec = Rec()
    c.dbg = set(dbg)
    c.ctx = ctx

    A = lambda name, shape, dt: nc.dram_tensor(name, shape, dt, kind="ExternalInput").ap()
    c.x_own = A("x_own", [NT, D], F32)
    c.x_ext = A("x_ext", [NT, D], F32)
    c.Wq = A("L0_attn_Wq", [D, D], F32)
    c.Wk = A("L0_attn_Wk", [D, D], F32)
    c.Wv = A("L0_attn_Wv", [D, D], F32)
    c.Wo = A("L0_attn_Wo", [D, D], F32)
    c.wg = A("L0_ffn_wg", [D, NE], F32)
    c.fc1_w = A("L0_ffn_fc1_w", [NE, HFF, D], F32)
    c.fc1_b = A("L0_ffn_fc1_b", [NE, HFF], F32)
    c.fc2_w = A("L0_ffn_fc2_w", [NE, HFF, D], F32)
    c.fc2_b = A("L0_ffn_fc2_b", [NE, D], F32)
    c.ln_g = A("L0_ffn_ln_gamma", [D], F32)
    c.ln_b = A("L0_ffn_ln_beta", [D], F32)
    c.W_hg = A("L1_attn_W_hg", [2 * DI, D], F32)
    c.W_out = A("L1_attn_W_out", [D, DI], F32)
    c.gate_w = A("L1_ffn_gate_w", [D, NE], F32)
    c.w1_in = A("L1_ffn_w1", [NE, D, HFF], F32)
    c.b1_in = A("L1_ffn_b1", [NE, HFF], F32)
    c.w2_in = A("L1_ffn_w2", [NE, HFF, D], F32)
    c.b2_in = A("L1_ffn_b2", [NE, D], F32)
    c.ident_bf = A("ident_bf", [128, 128], BF16)
    c.ident_f32 = A("ident_f32", [128, 128], F32)
    c.triu_f32 = A("triu_f32", [128, 128], F32)
    c.iota8 = A("iota8", [128, 8], F32)
    c.cmask = A("cmask", [128, 4, 512], BF16)
    c.vmask = A("vmask", [128, 32], F32)
    c.onespfx = A("ones_prefix", [8, 1], F32)
    c.selcar = A("sel_carry", [8, 1], F32)

    c.x2_out = nc.dram_tensor("x2_out", [NT, D], F32, kind="ExternalOutput").ap()

    c.fc1bf = nc.dram_tensor("fc1bf", [NE * HFF, D], BF16)
    c.acc_spill = nc.dram_tensor("acc_spill", [NT, D], F32)
    c.x1_spill = nc.dram_tensor("x1_spill", [NT, D], F32)
    c.acc2_spill = nc.dram_tensor("acc2_spill", [NT, D], F32)
    c.ag_cnt_in = nc.dram_tensor("ag_cnt_in", [1, 512], F32)
    c.ag_cnt_out = nc.dram_tensor("ag_cnt_out", [8, 512], F32)
    c.ag_car_in = nc.dram_tensor("ag_car_in", [16, 128], F32)
    c.ag_car_out = nc.dram_tensor("ag_car_out", [128, 128], F32)
    c.x1_out = c.x1_spill.ap()
    c.x1_src = c.x1_spill.ap()

    _alloc_common(c)
    _alloc_l1(c)

    _emit_phase0(c)
    _emit_keepw(c)
    c._p0_ev_done = c.v_ev.n
    _emit_attn(c)
    _emit_moe0(c)

    ep_end_vaux = c.v_aux.n
    x1_tot = (c.x1s0.n, c.x1s1.n)
    l0_cmb = c.v_cmb.n

    @c.rec.on("vector")
    def _(v):
        v.memset(c.half_t[:, :], 0.5)

    c.agq.bump(16)

    @c.rec.on("gpsimd")
    def _(g):
        g.dma_start(out=c.selcar_sb[0:8, :], in_=c.selcar).then_inc(c.agq.h, 16)

    gfw = c.gq.bump(16)
    c.wg_ld = gfw

    @c.rec.on("gpsimd")
    def _(g, wr=c._w8_done):
        g.wait_ge(c.v_r.h, wr)
        g.dma_start(out=c.wg_sb[:, :, :], in_=c.gate_w.rearrange("(k p) e -> p k e", p=128)).then_inc(c.gq.h, 16)

    c._tr_ev_pre = [c._ep_tr_pre[0], c._ep_tr_pre[1]]
    c._rt_prev = c._w8_done
    eg0 = [(c.x1s0.h, x1_tot[0]), (c.x1s1.h, x1_tot[1]), (c.v_aux.h, ep_end_vaux)]
    for tt in range(TT):
        _emit_ln_tile(c, c.x1_src, tt, vA, routing="l1",
                      extra_gates=(eg0 if tt < 2 else None))

    c._l1w8_ev_init = c._wT_done
    c._l1w8_cmb_gate = l0_cmb
    _emit_l1_w8(c)

    c._tb_pre = [c._tr_ev_pre[0], c._tr_ev_pre[1]]
    c._gen_ps_gate = [None, None]
    c._wsl_gate = [None, None]
    c._scan_consumed = ep_end_vaux
    c._scan_wrow_gate = (c.v_aux.h, ep_end_vaux)
    _emit_l1_scan(c)
    _emit_l1_outproj(c)

    c._l1fc2b_gate = l0_cmb
    _emit_moe1(c)
    _emit_l1_final(c)
    _emit_dbg(c)
    _emit_final_wait(c)

    with nc.Block() as block:
        c.rec.register(block)
    ctx.close()
    return nc


def shard_inputs_fused(inputs):
    maps = []
    x = np.ascontiguousarray(inputs["L0_attn_in_features"])
    for cid in range(NCORE):
        b, h = cid // 2, cid % 2
        m = dict(host_constants(cid))
        m["x_own"] = np.ascontiguousarray(x[b, h * NT:(h + 1) * NT])
        m["x_ext"] = np.ascontiguousarray(x[b, (1 - h) * NT:(2 - h) * NT])
        for k in ("L0_attn_Wq", "L0_attn_Wk", "L0_attn_Wv", "L0_attn_Wo", "L0_ffn_wg",
                  "L0_ffn_fc1_w", "L0_ffn_fc1_b", "L0_ffn_fc2_w", "L0_ffn_fc2_b",
                  "L0_ffn_ln_gamma", "L0_ffn_ln_beta",
                  "L1_attn_W_hg", "L1_attn_W_out", "L1_ffn_gate_w", "L1_ffn_w1",
                  "L1_ffn_b1", "L1_ffn_w2", "L1_ffn_b2"):
            m[k] = inputs[k]
        maps.append(m)
    return maps


def shard_inputs_l1(inputs, x1_shards):
    maps = []
    for cid in range(NCORE):
        m = dict(host_constants(cid))
        m["x1_own"] = np.ascontiguousarray(x1_shards[cid])
        m["L0_ffn_ln_gamma"] = inputs["L0_ffn_ln_gamma"]
        m["L0_ffn_ln_beta"] = inputs["L0_ffn_ln_beta"]
        for k in ("L1_attn_W_hg", "L1_attn_W_out", "L1_ffn_gate_w", "L1_ffn_w1",
                  "L1_ffn_b1", "L1_ffn_w2", "L1_ffn_b2"):
            m[k] = inputs[k]
        maps.append(m)
    return maps


_CACHE = {}


def kernel(**inputs):
    if "fused" not in _CACHE:
        _CACHE["fused"] = build_fused()
    nc = _CACHE["fused"]
    maps = shard_inputs_fused(inputs)
    res = run_bass_kernel_spmd(nc, maps, core_ids=list(range(NCORE)))
    out = np.empty((4, 4096, 1024), dtype=np.float32)
    for cid in range(NCORE):
        b, h = cid // 2, cid % 2
        out[b, h * NT:(h + 1) * NT] = res.results[cid]["x2_out"]
    return out


# version 04
# speedup vs baseline: 1.0839x; 1.0839x over previous
import numpy as np
import ml_dtypes
from contextlib import ExitStack

import concourse.bass as bass
import concourse.mybir as mybir
from concourse.bass_utils import run_bass_kernel_spmd

F32 = mybir.dt.float32
BF16 = mybir.dt.bfloat16
U32 = mybir.dt.uint32
U8 = mybir.dt.uint8
AF = mybir.ActivationFunctionType
ALU = mybir.AluOpType

D = 1024
NT = 2048
TT = NT // 128
NHEAD = 16
DH = 64
NE = 8
HFF = 4096
CAP = 5120
DI = 2048
NCORE = 8

ENGINES = ("sync", "scalar", "vector", "tensor", "gpsimd")
NO_CC = False


class Sem:
    def __init__(self, ctx, nc, name):
        self.h = ctx.enter_context(nc.semaphore(name))
        self.n = 0

    def bump(self, k=1):
        self.n += k
        return self.n


class Rec:
    def __init__(self):
        self.steps = {e: [] for e in ENGINES}

    def on(self, e):
        def deco(fn):
            self.steps[e].append(fn)
            return None
        return deco

    def register(self, block):
        for e in ENGINES:
            fns = self.steps[e]
            getattr(block, e)(lambda eng, fns=fns: [f(eng) for f in fns])


class Ctx:
    pass



def build_l0(dbg=(), moe=True, no_attn=False):
    nc = bass.Bass("TRN2", target_bir_lowering=False)
    ctx = ExitStack()
    c = Ctx()
    c.nc = nc
    c.rec = Rec()
    c.dbg = set(dbg)
    c.ctx = ctx

    A = lambda name, shape, dt: nc.dram_tensor(name, shape, dt, kind="ExternalInput").ap()
    c.x_own = A("x_own", [NT, D], F32)
    c.x_ext = A("x_ext", [NT, D], F32)
    c.Wq = A("L0_attn_Wq", [D, D], F32)
    c.Wk = A("L0_attn_Wk", [D, D], F32)
    c.Wv = A("L0_attn_Wv", [D, D], F32)
    c.Wo = A("L0_attn_Wo", [D, D], F32)
    c.wg = A("L0_ffn_wg", [D, NE], F32)
    c.fc1_w = A("L0_ffn_fc1_w", [NE, HFF, D], F32)
    c.fc1_b = A("L0_ffn_fc1_b", [NE, HFF], F32)
    c.fc2_w = A("L0_ffn_fc2_w", [NE, HFF, D], F32)
    c.fc2_b = A("L0_ffn_fc2_b", [NE, D], F32)
    c.ln_g = A("L0_ffn_ln_gamma", [D], F32)
    c.ln_b = A("L0_ffn_ln_beta", [D], F32)
    c.ident_bf = A("ident_bf", [128, 128], BF16)
    c.ident_f32 = A("ident_f32", [128, 128], F32)
    c.triu_f32 = A("triu_f32", [128, 128], F32)
    c.iota8 = A("iota8", [128, 8], F32)
    c.cmask = A("cmask", [128, 4, 512], BF16)
    c.vmask = A("vmask", [128, 32], F32)
    c.onespfx = A("ones_prefix", [8, 1], F32)

    c.x1_out = nc.dram_tensor("x1_out", [NT, D], F32, kind="ExternalOutput").ap()

    c.fc1bf = nc.dram_tensor("fc1bf", [NE * HFF, D], BF16)
    c.acc_spill = nc.dram_tensor("acc_spill", [NT, D], F32)
    c.ag_cnt_in = nc.dram_tensor("ag_cnt_in", [1, 512], F32)
    c.ag_cnt_out = nc.dram_tensor("ag_cnt_out", [8, 512], F32)

    _alloc_common(c)
    _emit_phase0(c)
    _emit_keepw(c)
    c._p0_ev_done = c.v_ev.n
    if not no_attn:
        _emit_attn(c)
    else:
        c._spill_all = None
        c._attn_end_mm = None
    if moe:
        _emit_moe0(c, no_attn=no_attn)
    _emit_dbg(c)
    _emit_final_wait(c)

    with nc.Block() as block:
        c.rec.register(block)
    ctx.close()
    return nc


def _alloc_common(c):
    nc, ctx = c.nc, c.ctx
    E = ctx.enter_context

    c.arA = E(nc.sbuf_tensor("arA", [128, 32768], U8))
    c.arB = E(nc.sbuf_tensor("arB", [128, 32768], U8))
    c.arC = E(nc.sbuf_tensor("arC", [128, 32768], U8))
    c.arD = E(nc.sbuf_tensor("arD", [128, 65536], U8))

    c.xs = E(nc.sbuf_tensor("xs", [128, 2, 1024], F32))
    c.xnb = E(nc.sbuf_tensor("xnb", [128, 2, 1024], BF16))
    c.stats = E(nc.sbuf_tensor("stats", [128, 2, 12], F32))
    c.mv = E(nc.sbuf_tensor("mv", [128, 2, 2], F32))
    c.rstd = E(nc.sbuf_tensor("rstd", [128, 2, 1], F32))
    c.eps_t = E(nc.sbuf_tensor("eps_t", [128, 1], F32))
    c.m20 = E(nc.sbuf_tensor("m20", [128, 1], F32))
    c.ones128_f = E(nc.sbuf_tensor("ones128_f", [128, 1], F32))
    c.ones1_f = E(nc.sbuf_tensor("ones1_f", [1, 128], F32))
    c.ident_bf_sb = E(nc.sbuf_tensor("ident_bf_sb", [128, 128], BF16))
    c.ident_f_sb = E(nc.sbuf_tensor("ident_f_sb", [128, 128], F32))
    c.triu_sb = E(nc.sbuf_tensor("triu_sb", [128, 128], F32))
    c.iota8_sb = E(nc.sbuf_tensor("iota8_sb", [128, 8], F32))
    c.cmask_sb = E(nc.sbuf_tensor("cmask_sb", [128, 4, 512], BF16))
    c.vmask_sb = E(nc.sbuf_tensor("vmask_sb", [128, 32], F32))
    c.onespfx_sb = E(nc.sbuf_tensor("onespfx_sb", [8, 1], F32))
    c.ones8_sb = E(nc.sbuf_tensor("ones8_sb", [8, 1], F32))
    c.wg_sb = E(nc.sbuf_tensor("wg_sb", [128, 8, 8], BF16))
    c.p1_all = E(nc.sbuf_tensor("p1_all", [128, 16, 16], F32))
    c.mask16f = E(nc.sbuf_tensor("mask16f", [128, 16, 16], F32))
    c.g_all = E(nc.sbuf_tensor("g_all", [128, 16, 2], F32))
    c.bases_all = E(nc.sbuf_tensor("bases_all", [1, 16, 16], F32))
    c.base_acc = E(nc.sbuf_tensor("base_acc", [1, 16], F32))
    c.cnt8 = E(nc.sbuf_tensor("cnt8", [8, 16], F32))
    c.allbase = E(nc.sbuf_tensor("allbase", [1, 16], F32))
    c.w8_all = E(nc.sbuf_tensor("w8_all", [128, 16, 8], F32))
    c.wT_all = E(nc.sbuf_tensor("wT_all", [8, NT], BF16))
    c.scr16 = E(nc.sbuf_tensor("scr16", [128, 16], F32))
    c.scr8 = E(nc.sbuf_tensor("scr8", [128, 8], F32))
    c.scrrow = E(nc.sbuf_tensor("scrrow", [1, 16], F32))
    c.zrow = E(nc.sbuf_tensor("zrow", [1, 512], F32))
    c.selk = E(nc.sbuf_tensor("selk", [128, 2], F32))
    c.if2 = E(nc.sbuf_tensor("if2", [128, 2], F32))
    c.topv = E(nc.sbuf_tensor("topv", [128, 8], F32))
    c.topi = E(nc.sbuf_tensor("topi", [128, 8], U32))
    c.gam_t = E(nc.sbuf_tensor("gam_t", [128, 1024], F32))
    c.bet_t = E(nc.sbuf_tensor("bet_t", [128, 1024], F32))

    c.b1col = E(nc.sbuf_tensor("b1col", [128, 2, 8], F32))
    c.b1row = E(nc.sbuf_tensor("b1row", [8, 2, 128], F32))
    c.fc2b_sb = E(nc.sbuf_tensor("fc2b_sb", [8, 1024], BF16))
    c.D_all = E(nc.sbuf_tensor("D_all", [16, 2048], F32))
    c.ginv_sb = E(nc.sbuf_tensor("ginv_sb", [128, 16], F32))

    c.ps = [E(nc.psum_tensor(f"ps{i}", [128, 512], F32)) for i in range(8)]

    for name in ("sq", "gq", "tq", "lq0", "lq1", "t_mm", "t_tr", "v_mv", "v_xnb", "v_ev",
                 "s_rstd", "s_act", "v_r", "cc", "st", "v_aux", "t_aux", "g_aux",
                 "tw0", "tw1", "gw0", "gw1", "bw0", "bw1", "aq0", "aq1", "s_silu", "v_cmb", "x1q", "agq", "fbq", "x1s0", "x1s1",
                 "wsl0", "wsl1", "a_mm", "a_exp", "a_msk", "a_pv", "dq", "dq1", "spq", "spq1", "tl0", "tl1"):
        setattr(c, name, Sem(c.ctx, c.nc, name))


def vA(c):
    return c.arA[:, :].bitcast(BF16).rearrange("p (k t) -> p k t", k=8)


def vB(c):
    return c.arB[:, :].bitcast(BF16).rearrange("p (k t) -> p k t", k=8)


def vC(c):
    return c.arC[:, :].bitcast(BF16).rearrange("p (k t) -> p k t", k=8)


def vD_acc(c):
    return c.arD[:, :].bitcast(F32).rearrange("p (t d) -> p t d", t=16)


def vD_comb(c):
    return c.arD[:, 0:32768].bitcast(BF16).rearrange("p (t d) -> p t d", t=16)


def vD_hTq(c):
    return c.arD[:, 32768:65536].bitcast(BF16).rearrange("p (k t) -> p k t", k=8)


def vW1(c):
    return c.arB[:, :].bitcast(BF16).rearrange("p (q k h) -> p q k h", q=2, k=8)


def vW2(c):
    return c.arC[:, :].bitcast(BF16).rearrange("p (q k d) -> p q k d", q=2, k=8)


def vEP_xn0n(c):
    return c.arD[:, 32768:36864].bitcast(BF16).rearrange("p (s d) -> p s d", s=2)


def vEP_r2(c):
    return c.arD[:, 36864:45056].bitcast(F32).rearrange("p (s d) -> p s d", s=2)


def vEP_acc(c):
    return c.arD[:, 45056:53248].bitcast(F32).rearrange("p (s d) -> p s d", s=2)


def vEP_x1t(c):
    return c.arD[:, 53248:61440].bitcast(F32).rearrange("p (s d) -> p s d", s=2)




def _emit_consts(c):
    on = c.rec.on

    @on("vector")
    def _(v):
        v.memset(c.eps_t[:, :], 1e-5)
        v.memset(c.m20[:, :], -20.0)
        v.memset(c.ones128_f[:, :], 1.0)
        v.memset(c.ones1_f[:, :], 1.0)
        v.memset(c.ones8_sb[:, :], 1.0)
        v.memset(c.base_acc[:, :], 0.0)
        v.memset(c.zrow[:, :], 0.0)
        v.engine_nop().then_inc(c.v_aux.h, 1)

    t_c = c.sq.bump(16 * 9)
    ms_gate = c.v_aux.bump()
    c._ms_gate = ms_gate

    @c.rec.on("scalar")
    def _(s_, ms_gate=ms_gate):
        s_.wait_ge(c.v_aux.h, ms_gate)

    @on("sync")
    def _(s):
        s.dma_start(out=c.ident_bf_sb[:, :], in_=c.ident_bf).then_inc(c.sq.h, 16)
        s.dma_start(out=c.ident_f_sb[:, :], in_=c.ident_f32).then_inc(c.sq.h, 16)
        s.dma_start(out=c.triu_sb[:, :], in_=c.triu_f32).then_inc(c.sq.h, 16)
        s.dma_start(out=c.iota8_sb[:, :], in_=c.iota8).then_inc(c.sq.h, 16)
        s.dma_start(out=c.cmask_sb[:, :, :], in_=c.cmask).then_inc(c.sq.h, 16)
        s.dma_start(out=c.vmask_sb[:, :], in_=c.vmask).then_inc(c.sq.h, 16)
        s.dma_start(out=c.onespfx_sb[:, :], in_=c.onespfx).then_inc(c.sq.h, 16)
        s.dma_start(out=vGBROW(c)[0:1, 0:1024], in_=c.ln_g[None, :]).then_inc(c.sq.h, 16)
        s.dma_start(out=vGBROW(c)[0:1, 1024:2048], in_=c.ln_b[None, :]).then_inc(c.sq.h, 16)

    c.wg_ld = c.gq.bump(16)

    @on("gpsimd")
    def _(g):
        g.dma_start(out=c.wg_sb[:, :, :], in_=c.wg.rearrange("(k p) e -> p k e", p=128)).then_inc(c.gq.h, 16)

    mm_a = c.t_mm.bump()
    ev_a = c.v_ev.bump(2)
    mm_b = c.t_mm.bump()
    ev_b = c.v_ev.bump(2)

    @on("tensor")
    def _(t, t_c=t_c, mm_a=mm_a, ev_a=ev_a, mm_b=mm_b, ms_gate=ms_gate):
        t.wait_ge(c.v_aux.h, ms_gate)
        t.wait_ge(c.sq.h, t_c)
        t.matmul(c.ps[6][:, :], c.ones1_f[:, :], vGBROW(c)[0:1, 0:512], start=True, stop=True)
        t.matmul(c.ps[7][:, :], c.ones1_f[:, :], vGBROW(c)[0:1, 1024:1536], start=True, stop=True).then_inc(c.t_mm.h, 1)
        t.wait_ge(c.v_ev.h, ev_a)
        t.matmul(c.ps[6][:, :], c.ones1_f[:, :], vGBROW(c)[0:1, 512:1024], start=True, stop=True)
        t.matmul(c.ps[7][:, :], c.ones1_f[:, :], vGBROW(c)[0:1, 1536:2048], start=True, stop=True).then_inc(c.t_mm.h, 1)

    @on("vector")
    def _(v, mm_a=mm_a, mm_b=mm_b):
        v.wait_ge(c.t_mm.h, mm_a)
        v.tensor_copy(out=c.gam_t[:, 0:512], in_=c.ps[6][:, :]).then_inc(c.v_ev.h, 1)
        v.tensor_copy(out=c.bet_t[:, 0:512], in_=c.ps[7][:, :]).then_inc(c.v_ev.h, 1)
        v.wait_ge(c.t_mm.h, mm_b)
        v.tensor_copy(out=c.gam_t[:, 512:1024], in_=c.ps[6][:, :]).then_inc(c.v_ev.h, 1)
        v.tensor_copy(out=c.bet_t[:, 512:1024], in_=c.ps[7][:, :]).then_inc(c.v_ev.h, 1)

    c.gb_done = ev_b


def _emit_ln_tile(c, src_ap, tt, dstT, routing, extra_gates=None):
    on = c.rec.on
    p = c._lnidx % 2
    c._lnidx += 1

    need_xs_free = c._xnb_cons[p]
    need_xnb_free = c._xnb_tr[p]
    lq = c.lq0 if p == 0 else c.lq1
    ld_t = lq.bump(16)

    @on("sync")
    def _(s, need_xs_free=need_xs_free, lq=lq, extra_gates=extra_gates):
        if extra_gates:
            for sem_h, val in extra_gates:
                s.wait_ge(sem_h, val)
        if need_xs_free is not None:
            s.wait_ge(c.v_xnb.h, need_xs_free)
        s.dma_start(out=c.xs[:, p, :], in_=src_ap[tt * 128:(tt + 1) * 128, :]).then_inc(lq.h, 16)

    mv_t = c.v_mv.bump()

    @on("vector")
    def _(v, ld_t=ld_t, lq=lq):
        v.wait_ge(lq.h, ld_t)
        v.bn_stats(c.stats[:, p, 0:6], c.xs[:, p, 0:512])
        v.bn_stats(c.stats[:, p, 6:12], c.xs[:, p, 512:1024])
        v.drain()
        v.bn_aggr(c.mv[:, p, :], c.stats[:, p, :]).then_inc(c.v_mv.h, 1)

    rs_t = c.s_rstd.bump()

    @on("scalar")
    def _(s, mv_t=mv_t):
        s.wait_ge(c.v_mv.h, mv_t)
        s.activation(out=c.rstd[:, p, :], in_=c.mv[:, p, 1:2], func=AF.Ln, bias=c.eps_t[:, 0:1], scale=1.0)
        s.drain()
        s.activation(out=c.rstd[:, p, :], in_=c.rstd[:, p, :], func=AF.Exp, bias=0.0, scale=-0.5).then_inc(c.s_rstd.h, 1)

    xnb_t = c.v_xnb.bump()
    c._xnb_cons[p] = xnb_t

    @on("vector")
    def _(v, rs_t=rs_t, need_xnb_free=need_xnb_free):
        v.wait_ge(c.s_rstd.h, rs_t)
        if need_xnb_free is not None:
            v.wait_ge(c.t_tr.h, need_xnb_free)
        v.tensor_scalar(out=c.xnb[:, p, :], in0=c.xs[:, p, :], scalar1=c.mv[:, p, 0:1],
                        scalar2=c.rstd[:, p, 0:1], op0=ALU.subtract, op1=ALU.mult).then_inc(c.v_xnb.h, 1)

    trs = [c.t_tr.bump() for _ in range(8)]
    c._xnb_tr[p] = trs[-1]
    evs = [c.v_ev.bump() for _ in range(8)]
    pre = (c._tr_ev_pre[0], c._tr_ev_pre[1])
    c._tr_ev_pre = [evs[6], evs[7]]

    @on("tensor")
    def _(t, xnb_t=xnb_t, pre=pre, evs=evs):
        t.wait_ge(c.v_xnb.h, xnb_t)
        for j in range(8):
            bank = 4 + (j % 2)
            gate = pre[j % 2] if j < 2 else evs[j - 2]
            if gate is not None:
                t.wait_ge(c.v_ev.h, gate)
            pv = c.ps[bank][:, :].bitcast(BF16)
            t.transpose(pv[:, 0:128], c.xnb[:, p, j * 128:(j + 1) * 128], c.ident_bf_sb[:, :]).then_inc(c.t_tr.h, 1)

    @on("vector")
    def _(v, trs=trs, evs=evs):
        for j in range(8):
            bank = 4 + (j % 2)
            pv = c.ps[bank][:, :].bitcast(BF16)
            v.wait_ge(c.t_tr.h, trs[j])
            v.tensor_copy(out=dstT(c)[:, j, tt * 128:(tt + 1) * 128], in_=pv[:, 0:128]).then_inc(c.v_ev.h, 1)

    if not routing:
        return

    lastev = evs[-1]
    rt_gate = c._rt_prev
    mm_t = c.t_mm.bump()

    @on("tensor")
    def _(t, lastev=lastev, rt_gate=rt_gate, wgld=c.wg_ld):
        t.wait_ge(c.v_ev.h, lastev)
        t.wait_ge(c.gq.h, wgld)
        if rt_gate is not None:
            t.wait_ge(c.v_r.h, rt_gate)
        for k in range(8):
            mm = t.matmul(c.ps[3][:, 0:8], vA(c)[:, k, tt * 128:(tt + 1) * 128], c.wg_sb[:, k, :],
                          start=(k == 0), stop=(k == 7))
        mm.then_inc(c.t_mm.h, 1)

    vr1 = c.v_r.bump()

    @on("vector")
    def _(v, mm_t=mm_t):
        v.wait_ge(c.t_mm.h, mm_t)
        v.max(c.topv[:, :], c.ps[3][:, 0:8])
        v.drain()
        v.max_index(c.topi[:, :], c.topv[:, :], c.ps[3][:, 0:8])
        v.tensor_tensor(out=c.if2[:, 0:1], in0=c.topv[:, 0:1], in1=c.topv[:, 1:2], op=ALU.subtract)
        v.drain()
        v.tensor_copy(out=c.if2[:, 1:2], in_=c.topi[:, 1:2])
        v.tensor_copy(out=c.scr16[:, 0:1], in_=c.topi[:, 0:1])
        v.drain()
        v.tensor_single_scalar(out=c.mask16f[:, tt, 0:8], in_=c.iota8_sb[:, :], scalar=c.scr16[:, 0:1], op=ALU.is_equal)
        v.tensor_single_scalar(out=c.mask16f[:, tt, 8:16], in_=c.iota8_sb[:, :], scalar=c.if2[:, 1:2], op=ALU.is_equal).then_inc(c.v_r.h, 1)

    sg_t = c.s_act.bump()

    @on("scalar")
    def _(s, vr1=vr1):
        s.wait_ge(c.v_r.h, vr1)
        s.activation(out=c.g_all[:, tt, 0:1], in_=c.if2[:, 0:1], func=AF.Sigmoid, bias=0.0, scale=1.0)
        s.drain()
        s.activation(out=c.g_all[:, tt, 1:2], in_=c.if2[:, 0:1], func=AF.Sigmoid, bias=0.0, scale=-1.0).then_inc(c.s_act.h, 1)

    if routing == "l1":
        vr2 = c.v_r.bump()
        c._rt_prev = vr2

        @on("vector")
        def _(v, sg_t=sg_t):
            v.wait_ge(c.s_act.h, sg_t)
            v.engine_nop().then_inc(c.v_r.h, 1)
        return

    mm2 = c.t_mm.bump()

    @on("tensor")
    def _(t, vr1=vr1):
        t.wait_ge(c.v_r.h, vr1)
        t.matmul(c.ps[2][:, 0:16], c.triu_sb[:, :], c.mask16f[:, tt, :], start=True, stop=True)
        t.matmul(c.ps[1][0:1, 0:16], c.ones128_f[:, :], c.mask16f[:, tt, :], start=True, stop=True).then_inc(c.t_mm.h, 1)

    vr2 = c.v_r.bump()
    c._rt_prev = vr2

    @on("vector")
    def _(v, mm2=mm2, sg_t=sg_t):
        v.wait_ge(c.t_mm.h, mm2)
        v.wait_ge(c.s_act.h, sg_t)
        v.tensor_copy(out=c.p1_all[:, tt, :], in_=c.ps[2][:, 0:16])
        v.tensor_copy(out=c.bases_all[:, tt, :], in_=c.base_acc[:, :])
        v.drain()
        v.tensor_tensor(out=c.base_acc[:, :], in0=c.base_acc[:, :], in1=c.ps[1][0:1, 0:16], op=ALU.add).then_inc(c.v_r.h, 1)


def _emit_phase0(c):
    on = c.rec.on
    c._lnidx = 0
    c._xnb_cons = [None, None]
    c._xnb_tr = [None, None]
    c._tr_ev_pre = [None, None]
    c._rt_prev = None

    _emit_consts(c)

    for tt in range(TT):
        _emit_ln_tile(c, c.x_own, tt, vA, routing='l0')
    for tt in range(TT):
        _emit_ln_tile(c, c.x_ext, tt, vB, routing=False)

    last_r = c._rt_prev
    g1 = c.agq.bump(16)
    c.agq.bump(16)
    cc1 = c.cc.bump(16 * NCORE if NO_CC else 1)
    g2 = c.agq.bump(16)

    @on("gpsimd")
    def _(g, last_r=last_r, g1=g1, cc1=cc1):
        g.wait_ge(c.v_r.h, last_r)
        g.dma_start(out=c.ag_cnt_in.ap(), in_=c.zrow[:, :]).then_inc(c.agq.h, 16)
        g.wait_ge(c.agq.h, g1)
        g.dma_start(out=c.ag_cnt_in.ap()[0:1, 0:16], in_=c.base_acc[:, :]).then_inc(c.agq.h, 16)
        g.wait_ge(c.agq.h, g1 + 16)
        if NO_CC:
            for r in range(NCORE):
                g.dma_start(out=c.ag_cnt_out.ap()[r:r + 1, :], in_=c.ag_cnt_in.ap()).then_inc(c.cc.h, 16)
        else:
            g.collective_compute(
                "AllGather", ALU.bypass,
                replica_groups=[list(range(NCORE))],
                ins=[c.ag_cnt_in.ap().opt()],
                outs=[c.ag_cnt_out.ap().opt()],
            ).then_inc(c.cc.h, 1)
        g.wait_ge(c.cc.h, cc1)
        g.dma_start(out=c.cnt8[:, :], in_=c.ag_cnt_out.ap()[:, 0:16]).then_inc(c.agq.h, 16)

    mm3 = c.t_mm.bump()

    @on("tensor")
    def _(t, g2=g2):
        t.wait_ge(c.agq.h, g2)
        t.matmul(c.ps[2][0:1, 0:16], c.onespfx_sb[:, :], c.cnt8[:, :], start=True, stop=True)
        t.matmul(c.ps[1][0:1, 0:16], c.ones8_sb[:, :], c.cnt8[:, :], start=True, stop=True).then_inc(c.t_mm.h, 1)

    ab_t = c.v_r.bump()
    c._allbase_done = ab_t

    @on("vector")
    def _(v, mm3=mm3):
        v.wait_ge(c.t_mm.h, mm3)
        v.memset(c.scrrow[0:1, 0:16], 0.0)
        v.drain()
        v.tensor_copy(out=c.scrrow[0:1, 8:16], in_=c.ps[1][0:1, 0:8])
        v.drain()
        v.tensor_tensor(out=c.allbase[:, :], in0=c.ps[2][0:1, 0:16], in1=c.scrrow[0:1, 0:16], op=ALU.add).then_inc(c.v_r.h, 1)



def _emit_keepw(c):
    on = c.rec.on
    kw_prev = None
    ev_prev = None
    for tt in range(TT):
        va = c.v_aux.bump()

        @on("vector")
        def _(v, tt=tt, first=(tt == 0), kw_prev=kw_prev):
            if first:
                v.wait_ge(c.v_r.h, c._allbase_done)
            v.tensor_tensor(out=c.scrrow[0:1, 0:16], in0=c.allbase[0:1, :], in1=c.bases_all[0:1, tt, :], op=ALU.add).then_inc(c.v_aux.h, 1)

        mm = c.t_mm.bump()

        @on("tensor")
        def _(t, va=va, kw_prev=kw_prev):
            t.wait_ge(c.v_aux.h, va)
            if kw_prev is not None:
                t.wait_ge(c.v_r.h, kw_prev)
            t.matmul(c.ps[3][:, 0:16], c.ones1_f[:, :], c.scrrow[0:1, 0:16], start=True, stop=True).then_inc(c.t_mm.h, 1)

        kw = c.v_r.bump()
        kw_prev = kw

        @on("vector")
        def _(v, tt=tt, mm=mm):
            v.wait_ge(c.t_mm.h, mm)
            v.tensor_tensor(out=c.scr16[:, 0:16], in0=c.p1_all[:, tt, :], in1=c.ps[3][:, 0:16], op=ALU.add)
            v.drain()
            for k in range(2):
                v.tensor_tensor(out=c.scr8[:, 0:8], in0=c.scr16[:, k * 8:(k + 1) * 8],
                                in1=c.mask16f[:, tt, k * 8:(k + 1) * 8], op=ALU.mult)
                v.drain()
                v.reduce_sum(c.selk[:, k:k + 1], c.scr8[:, 0:8], axis=mybir.AxisListType.X)
                v.drain()
            v.tensor_single_scalar(out=c.selk[:, 0:2], in_=c.selk[:, 0:2], scalar=float(CAP) + 0.5, op=ALU.is_lt)
            v.drain()
            v.tensor_tensor(out=c.selk[:, 0:2], in0=c.selk[:, 0:2], in1=c.g_all[:, tt, :], op=ALU.mult)
            v.drain()
            v.tensor_single_scalar(out=c.scr16[:, 0:8], in_=c.mask16f[:, tt, 8:16], scalar=c.selk[:, 1:2], op=ALU.mult)
            v.drain()
            v.scalar_tensor_tensor(out=c.w8_all[:, tt, :], in0=c.mask16f[:, tt, 0:8], scalar=c.selk[:, 0:1],
                                   in1=c.scr16[:, 0:8], op0=ALU.mult, op1=ALU.add).then_inc(c.v_r.h, 1)

        trw = c.t_tr.bump()

        @on("tensor")
        def _(t, tt=tt, kw=kw, ev_prev=ev_prev):
            t.wait_ge(c.v_r.h, kw)
            if ev_prev is not None:
                t.wait_ge(c.v_ev.h, ev_prev)
            t.transpose(c.ps[2][0:8, 0:128], c.w8_all[:, tt, :], c.ident_f_sb[:, :]).then_inc(c.t_tr.h, 1)

        evw = c.v_ev.bump()
        ev_prev = evw

        @on("vector")
        def _(v, tt=tt, trw=trw):
            v.wait_ge(c.t_tr.h, trw)
            v.tensor_copy(out=c.wT_all[0:8, tt * 128:(tt + 1) * 128], in_=c.ps[2][0:8, 0:128]).then_inc(c.v_ev.h, 1)

    c._w8_done = kw_prev
    c._wT_done = ev_prev






def vAT_qT2(c):
    return c.arD[:, 0:4096].bitcast(BF16)


def vAT_kT2(c):
    return c.arD[:, 4096:12288].bitcast(BF16)


def vAT_vT2(c):
    return c.arD[:, 12288:20480].bitcast(BF16)


def vAT_vaug(c):
    return c.arD[:, 20480:28928].bitcast(BF16).rearrange("p (k a) -> p k a", k=32)

def vAT_pt(c):
    return c.arD[:, 61440:65536].bitcast(BF16).rearrange("p (s a) -> p s a", s=4)

def vWSL(c):
    return c.arD[:, 32768:45056].bitcast(BF16).rearrange("p (q w k a) -> p q w k a", q=2, w=3, k=8)


def vWROW(c):
    return c.arD[:, 45056:57344].bitcast(BF16).rearrange("p (q w a) -> p q w a", q=2, w=3)


def vGBROW(c):
    return c.arD[0:1, 57344:65536].bitcast(F32)


def vDROW(c):
    return c.arD[0:1, 57344:61440].bitcast(F32).rearrange("p (s a) -> p s a", s=2)


def vWoT(c):
    return c.arB[:, 0:16384].bitcast(BF16).rearrange("p (k a) -> p k a", k=8)





def _emit_attn_pair(c, hp):
    on = c.rec.on
    par = hp % 2
    wsl = c.wsl0 if par == 0 else c.wsl1

    wload = [wsl.bump(16) for _ in range(3)]
    slot_gate = c._wsl_gate[par]

    @on("gpsimd")
    def _(g, hp=hp, par=par, wsl=wsl, slot_gate=slot_gate):
        if slot_gate is not None:
            g.wait_ge(c.t_mm.h, slot_gate)
        for w, W in enumerate((c.Wq, c.Wk, c.Wv)):
            g.dma_start(out=vWROW(c)[:, par, w, :], in_=W[hp * 128:(hp + 1) * 128, :]).then_inc(wsl.h, 16)

    trs = [c.t_tr.bump() for _ in range(24)]
    evs = [c.v_ev.bump() for _ in range(24)]
    pre = tuple(c._tb_pre)
    c._tb_pre = [evs[22], evs[23]]

    @on("tensor")
    def _(t, par=par, wsl=wsl, wload=wload, evs=evs, pre=pre,
          exp_gate=(c.a_exp.n if hp > 0 else None)):
        t.wait_ge(wsl.h, wload[2])
        if exp_gate is not None:
            t.wait_ge(c.a_exp.h, exp_gate)
        n = 0
        for w in range(3):
            for dblk in range(8):
                bank = 4 + (n % 2)
                gate = pre[n % 2] if n < 2 else evs[n - 2]
                if gate is not None:
                    t.wait_ge(c.v_ev.h, gate)
                pv = c.ps[bank][:, :].bitcast(BF16)
                t.transpose(pv[:, 0:128], vWROW(c)[:, par, w, dblk * 128:(dblk + 1) * 128],
                            c.ident_bf_sb[:, :]).then_inc(c.t_tr.h, 1)
                n += 1

    @on("vector")
    def _(v, par=par, trs=trs):
        n = 0
        for w in range(3):
            for dblk in range(8):
                bank = 4 + (n % 2)
                pv = c.ps[bank][:, :].bitcast(BF16)
                v.wait_ge(c.t_tr.h, trs[n])
                v.tensor_copy(out=vWSL(c)[:, par, w, dblk, :], in_=pv[:, 0:128]).then_inc(c.v_ev.h, 1)
                n += 1

    wsl_ev = evs[-1]

    specs = []
    for w, nchunk in ((0, 4), (1, 8), (2, 8)):
        for ch in range(nchunk):
            specs.append((w, ch))
    gen_sv0 = c.s_act.n
    for n, (w, ch) in enumerate(specs):
        mm = c.t_mm.bump()
        sv = c.s_act.bump()

        @on("tensor")
        def _(t, w=w, ch=ch, n=n, par=par, wsl_ev=wsl_ev,
              pg=(c._gen_ps_gate[n % 2]), bank=(n % 2),
              eg=(c.a_exp.n if (n < 2 and hp > 0) else None)):
            if n == 0:
                t.wait_ge(c.v_ev.h, wsl_ev)
            if eg is not None:
                t.wait_ge(c.a_exp.h, eg)
            if pg is not None:
                t.wait_ge(c.s_act.h, pg)
            src = vA if (w == 0 or ch < 4) else vB
            cc_ = ch if (w == 0 or ch < 4) else ch - 4
            for k in range(8):
                mmh = t.matmul(c.ps[bank][:, :], vWSL(c)[:, par, w, k, :],
                               src(c)[:, k, cc_ * 512:(cc_ + 1) * 512], start=(k == 0), stop=(k == 7))
            mmh.then_inc(c.t_mm.h, 1)

        @on("scalar")
        def _(s_, w=w, ch=ch, mm=mm, bank=(n % 2)):
            s_.wait_ge(c.t_mm.h, mm)
            dst = (vAT_qT2, vAT_kT2, vAT_vT2)[w]
            s_.activation(out=dst(c)[:, ch * 512:(ch + 1) * 512], in_=c.ps[bank][:, :],
                          func=AF.Copy, bias=0.0, scale=(0.125 if w == 0 else 1.0)).then_inc(c.s_act.h, 1)

        c._gen_ps_gate[n % 2] = sv

    gen_end_sv = c.s_act.n
    c._wsl_gate[par] = c.t_mm.n

    vtr = [c.t_tr.bump() for _ in range(32)]
    vev = [c.v_aux.bump() for _ in range(32)]
    pre2 = tuple(c._tb_pre)

    @on("tensor")
    def _(t, vtr=vtr, vev=vev, gen_end_sv=gen_end_sv, pre2=pre2):
        t.wait_ge(c.s_act.h, gen_end_sv)
        for kt in range(32):
            bank = 4 + (kt % 2)
            if kt < 2:
                gate = pre2[kt % 2]
                if gate is not None:
                    t.wait_ge(c.v_ev.h, gate)
            else:
                t.wait_ge(c.v_aux.h, vev[kt - 2])
            pv = c.ps[bank][:, :].bitcast(BF16)
            t.transpose(pv[:, 0:128], vAT_vT2(c)[:, kt * 128:(kt + 1) * 128],
                        c.ident_bf_sb[:, :]).then_inc(c.t_tr.h, 1)

    c._tb_pre = [None, None]
    c._tb_pre_aux = [vev[30], vev[31]]

    @on("vector")
    def _(v, vtr=vtr):
        for kt in range(32):
            bank = 4 + (kt % 2)
            pv = c.ps[bank][:, :].bitcast(BF16)
            v.wait_ge(c.t_tr.h, vtr[kt])
            v.tensor_single_scalar(out=vAT_vaug(c)[:, kt, 0:64], in_=pv[:, 0:64],
                                   scalar=c.vmask_sb[:, kt:kt + 1], op=ALU.mult)
            v.tensor_single_scalar(out=vAT_vaug(c)[:, kt, 66:130], in_=pv[:, 64:128],
                                   scalar=c.vmask_sb[:, kt:kt + 1], op=ALU.mult)
            v.tensor_copy(out=vAT_vaug(c)[:, kt, 64:65], in_=c.vmask_sb[:, kt:kt + 1])
            v.tensor_copy(out=vAT_vaug(c)[:, kt, 130:131], in_=c.vmask_sb[:, kt:kt + 1]).then_inc(c.v_aux.h, 1)

    vaug_ev = c.v_aux.n

    for h01 in range(2):
        for i in range(4):
            kts = list(range(0, 4 * i + 4)) + list(range(16, 32))
            nk = len(kts)
            pv_gate = c._pv_evac_gate[h01]
            mms, exs, mks, ready = [], [], [], []
            STB = (0, 1, 6, 7)
            for idx, kt in enumerate(kts):
                diag = (4 * i <= kt <= 4 * i + 3)
                mm = c.a_mm.bump()
                mms.append(mm)

                @on("tensor")
                def _(t, kt=kt, idx=idx, i=i, h01=h01, b=STB[idx % 4], diag=diag,
                      vg=(vaug_ev if idx == 0 else None),
                      sg=(gen_end_sv if idx < 4 else None),
                      gg=(c.gb_done if idx < 4 else None),
                      eg=(c.a_exp.n - 3 if idx >= 4 else (c.a_exp.n if c.a_exp.n else None)),
                      pveg=(pv_gate if idx == 0 else None)):
                    if vg is not None:
                        t.wait_ge(c.v_aux.h, vg)
                    if sg is not None:
                        t.wait_ge(c.s_act.h, sg)
                    if gg is not None:
                        t.wait_ge(c.v_ev.h, gg)
                    if pveg is not None:
                        t.wait_ge(c.s_act.h, pveg)
                    if eg is not None:
                        t.wait_ge(c.a_exp.h, eg)
                    t.matmul(c.ps[b][:, :], vAT_kT2(c)[h01 * 64:(h01 + 1) * 64, kt * 128:(kt + 1) * 128],
                             vAT_qT2(c)[h01 * 64:(h01 + 1) * 64, i * 512:(i + 1) * 512],
                             start=True, stop=True).then_inc(c.a_mm.h, 1)

                ex = c.a_exp.bump()
                exs.append(ex)

                @on("scalar")
                def _(s_, idx=idx, b=STB[idx % 4], mm=mm,
                      pvg=(c.a_pv.n - 3 if idx >= 4 else (c._pt_gate[idx % 4]))):
                    s_.wait_ge(c.a_mm.h, mm)
                    if pvg is not None:
                        s_.wait_ge(c.a_pv.h, pvg)
                    s_.activation(out=vAT_pt(c)[:, idx % 4, 0:512], in_=c.ps[b][:, :],
                                  func=AF.Exp, bias=c.m20[:, 0:1], scale=1.0).then_inc(c.a_exp.h, 1)

                if diag:
                    mk = c.a_msk.bump()
                    mks.append((idx, mk))

                    @on("vector")
                    def _(v, kt=kt, i=i, idx=idx, ex=ex):
                        v.wait_ge(c.a_exp.h, ex)
                        v.tensor_tensor(out=vAT_pt(c)[:, idx % 4, 0:512], in0=vAT_pt(c)[:, idx % 4, 0:512],
                                        in1=c.cmask_sb[:, kt - 4 * i, :], op=ALU.mult).then_inc(c.a_msk.h, 1)
                    ready.append(("m", mk))
                else:
                    ready.append(("e", ex))

                pvn = c.a_pv.bump()

                @on("tensor")
                def _(t, kt=kt, idx=idx, h01=h01, b=(idx % 4), nk=nk, r=ready[idx]):
                    t.wait_ge((c.a_msk.h if r[0] == "m" else c.a_exp.h), r[1])
                    t.matmul(c.ps[2 + h01][0:65, :], vAT_vaug(c)[:, kt, h01 * 66:h01 * 66 + 65],
                             vAT_pt(c)[:, b, 0:512], start=(idx == 0), stop=(idx == nk - 1)).then_inc(c.a_pv.h, 1)

            pvend = c.a_pv.n
            c._pt_gate = [pvend, pvend, pvend, pvend]

            dslot = c._dslot % 2
            dqs = c.dq if dslot == 0 else c.dq1
            dgate = c._drow_gate[dslot]
            sa1 = c.s_act.bump()
            sa2 = c.s_act.bump()
            dq_t = dqs.bump(16)
            c._drow_gate[dslot] = dq_t
            c._dslot += 1

            @on("scalar")
            def _(s_, hp=hp, h01=h01, i=i, pvend=pvend, dslot=dslot, dgate=dgate, dqs=dqs):
                s_.wait_ge(c.a_pv.h, pvend)
                if dgate is not None:
                    s_.wait_ge(dqs.h, dgate)
                s_.activation(out=vC(c)[h01 * 64:(h01 + 1) * 64, hp, i * 512:(i + 1) * 512],
                              in_=c.ps[2 + h01][0:64, :], func=AF.Copy, bias=0.0, scale=1.0).then_inc(c.s_act.h, 1)
                s_.activation(out=vDROW(c)[0:1, dslot, :], in_=c.ps[2 + h01][64:65, :],
                              func=AF.Copy, bias=0.0, scale=1.0).then_inc(c.s_act.h, 1)

            c._pv_evac_gate[h01] = c.s_act.n

            @on("gpsimd")
            def _(g, hp=hp, h01=h01, i=i, sa2=sa2, dslot=dslot, dqs=dqs):
                g.wait_ge(c.s_act.h, sa2)
                g.dma_start(out=c.D_all[2 * hp + h01:2 * hp + h01 + 1, i * 512:(i + 1) * 512],
                            in_=vDROW(c)[0:1, dslot, :]).then_inc(dqs.h, 16)


def _emit_attn(c):
    on = c.rec.on
    c._wsl_gate = [None, None]
    c._tb_pre = [c._tr_ev_pre[0], c._tr_ev_pre[1]]
    c._tb_pre_aux = [None, None]
    c._gen_ps_gate = [None, None]
    c._pv_evac_gate = [None, None]
    c._pt_gate = [None, None, None, None]
    c._drow_gate = [None, None]
    c._dslot = 0

    for hp in range(8):
        _emit_attn_pair(c, hp)

    dq_all = c.dq.n
    dq1_all = c.dq1.n
    rec_v = c.v_aux.bump()

    @on("vector")
    def _(v, dq_all=dq_all, dq1_all=dq1_all):
        v.wait_ge(c.dq.h, dq_all)
        v.wait_ge(c.dq1.h, dq1_all)
        v.reciprocal(out=c.D_all[0:16, :], in_=c.D_all[0:16, :])
        v.drain()
        v.engine_nop().then_inc(c.v_aux.h, 1)

    attn_exp_all = c.a_exp.n
    attn_sact_all = c.s_act.n
    prev_ev = None
    for ob in range(8):
        ld = c.wsl0.bump(16)

        @on("gpsimd")
        def _(g, ob=ob, prev_ev=prev_ev, gate=(c._wsl_gate[1] if ob == 0 else None)):
            if gate is not None:
                g.wait_ge(c.t_mm.h, gate)
            if prev_ev is not None:
                g.wait_ge(c.v_ev.h, prev_ev)
            g.dma_start(out=vWROW(c)[:, ob % 2, 0, :], in_=c.Wo[ob * 128:(ob + 1) * 128, :]).then_inc(c.wsl0.h, 16)

        trs = [c.t_tr.bump() for _ in range(8)]
        evs = [c.v_ev.bump() for _ in range(8)]
        pre = tuple(c._tb_pre_aux) if ob == 0 else (None, None)

        @on("tensor")
        def _(t, ob=ob, ld=ld, trs=trs, evs=evs, pre=pre, prev_ev=prev_ev):
            t.wait_ge(c.wsl0.h, ld)
            for ab in range(8):
                bank = 4 + (ab % 2)
                if ab < 2:
                    if ob == 0:
                        if pre[ab] is not None:
                            t.wait_ge(c.v_aux.h, pre[ab])
                    elif prev_ev is not None:
                        t.wait_ge(c.v_ev.h, prev_ev)
                else:
                    t.wait_ge(c.v_ev.h, evs[ab - 2])
                pv = c.ps[bank][:, :].bitcast(BF16)
                t.transpose(pv[:, 0:128], vWROW(c)[:, ob % 2, 0, ab * 128:(ab + 1) * 128],
                            c.ident_bf_sb[:, :]).then_inc(c.t_tr.h, 1)

        @on("vector")
        def _(v, ob=ob, trs=trs):
            for ab in range(8):
                bank = 4 + (ab % 2)
                pv = c.ps[bank][:, :].bitcast(BF16)
                v.wait_ge(c.t_tr.h, trs[ab])
                v.tensor_copy(out=vWoT(c)[:, ab, ob * 128:(ob + 1) * 128], in_=pv[:, 0:128]).then_inc(c.v_ev.h, 1)

        prev_ev = evs[-1]

    wo_ev_all = c.v_ev.n

    c._ginv_pre = None
    c._wo_ps_gate = [None, None]
    c._xres_gate = [None, None]
    c._spill_gate = [None, None]
    xnb_all = c.v_xnb.n
    for tt in range(TT):
        p = tt % 2
        gtr = c.t_tr.bump()
        gev = c.v_aux.bump()

        @on("tensor")
        def _(t, tt=tt, gtr=gtr, rec_v=rec_v, first=(tt == 0), gpre=c._ginv_pre, wo_ev_all=wo_ev_all):
            if first:
                t.wait_ge(c.v_aux.h, rec_v)
                t.wait_ge(c.v_ev.h, wo_ev_all)
            if gpre is not None:
                t.wait_ge(c.v_aux.h, gpre)
            t.transpose(c.ps[5][0:128, 0:16], c.D_all[0:16, tt * 128:(tt + 1) * 128],
                        c.ident_f_sb[0:16, 0:16]).then_inc(c.t_tr.h, 1)

        c._ginv_pre = gev

        @on("vector")
        def _(v, gtr=gtr):
            v.wait_ge(c.t_tr.h, gtr)
            v.tensor_copy(out=c.ginv_sb[:, :], in_=c.ps[5][0:128, 0:16]).then_inc(c.v_aux.h, 1)

        lq = c.lq0 if p == 0 else c.lq1
        xld = lq.bump(16)

        @on("sync")
        def _(s_, tt=tt, p=p, lq=lq, gate=c._xres_gate[p], xg=(xnb_all if tt < 2 else None)):
            if xg is not None:
                s_.wait_ge(c.v_xnb.h, xg)
            if gate is not None:
                s_.wait_ge(c.v_aux.h, gate)
            s_.dma_start(out=c.xs[:, p, :], in_=c.x_own[tt * 128:(tt + 1) * 128, :]).then_inc(lq.h, 16)

        for oc in range(2):
            for hd in range(16):
                b = hd % 2
                mm = c.t_mm.bump()

                @on("tensor")
                def _(t, tt=tt, oc=oc, hd=hd, b=b, first=(tt == 0 and oc == 0 and hd < 2),
                      wo_ev_all=wo_ev_all, sact=attn_sact_all, aexp=attn_exp_all,
                      wog=c._wo_ps_gate[b]):
                    if first:
                        t.wait_ge(c.v_ev.h, wo_ev_all)
                        t.wait_ge(c.s_act.h, sact)
                        t.wait_ge(c.a_exp.h, aexp)
                    if wog is not None:
                        t.wait_ge(c.v_cmb.h, wog)
                    t.matmul(c.ps[b][:, :], vC(c)[(hd % 2) * 64:(hd % 2) * 64 + 64, hd // 2, tt * 128:(tt + 1) * 128],
                             vWoT(c)[(hd % 2) * 64:(hd % 2) * 64 + 64, hd // 2, oc * 512:(oc + 1) * 512],
                             start=True, stop=True).then_inc(c.t_mm.h, 1)

                cm = c.v_cmb.bump()
                c._wo_ps_gate[b] = cm

                @on("vector")
                def _(v, tt=tt, oc=oc, hd=hd, b=b, mm=mm, gev=gev):
                    v.wait_ge(c.t_mm.h, mm)
                    if hd == 0:
                        v.wait_ge(c.v_aux.h, gev)
                        v.tensor_single_scalar(out=vD_acc(c)[:, tt, oc * 512:(oc + 1) * 512], in_=c.ps[b][:, :],
                                               scalar=c.ginv_sb[:, 0:1], op=ALU.mult)
                        v.drain()
                        v.engine_nop().then_inc(c.v_cmb.h, 1)
                    else:
                        v.scalar_tensor_tensor(out=vD_acc(c)[:, tt, oc * 512:(oc + 1) * 512], in0=c.ps[b][:, :],
                                               scalar=c.ginv_sb[:, hd:hd + 1],
                                               in1=vD_acc(c)[:, tt, oc * 512:(oc + 1) * 512],
                                               op0=ALU.mult, op1=ALU.add)
                        v.drain()
                        v.engine_nop().then_inc(c.v_cmb.h, 1)

        va = c.v_aux.bump()
        c._xres_gate[p] = va
        spqs = c.spq if p == 0 else c.spq1
        sp = spqs.bump(16)

        @on("vector")
        def _(v, tt=tt, p=p, xld=xld, lq=lq, spg=c._spill_gate[p], spqs=spqs):
            v.wait_ge(lq.h, xld)
            if spg is not None:
                v.wait_ge(spqs.h, spg)
            v.tensor_tensor(out=vD_acc(c)[:, tt, :], in0=vD_acc(c)[:, tt, :], in1=c.xs[:, p, :],
                            op=ALU.add).then_inc(c.v_aux.h, 1)

        c._spill_gate[p] = sp

        @on("gpsimd")
        def _(g, tt=tt, va=va, spqs=spqs):
            g.wait_ge(c.v_aux.h, va)
            g.dma_start(out=c.acc_spill.ap()[tt * 128:(tt + 1) * 128, :], in_=vD_acc(c)[:, tt, :]).then_inc(spqs.h, 16)

    c._spill_all = (c.spq.n, c.spq1.n)
    c._attn_end_mm = c.t_mm.n



def _emit_moe0_prep(c):
    on = c.rec.on
    c._gcast = Sem(c.ctx, c.nc, "gcast")
    casts = [c._gcast.bump(16) for _ in range(NE)]
    fb = c.fbq.bump(16)
    c._fc2b_ld = fb

    @on("gpsimd")
    def _(g, casts=casts, mb2=c.m_b2):
        for e in range(NE):
            if e > 0:
                g.wait_ge(c._gcast.h, casts[e - 1])
            g.dma_start(out=c.fc1bf.ap()[e * HFF:(e + 1) * HFF, :], in_=c.fc1_w[e]).then_inc(c._gcast.h, 16)
        g.dma_start(out=c.fc2b_sb[:, :], in_=mb2).then_inc(c.fbq.h, 16)


def _emit_moe0_combinit(c):
    on = c.rec.on
    comb = vD_comb
    prev_ci = None
    wT_done = c._wT_done
    fc2b_ld = c._fc2b_ld
    for tt in range(TT):
        mm = c.t_mm.bump()

        @on("tensor")
        def _(t, tt=tt, first=(tt == 0), prev_ci=prev_ci, wT_done=wT_done, fc2b_ld=fc2b_ld,
              spill=(c._spill_all if tt == 0 else None)):
            if first:
                t.wait_ge(c.v_ev.h, wT_done)
                t.wait_ge(c.fbq.h, fc2b_ld)
                if spill is not None:
                    t.wait_ge(c.spq.h, spill[0])
                    t.wait_ge(c.spq1.h, spill[1])
            else:
                t.wait_ge(c.v_cmb.h, prev_ci)
            t.matmul(c.ps[4][:, :], c.wT_all[0:8, tt * 128:(tt + 1) * 128], c.fc2b_sb[0:8, 0:512],
                     start=True, stop=True)
            t.matmul(c.ps[5][:, :], c.wT_all[0:8, tt * 128:(tt + 1) * 128], c.fc2b_sb[0:8, 512:1024],
                     start=True, stop=True).then_inc(c.t_mm.h, 1)

        ci = c.v_cmb.bump()
        prev_ci = ci

        @on("vector")
        def _(v, tt=tt, mm=mm):
            v.wait_ge(c.t_mm.h, mm)
            v.tensor_copy(out=comb(c)[:, tt, 0:512], in_=c.ps[4][:, :])
            v.tensor_copy(out=comb(c)[:, tt, 512:1024], in_=c.ps[5][:, :]).then_inc(c.v_cmb.h, 1)

    c._comb_init_done = c.v_cmb.n


def _emit_moe0_quarter(c, e, q, qi):
    on = c.rec.on
    qp = qi % 2
    if c.mcfg["l1"]:
        tw = c.tl0 if qp == 0 else c.tl1
    else:
        tw = c.tw0 if qp == 0 else c.tw1
    gw = c.gw0 if qp == 0 else c.gw1
    bw = c.bw0 if qp == 0 else c.bw1

    slot_free = c._q_slot_free[qp]

    l1 = c.mcfg["l1"]
    if l1:
        tw_t = tw.bump(16 * 8)

    if not l1:
        wev_last = None
        for hch in range(8):
            sp = hch % 2
            tws = c.tw0 if sp == 0 else c.tw1
            tw_t = tws.bump(16)
            stg_gate = c._fstg_gate[sp]

            @on("gpsimd")
            def _(g, e=e, q=q, hch=hch, sp=sp, tws=tws, slot_free=slot_free, gate=c._gcast_gate[e],
                  ag=(c._arena_gate_ev if qi == 0 and hch == 0 else None),
                  agm=(c._attn_end_mm if qi == 0 and hch == 0 else None),
                  stg_gate=stg_gate, first=(hch == 0)):
                if first:
                    g.wait_ge(c._gcast.h, gate)
                    if ag is not None:
                        g.wait_ge(c.v_ev.h, ag)
                    if agm is not None:
                        g.wait_ge(c.t_mm.h, agm)
                    if slot_free is not None:
                        g.wait_ge(c.t_mm.h, slot_free)
                if stg_gate is not None:
                    g.wait_ge(c.v_ev.h, stg_gate)
                base = e * HFF + q * 1024 + hch * 128
                g.dma_start(out=c.xnb[:, sp, :], in_=c.fc1bf.ap()[base:base + 128, :]).then_inc(tws.h, 16)

            trs = [c.t_tr.bump() for _ in range(8)]
            evs = [c.v_ev.bump() for _ in range(8)]
            c._fstg_gate[sp] = evs[-1]
            wev_last = evs[-1]
            pre = tuple(c._mtr_pre)
            c._mtr_pre = [evs[6], evs[7]]

            @on("tensor")
            def _(t, sp=sp, tw_t=tw_t, tws=tws, evs=evs, pre=pre,
                  civ=(getattr(c, "_comb_init_done", None) if (qi == 0 and hch == 0) else None)):
                t.wait_ge(tws.h, tw_t)
                if civ is not None:
                    t.wait_ge(c.v_cmb.h, civ)
                for dblk in range(8):
                    bank = 4 + (dblk % 2)
                    gate = pre[dblk % 2] if dblk < 2 else evs[dblk - 2]
                    if gate is not None:
                        t.wait_ge(c.v_ev.h, gate)
                    pv = c.ps[bank][:, :].bitcast(BF16)
                    t.transpose(pv[:, 0:128], c.xnb[:, sp, dblk * 128:(dblk + 1) * 128],
                                c.ident_bf_sb[:, :]).then_inc(c.t_tr.h, 1)

            @on("vector")
            def _(v, sp=sp, hch=hch, trs=trs):
                for dblk in range(8):
                    bank = 4 + (dblk % 2)
                    pv = c.ps[bank][:, :].bitcast(BF16)
                    v.wait_ge(c.t_tr.h, trs[dblk])
                    v.tensor_copy(out=vW1(c)[:, qp, dblk, hch * 128:(hch + 1) * 128],
                                  in_=pv[:, 0:128]).then_inc(c.v_ev.h, 1)

        c._w1tr_done = wev_last
        tw_t = None
    else:
        @on("gpsimd")
        def _(g, e=e, q=q, slot_free=slot_free,
              ag=(c._arena_gate_ev if qi == 0 else None),
              agm=(c._attn_end_mm if qi == 0 else None), mw1=c.m_w1):
            if ag is not None:
                g.wait_ge(c.v_ev.h, ag)
            if agm is not None:
                g.wait_ge(c.t_mm.h, agm)
            if slot_free is not None:
                g.wait_ge(c.t_mm.h, slot_free)
            for dblk in range(8):
                g.dma_start(out=vW1(c)[:, qp, dblk, :],
                            in_=mw1[e, dblk * 128:(dblk + 1) * 128, q * 1024:(q + 1) * 1024]).then_inc(tw.h, 16)

    gw_t = gw.bump(16 * 8)
    bw_t = bw.bump(16)

    @on("gpsimd")
    def _(g, e=e, q=q, slot_free=slot_free, ag=(c._arena_gate_ev if qi == 0 else None),
          agm=(c._attn_end_mm if qi == 0 else None), mw2=c.m_w2, mb1=c.m_b1):
        if ag is not None:
            g.wait_ge(c.v_ev.h, ag)
        if agm is not None:
            g.wait_ge(c.t_mm.h, agm)
        if slot_free is not None:
            g.wait_ge(c.t_mm.h, slot_free)
        for hb in range(8):
            g.dma_start(out=vW2(c)[:, qp, hb, :],
                        in_=mw2[e, q * 1024 + hb * 128:q * 1024 + (hb + 1) * 128, :]).then_inc(gw.h, 16)
        g.dma_start(out=c.b1row[0:8, qp, :],
                    in_=mb1[e, q * 1024:(q + 1) * 1024].rearrange("(c p) -> c p", p=128)).then_inc(bw.h, 16)

    btr = c.t_tr.bump()
    bev = c.v_ev.bump()

    @on("tensor")
    def _(t, bw_t=bw_t, gate=c._b1_ev_prev, ci_done=(c._comb_init_done if c._b1_ev_prev is None else None),
          mpre=c._mtr_pre[0]):
        t.wait_ge(bw.h, bw_t)
        if ci_done is not None:
            t.wait_ge(c.v_cmb.h, ci_done)
        if gate is not None:
            t.wait_ge(c.v_ev.h, gate)
        if mpre is not None:
            t.wait_ge(c.v_ev.h, mpre)
        t.transpose(c.ps[4][0:128, 0:8], c.b1row[0:8, qp, :], c.ident_f_sb[0:8, 0:8]).then_inc(c.t_tr.h, 1)

    c._b1_ev_prev = bev
    c._mtr_pre = [bev, c._mtr_pre[1]]

    @on("vector")
    def _(v, btr=btr):
        v.wait_ge(c.t_tr.h, btr)
        v.tensor_copy(out=c.b1col[:, qp, :], in_=c.ps[4][0:128, 0:8]).then_inc(c.v_ev.h, 1)

    hT = vD_hTq
    silu_start = c.s_silu.n
    mm_groups = []
    for gidx in range(32):
        ht, tc = gidx // 4, gidx % 4
        mm = c.t_mm.bump()
        mm_groups.append(mm)

        @on("tensor")
        def _(t, ht=ht, tc=tc, gidx=gidx, tw_t=tw_t, bev=bev,
              first=(gidx == 0), silu_start=silu_start, fg=(c._moe_first_gate if qi == 0 else None),
              wtr=(c._w1tr_done if (not l1 and gidx == 0) else None)):
            bank = gidx % 2
            if first:
                if fg is not None:
                    t.wait_ge(c.v_r.h, fg)
                if tw_t is not None:
                    t.wait_ge(tw.h, tw_t)
                if wtr is not None:
                    t.wait_ge(c.v_ev.h, wtr)
                t.wait_ge(c.v_ev.h, bev)
            if gidx >= 2:
                t.wait_ge(c.s_silu.h, silu_start + gidx - 1)
            for k in range(8):
                mmh = t.matmul(c.ps[bank][:, :], vW1(c)[:, qp, k, ht * 128:(ht + 1) * 128],
                               vA(c)[:, k, tc * 512:(tc + 1) * 512], start=(k == 0), stop=(k == 7))
            mmh.then_inc(c.t_mm.h, 1)

        sl = c.s_silu.bump()

        @on("scalar")
        def _(s_, ht=ht, tc=tc, gidx=gidx, mm=mm, act=c.mcfg["act"]):
            bank = gidx % 2
            s_.wait_ge(c.t_mm.h, mm)
            s_.activation(out=hT(c)[:, ht, tc * 512:(tc + 1) * 512], in_=c.ps[bank][:, :],
                          func=act, bias=c.b1col[:, qp, ht:ht + 1], scale=1.0).then_inc(c.s_silu.h, 1)

    silu_all = c.s_silu.n

    for tt in range(TT):
        pair = (2, 3) if tt % 2 == 0 else (6, 7)
        mm = c.t_mm.bump()

        @on("tensor")
        def _(t, tt=tt, pair=pair, first=(tt == 0), silu_all=silu_all, gw_t=gw_t,
              ygate=c._y_gate[tt % 2], wtd=(c._wT_done if qi == 0 and tt <= 1 else None)):
            if first:
                t.wait_ge(c.s_silu.h, silu_all)
                t.wait_ge(gw.h, gw_t)
            if wtd is not None:
                t.wait_ge(c.v_ev.h, wtd)
            if ygate is not None:
                t.wait_ge(c.v_cmb.h, ygate)
            for half in range(2):
                for k in range(8):
                    mmh = t.matmul(c.ps[pair[half]][:, :], hT(c)[:, k, tt * 128:(tt + 1) * 128],
                                   vW2(c)[:, qp, k, half * 512:(half + 1) * 512],
                                   start=(k == 0), stop=(k == 7))
            mmh.then_inc(c.t_mm.h, 1)

        cm = c.v_cmb.bump()
        c._y_gate[tt % 2] = cm

        @on("vector")
        def _(v, tt=tt, pair=pair, mm=mm, e=e):
            v.wait_ge(c.t_mm.h, mm)
            v.scalar_tensor_tensor(out=vD_comb(c)[:, tt, 0:512], in0=c.ps[pair[0]][:, :],
                                   scalar=c.w8_all[:, tt, e:e + 1], in1=vD_comb(c)[:, tt, 0:512],
                                   op0=ALU.mult, op1=ALU.add)
            v.scalar_tensor_tensor(out=vD_comb(c)[:, tt, 512:1024], in0=c.ps[pair[1]][:, :],
                                   scalar=c.w8_all[:, tt, e:e + 1], in1=vD_comb(c)[:, tt, 512:1024],
                                   op0=ALU.mult, op1=ALU.add).then_inc(c.v_cmb.h, 1)

    c._q_slot_free[qp] = c.t_mm.n


def _emit_moe0_epilogue(c):
    on = c.rec.on
    comb_all = c.v_cmb.n
    for tt in range(TT):
        p = tt % 2
        trs = [c.t_tr.bump() for _ in range(8)]
        evs = [c.v_ev.bump() for _ in range(8)]
        pre = (c._ep_tr_pre[0], c._ep_tr_pre[1])
        c._ep_tr_pre = [evs[6], evs[7]]

        @on("tensor")
        def _(t, tt=tt, p=p, trs=trs, evs=evs, pre=pre, first=(tt == 0), comb_all=comb_all,
              xgate=c._xn0n_gate[p]):
            if first:
                t.wait_ge(c.v_cmb.h, comb_all)
            if xgate is not None:
                t.wait_ge(c.v_aux.h, xgate)
            for j in range(8):
                bank = 4 + (j % 2)
                gate = pre[j % 2] if j < 2 else evs[j - 2]
                if gate is not None:
                    t.wait_ge(c.v_ev.h, gate)
                pv = c.ps[bank][:, :].bitcast(BF16)
                t.transpose(pv[:, 0:128], vA(c)[:, j, tt * 128:(tt + 1) * 128],
                            c.ident_bf_sb[:, :]).then_inc(c.t_tr.h, 1)

        @on("vector")
        def _(v, tt=tt, p=p, trs=trs):
            for j in range(8):
                bank = 4 + (j % 2)
                pv = c.ps[bank][:, :].bitcast(BF16)
                v.wait_ge(c.t_tr.h, trs[j])
                v.tensor_copy(out=vEP_xn0n(c)[:, p, j * 128:(j + 1) * 128], in_=pv[:, 0:128]).then_inc(c.v_ev.h, 1)

        aq = c.aq0 if p == 0 else c.aq1
        aq_t = aq.bump(16)

        @on("sync")
        def _(s_, tt=tt, p=p, aq=aq, agate=c._acc_gate[p], ca=(comb_all if tt < 2 else None),
              spa=(c._spill_all if tt < 2 else None)):
            if c._acc_pre is not None:
                s_.wait_ge(c.g_aux.h, c._acc_pre)
                c._acc_pre = None
            if ca is not None:
                s_.wait_ge(c.v_cmb.h, ca)
            if spa is not None:
                s_.wait_ge(c.spq.h, spa[0])
                s_.wait_ge(c.spq1.h, spa[1])
            if agate is not None:
                s_.wait_ge(c.v_aux.h, agate)
            s_.dma_start(out=vEP_acc(c)[:, p, :], in_=c.acc_spill.ap()[tt * 128:(tt + 1) * 128, :]).then_inc(aq.h, 16)

        lastev = evs[-1]
        x1sg = c._x1_store_gate[p]
        mvt = c.v_mv.bump()

        @on("vector")
        def _(v, tt=tt, p=p, lastev=lastev, xgate=x1sg):
            v.wait_ge(c.v_ev.h, lastev)
            if xgate is not None:
                v.wait_ge((c.x1s0 if p == 0 else c.x1s1).h, xgate)
            v.tensor_tensor(out=vEP_r2(c)[:, p, :], in0=vEP_xn0n(c)[:, p, :], in1=vD_comb(c)[:, tt, :], op=ALU.add)
            v.drain()
            v.bn_stats(c.stats[:, p, 0:6], vEP_r2(c)[:, p, 0:512])
            v.bn_stats(c.stats[:, p, 6:12], vEP_r2(c)[:, p, 512:1024])
            v.drain()
            v.bn_aggr(c.mv[:, p, :], c.stats[:, p, :]).then_inc(c.v_mv.h, 1)

        rs = c.s_rstd.bump()

        @on("scalar")
        def _(s_, p=p, mvt=mvt):
            s_.wait_ge(c.v_mv.h, mvt)
            s_.activation(out=c.rstd[:, p, :], in_=c.mv[:, p, 1:2], func=AF.Ln, bias=c.eps_t[:, 0:1], scale=1.0)
            s_.drain()
            s_.activation(out=c.rstd[:, p, :], in_=c.rstd[:, p, :], func=AF.Exp, bias=0.0, scale=-0.5).then_inc(c.s_rstd.h, 1)

        xa = c.v_aux.bump()
        c._xn0n_gate[p] = xa
        c._acc_gate[p] = xa
        x1s = c.x1s0 if p == 0 else c.x1s1
        st_t = x1s.bump(16)

        @on("vector")
        def _(v, tt=tt, p=p, rs=rs, aq_t=aq_t, aq=aq):
            v.wait_ge(c.s_rstd.h, rs)
            v.tensor_scalar(out=vEP_x1t(c)[:, p, :], in0=vEP_r2(c)[:, p, :], scalar1=c.mv[:, p, 0:1],
                            scalar2=c.rstd[:, p, 0:1], op0=ALU.subtract, op1=ALU.mult)
            v.drain()
            v.tensor_tensor(out=vEP_r2(c)[:, p, :], in0=vEP_x1t(c)[:, p, :], in1=c.gam_t[:, :], op=ALU.mult)
            v.drain()
            v.tensor_tensor(out=vEP_x1t(c)[:, p, :], in0=vEP_r2(c)[:, p, :], in1=c.bet_t[:, :], op=ALU.add)
            v.drain()
            v.wait_ge(aq.h, aq_t)
            v.tensor_tensor(out=vEP_r2(c)[:, p, :], in0=vEP_x1t(c)[:, p, :], in1=vEP_acc(c)[:, p, :], op=ALU.add).then_inc(c.v_aux.h, 1)

        c._x1_store_gate[p] = st_t

        @on("gpsimd")
        def _(g, tt=tt, p=p, xa=xa, x1s=x1s):
            g.wait_ge(c.v_aux.h, xa)
            g.dma_start(out=c.x1_out[tt * 128:(tt + 1) * 128, :], in_=vEP_r2(c)[:, p, :]).then_inc(x1s.h, 16)


def _emit_moe0(c, no_attn=False):
    on = c.rec.on
    if no_attn:
        ga = c.g_aux.bump(16)
        c._acc_pre = ga

        @on("gpsimd")
        def _(g):
            g.dma_start(out=c.acc_spill.ap(), in_=c.x_own).then_inc(c.g_aux.h, 16)
    else:
        c._acc_pre = None

    c.mcfg = {"l1": False, "act": AF.Silu}
    c.m_w1 = None
    c.m_w2 = c.fc2_w
    c.m_b1 = c.fc1_b
    c.m_b2 = c.fc2_b
    _emit_moe0_prep(c)
    c._fstg_gate = [None, None]
    c._mtr_pre = [None, None]
    c._mtr_init = c.v_aux.n
    c._arena_gate_ev = c._p0_ev_done
    c._gcast_gate = [16 * (e + 1) for e in range(NE)]
    c._moe_first_gate = c._w8_done
    c._q_slot_free = [None, None]
    c._b1_ev_prev = None
    c._y_gate = [None, None]
    _emit_moe0_combinit(c)
    for e in range(NE):
        for q in range(4):
            _emit_moe0_quarter(c, e, q, e * 4 + q)
    c._ep_tr_pre = [None, None]
    c._xn0n_gate = [None, None]
    c._acc_gate = [None, None]
    c._x1_store_gate = [None, None]
    _emit_moe0_epilogue(c)





def vL1_hid(c):
    return c.arD[:, 0:8192].bitcast(F32)


def vL1_gate(c):
    return c.arD[:, 8192:16384].bitcast(F32)


def vL1_c(c):
    return c.arD[:, 16384:24576].bitcast(F32)


def vL1_v(c):
    return c.arD[:, 24576:32768].bitcast(F32)


def vL1_scan(c):
    return c.arD[:, 57344:65536].bitcast(F32)


def vMK8(c):
    return c.arD[:, 24576:26624]


def vACC2ST(c):
    return c.arD[:, 32768:40960].bitcast(F32).rearrange("p (s d) -> p s d", s=2)


def vL1_wosl(c):
    return c.arD[:, 0:32768].bitcast(BF16).rearrange("p (cb d) -> p cb d", cb=16)


def vL1_car8(c):
    return c.arD[0:8, 32768:40960].bitcast(F32)


def vL1_carrow(c):
    return c.arD[0:1, 0:8192].bitcast(F32)


def vL1_acc2st(c):
    return c.arD[:, 49152:57344].bitcast(F32).rearrange("p (s d) -> p s d", s=2)


def hT_sc(c, sc):
    if sc < 8:
        return vB(c)[:, sc, :]
    return vC(c)[:, sc - 8, :]


def _emit_scan_pass(c, sc, passB):
    on = c.rec.on
    par = sc % 2
    wsl = c.wsl0 if par == 0 else c.wsl1
    wld = wsl.bump(32)
    slot_gate = c._wsl_gate[par]

    @on("gpsimd")
    def _(g, sc=sc, par=par, wsl=wsl, slot_gate=slot_gate, passB=passB,
          wg=(getattr(c, "_scan_wrow_gate", None) if (sc < 2 and not passB) else None)):
        if wg is not None:
            g.wait_ge(wg[0], wg[1])
        if slot_gate is not None:
            g.wait_ge(c.t_mm.h, slot_gate)
        g.dma_start(out=vWROW(c)[:, par, 0, :], in_=c.W_hg[sc * 128:(sc + 1) * 128, :]).then_inc(wsl.h, 16)
        g.dma_start(out=vWROW(c)[:, par, 1, :], in_=c.W_hg[DI + sc * 128:DI + (sc + 1) * 128, :]).then_inc(wsl.h, 16)

    trs = [c.t_tr.bump() for _ in range(16)]
    evs = [c.v_ev.bump() for _ in range(16)]
    pre = tuple(c._tb_pre)
    c._tb_pre = [evs[14], evs[15]]

    @on("tensor")
    def _(t, par=par, wsl=wsl, wld=wld, evs=evs, pre=pre):
        t.wait_ge(wsl.h, wld)
        n = 0
        for w in range(2):
            for dblk in range(8):
                bank = 4 + (n % 2)
                gate = pre[n % 2] if n < 2 else evs[n - 2]
                if gate is not None:
                    t.wait_ge(c.v_ev.h, gate)
                pv = c.ps[bank][:, :].bitcast(BF16)
                t.transpose(pv[:, 0:128], vWROW(c)[:, par, w, dblk * 128:(dblk + 1) * 128],
                            c.ident_bf_sb[:, :]).then_inc(c.t_tr.h, 1)
                n += 1

    @on("vector")
    def _(v, par=par, trs=trs):
        n = 0
        for w in range(2):
            for dblk in range(8):
                bank = 4 + (n % 2)
                pv = c.ps[bank][:, :].bitcast(BF16)
                v.wait_ge(c.t_tr.h, trs[n])
                v.tensor_copy(out=vWSL(c)[:, par, w, dblk, :], in_=pv[:, 0:128]).then_inc(c.v_ev.h, 1)
                n += 1

    wsl_ev = evs[-1]

    hg_sv = []
    for w in range(2):
        for ch in range(4):
            n = w * 4 + ch
            mm = c.t_mm.bump()
            sv = c.s_act.bump()
            hg_sv.append(sv)

            @on("tensor")
            def _(t, w=w, ch=ch, n=n, par=par, mm=mm, wsl_ev=wsl_ev,
                  pg=c._gen_ps_gate[n % 2], bank=(n % 2)):
                if n == 0:
                    t.wait_ge(c.v_ev.h, wsl_ev)
                if pg is not None:
                    t.wait_ge(c.s_act.h, pg)
                for k in range(8):
                    mmh = t.matmul(c.ps[bank][:, :], vWSL(c)[:, par, w, k, :],
                                   vA(c)[:, k, ch * 512:(ch + 1) * 512], start=(k == 0), stop=(k == 7))
                mmh.then_inc(c.t_mm.h, 1)

            @on("scalar")
            def _(s_, w=w, ch=ch, mm=mm, bank=(n % 2),
                  scg=(c._scan_consumed if n == 0 else None),
                  crg=(getattr(c, "_carrow_ev", None) if (n == 0 and passB and sc == 0) else None)):
                s_.wait_ge(c.t_mm.h, mm)
                if scg is not None:
                    s_.wait_ge(c.v_aux.h, scg)
                if crg is not None:
                    s_.wait_ge(c.v_ev.h, crg)
                dst = vL1_hid if w == 0 else vL1_gate
                s_.activation(out=dst(c)[:, ch * 512:(ch + 1) * 512], in_=c.ps[bank][:, :],
                              func=AF.Copy, bias=0.0, scale=1.0).then_inc(c.s_act.h, 1)

            c._gen_ps_gate[n % 2] = sv

    hg_done = c.s_act.n

    s1 = c.s_act.bump()
    s2 = c.s_act.bump()
    v1 = c.v_aux.bump()
    s3 = c.s_act.bump()
    v5 = c.v_aux.bump()

    @on("scalar")
    def _(s_, hg_done=hg_done, scg=c._scan_consumed):
        s_.wait_ge(c.s_act.h, hg_done)
        if scg is not None:
            s_.wait_ge(c.v_aux.h, scg)
        s_.activation(out=vL1_c(c)[:, :], in_=vL1_gate(c)[:, :], func=AF.Sigmoid, bias=0.0, scale=-1.0).then_inc(c.s_act.h, 1)
        s_.drain()
        s_.activation(out=vL1_gate(c)[:, :], in_=vL1_gate(c)[:, :], func=AF.Sigmoid, bias=0.0, scale=1.0).then_inc(c.s_act.h, 1)

    @on("vector")
    def _(v, hg_done=hg_done, scg=c._scan_consumed):
        v.wait_ge(c.s_act.h, hg_done)
        if scg is not None:
            v.wait_ge(c.v_aux.h, scg)
        v.tensor_single_scalar(out=vMK8(c)[:, :], in_=vL1_hid(c)[:, :], scalar=0.0, op=ALU.is_ge).then_inc(c.v_aux.h, 1)

    @on("scalar")
    def _(s_, v1=v1):
        s_.activation(out=vL1_scan(c)[:, :], in_=vL1_hid(c)[:, :], func=AF.Sigmoid, bias=0.0, scale=1.0)
        s_.drain()
        s_.wait_ge(c.v_aux.h, v1)
        s_.activation(out=vL1_hid(c)[:, :], in_=vL1_hid(c)[:, :], func=AF.Identity, bias=c.half_t[:, 0:1], scale=1.0).then_inc(c.s_act.h, 1)

    @on("vector")
    def _(v, sc=sc, s3=s3, s2=s2, passB=passB,
          cd=(c._carry_done_ev if (passB and sc == 0) else None)):
        v.wait_ge(c.s_act.h, s3)
        if cd is not None:
            v.wait_ge(c.v_ev.h, cd)
        v.select(out=vL1_scan(c)[:, :], mask=vMK8(c)[:, :], on_true=vL1_hid(c)[:, :], on_false=vL1_scan(c)[:, :], add_drain=True)
        v.drain()
        v.tensor_tensor(out=vL1_v(c)[:, :], in0=vL1_gate(c)[:, :], in1=vL1_scan(c)[:, :], op=ALU.mult)
        v.drain()
        init = c.carry_sb[:, sc:sc + 1] if passB else 0.0
        v.tensor_tensor_scan(out=vL1_scan(c)[:, :], data0=vL1_c(c)[:, :], data1=vL1_v(c)[:, :],
                             initial=init, op0=ALU.mult, op1=ALU.add)
        v.drain()
        v.tensor_copy(out=hT_sc(c, sc), in_=vL1_scan(c)[:, :])
        if not passB:
            v.tensor_copy(out=c.fin_sb[:, sc:sc + 1], in_=vL1_scan(c)[:, 2047:2048])
        v.drain()
        v.engine_nop().then_inc(c.v_aux.h, 1)

    c._scan_consumed = v5
    c._wsl_gate[par] = c.t_mm.n


def _emit_l1_scan(c):
    on = c.rec.on
    c._scan_consumed = None
    c._gen_ps_gate = [c._gen_ps_gate[0], c._gen_ps_gate[1]] if hasattr(c, "_gen_ps_gate") else [None, None]

    for sc in range(16):
        _emit_scan_pass(c, sc, passB=False)

    last = c._scan_consumed
    ev_all = c.v_ev.n
    ftr = c.t_tr.bump()
    fev = c.v_ev.bump()

    @on("tensor")
    def _(t, last=last, ev_all=ev_all):
        t.wait_ge(c.v_aux.h, last)
        t.wait_ge(c.v_ev.h, ev_all)
        t.transpose(c.ps[4][0:16, 0:128], c.fin_sb[:, :], c.ident_f_sb[:, :]).then_inc(c.t_tr.h, 1)

    @on("vector")
    def _(v, ftr=ftr):
        v.wait_ge(c.t_tr.h, ftr)
        v.tensor_copy(out=c.D_all[0:16, 0:128], in_=c.ps[4][0:16, 0:128]).then_inc(c.v_ev.h, 1)

    g1 = c.agq.bump(16)
    cc1 = c.cc.bump(16 * NCORE if NO_CC else 1)
    g2 = c.agq.bump(16)

    @on("gpsimd")
    def _(g, fev=fev, g1=g1, cc1=cc1):
        g.wait_ge(c.v_ev.h, fev)
        g.dma_start(out=c.ag_car_in.ap(), in_=c.D_all[0:16, 0:128]).then_inc(c.agq.h, 16)
        g.wait_ge(c.agq.h, g1)
        if NO_CC:
            for r in range(NCORE):
                g.dma_start(out=c.ag_car_out.ap()[r * 16:(r + 1) * 16, :], in_=c.ag_car_in.ap()).then_inc(c.cc.h, 16)
        else:
            g.collective_compute(
                "AllGather", ALU.bypass,
                replica_groups=[list(range(NCORE))],
                ins=[c.ag_car_in.ap().opt()],
                outs=[c.ag_car_out.ap().opt()],
            ).then_inc(c.cc.h, 1)
        g.wait_ge(c.cc.h, cc1)
        g.dma_start(out=vL1_car8(c)[0:8, :], in_=c.ag_car_out.ap().rearrange("(r i) p -> r (i p)", r=8)).then_inc(c.agq.h, 16)

    sel_evs = []
    prev_se = None
    for j in range(4):
        smm = c.t_mm.bump()
        se = c.v_ev.bump()
        sel_evs.append(se)

        @on("tensor")
        def _(t, j=j, g2=g2, fev=fev, prev_se=prev_se):
            if j == 0:
                t.wait_ge(c.agq.h, g2)
                t.wait_ge(c.v_ev.h, fev)
            else:
                t.wait_ge(c.v_ev.h, prev_se)
            t.matmul(c.ps[4][0:1, 0:512], c.selcar_sb[0:8, :], vL1_car8(c)[0:8, j * 512:(j + 1) * 512],
                     start=True, stop=True).then_inc(c.t_mm.h, 1)

        smm_t = c.t_mm.n

        @on("vector")
        def _(v, j=j, smm_t=smm_t):
            v.wait_ge(c.t_mm.h, smm_t)
            v.tensor_copy(out=vL1_carrow(c)[0:1, j * 512:(j + 1) * 512], in_=c.ps[4][0:1, 0:512]).then_inc(c.v_ev.h, 1)

        prev_se = se

    ctr = [c.t_tr.bump() for _ in range(16)]
    cev = [c.v_ev.bump() for _ in range(16)]

    @on("tensor")
    def _(t, ctr=ctr, cev=cev, prev_se=prev_se):
        t.wait_ge(c.v_ev.h, prev_se)
        for sc in range(16):
            if sc >= 2:
                t.wait_ge(c.v_ev.h, cev[sc - 2])
            t.transpose(c.ps[4 + sc % 2][0:128, 0:1], vL1_carrow(c)[0:1, sc * 128:(sc + 1) * 128],
                        c.ident_f_sb[0:1, 0:1]).then_inc(c.t_tr.h, 1)

    @on("vector")
    def _(v, ctr=ctr):
        for sc in range(16):
            v.wait_ge(c.t_tr.h, ctr[sc])
            v.tensor_copy(out=c.carry_sb[:, sc:sc + 1], in_=c.ps[4 + sc % 2][0:128, 0:1]).then_inc(c.v_ev.h, 1)

    c._carry_done_ev = cev[-1]
    c._carrow_ev = cev[-1]
    c._carry_done = None
    c._tb_pre = [cev[14], cev[15]]

    for sc in range(16):
        _emit_scan_pass(c, sc, passB=True)


def _emit_l1_outproj(c):
    on = c.rec.on
    scan_done = c._scan_consumed
    c._wosl_tr = [None, None]
    for cb in range(16):
        par = cb % 2
        wsl = c.wsl0 if par == 0 else c.wsl1
        wld = wsl.bump(16 * 8)
        slot_gate = c._wsl_gate[par]
        tr_gate = c._wosl_tr[par]

        @on("gpsimd")
        def _(g, cb=cb, par=par, wsl=wsl, slot_gate=slot_gate, sd=(scan_done if cb < 2 else None),
              tr_gate=tr_gate):
            if sd is not None:
                g.wait_ge(c.v_aux.h, sd)
            if slot_gate is not None:
                g.wait_ge(c.t_mm.h, slot_gate)
            if tr_gate is not None:
                g.wait_ge(c.t_tr.h, tr_gate)
            for dblk in range(8):
                g.dma_start(out=vWROW(c)[:, par, 0, dblk * 128:(dblk + 1) * 128],
                            in_=c.W_out[dblk * 128:(dblk + 1) * 128, cb * 128:(cb + 1) * 128]).then_inc(wsl.h, 16)

        trs = [c.t_tr.bump() for _ in range(8)]
        evs = [c.v_ev.bump() for _ in range(8)]
        pre = tuple(c._tb_pre)
        c._tb_pre = [evs[6], evs[7]]

        @on("tensor")
        def _(t, cb=cb, par=par, wsl=wsl, wld=wld, evs=evs, pre=pre, sd=(scan_done if cb == 0 else None)):
            t.wait_ge(wsl.h, wld)
            if sd is not None:
                t.wait_ge(c.v_aux.h, sd)
            for dblk in range(8):
                bank = 4 + (dblk % 2)
                gate = pre[dblk % 2] if dblk < 2 else evs[dblk - 2]
                if gate is not None:
                    t.wait_ge(c.v_ev.h, gate)
                pv = c.ps[bank][:, :].bitcast(BF16)
                t.transpose(pv[:, 0:128], vWROW(c)[:, par, 0, dblk * 128:(dblk + 1) * 128],
                            c.ident_bf_sb[:, :]).then_inc(c.t_tr.h, 1)

        @on("vector")
        def _(v, cb=cb, trs=trs, sd=(scan_done if cb == 0 else None)):
            if sd is not None:
                v.wait_ge(c.v_aux.h, sd)
            for dblk in range(8):
                bank = 4 + (dblk % 2)
                pv = c.ps[bank][:, :].bitcast(BF16)
                v.wait_ge(c.t_tr.h, trs[dblk])
                v.tensor_copy(out=vL1_wosl(c)[:, cb, dblk * 128:(dblk + 1) * 128], in_=pv[:, 0:128]).then_inc(c.v_ev.h, 1)

        c._wsl_gate[par] = c.t_mm.n
        c._wosl_tr[par] = trs[-1]

    wosl_ev = c.v_ev.n

    c._op_ps_gate = [None, None]
    c._x1ld_gate = [None, None]
    c._a2spill_gate = [None, None]
    xnb_all = c.v_xnb.n
    for tt in range(TT):
        p = tt % 2
        lq = c.lq0 if p == 0 else c.lq1
        xld = lq.bump(16)

        @on("sync")
        def _(s_, tt=tt, p=p, lq=lq, gate=c._x1ld_gate[p], xg=(xnb_all if tt < 2 else None)):
            if xg is not None:
                s_.wait_ge(c.v_xnb.h, xg)
            if gate is not None:
                s_.wait_ge(c.v_aux.h, gate)
            s_.dma_start(out=c.xs[:, p, :], in_=c.x1_src[tt * 128:(tt + 1) * 128, :]).then_inc(lq.h, 16)

        va_parts = []
        for half in range(2):
            b = half
            mm = c.t_mm.bump()

            @on("tensor")
            def _(t, tt=tt, half=half, b=b, first=(tt == 0 and half == 0), wosl_ev=wosl_ev,
                  og=c._op_ps_gate[b]):
                if first:
                    t.wait_ge(c.v_ev.h, wosl_ev)
                if og is not None:
                    t.wait_ge(c.v_aux.h, og)
                for cb in range(16):
                    mmh = t.matmul(c.ps[b][:, :], hT_sc(c, cb)[:, tt * 128:(tt + 1) * 128],
                                   vL1_wosl(c)[:, cb, half * 512:(half + 1) * 512],
                                   start=(cb == 0), stop=(cb == 15))
                mmh.then_inc(c.t_mm.h, 1)

            va = c.v_aux.bump()
            va_parts.append(va)
            c._op_ps_gate[b] = va

            @on("vector")
            def _(v, tt=tt, p=p, half=half, b=b, mm=mm, xld=xld, lq=lq, spg=c._a2spill_gate[p]):
                v.wait_ge(c.t_mm.h, mm)
                v.wait_ge(lq.h, xld)
                if spg is not None and half == 0:
                    v.wait_ge((c.spq if p == 0 else c.spq1).h, spg)
                v.tensor_tensor(out=vL1_acc2st(c)[:, p, half * 512:(half + 1) * 512], in0=c.ps[b][:, :],
                                in1=c.xs[:, p, half * 512:(half + 1) * 512], op=ALU.add).then_inc(c.v_aux.h, 1)

        spqs = c.spq if p == 0 else c.spq1
        sp = spqs.bump(16)
        c._a2spill_gate[p] = sp
        va2 = va_parts[-1]
        c._x1ld_gate[p] = va2

        @on("gpsimd")
        def _(g, tt=tt, p=p, va2=va2, spqs=spqs):
            g.wait_ge(c.v_aux.h, va2)
            g.dma_start(out=c.acc2_spill.ap()[tt * 128:(tt + 1) * 128, :], in_=vL1_acc2st(c)[:, p, :]).then_inc(spqs.h, 16)

    c._spill_all = (c.spq.n, c.spq1.n)
    c._attn_end_mm = c.t_mm.n


def _emit_l1_w8(c):
    on = c.rec.on
    kw_prev = None
    ev_prev = getattr(c, "_l1w8_ev_init", None)
    w8gate_cmb = getattr(c, "_l1w8_cmb_gate", None)
    last_r = c._rt_prev
    for tt in range(TT):
        kw = c.v_r.bump()

        @on("vector")
        def _(v, tt=tt, first=(tt == 0), last_r=last_r, wg8=(w8gate_cmb if tt == 0 else None)):
            if first:
                v.wait_ge(c.v_r.h, last_r)
            if wg8 is not None:
                v.wait_ge(c.v_cmb.h, wg8)
            v.tensor_single_scalar(out=c.scr16[:, 0:8], in_=c.mask16f[:, tt, 8:16], scalar=c.g_all[:, tt, 1:2], op=ALU.mult)
            v.drain()
            v.scalar_tensor_tensor(out=c.w8_all[:, tt, :], in0=c.mask16f[:, tt, 0:8], scalar=c.g_all[:, tt, 0:1],
                                   in1=c.scr16[:, 0:8], op0=ALU.mult, op1=ALU.add).then_inc(c.v_r.h, 1)

        kw_prev = kw
        trw = c.t_tr.bump()

        @on("tensor")
        def _(t, tt=tt, kw=kw, ev_prev=ev_prev):
            t.wait_ge(c.v_r.h, kw)
            if ev_prev is not None:
                t.wait_ge(c.v_ev.h, ev_prev)
            t.transpose(c.ps[2][0:8, 0:128], c.w8_all[:, tt, :], c.ident_f_sb[:, :]).then_inc(c.t_tr.h, 1)

        evw = c.v_ev.bump()
        ev_prev = evw

        @on("vector")
        def _(v, tt=tt, trw=trw):
            v.wait_ge(c.t_tr.h, trw)
            v.tensor_copy(out=c.wT_all[0:8, tt * 128:(tt + 1) * 128], in_=c.ps[2][0:8, 0:128]).then_inc(c.v_ev.h, 1)

    c._w8_done = kw_prev
    c._wT_done = ev_prev
    c._moe_first_gate = kw_prev


def _emit_l1_final(c):
    on = c.rec.on
    comb_all = c.v_cmb.n
    g_acc2 = [None, None]
    x2s = [c.x1s0, c.x1s1]
    x2gate = [None, None]
    for tt in range(TT):
        p = tt % 2
        aq = c.aq0 if p == 0 else c.aq1
        aq_t = aq.bump(16)

        @on("sync")
        def _(s_, tt=tt, p=p, aq=aq, agate=g_acc2[p], ca=(comb_all if tt < 2 else None),
              spa=(c._spill_all if tt < 2 else None), xg=x2gate[p], x1s=x2s[p]):
            if ca is not None:
                s_.wait_ge(c.v_cmb.h, ca)
            if spa is not None:
                s_.wait_ge(c.spq.h, spa[0])
                s_.wait_ge(c.spq1.h, spa[1])
            if agate is not None:
                s_.wait_ge(c.v_aux.h, agate)
            if xg is not None:
                s_.wait_ge(x1s.h, xg)
            s_.dma_start(out=vACC2ST(c)[:, p, :], in_=c.acc2_spill.ap()[tt * 128:(tt + 1) * 128, :]).then_inc(aq.h, 16)

        va = c.v_aux.bump()
        g_acc2[p] = va
        x1s = x2s[p]
        st_t = x1s.bump(16)

        @on("vector")
        def _(v, tt=tt, p=p, aq_t=aq_t, aq=aq):
            v.wait_ge(aq.h, aq_t)
            v.tensor_tensor(out=vACC2ST(c)[:, p, :], in0=vACC2ST(c)[:, p, :], in1=vD_comb(c)[:, tt, :],
                            op=ALU.add).then_inc(c.v_aux.h, 1)

        x2gate[p] = st_t

        @on("gpsimd")
        def _(g, tt=tt, p=p, va=va, x1s=x1s):
            g.wait_ge(c.v_aux.h, va)
            g.dma_start(out=c.x2_out[tt * 128:(tt + 1) * 128, :], in_=vACC2ST(c)[:, p, :]).then_inc(x1s.h, 16)


def build_l1(dbg=(), moe=True):
    nc = bass.Bass("TRN2", target_bir_lowering=False)
    ctx = ExitStack()
    c = Ctx()
    c.nc = nc
    c.rec = Rec()
    c.dbg = set(dbg)
    c.ctx = ctx

    A = lambda name, shape, dt: nc.dram_tensor(name, shape, dt, kind="ExternalInput").ap()
    c.x1_src = A("x1_own", [NT, D], F32)
    c.W_hg = A("L1_attn_W_hg", [2 * DI, D], F32)
    c.W_out = A("L1_attn_W_out", [D, DI], F32)
    c.wg = A("L1_ffn_gate_w", [D, NE], F32)
    c.w1_in = A("L1_ffn_w1", [NE, D, HFF], F32)
    c.b1_in = A("L1_ffn_b1", [NE, HFF], F32)
    c.w2_in = A("L1_ffn_w2", [NE, HFF, D], F32)
    c.b2_in = A("L1_ffn_b2", [NE, D], F32)
    c.ident_bf = A("ident_bf", [128, 128], BF16)
    c.ident_f32 = A("ident_f32", [128, 128], F32)
    c.iota8 = A("iota8", [128, 8], F32)
    c.selcar = A("sel_carry", [8, 1], F32)
    c.triu_f32 = A("triu_f32", [128, 128], F32)
    c.cmask = A("cmask", [128, 4, 512], BF16)
    c.vmask = A("vmask", [128, 32], F32)
    c.onespfx = A("ones_prefix", [8, 1], F32)
    c.ln_g = A("L0_ffn_ln_gamma", [D], F32)
    c.ln_b = A("L0_ffn_ln_beta", [D], F32)

    c.x2_out = nc.dram_tensor("x2_out", [NT, D], F32, kind="ExternalOutput").ap()

    c.acc2_spill = nc.dram_tensor("acc2_spill", [NT, D], F32)
    c.ag_car_in = nc.dram_tensor("ag_car_in", [16, 128], F32)
    c.ag_car_out = nc.dram_tensor("ag_car_out", [128, 128], F32)

    _alloc_common(c)
    _alloc_l1(c)
    _emit_phase0_l1(c)
    _emit_l1_w8(c)
    _emit_l1_scan(c)
    if moe != "scanonly":
        _emit_l1_outproj(c)
    if moe is True:
        _emit_moe1(c)
        _emit_l1_final(c)
    _emit_dbg(c)
    _emit_final_wait(c)

    with nc.Block() as block:
        c.rec.register(block)
    ctx.close()
    return nc


def _alloc_l1(c):
    nc, ctx = c.nc, c.ctx
    E = ctx.enter_context
    c.half_t = E(nc.sbuf_tensor("half_t", [128, 1], F32))
    c.fin_sb = E(nc.sbuf_tensor("fin_sb", [128, 16], F32))
    c.carry_sb = E(nc.sbuf_tensor("carry_sb", [128, 16], F32))
    c.selcar_sb = E(nc.sbuf_tensor("selcar_sb", [8, 1], F32))


def _emit_phase0_l1(c):
    on = c.rec.on
    c._lnidx = 0
    c._xnb_cons = [None, None]
    c._xnb_tr = [None, None]
    c._tr_ev_pre = [None, None]
    c._rt_prev = None

    _emit_consts(c)

    @on("vector")
    def _(v):
        v.memset(c.half_t[:, :], 0.5)

    sld = c.agq.bump(16)

    @on("gpsimd")
    def _(s_):
        s_.dma_start(out=c.selcar_sb[0:8, :], in_=c.selcar).then_inc(c.agq.h, 16)

    c._selcar_ld = sld

    for tt in range(TT):
        _emit_ln_tile(c, c.x1_src, tt, vA, routing="l1")

    c._tb_pre = [c._tr_ev_pre[0], c._tr_ev_pre[1]]
    c._gen_ps_gate = [None, None]
    c._wsl_gate = [None, None]


def _emit_moe1(c):
    on = c.rec.on
    c.mcfg = {"l1": True, "act": AF.Gelu}
    c.m_w1 = c.w1_in
    c.m_w2 = c.w2_in
    c.m_b1 = c.b1_in
    c.m_b2 = c.b2_in
    c._arena_gate_ev = c.v_ev.n
    c._fstg_gate = [None, None]
    c._mtr_pre = [None, None]
    c._mtr_init = None
    c._gcast_gate = [None] * NE
    c._moe_first_gate = c._w8_done
    c._q_slot_free = [None, None]
    c._b1_ev_prev = None
    c._y_gate = [None, None]
    c._acc_pre = None
    fb = c.fbq.bump(16)
    c._fc2b_ld = fb

    @on("gpsimd")
    def _(g, fbg=getattr(c, "_l1fc2b_gate", None), mb2=c.m_b2):
        if fbg is not None:
            g.wait_ge(c.v_cmb.h, fbg)
        g.dma_start(out=c.fc2b_sb[:, :], in_=mb2).then_inc(c.fbq.h, 16)

    _emit_moe0_combinit(c)
    for e in range(NE):
        for q in range(4):
            _emit_moe0_quarter(c, e, q, e * 4 + q)



def _emit_dbg(c):
    on = c.rec.on
    nc = c.nc
    if "l1scan" in c.dbg:
        dh = nc.dram_tensor("dbg_hT", [16, 128, NT], BF16, kind="ExternalOutput").ap()
        dx = nc.dram_tensor("dbg_xn1T", [8, 128, NT], BF16, kind="ExternalOutput").ap()
        dc = nc.dram_tensor("dbg_carry", [128, 16], F32, kind="ExternalOutput").ap()
        dfin = nc.dram_tensor("dbg_fin", [128, 16], F32, kind="ExternalOutput").ap()
        dmid = nc.dram_tensor("dbg_mid", [4, 128, NT], F32, kind="ExternalOutput").ap()
        c.st.bump(16 * 30)
        va_all = c.v_aux.n
        ev_all = c.v_ev.n

        @on("gpsimd")
        def _(g, va_all=va_all, ev_all=ev_all):
            g.wait_ge(c.v_aux.h, va_all)
            g.wait_ge(c.v_ev.h, ev_all)
            for sc in range(16):
                g.dma_start(out=dh[sc], in_=hT_sc(c, sc)).then_inc(c.st.h, 16)
            for k in range(8):
                g.dma_start(out=dx[k], in_=vA(c)[:, k, :]).then_inc(c.st.h, 16)
            g.dma_start(out=dc, in_=c.carry_sb[:, :]).then_inc(c.st.h, 16)
            g.dma_start(out=dfin, in_=c.fin_sb[:, :]).then_inc(c.st.h, 16)
            g.dma_start(out=dmid[0], in_=vL1_hid(c)[:, :]).then_inc(c.st.h, 16)
            g.dma_start(out=dmid[1], in_=vL1_gate(c)[:, :]).then_inc(c.st.h, 16)
            g.dma_start(out=dmid[2], in_=vL1_c(c)[:, :]).then_inc(c.st.h, 16)
            g.dma_start(out=dmid[3], in_=vL1_v(c)[:, :]).then_inc(c.st.h, 16)

    if "xnT" in c.dbg:
        d = nc.dram_tensor("dbg_xnT", [8, 128, NT], BF16, kind="ExternalOutput").ap()
        d2 = nc.dram_tensor("dbg_xnTe", [8, 128, NT], BF16, kind="ExternalOutput").ap()
        c.st.bump(16 * 16)
        ev_all = c.v_ev.n

        @on("gpsimd")
        def _(g, ev_all=ev_all):
            g.wait_ge(c.v_ev.h, ev_all)
            for k in range(8):
                g.dma_start(out=d[k], in_=vA(c)[:, k, :]).then_inc(c.st.h, 16)
                g.dma_start(out=d2[k], in_=vB(c)[:, k, :]).then_inc(c.st.h, 16)

    if "route" in c.dbg:
        dw = nc.dram_tensor("dbg_w8", [128, 16, 8], F32, kind="ExternalOutput").ap()
        dm = nc.dram_tensor("dbg_mask", [128, 16, 16], F32, kind="ExternalOutput").ap()
        dp = nc.dram_tensor("dbg_p1", [128, 16, 16], F32, kind="ExternalOutput").ap()
        db = nc.dram_tensor("dbg_allbase", [1, 16], F32, kind="ExternalOutput").ap()
        dbs = nc.dram_tensor("dbg_bases", [1, 16, 16], F32, kind="ExternalOutput").ap()
        dwt = nc.dram_tensor("dbg_wT", [8, NT], BF16, kind="ExternalOutput").ap()
        c.st.bump(16 * 6)
        vr_all = c.v_r.n
        ev_all = c.v_ev.n

        @on("gpsimd")
        def _(g, vr_all=vr_all, ev_all=ev_all):
            g.wait_ge(c.v_r.h, vr_all)
            g.wait_ge(c.v_ev.h, ev_all)
            g.dma_start(out=dw, in_=c.w8_all[:, :, :]).then_inc(c.st.h, 16)
            g.dma_start(out=dm, in_=c.mask16f[:, :, :]).then_inc(c.st.h, 16)
            g.dma_start(out=dp, in_=c.p1_all[:, :, :]).then_inc(c.st.h, 16)
            g.dma_start(out=db, in_=c.allbase[:, :]).then_inc(c.st.h, 16)
            g.dma_start(out=dbs, in_=c.bases_all[:, :, :]).then_inc(c.st.h, 16)
            g.dma_start(out=dwt, in_=c.wT_all[:, :]).then_inc(c.st.h, 16)


def _emit_final_wait(c):
    on = c.rec.on
    target = c.st.n
    t0, t1 = c.x1s0.n, c.x1s1.n

    @on("gpsimd")
    def _(g, target=target, t0=t0, t1=t1):
        if target:
            g.wait_ge(c.st.h, target)
        if t0:
            g.wait_ge(c.x1s0.h, t0)
        if t1:
            g.wait_ge(c.x1s1.h, t1)



def host_constants(core_id):
    b, h = core_id // 2, core_id % 2
    ident = np.eye(128, dtype=np.float32)
    cm = np.zeros((128, 4, 512), dtype=np.float32)
    for j in range(4):
        for jj in range(4):
            blk = cm[:, j, jj * 128:(jj + 1) * 128]
            if jj > j:
                blk[:] = 1.0
            elif jj == j:
                blk[:] = np.tril(np.ones((128, 128), np.float32)).T
    vm = np.zeros((128, 32), dtype=np.float32)
    vm[:, 0:16] = 1.0
    if h == 1:
        vm[:, 16:32] = 1.0
    onespfx = np.zeros((8, 1), dtype=np.float32)
    onespfx[:core_id, 0] = 1.0
    sel = np.zeros((8, 1), dtype=np.float32)
    if h == 1:
        sel[core_id - 1, 0] = 1.0
    return {
        "ident_bf": ident.astype(ml_dtypes.bfloat16),
        "ident_f32": ident,
        "triu_f32": np.triu(np.ones((128, 128), np.float32)),
        "iota8": np.tile(np.arange(8, dtype=np.float32), (128, 1)),
        "cmask": cm.astype(ml_dtypes.bfloat16),
        "vmask": vm,
        "ones_prefix": onespfx,
        "sel_carry": sel,
    }


def shard_inputs(inputs):
    x = np.ascontiguousarray(inputs["L0_attn_in_features"])
    maps = []
    for cid in range(NCORE):
        b, h = cid // 2, cid % 2
        m = dict(host_constants(cid))
        m.pop("sel_carry")
        m["x_own"] = np.ascontiguousarray(x[b, h * NT:(h + 1) * NT])
        m["x_ext"] = np.ascontiguousarray(x[b, (1 - h) * NT:(2 - h) * NT])
        for k in ("L0_attn_Wq", "L0_attn_Wk", "L0_attn_Wv", "L0_attn_Wo", "L0_ffn_wg",
                  "L0_ffn_fc1_w", "L0_ffn_fc1_b", "L0_ffn_fc2_w", "L0_ffn_fc2_b",
                  "L0_ffn_ln_gamma", "L0_ffn_ln_beta"):
            m[k] = inputs[k]
        maps.append(m)
    return maps





def build_fused(dbg=()):
    nc = bass.Bass("TRN2", target_bir_lowering=False)
    ctx = ExitStack()
    c = Ctx()
    c.nc = nc
    c.rec = Rec()
    c.dbg = set(dbg)
    c.ctx = ctx

    A = lambda name, shape, dt: nc.dram_tensor(name, shape, dt, kind="ExternalInput").ap()
    c.x_own = A("x_own", [NT, D], F32)
    c.x_ext = A("x_ext", [NT, D], F32)
    c.Wq = A("L0_attn_Wq", [D, D], F32)
    c.Wk = A("L0_attn_Wk", [D, D], F32)
    c.Wv = A("L0_attn_Wv", [D, D], F32)
    c.Wo = A("L0_attn_Wo", [D, D], F32)
    c.wg = A("L0_ffn_wg", [D, NE], F32)
    c.fc1_w = A("L0_ffn_fc1_w", [NE, HFF, D], F32)
    c.fc1_b = A("L0_ffn_fc1_b", [NE, HFF], F32)
    c.fc2_w = A("L0_ffn_fc2_w", [NE, HFF, D], F32)
    c.fc2_b = A("L0_ffn_fc2_b", [NE, D], F32)
    c.ln_g = A("L0_ffn_ln_gamma", [D], F32)
    c.ln_b = A("L0_ffn_ln_beta", [D], F32)
    c.W_hg = A("L1_attn_W_hg", [2 * DI, D], F32)
    c.W_out = A("L1_attn_W_out", [D, DI], F32)
    c.gate_w = A("L1_ffn_gate_w", [D, NE], F32)
    c.w1_in = A("L1_ffn_w1", [NE, D, HFF], F32)
    c.b1_in = A("L1_ffn_b1", [NE, HFF], F32)
    c.w2_in = A("L1_ffn_w2", [NE, HFF, D], F32)
    c.b2_in = A("L1_ffn_b2", [NE, D], F32)
    c.ident_bf = A("ident_bf", [128, 128], BF16)
    c.ident_f32 = A("ident_f32", [128, 128], F32)
    c.triu_f32 = A("triu_f32", [128, 128], F32)
    c.iota8 = A("iota8", [128, 8], F32)
    c.cmask = A("cmask", [128, 4, 512], BF16)
    c.vmask = A("vmask", [128, 32], F32)
    c.onespfx = A("ones_prefix", [8, 1], F32)
    c.selcar = A("sel_carry", [8, 1], F32)

    c.x2_out = nc.dram_tensor("x2_out", [NT, D], F32, kind="ExternalOutput").ap()

    c.fc1bf = nc.dram_tensor("fc1bf", [NE * HFF, D], BF16)
    c.acc_spill = nc.dram_tensor("acc_spill", [NT, D], F32)
    c.x1_spill = nc.dram_tensor("x1_spill", [NT, D], F32)
    c.acc2_spill = nc.dram_tensor("acc2_spill", [NT, D], F32)
    c.ag_cnt_in = nc.dram_tensor("ag_cnt_in", [1, 512], F32)
    c.ag_cnt_out = nc.dram_tensor("ag_cnt_out", [8, 512], F32)
    c.ag_car_in = nc.dram_tensor("ag_car_in", [16, 128], F32)
    c.ag_car_out = nc.dram_tensor("ag_car_out", [128, 128], F32)
    c.x1_out = c.x1_spill.ap()
    c.x1_src = c.x1_spill.ap()

    _alloc_common(c)
    _alloc_l1(c)

    _emit_phase0(c)
    _emit_keepw(c)
    c._p0_ev_done = c.v_ev.n
    _emit_attn(c)
    _emit_moe0(c)

    ep_end_vaux = c.v_aux.n
    x1_tot = (c.x1s0.n, c.x1s1.n)
    l0_cmb = c.v_cmb.n

    @c.rec.on("vector")
    def _(v):
        v.memset(c.half_t[:, :], 0.5)

    c.agq.bump(16)

    @c.rec.on("gpsimd")
    def _(g):
        g.dma_start(out=c.selcar_sb[0:8, :], in_=c.selcar).then_inc(c.agq.h, 16)

    gfw = c.gq.bump(16)
    c.wg_ld = gfw

    @c.rec.on("gpsimd")
    def _(g, wr=c._w8_done):
        g.wait_ge(c.v_r.h, wr)
        g.dma_start(out=c.wg_sb[:, :, :], in_=c.gate_w.rearrange("(k p) e -> p k e", p=128)).then_inc(c.gq.h, 16)

    c._tr_ev_pre = [c._ep_tr_pre[0], c._ep_tr_pre[1]]
    c._rt_prev = c._w8_done
    eg0 = [(c.x1s0.h, x1_tot[0]), (c.x1s1.h, x1_tot[1]), (c.v_aux.h, ep_end_vaux)]
    for tt in range(TT):
        _emit_ln_tile(c, c.x1_src, tt, vA, routing="l1",
                      extra_gates=(eg0 if tt < 2 else None))

    c._l1w8_ev_init = c._wT_done
    c._l1w8_cmb_gate = l0_cmb
    _emit_l1_w8(c)

    c._tb_pre = [c._tr_ev_pre[0], c._tr_ev_pre[1]]
    c._gen_ps_gate = [None, None]
    c._wsl_gate = [None, None]
    c._scan_consumed = ep_end_vaux
    c._scan_wrow_gate = (c.v_aux.h, ep_end_vaux)
    _emit_l1_scan(c)
    _emit_l1_outproj(c)

    c._l1fc2b_gate = l0_cmb
    _emit_moe1(c)
    _emit_l1_final(c)
    _emit_dbg(c)
    _emit_final_wait(c)

    with nc.Block() as block:
        c.rec.register(block)
    ctx.close()
    return nc


def shard_inputs_fused(inputs):
    maps = []
    x = np.ascontiguousarray(inputs["L0_attn_in_features"])
    for cid in range(NCORE):
        b, h = cid // 2, cid % 2
        m = dict(host_constants(cid))
        m["x_own"] = np.ascontiguousarray(x[b, h * NT:(h + 1) * NT])
        m["x_ext"] = np.ascontiguousarray(x[b, (1 - h) * NT:(2 - h) * NT])
        for k in ("L0_attn_Wq", "L0_attn_Wk", "L0_attn_Wv", "L0_attn_Wo", "L0_ffn_wg",
                  "L0_ffn_fc1_w", "L0_ffn_fc1_b", "L0_ffn_fc2_w", "L0_ffn_fc2_b",
                  "L0_ffn_ln_gamma", "L0_ffn_ln_beta",
                  "L1_attn_W_hg", "L1_attn_W_out", "L1_ffn_gate_w", "L1_ffn_w1",
                  "L1_ffn_b1", "L1_ffn_w2", "L1_ffn_b2"):
            m[k] = inputs[k]
        maps.append(m)
    return maps


def shard_inputs_l1(inputs, x1_shards):
    maps = []
    for cid in range(NCORE):
        m = dict(host_constants(cid))
        m["x1_own"] = np.ascontiguousarray(x1_shards[cid])
        m["L0_ffn_ln_gamma"] = inputs["L0_ffn_ln_gamma"]
        m["L0_ffn_ln_beta"] = inputs["L0_ffn_ln_beta"]
        for k in ("L1_attn_W_hg", "L1_attn_W_out", "L1_ffn_gate_w", "L1_ffn_w1",
                  "L1_ffn_b1", "L1_ffn_w2", "L1_ffn_b2"):
            m[k] = inputs[k]
        maps.append(m)
    return maps


_CACHE = {}


def kernel(**inputs):
    if "fused" not in _CACHE:
        _CACHE["fused"] = build_fused()
    nc = _CACHE["fused"]
    maps = shard_inputs_fused(inputs)
    res = run_bass_kernel_spmd(nc, maps, core_ids=list(range(NCORE)))
    out = np.empty((4, 4096, 1024), dtype=np.float32)
    for cid in range(NCORE):
        b, h = cid // 2, cid % 2
        out[b, h * NT:(h + 1) * NT] = res.results[cid]["x2_out"]
    return out


# version 05
# speedup vs baseline: 1.2112x; 1.1174x over previous
import numpy as np
import ml_dtypes
from contextlib import ExitStack

import concourse.bass as bass
import concourse.mybir as mybir
from concourse.bass_utils import run_bass_kernel_spmd

F32 = mybir.dt.float32
BF16 = mybir.dt.bfloat16
U32 = mybir.dt.uint32
U8 = mybir.dt.uint8
AF = mybir.ActivationFunctionType
ALU = mybir.AluOpType

D = 1024
NT = 2048
TT = NT // 128
NHEAD = 16
DH = 64
NE = 8
HFF = 4096
CAP = 5120
DI = 2048
NCORE = 8

ENGINES = ("sync", "scalar", "vector", "tensor", "gpsimd")
NO_CC = False


class Sem:
    def __init__(self, ctx, nc, name):
        self.h = ctx.enter_context(nc.semaphore(name))
        self.n = 0

    def bump(self, k=1):
        self.n += k
        return self.n


class Rec:
    def __init__(self):
        self.steps = {e: [] for e in ENGINES}

    def on(self, e):
        def deco(fn):
            self.steps[e].append(fn)
            return None
        return deco

    def register(self, block):
        for e in ENGINES:
            fns = self.steps[e]
            getattr(block, e)(lambda eng, fns=fns: [f(eng) for f in fns])


class Ctx:
    pass



def build_l0(dbg=(), moe=True, no_attn=False):
    nc = bass.Bass("TRN2", target_bir_lowering=False)
    ctx = ExitStack()
    c = Ctx()
    c.nc = nc
    c.rec = Rec()
    c.dbg = set(dbg)
    c.ctx = ctx

    A = lambda name, shape, dt: nc.dram_tensor(name, shape, dt, kind="ExternalInput").ap()
    c.x_own = A("x_own", [NT, D], F32)
    c.x_ext = A("x_ext", [NT, D], F32)
    c.Wq = A("L0_attn_Wq", [D, D], F32)
    c.Wk = A("L0_attn_Wk", [D, D], F32)
    c.Wv = A("L0_attn_Wv", [D, D], F32)
    c.Wo = A("L0_attn_Wo", [D, D], F32)
    c.wg = A("L0_ffn_wg", [D, NE], F32)
    c.fc1_w = A("L0_ffn_fc1_w", [NE, HFF, D], F32)
    c.fc1_b = A("L0_ffn_fc1_b", [NE, HFF], F32)
    c.fc2_w = A("L0_ffn_fc2_w", [NE, HFF, D], F32)
    c.fc2_b = A("L0_ffn_fc2_b", [NE, D], F32)
    c.ln_g = A("L0_ffn_ln_gamma", [D], F32)
    c.ln_b = A("L0_ffn_ln_beta", [D], F32)
    c.ident_bf = A("ident_bf", [128, 128], BF16)
    c.ident_f32 = A("ident_f32", [128, 128], F32)
    c.triu_f32 = A("triu_f32", [128, 128], F32)
    c.iota8 = A("iota8", [128, 8], F32)
    c.cmask = A("cmask", [128, 4, 512], BF16)
    c.vmask = A("vmask", [128, 32], F32)
    c.onespfx = A("ones_prefix", [8, 1], F32)

    c.x1_out = nc.dram_tensor("x1_out", [NT, D], F32, kind="ExternalOutput").ap()

    c.fc1bf = nc.dram_tensor("fc1bf", [NE * HFF, D], BF16)
    c.acc_spill = nc.dram_tensor("acc_spill", [NT, D], F32)
    c.ag_cnt_in = nc.dram_tensor("ag_cnt_in", [1, 512], F32)
    c.ag_cnt_out = nc.dram_tensor("ag_cnt_out", [8, 512], F32)

    _alloc_common(c)
    _emit_phase0(c)
    _emit_keepw(c)
    c._p0_ev_done = c.v_ev.n
    if not no_attn:
        _emit_attn(c)
    else:
        c._spill_all = None
        c._attn_end_mm = None
    if moe:
        _emit_moe0(c, no_attn=no_attn)
    _emit_dbg(c)
    _emit_final_wait(c)

    with nc.Block() as block:
        c.rec.register(block)
    ctx.close()
    return nc


def _alloc_common(c):
    nc, ctx = c.nc, c.ctx
    E = ctx.enter_context

    c.arA = E(nc.sbuf_tensor("arA", [128, 32768], U8))
    c.arB = E(nc.sbuf_tensor("arB", [128, 32768], U8))
    c.arC = E(nc.sbuf_tensor("arC", [128, 32768], U8))
    c.arD = E(nc.sbuf_tensor("arD", [128, 65536], U8))

    c.xs = E(nc.sbuf_tensor("xs", [128, 2, 1024], F32))
    c.xnb = E(nc.sbuf_tensor("xnb", [128, 2, 1024], BF16))
    c.stats = E(nc.sbuf_tensor("stats", [128, 2, 12], F32))
    c.mv = E(nc.sbuf_tensor("mv", [128, 2, 2], F32))
    c.rstd = E(nc.sbuf_tensor("rstd", [128, 2, 1], F32))
    c.eps_t = E(nc.sbuf_tensor("eps_t", [128, 1], F32))
    c.m20 = E(nc.sbuf_tensor("m20", [128, 1], F32))
    c.ones128_f = E(nc.sbuf_tensor("ones128_f", [128, 1], F32))
    c.ones1_f = E(nc.sbuf_tensor("ones1_f", [1, 128], F32))
    c.ident_bf_sb = E(nc.sbuf_tensor("ident_bf_sb", [128, 128], BF16))
    c.ident_f_sb = E(nc.sbuf_tensor("ident_f_sb", [128, 128], F32))
    c.triu_sb = E(nc.sbuf_tensor("triu_sb", [128, 128], F32))
    c.iota8_sb = E(nc.sbuf_tensor("iota8_sb", [128, 8], F32))
    c.cmask_sb = E(nc.sbuf_tensor("cmask_sb", [128, 4, 512], BF16))
    c.vmask_sb = E(nc.sbuf_tensor("vmask_sb", [128, 32], F32))
    c.onespfx_sb = E(nc.sbuf_tensor("onespfx_sb", [8, 1], F32))
    c.ones8_sb = E(nc.sbuf_tensor("ones8_sb", [8, 1], F32))
    c.wg_sb = E(nc.sbuf_tensor("wg_sb", [128, 8, 8], BF16))
    c.p1_all = E(nc.sbuf_tensor("p1_all", [128, 16, 16], F32))
    c.mask16f = E(nc.sbuf_tensor("mask16f", [128, 16, 16], F32))
    c.g_all = E(nc.sbuf_tensor("g_all", [128, 16, 2], F32))
    c.bases_all = E(nc.sbuf_tensor("bases_all", [1, 16, 16], F32))
    c.base_acc = E(nc.sbuf_tensor("base_acc", [1, 16], F32))
    c.cnt8 = E(nc.sbuf_tensor("cnt8", [8, 16], F32))
    c.allbase = E(nc.sbuf_tensor("allbase", [1, 16], F32))
    c.w8_all = E(nc.sbuf_tensor("w8_all", [128, 16, 8], F32))
    c.wT_all = E(nc.sbuf_tensor("wT_all", [8, NT], BF16))
    c.scr16 = E(nc.sbuf_tensor("scr16", [128, 16], F32))
    c.scr8 = E(nc.sbuf_tensor("scr8", [128, 8], F32))
    c.scrrow = E(nc.sbuf_tensor("scrrow", [1, 16], F32))
    c.zrow = E(nc.sbuf_tensor("zrow", [1, 512], F32))
    c.selk = E(nc.sbuf_tensor("selk", [128, 2], F32))
    c.if2 = E(nc.sbuf_tensor("if2", [128, 2], F32))
    c.topv = E(nc.sbuf_tensor("topv", [128, 8], F32))
    c.topi = E(nc.sbuf_tensor("topi", [128, 8], U32))
    c.gam_t = E(nc.sbuf_tensor("gam_t", [128, 1024], F32))
    c.bet_t = E(nc.sbuf_tensor("bet_t", [128, 1024], F32))

    c.b1col = E(nc.sbuf_tensor("b1col", [128, 2, 8], F32))
    c.b1row = E(nc.sbuf_tensor("b1row", [8, 2, 128], F32))
    c.fc2b_sb = E(nc.sbuf_tensor("fc2b_sb", [8, 1024], BF16))
    c.D_all = E(nc.sbuf_tensor("D_all", [16, 2048], F32))
    c.ginv_sb = E(nc.sbuf_tensor("ginv_sb", [128, 16], F32))

    c.ps = [E(nc.psum_tensor(f"ps{i}", [128, 512], F32)) for i in range(8)]

    for name in ("sq", "gq", "tq", "lq0", "lq1", "t_mm", "t_tr", "v_mv", "v_xnb", "v_ev",
                 "s_rstd", "s_act", "v_r", "cc", "st", "v_aux", "t_aux", "g_aux",
                 "tw0", "tw1", "gw0", "gw1", "bw0", "bw1", "aq0", "aq1", "s_silu", "v_cmb", "x1q", "agq", "fbq", "x1s0", "x1s1",
                 "wsl0", "wsl1", "a_mm", "a_exp", "a_msk", "a_pv", "dq", "dq1", "spq", "spq1", "tl0", "tl1"):
        setattr(c, name, Sem(c.ctx, c.nc, name))


def vA(c):
    return c.arA[:, :].bitcast(BF16).rearrange("p (k t) -> p k t", k=8)


def vB(c):
    return c.arB[:, :].bitcast(BF16).rearrange("p (k t) -> p k t", k=8)


def vC(c):
    return c.arC[:, :].bitcast(BF16).rearrange("p (k t) -> p k t", k=8)


def vD_acc(c):
    return c.arD[:, :].bitcast(F32).rearrange("p (t d) -> p t d", t=16)


def vD_comb(c):
    return c.arD[:, 0:32768].bitcast(BF16).rearrange("p (t d) -> p t d", t=16)


def vD_hTq(c):
    return c.arD[:, 32768:65536].bitcast(BF16).rearrange("p (k t) -> p k t", k=8)


def vW1(c):
    return c.arB[:, :].bitcast(BF16).rearrange("p (q k h) -> p q k h", q=2, k=8)


def vW2(c):
    return c.arC[:, :].bitcast(BF16).rearrange("p (q k d) -> p q k d", q=2, k=8)


def vEP_xn0n(c):
    return c.arD[:, 32768:36864].bitcast(BF16).rearrange("p (s d) -> p s d", s=2)


def vEP_r2(c):
    return c.arD[:, 36864:45056].bitcast(F32).rearrange("p (s d) -> p s d", s=2)


def vEP_acc(c):
    return c.arD[:, 45056:53248].bitcast(F32).rearrange("p (s d) -> p s d", s=2)


def vEP_x1t(c):
    return c.arD[:, 53248:61440].bitcast(F32).rearrange("p (s d) -> p s d", s=2)




def _emit_consts(c):
    on = c.rec.on

    @on("vector")
    def _(v):
        v.memset(c.eps_t[:, :], 1e-5)
        v.memset(c.m20[:, :], -20.0)
        v.memset(c.ones128_f[:, :], 1.0)
        v.memset(c.ones1_f[:, :], 1.0)
        v.memset(c.ones8_sb[:, :], 1.0)
        v.memset(c.base_acc[:, :], 0.0)
        v.memset(c.zrow[:, :], 0.0)
        v.engine_nop().then_inc(c.v_aux.h, 1)

    t_c = c.sq.bump(16 * 9)
    ms_gate = c.v_aux.bump()
    c._ms_gate = ms_gate

    @c.rec.on("scalar")
    def _(s_, ms_gate=ms_gate):
        s_.wait_ge(c.v_aux.h, ms_gate)

    @on("sync")
    def _(s):
        s.dma_start(out=c.ident_bf_sb[:, :], in_=c.ident_bf).then_inc(c.sq.h, 16)
        s.dma_start(out=c.ident_f_sb[:, :], in_=c.ident_f32).then_inc(c.sq.h, 16)
        s.dma_start(out=c.triu_sb[:, :], in_=c.triu_f32).then_inc(c.sq.h, 16)
        s.dma_start(out=c.iota8_sb[:, :], in_=c.iota8).then_inc(c.sq.h, 16)
        s.dma_start(out=c.cmask_sb[:, :, :], in_=c.cmask).then_inc(c.sq.h, 16)
        s.dma_start(out=c.vmask_sb[:, :], in_=c.vmask).then_inc(c.sq.h, 16)
        s.dma_start(out=c.onespfx_sb[:, :], in_=c.onespfx).then_inc(c.sq.h, 16)
        s.dma_start(out=vGBROW(c)[0:1, 0:1024], in_=c.ln_g[None, :]).then_inc(c.sq.h, 16)
        s.dma_start(out=vGBROW(c)[0:1, 1024:2048], in_=c.ln_b[None, :]).then_inc(c.sq.h, 16)

    c.wg_ld = c.gq.bump(16)

    @on("gpsimd")
    def _(g):
        g.dma_start(out=c.wg_sb[:, :, :], in_=c.wg.rearrange("(k p) e -> p k e", p=128)).then_inc(c.gq.h, 16)

    mm_a = c.t_mm.bump()
    ev_a = c.v_ev.bump(2)
    mm_b = c.t_mm.bump()
    ev_b = c.v_ev.bump(2)

    @on("tensor")
    def _(t, t_c=t_c, mm_a=mm_a, ev_a=ev_a, mm_b=mm_b, ms_gate=ms_gate):
        t.wait_ge(c.v_aux.h, ms_gate)
        t.wait_ge(c.sq.h, t_c)
        t.matmul(c.ps[6][:, :], c.ones1_f[:, :], vGBROW(c)[0:1, 0:512], start=True, stop=True)
        t.matmul(c.ps[7][:, :], c.ones1_f[:, :], vGBROW(c)[0:1, 1024:1536], start=True, stop=True).then_inc(c.t_mm.h, 1)
        t.wait_ge(c.v_ev.h, ev_a)
        t.matmul(c.ps[6][:, :], c.ones1_f[:, :], vGBROW(c)[0:1, 512:1024], start=True, stop=True)
        t.matmul(c.ps[7][:, :], c.ones1_f[:, :], vGBROW(c)[0:1, 1536:2048], start=True, stop=True).then_inc(c.t_mm.h, 1)

    @on("vector")
    def _(v, mm_a=mm_a, mm_b=mm_b):
        v.wait_ge(c.t_mm.h, mm_a)
        v.tensor_copy(out=c.gam_t[:, 0:512], in_=c.ps[6][:, :]).then_inc(c.v_ev.h, 1)
        v.tensor_copy(out=c.bet_t[:, 0:512], in_=c.ps[7][:, :]).then_inc(c.v_ev.h, 1)
        v.wait_ge(c.t_mm.h, mm_b)
        v.tensor_copy(out=c.gam_t[:, 512:1024], in_=c.ps[6][:, :]).then_inc(c.v_ev.h, 1)
        v.tensor_copy(out=c.bet_t[:, 512:1024], in_=c.ps[7][:, :]).then_inc(c.v_ev.h, 1)

    c.gb_done = ev_b


def _emit_ln_tile(c, src_ap, tt, dstT, routing, extra_gates=None):
    on = c.rec.on
    p = c._lnidx % 2
    c._lnidx += 1

    need_xs_free = c._xnb_cons[p]
    need_xnb_free = c._xnb_tr[p]
    lq = c.lq0 if p == 0 else c.lq1
    ld_t = lq.bump(16)

    @on("sync")
    def _(s, need_xs_free=need_xs_free, lq=lq, extra_gates=extra_gates):
        if extra_gates:
            for sem_h, val in extra_gates:
                s.wait_ge(sem_h, val)
        if need_xs_free is not None:
            s.wait_ge(c.v_xnb.h, need_xs_free)
        s.dma_start(out=c.xs[:, p, :], in_=src_ap[tt * 128:(tt + 1) * 128, :]).then_inc(lq.h, 16)

    mv_t = c.v_mv.bump()

    @on("vector")
    def _(v, ld_t=ld_t, lq=lq):
        v.wait_ge(lq.h, ld_t)
        v.bn_stats(c.stats[:, p, 0:6], c.xs[:, p, 0:512])
        v.bn_stats(c.stats[:, p, 6:12], c.xs[:, p, 512:1024])
        v.drain()
        v.bn_aggr(c.mv[:, p, :], c.stats[:, p, :]).then_inc(c.v_mv.h, 1)

    rs_t = c.s_rstd.bump()

    @on("scalar")
    def _(s, mv_t=mv_t):
        s.wait_ge(c.v_mv.h, mv_t)
        s.activation(out=c.rstd[:, p, :], in_=c.mv[:, p, 1:2], func=AF.Ln, bias=c.eps_t[:, 0:1], scale=1.0)
        s.drain()
        s.activation(out=c.rstd[:, p, :], in_=c.rstd[:, p, :], func=AF.Exp, bias=0.0, scale=-0.5).then_inc(c.s_rstd.h, 1)

    xnb_t = c.v_xnb.bump()
    c._xnb_cons[p] = xnb_t

    @on("vector")
    def _(v, rs_t=rs_t, need_xnb_free=need_xnb_free):
        v.wait_ge(c.s_rstd.h, rs_t)
        if need_xnb_free is not None:
            v.wait_ge(c.t_tr.h, need_xnb_free)
        v.tensor_scalar(out=c.xnb[:, p, :], in0=c.xs[:, p, :], scalar1=c.mv[:, p, 0:1],
                        scalar2=c.rstd[:, p, 0:1], op0=ALU.subtract, op1=ALU.mult).then_inc(c.v_xnb.h, 1)

    trs = [c.t_tr.bump() for _ in range(8)]
    c._xnb_tr[p] = trs[-1]
    evs = [c.v_ev.bump() for _ in range(8)]
    pre = (c._tr_ev_pre[0], c._tr_ev_pre[1])
    c._tr_ev_pre = [evs[6], evs[7]]

    @on("tensor")
    def _(t, xnb_t=xnb_t, pre=pre, evs=evs):
        t.wait_ge(c.v_xnb.h, xnb_t)
        for j in range(8):
            bank = 4 + (j % 2)
            gate = pre[j % 2] if j < 2 else evs[j - 2]
            if gate is not None:
                t.wait_ge(c.v_ev.h, gate)
            pv = c.ps[bank][:, :].bitcast(BF16)
            t.transpose(pv[:, 0:128], c.xnb[:, p, j * 128:(j + 1) * 128], c.ident_bf_sb[:, :]).then_inc(c.t_tr.h, 1)

    @on("vector")
    def _(v, trs=trs, evs=evs):
        for j in range(8):
            bank = 4 + (j % 2)
            pv = c.ps[bank][:, :].bitcast(BF16)
            v.wait_ge(c.t_tr.h, trs[j])
            v.tensor_copy(out=dstT(c)[:, j, tt * 128:(tt + 1) * 128], in_=pv[:, 0:128]).then_inc(c.v_ev.h, 1)

    if not routing:
        return

    lastev = evs[-1]
    rt_gate = c._rt_prev
    mm_t = c.t_mm.bump()

    @on("tensor")
    def _(t, lastev=lastev, rt_gate=rt_gate, wgld=c.wg_ld):
        t.wait_ge(c.v_ev.h, lastev)
        t.wait_ge(c.gq.h, wgld)
        if rt_gate is not None:
            t.wait_ge(c.v_r.h, rt_gate)
        for k in range(8):
            mm = t.matmul(c.ps[3][:, 0:8], vA(c)[:, k, tt * 128:(tt + 1) * 128], c.wg_sb[:, k, :],
                          start=(k == 0), stop=(k == 7))
        mm.then_inc(c.t_mm.h, 1)

    vr1 = c.v_r.bump()

    @on("vector")
    def _(v, mm_t=mm_t):
        v.wait_ge(c.t_mm.h, mm_t)
        v.max(c.topv[:, :], c.ps[3][:, 0:8])
        v.drain()
        v.max_index(c.topi[:, :], c.topv[:, :], c.ps[3][:, 0:8])
        v.tensor_tensor(out=c.if2[:, 0:1], in0=c.topv[:, 0:1], in1=c.topv[:, 1:2], op=ALU.subtract)
        v.drain()
        v.tensor_copy(out=c.if2[:, 1:2], in_=c.topi[:, 1:2])
        v.tensor_copy(out=c.scr16[:, 0:1], in_=c.topi[:, 0:1])
        v.drain()
        v.tensor_single_scalar(out=c.mask16f[:, tt, 0:8], in_=c.iota8_sb[:, :], scalar=c.scr16[:, 0:1], op=ALU.is_equal)
        v.tensor_single_scalar(out=c.mask16f[:, tt, 8:16], in_=c.iota8_sb[:, :], scalar=c.if2[:, 1:2], op=ALU.is_equal).then_inc(c.v_r.h, 1)

    sg_t = c.s_act.bump()

    @on("scalar")
    def _(s, vr1=vr1):
        s.wait_ge(c.v_r.h, vr1)
        s.activation(out=c.g_all[:, tt, 0:1], in_=c.if2[:, 0:1], func=AF.Sigmoid, bias=0.0, scale=1.0)
        s.drain()
        s.activation(out=c.g_all[:, tt, 1:2], in_=c.if2[:, 0:1], func=AF.Sigmoid, bias=0.0, scale=-1.0).then_inc(c.s_act.h, 1)

    if routing == "l1":
        vr2 = c.v_r.bump()
        c._rt_prev = vr2

        @on("vector")
        def _(v, sg_t=sg_t):
            v.wait_ge(c.s_act.h, sg_t)
            v.engine_nop().then_inc(c.v_r.h, 1)
        return

    mm2 = c.t_mm.bump()

    @on("tensor")
    def _(t, vr1=vr1):
        t.wait_ge(c.v_r.h, vr1)
        t.matmul(c.ps[2][:, 0:16], c.triu_sb[:, :], c.mask16f[:, tt, :], start=True, stop=True)
        t.matmul(c.ps[1][0:1, 0:16], c.ones128_f[:, :], c.mask16f[:, tt, :], start=True, stop=True).then_inc(c.t_mm.h, 1)

    vr2 = c.v_r.bump()
    c._rt_prev = vr2

    @on("vector")
    def _(v, mm2=mm2, sg_t=sg_t):
        v.wait_ge(c.t_mm.h, mm2)
        v.wait_ge(c.s_act.h, sg_t)
        v.tensor_copy(out=c.p1_all[:, tt, :], in_=c.ps[2][:, 0:16])
        v.tensor_copy(out=c.bases_all[:, tt, :], in_=c.base_acc[:, :])
        v.drain()
        v.tensor_tensor(out=c.base_acc[:, :], in0=c.base_acc[:, :], in1=c.ps[1][0:1, 0:16], op=ALU.add).then_inc(c.v_r.h, 1)


def _emit_phase0(c):
    on = c.rec.on
    c._lnidx = 0
    c._xnb_cons = [None, None]
    c._xnb_tr = [None, None]
    c._tr_ev_pre = [None, None]
    c._rt_prev = None

    _emit_consts(c)

    for tt in range(TT):
        _emit_ln_tile(c, c.x_own, tt, vA, routing='l0')
    for tt in range(TT):
        _emit_ln_tile(c, c.x_ext, tt, vB, routing=False)

    last_r = c._rt_prev
    g1 = c.agq.bump(16)
    c.agq.bump(16)
    cc1 = c.cc.bump(16 * NCORE if NO_CC else 1)
    g2 = c.agq.bump(16)

    @on("gpsimd")
    def _(g, last_r=last_r, g1=g1, cc1=cc1):
        g.wait_ge(c.v_r.h, last_r)
        g.dma_start(out=c.ag_cnt_in.ap(), in_=c.zrow[:, :]).then_inc(c.agq.h, 16)
        g.wait_ge(c.agq.h, g1)
        g.dma_start(out=c.ag_cnt_in.ap()[0:1, 0:16], in_=c.base_acc[:, :]).then_inc(c.agq.h, 16)
        g.wait_ge(c.agq.h, g1 + 16)
        if NO_CC:
            for r in range(NCORE):
                g.dma_start(out=c.ag_cnt_out.ap()[r:r + 1, :], in_=c.ag_cnt_in.ap()).then_inc(c.cc.h, 16)
        else:
            g.collective_compute(
                "AllGather", ALU.bypass,
                replica_groups=[list(range(NCORE))],
                ins=[c.ag_cnt_in.ap().opt()],
                outs=[c.ag_cnt_out.ap().opt()],
            ).then_inc(c.cc.h, 1)
        g.wait_ge(c.cc.h, cc1)
        g.dma_start(out=c.cnt8[:, :], in_=c.ag_cnt_out.ap()[:, 0:16]).then_inc(c.agq.h, 16)

    mm3 = c.t_mm.bump()

    @on("tensor")
    def _(t, g2=g2):
        t.wait_ge(c.agq.h, g2)
        t.matmul(c.ps[2][0:1, 0:16], c.onespfx_sb[:, :], c.cnt8[:, :], start=True, stop=True)
        t.matmul(c.ps[1][0:1, 0:16], c.ones8_sb[:, :], c.cnt8[:, :], start=True, stop=True).then_inc(c.t_mm.h, 1)

    ab_t = c.v_r.bump()
    c._allbase_done = ab_t

    @on("vector")
    def _(v, mm3=mm3):
        v.wait_ge(c.t_mm.h, mm3)
        v.memset(c.scrrow[0:1, 0:16], 0.0)
        v.drain()
        v.tensor_copy(out=c.scrrow[0:1, 8:16], in_=c.ps[1][0:1, 0:8])
        v.drain()
        v.tensor_tensor(out=c.allbase[:, :], in0=c.ps[2][0:1, 0:16], in1=c.scrrow[0:1, 0:16], op=ALU.add).then_inc(c.v_r.h, 1)



def _emit_keepw(c):
    on = c.rec.on
    kw_prev = None
    ev_prev = None
    for tt in range(TT):
        va = c.v_aux.bump()

        @on("vector")
        def _(v, tt=tt, first=(tt == 0), kw_prev=kw_prev):
            if first:
                v.wait_ge(c.v_r.h, c._allbase_done)
            v.tensor_tensor(out=c.scrrow[0:1, 0:16], in0=c.allbase[0:1, :], in1=c.bases_all[0:1, tt, :], op=ALU.add).then_inc(c.v_aux.h, 1)

        mm = c.t_mm.bump()

        @on("tensor")
        def _(t, va=va, kw_prev=kw_prev):
            t.wait_ge(c.v_aux.h, va)
            if kw_prev is not None:
                t.wait_ge(c.v_r.h, kw_prev)
            t.matmul(c.ps[3][:, 0:16], c.ones1_f[:, :], c.scrrow[0:1, 0:16], start=True, stop=True).then_inc(c.t_mm.h, 1)

        kw = c.v_r.bump()
        kw_prev = kw

        @on("vector")
        def _(v, tt=tt, mm=mm):
            v.wait_ge(c.t_mm.h, mm)
            v.tensor_tensor(out=c.scr16[:, 0:16], in0=c.p1_all[:, tt, :], in1=c.ps[3][:, 0:16], op=ALU.add)
            v.drain()
            for k in range(2):
                v.tensor_tensor(out=c.scr8[:, 0:8], in0=c.scr16[:, k * 8:(k + 1) * 8],
                                in1=c.mask16f[:, tt, k * 8:(k + 1) * 8], op=ALU.mult)
                v.drain()
                v.reduce_sum(c.selk[:, k:k + 1], c.scr8[:, 0:8], axis=mybir.AxisListType.X)
                v.drain()
            v.tensor_single_scalar(out=c.selk[:, 0:2], in_=c.selk[:, 0:2], scalar=float(CAP) + 0.5, op=ALU.is_lt)
            v.drain()
            v.tensor_tensor(out=c.selk[:, 0:2], in0=c.selk[:, 0:2], in1=c.g_all[:, tt, :], op=ALU.mult)
            v.drain()
            v.tensor_single_scalar(out=c.scr16[:, 0:8], in_=c.mask16f[:, tt, 8:16], scalar=c.selk[:, 1:2], op=ALU.mult)
            v.drain()
            v.scalar_tensor_tensor(out=c.w8_all[:, tt, :], in0=c.mask16f[:, tt, 0:8], scalar=c.selk[:, 0:1],
                                   in1=c.scr16[:, 0:8], op0=ALU.mult, op1=ALU.add).then_inc(c.v_r.h, 1)

        trw = c.t_tr.bump()

        @on("tensor")
        def _(t, tt=tt, kw=kw, ev_prev=ev_prev):
            t.wait_ge(c.v_r.h, kw)
            if ev_prev is not None:
                t.wait_ge(c.v_ev.h, ev_prev)
            t.transpose(c.ps[2][0:8, 0:128], c.w8_all[:, tt, :], c.ident_f_sb[:, :]).then_inc(c.t_tr.h, 1)

        evw = c.v_ev.bump()
        ev_prev = evw

        @on("vector")
        def _(v, tt=tt, trw=trw):
            v.wait_ge(c.t_tr.h, trw)
            v.tensor_copy(out=c.wT_all[0:8, tt * 128:(tt + 1) * 128], in_=c.ps[2][0:8, 0:128]).then_inc(c.v_ev.h, 1)

    c._w8_done = kw_prev
    c._wT_done = ev_prev






def vAT_qT2(c):
    return c.arD[:, 0:4096].bitcast(BF16)


def vAT_kT2(c):
    return c.arD[:, 4096:12288].bitcast(BF16)


def vAT_vT2(c):
    return c.arD[:, 12288:20480].bitcast(BF16)


def vAT_vaug(c):
    return c.arD[:, 20480:28928].bitcast(BF16).rearrange("p (k a) -> p k a", k=32)

def vAT_pt(c):
    return c.arD[:, 61440:65536].bitcast(BF16).rearrange("p (s a) -> p s a", s=4)

def vWSL(c):
    return c.arD[:, 32768:45056].bitcast(BF16).rearrange("p (q w k a) -> p q w k a", q=2, w=3, k=8)


def vWROW(c):
    return c.arD[:, 45056:57344].bitcast(BF16).rearrange("p (q w a) -> p q w a", q=2, w=3)


def vGBROW(c):
    return c.arD[0:1, 57344:65536].bitcast(F32)


def vDROW(c):
    return c.arD[0:1, 57344:61440].bitcast(F32).rearrange("p (s a) -> p s a", s=2)


def vWoT(c):
    return c.arB[:, 0:16384].bitcast(BF16).rearrange("p (k a) -> p k a", k=8)





def _emit_attn_pair(c, hp):
    on = c.rec.on
    par = hp % 2
    wsl = c.wsl0 if par == 0 else c.wsl1

    wload = [wsl.bump(16) for _ in range(3)]
    slot_gate = c._wsl_gate[par]

    @on("gpsimd")
    def _(g, hp=hp, par=par, wsl=wsl, slot_gate=slot_gate):
        if slot_gate is not None:
            g.wait_ge(c.t_mm.h, slot_gate)
        for w, W in enumerate((c.Wq, c.Wk, c.Wv)):
            g.dma_start(out=vWROW(c)[:, par, w, :], in_=W[hp * 128:(hp + 1) * 128, :]).then_inc(wsl.h, 16)

    trs = [c.t_tr.bump() for _ in range(24)]
    evs = [c.v_ev.bump() for _ in range(24)]
    pre = tuple(c._tb_pre)
    c._tb_pre = [evs[22], evs[23]]

    @on("tensor")
    def _(t, par=par, wsl=wsl, wload=wload, evs=evs, pre=pre,
          exp_gate=(c.a_exp.n if hp > 0 else None)):
        t.wait_ge(wsl.h, wload[2])
        if exp_gate is not None:
            t.wait_ge(c.a_exp.h, exp_gate)
        n = 0
        for w in range(3):
            for dblk in range(8):
                bank = 4 + (n % 2)
                gate = pre[n % 2] if n < 2 else evs[n - 2]
                if gate is not None:
                    t.wait_ge(c.v_ev.h, gate)
                pv = c.ps[bank][:, :].bitcast(BF16)
                t.transpose(pv[:, 0:128], vWROW(c)[:, par, w, dblk * 128:(dblk + 1) * 128],
                            c.ident_bf_sb[:, :]).then_inc(c.t_tr.h, 1)
                n += 1

    @on("vector")
    def _(v, par=par, trs=trs):
        n = 0
        for w in range(3):
            for dblk in range(8):
                bank = 4 + (n % 2)
                pv = c.ps[bank][:, :].bitcast(BF16)
                v.wait_ge(c.t_tr.h, trs[n])
                v.tensor_copy(out=vWSL(c)[:, par, w, dblk, :], in_=pv[:, 0:128]).then_inc(c.v_ev.h, 1)
                n += 1

    wsl_ev = evs[-1]

    specs = []
    for w, nchunk in ((0, 4), (1, 8), (2, 8)):
        for ch in range(nchunk):
            specs.append((w, ch))
    gen_sv0 = c.s_act.n
    for n, (w, ch) in enumerate(specs):
        mm = c.t_mm.bump()
        sv = c.s_act.bump()

        @on("tensor")
        def _(t, w=w, ch=ch, n=n, par=par, wsl_ev=wsl_ev,
              pg=(c._gen_ps_gate[n % 2]), bank=(n % 2),
              eg=(c.a_exp.n if (n < 2 and hp > 0) else None)):
            if n == 0:
                t.wait_ge(c.v_ev.h, wsl_ev)
            if eg is not None:
                t.wait_ge(c.a_exp.h, eg)
            if pg is not None:
                t.wait_ge(c.s_act.h, pg)
            src = vA if (w == 0 or ch < 4) else vB
            cc_ = ch if (w == 0 or ch < 4) else ch - 4
            for k in range(8):
                mmh = t.matmul(c.ps[bank][:, :], vWSL(c)[:, par, w, k, :],
                               src(c)[:, k, cc_ * 512:(cc_ + 1) * 512], start=(k == 0), stop=(k == 7))
            mmh.then_inc(c.t_mm.h, 1)

        @on("scalar")
        def _(s_, w=w, ch=ch, mm=mm, bank=(n % 2)):
            s_.wait_ge(c.t_mm.h, mm)
            dst = (vAT_qT2, vAT_kT2, vAT_vT2)[w]
            s_.activation(out=dst(c)[:, ch * 512:(ch + 1) * 512], in_=c.ps[bank][:, :],
                          func=AF.Copy, bias=0.0, scale=(0.125 if w == 0 else 1.0)).then_inc(c.s_act.h, 1)

        c._gen_ps_gate[n % 2] = sv

    gen_end_sv = c.s_act.n
    c._wsl_gate[par] = c.t_mm.n

    vtr = [c.t_tr.bump() for _ in range(32)]
    vev = [c.v_aux.bump() for _ in range(32)]
    pre2 = tuple(c._tb_pre)

    @on("tensor")
    def _(t, vtr=vtr, vev=vev, gen_end_sv=gen_end_sv, pre2=pre2):
        t.wait_ge(c.s_act.h, gen_end_sv)
        for kt in range(32):
            bank = 4 + (kt % 2)
            if kt < 2:
                gate = pre2[kt % 2]
                if gate is not None:
                    t.wait_ge(c.v_ev.h, gate)
            else:
                t.wait_ge(c.v_aux.h, vev[kt - 2])
            pv = c.ps[bank][:, :].bitcast(BF16)
            t.transpose(pv[:, 0:128], vAT_vT2(c)[:, kt * 128:(kt + 1) * 128],
                        c.ident_bf_sb[:, :]).then_inc(c.t_tr.h, 1)

    c._tb_pre = [None, None]
    c._tb_pre_aux = [vev[30], vev[31]]

    @on("vector")
    def _(v, vtr=vtr):
        for kt in range(32):
            bank = 4 + (kt % 2)
            pv = c.ps[bank][:, :].bitcast(BF16)
            v.wait_ge(c.t_tr.h, vtr[kt])
            v.tensor_single_scalar(out=vAT_vaug(c)[:, kt, 0:64], in_=pv[:, 0:64],
                                   scalar=c.vmask_sb[:, kt:kt + 1], op=ALU.mult)
            v.tensor_single_scalar(out=vAT_vaug(c)[:, kt, 66:130], in_=pv[:, 64:128],
                                   scalar=c.vmask_sb[:, kt:kt + 1], op=ALU.mult)
            v.tensor_copy(out=vAT_vaug(c)[:, kt, 64:65], in_=c.vmask_sb[:, kt:kt + 1])
            v.tensor_copy(out=vAT_vaug(c)[:, kt, 130:131], in_=c.vmask_sb[:, kt:kt + 1]).then_inc(c.v_aux.h, 1)

    vaug_ev = c.v_aux.n

    for h01 in range(2):
        for i in range(4):
            kts = list(range(0, 4 * i + 4)) + list(range(16, 32))
            nk = len(kts)
            pv_gate = c._pv_evac_gate[h01]
            mms, exs, mks, ready = [], [], [], []
            STB = (0, 1, 6, 7)
            LAG = 2
            ex_t = []
            pv_t = []
            pend = []
            for idx, kt in enumerate(kts):
                diag = (4 * i <= kt <= 4 * i + 3)
                mm = c.a_mm.bump()
                mms.append(mm)

                @on("tensor")
                def _(t, kt=kt, idx=idx, i=i, h01=h01, b=STB[idx % 4], diag=diag,
                      vg=(vaug_ev if idx == 0 else None),
                      sg=(gen_end_sv if idx < 4 else None),
                      gg=(c.gb_done if idx < 4 else None),
                      eg=(ex_t[idx - 4] if idx >= 4 else None),
                      pveg=(pv_gate if idx == 0 else None)):
                    if vg is not None:
                        t.wait_ge(c.v_aux.h, vg)
                    if sg is not None:
                        t.wait_ge(c.s_act.h, sg)
                    if gg is not None:
                        t.wait_ge(c.v_ev.h, gg)
                    if pveg is not None:
                        t.wait_ge(c.s_act.h, pveg)
                    if eg is not None:
                        t.wait_ge(c.a_exp.h, eg)
                    t.matmul(c.ps[b][:, :], vAT_kT2(c)[h01 * 64:(h01 + 1) * 64, kt * 128:(kt + 1) * 128],
                             vAT_qT2(c)[h01 * 64:(h01 + 1) * 64, i * 512:(i + 1) * 512],
                             start=True, stop=True).then_inc(c.a_mm.h, 1)

                ex = c.a_exp.bump()
                exs.append(ex)
                ex_t.append(ex)

                @on("scalar")
                def _(s_, idx=idx, b=STB[idx % 4], mm=mm,
                      pvg=(pv_t[idx - 4] if idx >= 4 else (c._pt_gate[idx % 4]))):
                    s_.wait_ge(c.a_mm.h, mm)
                    if pvg is not None:
                        s_.wait_ge(c.a_pv.h, pvg)
                    s_.activation(out=vAT_pt(c)[:, idx % 4, 0:512], in_=c.ps[b][:, :],
                                  func=AF.Exp, bias=c.m20[:, 0:1], scale=1.0).then_inc(c.a_exp.h, 1)

                if diag:
                    mk = c.a_msk.bump()

                    @on("vector")
                    def _(v, kt=kt, i=i, idx=idx, ex=ex):
                        v.wait_ge(c.a_exp.h, ex)
                        v.tensor_tensor(out=vAT_pt(c)[:, idx % 4, 0:512], in0=vAT_pt(c)[:, idx % 4, 0:512],
                                        in1=c.cmask_sb[:, kt - 4 * i, :], op=ALU.mult).then_inc(c.a_msk.h, 1)
                    ready.append(("m", mk))
                else:
                    ready.append(("e", ex))

                pvn = c.a_pv.bump()
                pv_t.append(pvn)

                def mk_pv(kt=kt, idx=idx, h01=h01, b=(idx % 4), nk=nk, r=ready[idx]):
                    @on("tensor")
                    def _(t):
                        t.wait_ge((c.a_msk.h if r[0] == "m" else c.a_exp.h), r[1])
                        t.matmul(c.ps[2 + h01][0:65, :], vAT_vaug(c)[:, kt, h01 * 66:h01 * 66 + 65],
                                 vAT_pt(c)[:, b, 0:512], start=(idx == 0), stop=(idx == nk - 1)).then_inc(c.a_pv.h, 1)
                pend.append(mk_pv)
                if idx >= LAG:
                    pend[idx - LAG]()
            for j in range(max(0, nk - LAG), nk):
                pend[j]()

            pvend = c.a_pv.n
            c._pt_gate = [pvend, pvend, pvend, pvend]

            dslot = c._dslot % 2
            dqs = c.dq if dslot == 0 else c.dq1
            dgate = c._drow_gate[dslot]
            sa1 = c.s_act.bump()
            sa2 = c.s_act.bump()
            dq_t = dqs.bump(16)
            c._drow_gate[dslot] = dq_t
            c._dslot += 1

            @on("scalar")
            def _(s_, hp=hp, h01=h01, i=i, pvend=pvend, dslot=dslot, dgate=dgate, dqs=dqs):
                s_.wait_ge(c.a_pv.h, pvend)
                if dgate is not None:
                    s_.wait_ge(dqs.h, dgate)
                s_.activation(out=vC(c)[h01 * 64:(h01 + 1) * 64, hp, i * 512:(i + 1) * 512],
                              in_=c.ps[2 + h01][0:64, :], func=AF.Copy, bias=0.0, scale=1.0).then_inc(c.s_act.h, 1)
                s_.activation(out=vDROW(c)[0:1, dslot, :], in_=c.ps[2 + h01][64:65, :],
                              func=AF.Copy, bias=0.0, scale=1.0).then_inc(c.s_act.h, 1)

            c._pv_evac_gate[h01] = c.s_act.n

            @on("gpsimd")
            def _(g, hp=hp, h01=h01, i=i, sa2=sa2, dslot=dslot, dqs=dqs):
                g.wait_ge(c.s_act.h, sa2)
                g.dma_start(out=c.D_all[2 * hp + h01:2 * hp + h01 + 1, i * 512:(i + 1) * 512],
                            in_=vDROW(c)[0:1, dslot, :]).then_inc(dqs.h, 16)


def _emit_attn(c):
    on = c.rec.on
    c._wsl_gate = [None, None]
    c._tb_pre = [c._tr_ev_pre[0], c._tr_ev_pre[1]]
    c._tb_pre_aux = [None, None]
    c._gen_ps_gate = [None, None]
    c._pv_evac_gate = [None, None]
    c._pt_gate = [None, None, None, None]
    c._drow_gate = [None, None]
    c._dslot = 0

    for hp in range(8):
        _emit_attn_pair(c, hp)

    dq_all = c.dq.n
    dq1_all = c.dq1.n
    rec_v = c.v_aux.bump()

    @on("vector")
    def _(v, dq_all=dq_all, dq1_all=dq1_all):
        v.wait_ge(c.dq.h, dq_all)
        v.wait_ge(c.dq1.h, dq1_all)
        v.reciprocal(out=c.D_all[0:16, :], in_=c.D_all[0:16, :])
        v.drain()
        v.engine_nop().then_inc(c.v_aux.h, 1)

    attn_exp_all = c.a_exp.n
    attn_sact_all = c.s_act.n
    prev_ev = None
    for ob in range(8):
        ld = c.wsl0.bump(16)

        @on("gpsimd")
        def _(g, ob=ob, prev_ev=prev_ev, gate=(c._wsl_gate[1] if ob == 0 else None)):
            if gate is not None:
                g.wait_ge(c.t_mm.h, gate)
            if prev_ev is not None:
                g.wait_ge(c.v_ev.h, prev_ev)
            g.dma_start(out=vWROW(c)[:, ob % 2, 0, :], in_=c.Wo[ob * 128:(ob + 1) * 128, :]).then_inc(c.wsl0.h, 16)

        trs = [c.t_tr.bump() for _ in range(8)]
        evs = [c.v_ev.bump() for _ in range(8)]
        pre = tuple(c._tb_pre_aux) if ob == 0 else (None, None)

        @on("tensor")
        def _(t, ob=ob, ld=ld, trs=trs, evs=evs, pre=pre, prev_ev=prev_ev):
            t.wait_ge(c.wsl0.h, ld)
            for ab in range(8):
                bank = 4 + (ab % 2)
                if ab < 2:
                    if ob == 0:
                        if pre[ab] is not None:
                            t.wait_ge(c.v_aux.h, pre[ab])
                    elif prev_ev is not None:
                        t.wait_ge(c.v_ev.h, prev_ev)
                else:
                    t.wait_ge(c.v_ev.h, evs[ab - 2])
                pv = c.ps[bank][:, :].bitcast(BF16)
                t.transpose(pv[:, 0:128], vWROW(c)[:, ob % 2, 0, ab * 128:(ab + 1) * 128],
                            c.ident_bf_sb[:, :]).then_inc(c.t_tr.h, 1)

        @on("vector")
        def _(v, ob=ob, trs=trs):
            for ab in range(8):
                bank = 4 + (ab % 2)
                pv = c.ps[bank][:, :].bitcast(BF16)
                v.wait_ge(c.t_tr.h, trs[ab])
                v.tensor_copy(out=vWoT(c)[:, ab, ob * 128:(ob + 1) * 128], in_=pv[:, 0:128]).then_inc(c.v_ev.h, 1)

        prev_ev = evs[-1]

    wo_ev_all = c.v_ev.n

    c._ginv_pre = None
    c._wo_ps_gate = [None, None]
    c._xres_gate = [None, None]
    c._spill_gate = [None, None]
    xnb_all = c.v_xnb.n
    for tt in range(TT):
        p = tt % 2
        gtr = c.t_tr.bump()
        gev = c.v_aux.bump()

        @on("tensor")
        def _(t, tt=tt, gtr=gtr, rec_v=rec_v, first=(tt == 0), gpre=c._ginv_pre, wo_ev_all=wo_ev_all):
            if first:
                t.wait_ge(c.v_aux.h, rec_v)
                t.wait_ge(c.v_ev.h, wo_ev_all)
            if gpre is not None:
                t.wait_ge(c.v_aux.h, gpre)
            t.transpose(c.ps[5][0:128, 0:16], c.D_all[0:16, tt * 128:(tt + 1) * 128],
                        c.ident_f_sb[0:16, 0:16]).then_inc(c.t_tr.h, 1)

        c._ginv_pre = gev

        @on("vector")
        def _(v, gtr=gtr):
            v.wait_ge(c.t_tr.h, gtr)
            v.tensor_copy(out=c.ginv_sb[:, :], in_=c.ps[5][0:128, 0:16]).then_inc(c.v_aux.h, 1)

        lq = c.lq0 if p == 0 else c.lq1
        xld = lq.bump(16)

        @on("sync")
        def _(s_, tt=tt, p=p, lq=lq, gate=c._xres_gate[p], xg=(xnb_all if tt < 2 else None)):
            if xg is not None:
                s_.wait_ge(c.v_xnb.h, xg)
            if gate is not None:
                s_.wait_ge(c.v_aux.h, gate)
            s_.dma_start(out=c.xs[:, p, :], in_=c.x_own[tt * 128:(tt + 1) * 128, :]).then_inc(lq.h, 16)

        for oc in range(2):
            for hd in range(16):
                b = hd % 2
                mm = c.t_mm.bump()

                @on("tensor")
                def _(t, tt=tt, oc=oc, hd=hd, b=b, first=(tt == 0 and oc == 0 and hd < 2),
                      wo_ev_all=wo_ev_all, sact=attn_sact_all, aexp=attn_exp_all,
                      wog=c._wo_ps_gate[b]):
                    if first:
                        t.wait_ge(c.v_ev.h, wo_ev_all)
                        t.wait_ge(c.s_act.h, sact)
                        t.wait_ge(c.a_exp.h, aexp)
                    if wog is not None:
                        t.wait_ge(c.v_cmb.h, wog)
                    t.matmul(c.ps[b][:, :], vC(c)[(hd % 2) * 64:(hd % 2) * 64 + 64, hd // 2, tt * 128:(tt + 1) * 128],
                             vWoT(c)[(hd % 2) * 64:(hd % 2) * 64 + 64, hd // 2, oc * 512:(oc + 1) * 512],
                             start=True, stop=True).then_inc(c.t_mm.h, 1)

                cm = c.v_cmb.bump()
                c._wo_ps_gate[b] = cm

                @on("vector")
                def _(v, tt=tt, oc=oc, hd=hd, b=b, mm=mm, gev=gev):
                    v.wait_ge(c.t_mm.h, mm)
                    if hd == 0:
                        v.wait_ge(c.v_aux.h, gev)
                        v.tensor_single_scalar(out=vD_acc(c)[:, tt, oc * 512:(oc + 1) * 512], in_=c.ps[b][:, :],
                                               scalar=c.ginv_sb[:, 0:1], op=ALU.mult)
                        v.drain()
                        v.engine_nop().then_inc(c.v_cmb.h, 1)
                    else:
                        v.scalar_tensor_tensor(out=vD_acc(c)[:, tt, oc * 512:(oc + 1) * 512], in0=c.ps[b][:, :],
                                               scalar=c.ginv_sb[:, hd:hd + 1],
                                               in1=vD_acc(c)[:, tt, oc * 512:(oc + 1) * 512],
                                               op0=ALU.mult, op1=ALU.add)
                        v.drain()
                        v.engine_nop().then_inc(c.v_cmb.h, 1)

        va = c.v_aux.bump()
        c._xres_gate[p] = va
        spqs = c.spq if p == 0 else c.spq1
        sp = spqs.bump(16)

        @on("vector")
        def _(v, tt=tt, p=p, xld=xld, lq=lq, spg=c._spill_gate[p], spqs=spqs):
            v.wait_ge(lq.h, xld)
            if spg is not None:
                v.wait_ge(spqs.h, spg)
            v.tensor_tensor(out=vD_acc(c)[:, tt, :], in0=vD_acc(c)[:, tt, :], in1=c.xs[:, p, :],
                            op=ALU.add).then_inc(c.v_aux.h, 1)

        c._spill_gate[p] = sp

        @on("gpsimd")
        def _(g, tt=tt, va=va, spqs=spqs):
            g.wait_ge(c.v_aux.h, va)
            g.dma_start(out=c.acc_spill.ap()[tt * 128:(tt + 1) * 128, :], in_=vD_acc(c)[:, tt, :]).then_inc(spqs.h, 16)

    c._spill_all = (c.spq.n, c.spq1.n)
    c._attn_end_mm = c.t_mm.n



def _emit_moe0_prep(c):
    on = c.rec.on
    c._gcast = Sem(c.ctx, c.nc, "gcast")
    casts = [c._gcast.bump(16) for _ in range(NE)]
    fb = c.fbq.bump(16)
    c._fc2b_ld = fb

    @on("gpsimd")
    def _(g, casts=casts, mb2=c.m_b2):
        for e in range(NE):
            if e > 0:
                g.wait_ge(c._gcast.h, casts[e - 1])
            g.dma_start(out=c.fc1bf.ap()[e * HFF:(e + 1) * HFF, :], in_=c.fc1_w[e]).then_inc(c._gcast.h, 16)
        g.dma_start(out=c.fc2b_sb[:, :], in_=mb2).then_inc(c.fbq.h, 16)


def _emit_moe0_combinit(c):
    on = c.rec.on
    comb = vD_comb
    prev_ci = None
    wT_done = c._wT_done
    fc2b_ld = c._fc2b_ld
    for tt in range(TT):
        mm = c.t_mm.bump()

        @on("tensor")
        def _(t, tt=tt, first=(tt == 0), prev_ci=prev_ci, wT_done=wT_done, fc2b_ld=fc2b_ld,
              spill=(c._spill_all if tt == 0 else None)):
            if first:
                t.wait_ge(c.v_ev.h, wT_done)
                t.wait_ge(c.fbq.h, fc2b_ld)
                if spill is not None:
                    t.wait_ge(c.spq.h, spill[0])
                    t.wait_ge(c.spq1.h, spill[1])
            else:
                t.wait_ge(c.v_cmb.h, prev_ci)
            t.matmul(c.ps[4][:, :], c.wT_all[0:8, tt * 128:(tt + 1) * 128], c.fc2b_sb[0:8, 0:512],
                     start=True, stop=True)
            t.matmul(c.ps[5][:, :], c.wT_all[0:8, tt * 128:(tt + 1) * 128], c.fc2b_sb[0:8, 512:1024],
                     start=True, stop=True).then_inc(c.t_mm.h, 1)

        ci = c.v_cmb.bump()
        prev_ci = ci

        @on("vector")
        def _(v, tt=tt, mm=mm):
            v.wait_ge(c.t_mm.h, mm)
            v.tensor_copy(out=comb(c)[:, tt, 0:512], in_=c.ps[4][:, :])
            v.tensor_copy(out=comb(c)[:, tt, 512:1024], in_=c.ps[5][:, :]).then_inc(c.v_cmb.h, 1)

    c._comb_init_done = c.v_cmb.n


def _emit_moe0_quarter(c, e, q, qi):
    on = c.rec.on
    qp = qi % 2
    if c.mcfg["l1"]:
        tw = c.tl0 if qp == 0 else c.tl1
    else:
        tw = c.tw0 if qp == 0 else c.tw1
    gw = c.gw0 if qp == 0 else c.gw1
    bw = c.bw0 if qp == 0 else c.bw1

    slot_free = c._q_slot_free[qp]

    l1 = c.mcfg["l1"]
    if l1:
        tw_t = tw.bump(16 * 8)

    if not l1:
        wev_last = None
        for hch in range(8):
            sp = hch % 2
            tws = c.tw0 if sp == 0 else c.tw1
            tw_t = tws.bump(16)
            stg_gate = c._fstg_gate[sp]

            @on("gpsimd")
            def _(g, e=e, q=q, hch=hch, sp=sp, tws=tws, slot_free=slot_free, gate=c._gcast_gate[e],
                  ag=(c._arena_gate_ev if qi == 0 and hch == 0 else None),
                  agm=(c._attn_end_mm if qi == 0 and hch == 0 else None),
                  stg_gate=stg_gate, first=(hch == 0)):
                if first:
                    g.wait_ge(c._gcast.h, gate)
                    if ag is not None:
                        g.wait_ge(c.v_ev.h, ag)
                    if agm is not None:
                        g.wait_ge(c.t_mm.h, agm)
                    if slot_free is not None:
                        g.wait_ge(c.t_mm.h, slot_free)
                if stg_gate is not None:
                    g.wait_ge(c.v_ev.h, stg_gate)
                base = e * HFF + q * 1024 + hch * 128
                g.dma_start(out=c.xnb[:, sp, :], in_=c.fc1bf.ap()[base:base + 128, :]).then_inc(tws.h, 16)

            trs = [c.t_tr.bump() for _ in range(8)]
            evs = [c.v_ev.bump() for _ in range(8)]
            c._fstg_gate[sp] = evs[-1]
            wev_last = evs[-1]
            pre = tuple(c._mtr_pre)
            c._mtr_pre = [evs[6], evs[7]]

            @on("tensor")
            def _(t, sp=sp, tw_t=tw_t, tws=tws, evs=evs, pre=pre,
                  civ=(getattr(c, "_comb_init_done", None) if (qi == 0 and hch == 0) else None)):
                t.wait_ge(tws.h, tw_t)
                if civ is not None:
                    t.wait_ge(c.v_cmb.h, civ)
                for dblk in range(8):
                    bank = 4 + (dblk % 2)
                    gate = pre[dblk % 2] if dblk < 2 else evs[dblk - 2]
                    if gate is not None:
                        t.wait_ge(c.v_ev.h, gate)
                    pv = c.ps[bank][:, :].bitcast(BF16)
                    t.transpose(pv[:, 0:128], c.xnb[:, sp, dblk * 128:(dblk + 1) * 128],
                                c.ident_bf_sb[:, :]).then_inc(c.t_tr.h, 1)

            @on("vector")
            def _(v, sp=sp, hch=hch, trs=trs):
                for dblk in range(8):
                    bank = 4 + (dblk % 2)
                    pv = c.ps[bank][:, :].bitcast(BF16)
                    v.wait_ge(c.t_tr.h, trs[dblk])
                    v.tensor_copy(out=vW1(c)[:, qp, dblk, hch * 128:(hch + 1) * 128],
                                  in_=pv[:, 0:128]).then_inc(c.v_ev.h, 1)

        c._w1tr_done = wev_last
        tw_t = None
    else:
        @on("gpsimd")
        def _(g, e=e, q=q, slot_free=slot_free,
              ag=(c._arena_gate_ev if qi == 0 else None),
              agm=(c._attn_end_mm if qi == 0 else None), mw1=c.m_w1):
            if ag is not None:
                g.wait_ge(c.v_ev.h, ag)
            if agm is not None:
                g.wait_ge(c.t_mm.h, agm)
            if slot_free is not None:
                g.wait_ge(c.t_mm.h, slot_free)
            for dblk in range(8):
                g.dma_start(out=vW1(c)[:, qp, dblk, :],
                            in_=mw1[e, dblk * 128:(dblk + 1) * 128, q * 1024:(q + 1) * 1024]).then_inc(tw.h, 16)

    gw_t = gw.bump(16 * 8)
    bw_t = bw.bump(16)

    @on("gpsimd")
    def _(g, e=e, q=q, slot_free=slot_free, ag=(c._arena_gate_ev if qi == 0 else None),
          agm=(c._attn_end_mm if qi == 0 else None), mw2=c.m_w2, mb1=c.m_b1):
        if ag is not None:
            g.wait_ge(c.v_ev.h, ag)
        if agm is not None:
            g.wait_ge(c.t_mm.h, agm)
        if slot_free is not None:
            g.wait_ge(c.t_mm.h, slot_free)
        for hb in range(8):
            g.dma_start(out=vW2(c)[:, qp, hb, :],
                        in_=mw2[e, q * 1024 + hb * 128:q * 1024 + (hb + 1) * 128, :]).then_inc(gw.h, 16)
        g.dma_start(out=c.b1row[0:8, qp, :],
                    in_=mb1[e, q * 1024:(q + 1) * 1024].rearrange("(c p) -> c p", p=128)).then_inc(bw.h, 16)

    btr = c.t_tr.bump()
    bev = c.v_ev.bump()

    @on("tensor")
    def _(t, bw_t=bw_t, gate=c._b1_ev_prev, ci_done=(c._comb_init_done if c._b1_ev_prev is None else None),
          mpre=c._mtr_pre[0]):
        t.wait_ge(bw.h, bw_t)
        if ci_done is not None:
            t.wait_ge(c.v_cmb.h, ci_done)
        if gate is not None:
            t.wait_ge(c.v_ev.h, gate)
        if mpre is not None:
            t.wait_ge(c.v_ev.h, mpre)
        t.transpose(c.ps[4][0:128, 0:8], c.b1row[0:8, qp, :], c.ident_f_sb[0:8, 0:8]).then_inc(c.t_tr.h, 1)

    c._b1_ev_prev = bev
    c._mtr_pre = [bev, c._mtr_pre[1]]

    @on("vector")
    def _(v, btr=btr):
        v.wait_ge(c.t_tr.h, btr)
        v.tensor_copy(out=c.b1col[:, qp, :], in_=c.ps[4][0:128, 0:8]).then_inc(c.v_ev.h, 1)

    hT = vD_hTq
    silu_start = c.s_silu.n
    mm_groups = []
    for gidx in range(32):
        ht, tc = gidx // 4, gidx % 4
        mm = c.t_mm.bump()
        mm_groups.append(mm)

        @on("tensor")
        def _(t, ht=ht, tc=tc, gidx=gidx, tw_t=tw_t, bev=bev,
              first=(gidx == 0), silu_start=silu_start, fg=(c._moe_first_gate if qi == 0 else None),
              wtr=(c._w1tr_done if (not l1 and gidx == 0) else None)):
            bank = gidx % 2
            if first:
                if fg is not None:
                    t.wait_ge(c.v_r.h, fg)
                if tw_t is not None:
                    t.wait_ge(tw.h, tw_t)
                if wtr is not None:
                    t.wait_ge(c.v_ev.h, wtr)
                t.wait_ge(c.v_ev.h, bev)
            if gidx >= 2:
                t.wait_ge(c.s_silu.h, silu_start + gidx - 1)
            for k in range(8):
                mmh = t.matmul(c.ps[bank][:, :], vW1(c)[:, qp, k, ht * 128:(ht + 1) * 128],
                               vA(c)[:, k, tc * 512:(tc + 1) * 512], start=(k == 0), stop=(k == 7))
            mmh.then_inc(c.t_mm.h, 1)

        sl = c.s_silu.bump()

        @on("scalar")
        def _(s_, ht=ht, tc=tc, gidx=gidx, mm=mm, act=c.mcfg["act"]):
            bank = gidx % 2
            s_.wait_ge(c.t_mm.h, mm)
            s_.activation(out=hT(c)[:, ht, tc * 512:(tc + 1) * 512], in_=c.ps[bank][:, :],
                          func=act, bias=c.b1col[:, qp, ht:ht + 1], scale=1.0).then_inc(c.s_silu.h, 1)

    silu_all = c.s_silu.n

    for tt in range(TT):
        pair = (2, 3) if tt % 2 == 0 else (6, 7)
        mm = c.t_mm.bump()

        @on("tensor")
        def _(t, tt=tt, pair=pair, first=(tt == 0), silu_all=silu_all, gw_t=gw_t,
              ygate=c._y_gate[tt % 2], wtd=(c._wT_done if qi == 0 and tt <= 1 else None)):
            if first:
                t.wait_ge(c.s_silu.h, silu_all)
                t.wait_ge(gw.h, gw_t)
            if wtd is not None:
                t.wait_ge(c.v_ev.h, wtd)
            if ygate is not None:
                t.wait_ge(c.v_cmb.h, ygate)
            for half in range(2):
                for k in range(8):
                    mmh = t.matmul(c.ps[pair[half]][:, :], hT(c)[:, k, tt * 128:(tt + 1) * 128],
                                   vW2(c)[:, qp, k, half * 512:(half + 1) * 512],
                                   start=(k == 0), stop=(k == 7))
            mmh.then_inc(c.t_mm.h, 1)

        cm = c.v_cmb.bump()
        c._y_gate[tt % 2] = cm

        @on("vector")
        def _(v, tt=tt, pair=pair, mm=mm, e=e):
            v.wait_ge(c.t_mm.h, mm)
            v.scalar_tensor_tensor(out=vD_comb(c)[:, tt, 0:512], in0=c.ps[pair[0]][:, :],
                                   scalar=c.w8_all[:, tt, e:e + 1], in1=vD_comb(c)[:, tt, 0:512],
                                   op0=ALU.mult, op1=ALU.add)
            v.scalar_tensor_tensor(out=vD_comb(c)[:, tt, 512:1024], in0=c.ps[pair[1]][:, :],
                                   scalar=c.w8_all[:, tt, e:e + 1], in1=vD_comb(c)[:, tt, 512:1024],
                                   op0=ALU.mult, op1=ALU.add).then_inc(c.v_cmb.h, 1)

    c._q_slot_free[qp] = c.t_mm.n


def _emit_moe0_epilogue(c):
    on = c.rec.on
    comb_all = c.v_cmb.n
    for tt in range(TT):
        p = tt % 2
        trs = [c.t_tr.bump() for _ in range(8)]
        evs = [c.v_ev.bump() for _ in range(8)]
        pre = (c._ep_tr_pre[0], c._ep_tr_pre[1])
        c._ep_tr_pre = [evs[6], evs[7]]

        @on("tensor")
        def _(t, tt=tt, p=p, trs=trs, evs=evs, pre=pre, first=(tt == 0), comb_all=comb_all,
              xgate=c._xn0n_gate[p]):
            if first:
                t.wait_ge(c.v_cmb.h, comb_all)
            if xgate is not None:
                t.wait_ge(c.v_aux.h, xgate)
            for j in range(8):
                bank = 4 + (j % 2)
                gate = pre[j % 2] if j < 2 else evs[j - 2]
                if gate is not None:
                    t.wait_ge(c.v_ev.h, gate)
                pv = c.ps[bank][:, :].bitcast(BF16)
                t.transpose(pv[:, 0:128], vA(c)[:, j, tt * 128:(tt + 1) * 128],
                            c.ident_bf_sb[:, :]).then_inc(c.t_tr.h, 1)

        @on("vector")
        def _(v, tt=tt, p=p, trs=trs):
            for j in range(8):
                bank = 4 + (j % 2)
                pv = c.ps[bank][:, :].bitcast(BF16)
                v.wait_ge(c.t_tr.h, trs[j])
                v.tensor_copy(out=vEP_xn0n(c)[:, p, j * 128:(j + 1) * 128], in_=pv[:, 0:128]).then_inc(c.v_ev.h, 1)

        aq = c.aq0 if p == 0 else c.aq1
        aq_t = aq.bump(16)

        @on("sync")
        def _(s_, tt=tt, p=p, aq=aq, agate=c._acc_gate[p], ca=(comb_all if tt < 2 else None),
              spa=(c._spill_all if tt < 2 else None)):
            if c._acc_pre is not None:
                s_.wait_ge(c.g_aux.h, c._acc_pre)
                c._acc_pre = None
            if ca is not None:
                s_.wait_ge(c.v_cmb.h, ca)
            if spa is not None:
                s_.wait_ge(c.spq.h, spa[0])
                s_.wait_ge(c.spq1.h, spa[1])
            if agate is not None:
                s_.wait_ge(c.v_aux.h, agate)
            s_.dma_start(out=vEP_acc(c)[:, p, :], in_=c.acc_spill.ap()[tt * 128:(tt + 1) * 128, :]).then_inc(aq.h, 16)

        lastev = evs[-1]
        x1sg = c._x1_store_gate[p]
        mvt = c.v_mv.bump()

        @on("vector")
        def _(v, tt=tt, p=p, lastev=lastev, xgate=x1sg):
            v.wait_ge(c.v_ev.h, lastev)
            if xgate is not None:
                v.wait_ge((c.x1s0 if p == 0 else c.x1s1).h, xgate)
            v.tensor_tensor(out=vEP_r2(c)[:, p, :], in0=vEP_xn0n(c)[:, p, :], in1=vD_comb(c)[:, tt, :], op=ALU.add)
            v.drain()
            v.bn_stats(c.stats[:, p, 0:6], vEP_r2(c)[:, p, 0:512])
            v.bn_stats(c.stats[:, p, 6:12], vEP_r2(c)[:, p, 512:1024])
            v.drain()
            v.bn_aggr(c.mv[:, p, :], c.stats[:, p, :]).then_inc(c.v_mv.h, 1)

        rs = c.s_rstd.bump()

        @on("scalar")
        def _(s_, p=p, mvt=mvt):
            s_.wait_ge(c.v_mv.h, mvt)
            s_.activation(out=c.rstd[:, p, :], in_=c.mv[:, p, 1:2], func=AF.Ln, bias=c.eps_t[:, 0:1], scale=1.0)
            s_.drain()
            s_.activation(out=c.rstd[:, p, :], in_=c.rstd[:, p, :], func=AF.Exp, bias=0.0, scale=-0.5).then_inc(c.s_rstd.h, 1)

        xa = c.v_aux.bump()
        c._xn0n_gate[p] = xa
        c._acc_gate[p] = xa
        x1s = c.x1s0 if p == 0 else c.x1s1
        st_t = x1s.bump(16)

        @on("vector")
        def _(v, tt=tt, p=p, rs=rs, aq_t=aq_t, aq=aq):
            v.wait_ge(c.s_rstd.h, rs)
            v.tensor_scalar(out=vEP_x1t(c)[:, p, :], in0=vEP_r2(c)[:, p, :], scalar1=c.mv[:, p, 0:1],
                            scalar2=c.rstd[:, p, 0:1], op0=ALU.subtract, op1=ALU.mult)
            v.drain()
            v.tensor_tensor(out=vEP_r2(c)[:, p, :], in0=vEP_x1t(c)[:, p, :], in1=c.gam_t[:, :], op=ALU.mult)
            v.drain()
            v.tensor_tensor(out=vEP_x1t(c)[:, p, :], in0=vEP_r2(c)[:, p, :], in1=c.bet_t[:, :], op=ALU.add)
            v.drain()
            v.wait_ge(aq.h, aq_t)
            v.tensor_tensor(out=vEP_r2(c)[:, p, :], in0=vEP_x1t(c)[:, p, :], in1=vEP_acc(c)[:, p, :], op=ALU.add).then_inc(c.v_aux.h, 1)

        c._x1_store_gate[p] = st_t

        @on("gpsimd")
        def _(g, tt=tt, p=p, xa=xa, x1s=x1s):
            g.wait_ge(c.v_aux.h, xa)
            g.dma_start(out=c.x1_out[tt * 128:(tt + 1) * 128, :], in_=vEP_r2(c)[:, p, :]).then_inc(x1s.h, 16)


def _emit_moe0(c, no_attn=False):
    on = c.rec.on
    if no_attn:
        ga = c.g_aux.bump(16)
        c._acc_pre = ga

        @on("gpsimd")
        def _(g):
            g.dma_start(out=c.acc_spill.ap(), in_=c.x_own).then_inc(c.g_aux.h, 16)
    else:
        c._acc_pre = None

    c.mcfg = {"l1": False, "act": AF.Silu}
    c.m_w1 = None
    c.m_w2 = c.fc2_w
    c.m_b1 = c.fc1_b
    c.m_b2 = c.fc2_b
    _emit_moe0_prep(c)
    c._fstg_gate = [None, None]
    c._mtr_pre = [None, None]
    c._mtr_init = c.v_aux.n
    c._arena_gate_ev = c._p0_ev_done
    c._gcast_gate = [16 * (e + 1) for e in range(NE)]
    c._moe_first_gate = c._w8_done
    c._q_slot_free = [None, None]
    c._b1_ev_prev = None
    c._y_gate = [None, None]
    _emit_moe0_combinit(c)
    for e in range(NE):
        for q in range(4):
            _emit_moe0_quarter(c, e, q, e * 4 + q)
    c._ep_tr_pre = [None, None]
    c._xn0n_gate = [None, None]
    c._acc_gate = [None, None]
    c._x1_store_gate = [None, None]
    _emit_moe0_epilogue(c)





def vL1_hid(c):
    return c.arD[:, 0:8192].bitcast(F32)


def vL1_gate(c):
    return c.arD[:, 8192:16384].bitcast(F32)


def vL1_c(c):
    return c.arD[:, 16384:24576].bitcast(F32)


def vL1_v(c):
    return c.arD[:, 24576:32768].bitcast(F32)


def vL1_scan(c):
    return c.arD[:, 57344:65536].bitcast(F32)


def vMK8(c):
    return c.arD[:, 24576:26624]


def vACC2ST(c):
    return c.arD[:, 32768:40960].bitcast(F32).rearrange("p (s d) -> p s d", s=2)


def vL1_wosl(c):
    return c.arD[:, 0:32768].bitcast(BF16).rearrange("p (cb d) -> p cb d", cb=16)


def vL1_car8(c):
    return c.arD[0:8, 32768:40960].bitcast(F32)


def vL1_carrow(c):
    return c.arD[0:1, 0:8192].bitcast(F32)


def vL1_acc2st(c):
    return c.arD[:, 49152:57344].bitcast(F32).rearrange("p (s d) -> p s d", s=2)


def hT_sc(c, sc):
    if sc < 8:
        return vB(c)[:, sc, :]
    return vC(c)[:, sc - 8, :]


def _emit_scan_pass(c, sc, passB):
    on = c.rec.on
    par = sc % 2
    wsl = c.wsl0 if par == 0 else c.wsl1
    wld = wsl.bump(32)
    slot_gate = c._wsl_gate[par]

    @on("gpsimd")
    def _(g, sc=sc, par=par, wsl=wsl, slot_gate=slot_gate, passB=passB,
          wg=(getattr(c, "_scan_wrow_gate", None) if (sc < 2 and not passB) else None)):
        if wg is not None:
            g.wait_ge(wg[0], wg[1])
        if slot_gate is not None:
            g.wait_ge(c.t_mm.h, slot_gate)
        g.dma_start(out=vWROW(c)[:, par, 0, :], in_=c.W_hg[sc * 128:(sc + 1) * 128, :]).then_inc(wsl.h, 16)
        g.dma_start(out=vWROW(c)[:, par, 1, :], in_=c.W_hg[DI + sc * 128:DI + (sc + 1) * 128, :]).then_inc(wsl.h, 16)

    trs = [c.t_tr.bump() for _ in range(16)]
    evs = [c.v_ev.bump() for _ in range(16)]
    pre = tuple(c._tb_pre)
    c._tb_pre = [evs[14], evs[15]]

    @on("tensor")
    def _(t, par=par, wsl=wsl, wld=wld, evs=evs, pre=pre):
        t.wait_ge(wsl.h, wld)
        n = 0
        for w in range(2):
            for dblk in range(8):
                bank = 4 + (n % 2)
                gate = pre[n % 2] if n < 2 else evs[n - 2]
                if gate is not None:
                    t.wait_ge(c.v_ev.h, gate)
                pv = c.ps[bank][:, :].bitcast(BF16)
                t.transpose(pv[:, 0:128], vWROW(c)[:, par, w, dblk * 128:(dblk + 1) * 128],
                            c.ident_bf_sb[:, :]).then_inc(c.t_tr.h, 1)
                n += 1

    @on("vector")
    def _(v, par=par, trs=trs):
        n = 0
        for w in range(2):
            for dblk in range(8):
                bank = 4 + (n % 2)
                pv = c.ps[bank][:, :].bitcast(BF16)
                v.wait_ge(c.t_tr.h, trs[n])
                v.tensor_copy(out=vWSL(c)[:, par, w, dblk, :], in_=pv[:, 0:128]).then_inc(c.v_ev.h, 1)
                n += 1

    wsl_ev = evs[-1]

    hg_sv = []
    for w in range(2):
        for ch in range(4):
            n = w * 4 + ch
            mm = c.t_mm.bump()
            sv = c.s_act.bump()
            hg_sv.append(sv)

            @on("tensor")
            def _(t, w=w, ch=ch, n=n, par=par, mm=mm, wsl_ev=wsl_ev,
                  pg=c._gen_ps_gate[n % 2], bank=(n % 2)):
                if n == 0:
                    t.wait_ge(c.v_ev.h, wsl_ev)
                if pg is not None:
                    t.wait_ge(c.s_act.h, pg)
                for k in range(8):
                    mmh = t.matmul(c.ps[bank][:, :], vWSL(c)[:, par, w, k, :],
                                   vA(c)[:, k, ch * 512:(ch + 1) * 512], start=(k == 0), stop=(k == 7))
                mmh.then_inc(c.t_mm.h, 1)

            @on("scalar")
            def _(s_, w=w, ch=ch, mm=mm, bank=(n % 2),
                  scg=(c._scan_consumed if n == 0 else None),
                  crg=(getattr(c, "_carrow_ev", None) if (n == 0 and passB and sc == 0) else None)):
                s_.wait_ge(c.t_mm.h, mm)
                if scg is not None:
                    s_.wait_ge(c.v_aux.h, scg)
                if crg is not None:
                    s_.wait_ge(c.v_ev.h, crg)
                dst = vL1_hid if w == 0 else vL1_gate
                s_.activation(out=dst(c)[:, ch * 512:(ch + 1) * 512], in_=c.ps[bank][:, :],
                              func=AF.Copy, bias=0.0, scale=1.0).then_inc(c.s_act.h, 1)

            c._gen_ps_gate[n % 2] = sv

    hg_done = c.s_act.n

    s1 = c.s_act.bump()
    s2 = c.s_act.bump()
    v1 = c.v_aux.bump()
    s3 = c.s_act.bump()
    v5 = c.v_aux.bump()

    @on("scalar")
    def _(s_, hg_done=hg_done, scg=c._scan_consumed):
        s_.wait_ge(c.s_act.h, hg_done)
        if scg is not None:
            s_.wait_ge(c.v_aux.h, scg)
        s_.activation(out=vL1_c(c)[:, :], in_=vL1_gate(c)[:, :], func=AF.Sigmoid, bias=0.0, scale=-1.0).then_inc(c.s_act.h, 1)
        s_.drain()
        s_.activation(out=vL1_gate(c)[:, :], in_=vL1_gate(c)[:, :], func=AF.Sigmoid, bias=0.0, scale=1.0).then_inc(c.s_act.h, 1)

    @on("vector")
    def _(v, hg_done=hg_done, scg=c._scan_consumed):
        v.wait_ge(c.s_act.h, hg_done)
        if scg is not None:
            v.wait_ge(c.v_aux.h, scg)
        v.tensor_single_scalar(out=vMK8(c)[:, :], in_=vL1_hid(c)[:, :], scalar=0.0, op=ALU.is_ge).then_inc(c.v_aux.h, 1)

    @on("scalar")
    def _(s_, v1=v1):
        s_.activation(out=vL1_scan(c)[:, :], in_=vL1_hid(c)[:, :], func=AF.Sigmoid, bias=0.0, scale=1.0)
        s_.drain()
        s_.wait_ge(c.v_aux.h, v1)
        s_.activation(out=vL1_hid(c)[:, :], in_=vL1_hid(c)[:, :], func=AF.Identity, bias=c.half_t[:, 0:1], scale=1.0).then_inc(c.s_act.h, 1)

    @on("vector")
    def _(v, sc=sc, s3=s3, s2=s2, passB=passB,
          cd=(c._carry_done_ev if (passB and sc == 0) else None)):
        v.wait_ge(c.s_act.h, s3)
        if cd is not None:
            v.wait_ge(c.v_ev.h, cd)
        v.select(out=vL1_scan(c)[:, :], mask=vMK8(c)[:, :], on_true=vL1_hid(c)[:, :], on_false=vL1_scan(c)[:, :], add_drain=True)
        v.drain()
        v.tensor_tensor(out=vL1_v(c)[:, :], in0=vL1_gate(c)[:, :], in1=vL1_scan(c)[:, :], op=ALU.mult)
        v.drain()
        init = c.carry_sb[:, sc:sc + 1] if passB else 0.0
        v.tensor_tensor_scan(out=vL1_scan(c)[:, :], data0=vL1_c(c)[:, :], data1=vL1_v(c)[:, :],
                             initial=init, op0=ALU.mult, op1=ALU.add)
        v.drain()
        v.tensor_copy(out=hT_sc(c, sc), in_=vL1_scan(c)[:, :])
        if not passB:
            v.tensor_copy(out=c.fin_sb[:, sc:sc + 1], in_=vL1_scan(c)[:, 2047:2048])
        v.drain()
        v.engine_nop().then_inc(c.v_aux.h, 1)

    c._scan_consumed = v5
    c._wsl_gate[par] = c.t_mm.n


def _emit_l1_scan(c):
    on = c.rec.on
    c._scan_consumed = None
    c._gen_ps_gate = [c._gen_ps_gate[0], c._gen_ps_gate[1]] if hasattr(c, "_gen_ps_gate") else [None, None]

    for sc in range(16):
        _emit_scan_pass(c, sc, passB=False)

    last = c._scan_consumed
    ev_all = c.v_ev.n
    ftr = c.t_tr.bump()
    fev = c.v_ev.bump()

    @on("tensor")
    def _(t, last=last, ev_all=ev_all):
        t.wait_ge(c.v_aux.h, last)
        t.wait_ge(c.v_ev.h, ev_all)
        t.transpose(c.ps[4][0:16, 0:128], c.fin_sb[:, :], c.ident_f_sb[:, :]).then_inc(c.t_tr.h, 1)

    @on("vector")
    def _(v, ftr=ftr):
        v.wait_ge(c.t_tr.h, ftr)
        v.tensor_copy(out=c.D_all[0:16, 0:128], in_=c.ps[4][0:16, 0:128]).then_inc(c.v_ev.h, 1)

    g1 = c.agq.bump(16)
    cc1 = c.cc.bump(16 * NCORE if NO_CC else 1)
    g2 = c.agq.bump(16)

    @on("gpsimd")
    def _(g, fev=fev, g1=g1, cc1=cc1):
        g.wait_ge(c.v_ev.h, fev)
        g.dma_start(out=c.ag_car_in.ap(), in_=c.D_all[0:16, 0:128]).then_inc(c.agq.h, 16)
        g.wait_ge(c.agq.h, g1)
        if NO_CC:
            for r in range(NCORE):
                g.dma_start(out=c.ag_car_out.ap()[r * 16:(r + 1) * 16, :], in_=c.ag_car_in.ap()).then_inc(c.cc.h, 16)
        else:
            g.collective_compute(
                "AllGather", ALU.bypass,
                replica_groups=[list(range(NCORE))],
                ins=[c.ag_car_in.ap().opt()],
                outs=[c.ag_car_out.ap().opt()],
            ).then_inc(c.cc.h, 1)
        g.wait_ge(c.cc.h, cc1)
        g.dma_start(out=vL1_car8(c)[0:8, :], in_=c.ag_car_out.ap().rearrange("(r i) p -> r (i p)", r=8)).then_inc(c.agq.h, 16)

    sel_evs = []
    prev_se = None
    for j in range(4):
        smm = c.t_mm.bump()
        se = c.v_ev.bump()
        sel_evs.append(se)

        @on("tensor")
        def _(t, j=j, g2=g2, fev=fev, prev_se=prev_se):
            if j == 0:
                t.wait_ge(c.agq.h, g2)
                t.wait_ge(c.v_ev.h, fev)
            else:
                t.wait_ge(c.v_ev.h, prev_se)
            t.matmul(c.ps[4][0:1, 0:512], c.selcar_sb[0:8, :], vL1_car8(c)[0:8, j * 512:(j + 1) * 512],
                     start=True, stop=True).then_inc(c.t_mm.h, 1)

        smm_t = c.t_mm.n

        @on("vector")
        def _(v, j=j, smm_t=smm_t):
            v.wait_ge(c.t_mm.h, smm_t)
            v.tensor_copy(out=vL1_carrow(c)[0:1, j * 512:(j + 1) * 512], in_=c.ps[4][0:1, 0:512]).then_inc(c.v_ev.h, 1)

        prev_se = se

    ctr = [c.t_tr.bump() for _ in range(16)]
    cev = [c.v_ev.bump() for _ in range(16)]

    @on("tensor")
    def _(t, ctr=ctr, cev=cev, prev_se=prev_se):
        t.wait_ge(c.v_ev.h, prev_se)
        for sc in range(16):
            if sc >= 2:
                t.wait_ge(c.v_ev.h, cev[sc - 2])
            t.transpose(c.ps[4 + sc % 2][0:128, 0:1], vL1_carrow(c)[0:1, sc * 128:(sc + 1) * 128],
                        c.ident_f_sb[0:1, 0:1]).then_inc(c.t_tr.h, 1)

    @on("vector")
    def _(v, ctr=ctr):
        for sc in range(16):
            v.wait_ge(c.t_tr.h, ctr[sc])
            v.tensor_copy(out=c.carry_sb[:, sc:sc + 1], in_=c.ps[4 + sc % 2][0:128, 0:1]).then_inc(c.v_ev.h, 1)

    c._carry_done_ev = cev[-1]
    c._carrow_ev = cev[-1]
    c._carry_done = None
    c._tb_pre = [cev[14], cev[15]]

    for sc in range(16):
        _emit_scan_pass(c, sc, passB=True)


def _emit_l1_outproj(c):
    on = c.rec.on
    scan_done = c._scan_consumed
    c._wosl_tr = [None, None]
    for cb in range(16):
        par = cb % 2
        wsl = c.wsl0 if par == 0 else c.wsl1
        wld = wsl.bump(16 * 8)
        slot_gate = c._wsl_gate[par]
        tr_gate = c._wosl_tr[par]

        @on("gpsimd")
        def _(g, cb=cb, par=par, wsl=wsl, slot_gate=slot_gate, sd=(scan_done if cb < 2 else None),
              tr_gate=tr_gate):
            if sd is not None:
                g.wait_ge(c.v_aux.h, sd)
            if slot_gate is not None:
                g.wait_ge(c.t_mm.h, slot_gate)
            if tr_gate is not None:
                g.wait_ge(c.t_tr.h, tr_gate)
            for dblk in range(8):
                g.dma_start(out=vWROW(c)[:, par, 0, dblk * 128:(dblk + 1) * 128],
                            in_=c.W_out[dblk * 128:(dblk + 1) * 128, cb * 128:(cb + 1) * 128]).then_inc(wsl.h, 16)

        trs = [c.t_tr.bump() for _ in range(8)]
        evs = [c.v_ev.bump() for _ in range(8)]
        pre = tuple(c._tb_pre)
        c._tb_pre = [evs[6], evs[7]]

        @on("tensor")
        def _(t, cb=cb, par=par, wsl=wsl, wld=wld, evs=evs, pre=pre, sd=(scan_done if cb == 0 else None)):
            t.wait_ge(wsl.h, wld)
            if sd is not None:
                t.wait_ge(c.v_aux.h, sd)
            for dblk in range(8):
                bank = 4 + (dblk % 2)
                gate = pre[dblk % 2] if dblk < 2 else evs[dblk - 2]
                if gate is not None:
                    t.wait_ge(c.v_ev.h, gate)
                pv = c.ps[bank][:, :].bitcast(BF16)
                t.transpose(pv[:, 0:128], vWROW(c)[:, par, 0, dblk * 128:(dblk + 1) * 128],
                            c.ident_bf_sb[:, :]).then_inc(c.t_tr.h, 1)

        @on("vector")
        def _(v, cb=cb, trs=trs, sd=(scan_done if cb == 0 else None)):
            if sd is not None:
                v.wait_ge(c.v_aux.h, sd)
            for dblk in range(8):
                bank = 4 + (dblk % 2)
                pv = c.ps[bank][:, :].bitcast(BF16)
                v.wait_ge(c.t_tr.h, trs[dblk])
                v.tensor_copy(out=vL1_wosl(c)[:, cb, dblk * 128:(dblk + 1) * 128], in_=pv[:, 0:128]).then_inc(c.v_ev.h, 1)

        c._wsl_gate[par] = c.t_mm.n
        c._wosl_tr[par] = trs[-1]

    wosl_ev = c.v_ev.n

    c._op_ps_gate = [None, None]
    c._x1ld_gate = [None, None]
    c._a2spill_gate = [None, None]
    xnb_all = c.v_xnb.n
    for tt in range(TT):
        p = tt % 2
        lq = c.lq0 if p == 0 else c.lq1
        xld = lq.bump(16)

        @on("sync")
        def _(s_, tt=tt, p=p, lq=lq, gate=c._x1ld_gate[p], xg=(xnb_all if tt < 2 else None)):
            if xg is not None:
                s_.wait_ge(c.v_xnb.h, xg)
            if gate is not None:
                s_.wait_ge(c.v_aux.h, gate)
            s_.dma_start(out=c.xs[:, p, :], in_=c.x1_src[tt * 128:(tt + 1) * 128, :]).then_inc(lq.h, 16)

        va_parts = []
        for half in range(2):
            b = half
            mm = c.t_mm.bump()

            @on("tensor")
            def _(t, tt=tt, half=half, b=b, first=(tt == 0 and half == 0), wosl_ev=wosl_ev,
                  og=c._op_ps_gate[b]):
                if first:
                    t.wait_ge(c.v_ev.h, wosl_ev)
                if og is not None:
                    t.wait_ge(c.v_aux.h, og)
                for cb in range(16):
                    mmh = t.matmul(c.ps[b][:, :], hT_sc(c, cb)[:, tt * 128:(tt + 1) * 128],
                                   vL1_wosl(c)[:, cb, half * 512:(half + 1) * 512],
                                   start=(cb == 0), stop=(cb == 15))
                mmh.then_inc(c.t_mm.h, 1)

            va = c.v_aux.bump()
            va_parts.append(va)
            c._op_ps_gate[b] = va

            @on("vector")
            def _(v, tt=tt, p=p, half=half, b=b, mm=mm, xld=xld, lq=lq, spg=c._a2spill_gate[p]):
                v.wait_ge(c.t_mm.h, mm)
                v.wait_ge(lq.h, xld)
                if spg is not None and half == 0:
                    v.wait_ge((c.spq if p == 0 else c.spq1).h, spg)
                v.tensor_tensor(out=vL1_acc2st(c)[:, p, half * 512:(half + 1) * 512], in0=c.ps[b][:, :],
                                in1=c.xs[:, p, half * 512:(half + 1) * 512], op=ALU.add).then_inc(c.v_aux.h, 1)

        spqs = c.spq if p == 0 else c.spq1
        sp = spqs.bump(16)
        c._a2spill_gate[p] = sp
        va2 = va_parts[-1]
        c._x1ld_gate[p] = va2

        @on("gpsimd")
        def _(g, tt=tt, p=p, va2=va2, spqs=spqs):
            g.wait_ge(c.v_aux.h, va2)
            g.dma_start(out=c.acc2_spill.ap()[tt * 128:(tt + 1) * 128, :], in_=vL1_acc2st(c)[:, p, :]).then_inc(spqs.h, 16)

    c._spill_all = (c.spq.n, c.spq1.n)
    c._attn_end_mm = c.t_mm.n


def _emit_l1_w8(c):
    on = c.rec.on
    kw_prev = None
    ev_prev = getattr(c, "_l1w8_ev_init", None)
    w8gate_cmb = getattr(c, "_l1w8_cmb_gate", None)
    last_r = c._rt_prev
    for tt in range(TT):
        kw = c.v_r.bump()

        @on("vector")
        def _(v, tt=tt, first=(tt == 0), last_r=last_r, wg8=(w8gate_cmb if tt == 0 else None)):
            if first:
                v.wait_ge(c.v_r.h, last_r)
            if wg8 is not None:
                v.wait_ge(c.v_cmb.h, wg8)
            v.tensor_single_scalar(out=c.scr16[:, 0:8], in_=c.mask16f[:, tt, 8:16], scalar=c.g_all[:, tt, 1:2], op=ALU.mult)
            v.drain()
            v.scalar_tensor_tensor(out=c.w8_all[:, tt, :], in0=c.mask16f[:, tt, 0:8], scalar=c.g_all[:, tt, 0:1],
                                   in1=c.scr16[:, 0:8], op0=ALU.mult, op1=ALU.add).then_inc(c.v_r.h, 1)

        kw_prev = kw
        trw = c.t_tr.bump()

        @on("tensor")
        def _(t, tt=tt, kw=kw, ev_prev=ev_prev):
            t.wait_ge(c.v_r.h, kw)
            if ev_prev is not None:
                t.wait_ge(c.v_ev.h, ev_prev)
            t.transpose(c.ps[2][0:8, 0:128], c.w8_all[:, tt, :], c.ident_f_sb[:, :]).then_inc(c.t_tr.h, 1)

        evw = c.v_ev.bump()
        ev_prev = evw

        @on("vector")
        def _(v, tt=tt, trw=trw):
            v.wait_ge(c.t_tr.h, trw)
            v.tensor_copy(out=c.wT_all[0:8, tt * 128:(tt + 1) * 128], in_=c.ps[2][0:8, 0:128]).then_inc(c.v_ev.h, 1)

    c._w8_done = kw_prev
    c._wT_done = ev_prev
    c._moe_first_gate = kw_prev


def _emit_l1_final(c):
    on = c.rec.on
    comb_all = c.v_cmb.n
    g_acc2 = [None, None]
    x2s = [c.x1s0, c.x1s1]
    x2gate = [None, None]
    for tt in range(TT):
        p = tt % 2
        aq = c.aq0 if p == 0 else c.aq1
        aq_t = aq.bump(16)

        @on("sync")
        def _(s_, tt=tt, p=p, aq=aq, agate=g_acc2[p], ca=(comb_all if tt < 2 else None),
              spa=(c._spill_all if tt < 2 else None), xg=x2gate[p], x1s=x2s[p]):
            if ca is not None:
                s_.wait_ge(c.v_cmb.h, ca)
            if spa is not None:
                s_.wait_ge(c.spq.h, spa[0])
                s_.wait_ge(c.spq1.h, spa[1])
            if agate is not None:
                s_.wait_ge(c.v_aux.h, agate)
            if xg is not None:
                s_.wait_ge(x1s.h, xg)
            s_.dma_start(out=vACC2ST(c)[:, p, :], in_=c.acc2_spill.ap()[tt * 128:(tt + 1) * 128, :]).then_inc(aq.h, 16)

        va = c.v_aux.bump()
        g_acc2[p] = va
        x1s = x2s[p]
        st_t = x1s.bump(16)

        @on("vector")
        def _(v, tt=tt, p=p, aq_t=aq_t, aq=aq):
            v.wait_ge(aq.h, aq_t)
            v.tensor_tensor(out=vACC2ST(c)[:, p, :], in0=vACC2ST(c)[:, p, :], in1=vD_comb(c)[:, tt, :],
                            op=ALU.add).then_inc(c.v_aux.h, 1)

        x2gate[p] = st_t

        @on("gpsimd")
        def _(g, tt=tt, p=p, va=va, x1s=x1s):
            g.wait_ge(c.v_aux.h, va)
            g.dma_start(out=c.x2_out[tt * 128:(tt + 1) * 128, :], in_=vACC2ST(c)[:, p, :]).then_inc(x1s.h, 16)


def build_l1(dbg=(), moe=True):
    nc = bass.Bass("TRN2", target_bir_lowering=False)
    ctx = ExitStack()
    c = Ctx()
    c.nc = nc
    c.rec = Rec()
    c.dbg = set(dbg)
    c.ctx = ctx

    A = lambda name, shape, dt: nc.dram_tensor(name, shape, dt, kind="ExternalInput").ap()
    c.x1_src = A("x1_own", [NT, D], F32)
    c.W_hg = A("L1_attn_W_hg", [2 * DI, D], F32)
    c.W_out = A("L1_attn_W_out", [D, DI], F32)
    c.wg = A("L1_ffn_gate_w", [D, NE], F32)
    c.w1_in = A("L1_ffn_w1", [NE, D, HFF], F32)
    c.b1_in = A("L1_ffn_b1", [NE, HFF], F32)
    c.w2_in = A("L1_ffn_w2", [NE, HFF, D], F32)
    c.b2_in = A("L1_ffn_b2", [NE, D], F32)
    c.ident_bf = A("ident_bf", [128, 128], BF16)
    c.ident_f32 = A("ident_f32", [128, 128], F32)
    c.iota8 = A("iota8", [128, 8], F32)
    c.selcar = A("sel_carry", [8, 1], F32)
    c.triu_f32 = A("triu_f32", [128, 128], F32)
    c.cmask = A("cmask", [128, 4, 512], BF16)
    c.vmask = A("vmask", [128, 32], F32)
    c.onespfx = A("ones_prefix", [8, 1], F32)
    c.ln_g = A("L0_ffn_ln_gamma", [D], F32)
    c.ln_b = A("L0_ffn_ln_beta", [D], F32)

    c.x2_out = nc.dram_tensor("x2_out", [NT, D], F32, kind="ExternalOutput").ap()

    c.acc2_spill = nc.dram_tensor("acc2_spill", [NT, D], F32)
    c.ag_car_in = nc.dram_tensor("ag_car_in", [16, 128], F32)
    c.ag_car_out = nc.dram_tensor("ag_car_out", [128, 128], F32)

    _alloc_common(c)
    _alloc_l1(c)
    _emit_phase0_l1(c)
    _emit_l1_w8(c)
    _emit_l1_scan(c)
    if moe != "scanonly":
        _emit_l1_outproj(c)
    if moe is True:
        _emit_moe1(c)
        _emit_l1_final(c)
    _emit_dbg(c)
    _emit_final_wait(c)

    with nc.Block() as block:
        c.rec.register(block)
    ctx.close()
    return nc


def _alloc_l1(c):
    nc, ctx = c.nc, c.ctx
    E = ctx.enter_context
    c.half_t = E(nc.sbuf_tensor("half_t", [128, 1], F32))
    c.fin_sb = E(nc.sbuf_tensor("fin_sb", [128, 16], F32))
    c.carry_sb = E(nc.sbuf_tensor("carry_sb", [128, 16], F32))
    c.selcar_sb = E(nc.sbuf_tensor("selcar_sb", [8, 1], F32))


def _emit_phase0_l1(c):
    on = c.rec.on
    c._lnidx = 0
    c._xnb_cons = [None, None]
    c._xnb_tr = [None, None]
    c._tr_ev_pre = [None, None]
    c._rt_prev = None

    _emit_consts(c)

    @on("vector")
    def _(v):
        v.memset(c.half_t[:, :], 0.5)

    sld = c.agq.bump(16)

    @on("gpsimd")
    def _(s_):
        s_.dma_start(out=c.selcar_sb[0:8, :], in_=c.selcar).then_inc(c.agq.h, 16)

    c._selcar_ld = sld

    for tt in range(TT):
        _emit_ln_tile(c, c.x1_src, tt, vA, routing="l1")

    c._tb_pre = [c._tr_ev_pre[0], c._tr_ev_pre[1]]
    c._gen_ps_gate = [None, None]
    c._wsl_gate = [None, None]


def _emit_moe1(c):
    on = c.rec.on
    c.mcfg = {"l1": True, "act": AF.Gelu}
    c.m_w1 = c.w1_in
    c.m_w2 = c.w2_in
    c.m_b1 = c.b1_in
    c.m_b2 = c.b2_in
    c._arena_gate_ev = c.v_ev.n
    c._fstg_gate = [None, None]
    c._mtr_pre = [None, None]
    c._mtr_init = None
    c._gcast_gate = [None] * NE
    c._moe_first_gate = c._w8_done
    c._q_slot_free = [None, None]
    c._b1_ev_prev = None
    c._y_gate = [None, None]
    c._acc_pre = None
    fb = c.fbq.bump(16)
    c._fc2b_ld = fb

    @on("gpsimd")
    def _(g, fbg=getattr(c, "_l1fc2b_gate", None), mb2=c.m_b2):
        if fbg is not None:
            g.wait_ge(c.v_cmb.h, fbg)
        g.dma_start(out=c.fc2b_sb[:, :], in_=mb2).then_inc(c.fbq.h, 16)

    _emit_moe0_combinit(c)
    for e in range(NE):
        for q in range(4):
            _emit_moe0_quarter(c, e, q, e * 4 + q)



def _emit_dbg(c):
    on = c.rec.on
    nc = c.nc
    if "l1scan" in c.dbg:
        dh = nc.dram_tensor("dbg_hT", [16, 128, NT], BF16, kind="ExternalOutput").ap()
        dx = nc.dram_tensor("dbg_xn1T", [8, 128, NT], BF16, kind="ExternalOutput").ap()
        dc = nc.dram_tensor("dbg_carry", [128, 16], F32, kind="ExternalOutput").ap()
        dfin = nc.dram_tensor("dbg_fin", [128, 16], F32, kind="ExternalOutput").ap()
        dmid = nc.dram_tensor("dbg_mid", [4, 128, NT], F32, kind="ExternalOutput").ap()
        c.st.bump(16 * 30)
        va_all = c.v_aux.n
        ev_all = c.v_ev.n

        @on("gpsimd")
        def _(g, va_all=va_all, ev_all=ev_all):
            g.wait_ge(c.v_aux.h, va_all)
            g.wait_ge(c.v_ev.h, ev_all)
            for sc in range(16):
                g.dma_start(out=dh[sc], in_=hT_sc(c, sc)).then_inc(c.st.h, 16)
            for k in range(8):
                g.dma_start(out=dx[k], in_=vA(c)[:, k, :]).then_inc(c.st.h, 16)
            g.dma_start(out=dc, in_=c.carry_sb[:, :]).then_inc(c.st.h, 16)
            g.dma_start(out=dfin, in_=c.fin_sb[:, :]).then_inc(c.st.h, 16)
            g.dma_start(out=dmid[0], in_=vL1_hid(c)[:, :]).then_inc(c.st.h, 16)
            g.dma_start(out=dmid[1], in_=vL1_gate(c)[:, :]).then_inc(c.st.h, 16)
            g.dma_start(out=dmid[2], in_=vL1_c(c)[:, :]).then_inc(c.st.h, 16)
            g.dma_start(out=dmid[3], in_=vL1_v(c)[:, :]).then_inc(c.st.h, 16)

    if "xnT" in c.dbg:
        d = nc.dram_tensor("dbg_xnT", [8, 128, NT], BF16, kind="ExternalOutput").ap()
        d2 = nc.dram_tensor("dbg_xnTe", [8, 128, NT], BF16, kind="ExternalOutput").ap()
        c.st.bump(16 * 16)
        ev_all = c.v_ev.n

        @on("gpsimd")
        def _(g, ev_all=ev_all):
            g.wait_ge(c.v_ev.h, ev_all)
            for k in range(8):
                g.dma_start(out=d[k], in_=vA(c)[:, k, :]).then_inc(c.st.h, 16)
                g.dma_start(out=d2[k], in_=vB(c)[:, k, :]).then_inc(c.st.h, 16)

    if "route" in c.dbg:
        dw = nc.dram_tensor("dbg_w8", [128, 16, 8], F32, kind="ExternalOutput").ap()
        dm = nc.dram_tensor("dbg_mask", [128, 16, 16], F32, kind="ExternalOutput").ap()
        dp = nc.dram_tensor("dbg_p1", [128, 16, 16], F32, kind="ExternalOutput").ap()
        db = nc.dram_tensor("dbg_allbase", [1, 16], F32, kind="ExternalOutput").ap()
        dbs = nc.dram_tensor("dbg_bases", [1, 16, 16], F32, kind="ExternalOutput").ap()
        dwt = nc.dram_tensor("dbg_wT", [8, NT], BF16, kind="ExternalOutput").ap()
        c.st.bump(16 * 6)
        vr_all = c.v_r.n
        ev_all = c.v_ev.n

        @on("gpsimd")
        def _(g, vr_all=vr_all, ev_all=ev_all):
            g.wait_ge(c.v_r.h, vr_all)
            g.wait_ge(c.v_ev.h, ev_all)
            g.dma_start(out=dw, in_=c.w8_all[:, :, :]).then_inc(c.st.h, 16)
            g.dma_start(out=dm, in_=c.mask16f[:, :, :]).then_inc(c.st.h, 16)
            g.dma_start(out=dp, in_=c.p1_all[:, :, :]).then_inc(c.st.h, 16)
            g.dma_start(out=db, in_=c.allbase[:, :]).then_inc(c.st.h, 16)
            g.dma_start(out=dbs, in_=c.bases_all[:, :, :]).then_inc(c.st.h, 16)
            g.dma_start(out=dwt, in_=c.wT_all[:, :]).then_inc(c.st.h, 16)


def _emit_final_wait(c):
    on = c.rec.on
    target = c.st.n
    t0, t1 = c.x1s0.n, c.x1s1.n

    @on("gpsimd")
    def _(g, target=target, t0=t0, t1=t1):
        if target:
            g.wait_ge(c.st.h, target)
        if t0:
            g.wait_ge(c.x1s0.h, t0)
        if t1:
            g.wait_ge(c.x1s1.h, t1)



def host_constants(core_id):
    b, h = core_id // 2, core_id % 2
    ident = np.eye(128, dtype=np.float32)
    cm = np.zeros((128, 4, 512), dtype=np.float32)
    for j in range(4):
        for jj in range(4):
            blk = cm[:, j, jj * 128:(jj + 1) * 128]
            if jj > j:
                blk[:] = 1.0
            elif jj == j:
                blk[:] = np.tril(np.ones((128, 128), np.float32)).T
    vm = np.zeros((128, 32), dtype=np.float32)
    vm[:, 0:16] = 1.0
    if h == 1:
        vm[:, 16:32] = 1.0
    onespfx = np.zeros((8, 1), dtype=np.float32)
    onespfx[:core_id, 0] = 1.0
    sel = np.zeros((8, 1), dtype=np.float32)
    if h == 1:
        sel[core_id - 1, 0] = 1.0
    return {
        "ident_bf": ident.astype(ml_dtypes.bfloat16),
        "ident_f32": ident,
        "triu_f32": np.triu(np.ones((128, 128), np.float32)),
        "iota8": np.tile(np.arange(8, dtype=np.float32), (128, 1)),
        "cmask": cm.astype(ml_dtypes.bfloat16),
        "vmask": vm,
        "ones_prefix": onespfx,
        "sel_carry": sel,
    }


def shard_inputs(inputs):
    x = np.ascontiguousarray(inputs["L0_attn_in_features"])
    maps = []
    for cid in range(NCORE):
        b, h = cid // 2, cid % 2
        m = dict(host_constants(cid))
        m.pop("sel_carry")
        m["x_own"] = np.ascontiguousarray(x[b, h * NT:(h + 1) * NT])
        m["x_ext"] = np.ascontiguousarray(x[b, (1 - h) * NT:(2 - h) * NT])
        for k in ("L0_attn_Wq", "L0_attn_Wk", "L0_attn_Wv", "L0_attn_Wo", "L0_ffn_wg",
                  "L0_ffn_fc1_w", "L0_ffn_fc1_b", "L0_ffn_fc2_w", "L0_ffn_fc2_b",
                  "L0_ffn_ln_gamma", "L0_ffn_ln_beta"):
            m[k] = inputs[k]
        maps.append(m)
    return maps





def build_fused(dbg=()):
    nc = bass.Bass("TRN2", target_bir_lowering=False)
    ctx = ExitStack()
    c = Ctx()
    c.nc = nc
    c.rec = Rec()
    c.dbg = set(dbg)
    c.ctx = ctx

    A = lambda name, shape, dt: nc.dram_tensor(name, shape, dt, kind="ExternalInput").ap()
    c.x_own = A("x_own", [NT, D], F32)
    c.x_ext = A("x_ext", [NT, D], F32)
    c.Wq = A("L0_attn_Wq", [D, D], F32)
    c.Wk = A("L0_attn_Wk", [D, D], F32)
    c.Wv = A("L0_attn_Wv", [D, D], F32)
    c.Wo = A("L0_attn_Wo", [D, D], F32)
    c.wg = A("L0_ffn_wg", [D, NE], F32)
    c.fc1_w = A("L0_ffn_fc1_w", [NE, HFF, D], F32)
    c.fc1_b = A("L0_ffn_fc1_b", [NE, HFF], F32)
    c.fc2_w = A("L0_ffn_fc2_w", [NE, HFF, D], F32)
    c.fc2_b = A("L0_ffn_fc2_b", [NE, D], F32)
    c.ln_g = A("L0_ffn_ln_gamma", [D], F32)
    c.ln_b = A("L0_ffn_ln_beta", [D], F32)
    c.W_hg = A("L1_attn_W_hg", [2 * DI, D], F32)
    c.W_out = A("L1_attn_W_out", [D, DI], F32)
    c.gate_w = A("L1_ffn_gate_w", [D, NE], F32)
    c.w1_in = A("L1_ffn_w1", [NE, D, HFF], F32)
    c.b1_in = A("L1_ffn_b1", [NE, HFF], F32)
    c.w2_in = A("L1_ffn_w2", [NE, HFF, D], F32)
    c.b2_in = A("L1_ffn_b2", [NE, D], F32)
    c.ident_bf = A("ident_bf", [128, 128], BF16)
    c.ident_f32 = A("ident_f32", [128, 128], F32)
    c.triu_f32 = A("triu_f32", [128, 128], F32)
    c.iota8 = A("iota8", [128, 8], F32)
    c.cmask = A("cmask", [128, 4, 512], BF16)
    c.vmask = A("vmask", [128, 32], F32)
    c.onespfx = A("ones_prefix", [8, 1], F32)
    c.selcar = A("sel_carry", [8, 1], F32)

    c.x2_out = nc.dram_tensor("x2_out", [NT, D], F32, kind="ExternalOutput").ap()

    c.fc1bf = nc.dram_tensor("fc1bf", [NE * HFF, D], BF16)
    c.acc_spill = nc.dram_tensor("acc_spill", [NT, D], F32)
    c.x1_spill = nc.dram_tensor("x1_spill", [NT, D], F32)
    c.acc2_spill = nc.dram_tensor("acc2_spill", [NT, D], F32)
    c.ag_cnt_in = nc.dram_tensor("ag_cnt_in", [1, 512], F32)
    c.ag_cnt_out = nc.dram_tensor("ag_cnt_out", [8, 512], F32)
    c.ag_car_in = nc.dram_tensor("ag_car_in", [16, 128], F32)
    c.ag_car_out = nc.dram_tensor("ag_car_out", [128, 128], F32)
    c.x1_out = c.x1_spill.ap()
    c.x1_src = c.x1_spill.ap()

    _alloc_common(c)
    _alloc_l1(c)

    _emit_phase0(c)
    _emit_keepw(c)
    c._p0_ev_done = c.v_ev.n
    _emit_attn(c)
    _emit_moe0(c)

    ep_end_vaux = c.v_aux.n
    x1_tot = (c.x1s0.n, c.x1s1.n)
    l0_cmb = c.v_cmb.n

    @c.rec.on("vector")
    def _(v):
        v.memset(c.half_t[:, :], 0.5)

    c.agq.bump(16)

    @c.rec.on("gpsimd")
    def _(g):
        g.dma_start(out=c.selcar_sb[0:8, :], in_=c.selcar).then_inc(c.agq.h, 16)

    gfw = c.gq.bump(16)
    c.wg_ld = gfw

    @c.rec.on("gpsimd")
    def _(g, wr=c._w8_done):
        g.wait_ge(c.v_r.h, wr)
        g.dma_start(out=c.wg_sb[:, :, :], in_=c.gate_w.rearrange("(k p) e -> p k e", p=128)).then_inc(c.gq.h, 16)

    c._tr_ev_pre = [c._ep_tr_pre[0], c._ep_tr_pre[1]]
    c._rt_prev = c._w8_done
    eg0 = [(c.x1s0.h, x1_tot[0]), (c.x1s1.h, x1_tot[1]), (c.v_aux.h, ep_end_vaux)]
    for tt in range(TT):
        _emit_ln_tile(c, c.x1_src, tt, vA, routing="l1",
                      extra_gates=(eg0 if tt < 2 else None))

    c._l1w8_ev_init = c._wT_done
    c._l1w8_cmb_gate = l0_cmb
    _emit_l1_w8(c)

    c._tb_pre = [c._tr_ev_pre[0], c._tr_ev_pre[1]]
    c._gen_ps_gate = [None, None]
    c._wsl_gate = [None, None]
    c._scan_consumed = ep_end_vaux
    c._scan_wrow_gate = (c.v_aux.h, ep_end_vaux)
    _emit_l1_scan(c)
    _emit_l1_outproj(c)

    c._l1fc2b_gate = l0_cmb
    _emit_moe1(c)
    _emit_l1_final(c)
    _emit_dbg(c)
    _emit_final_wait(c)

    with nc.Block() as block:
        c.rec.register(block)
    ctx.close()
    return nc


def shard_inputs_fused(inputs):
    maps = []
    x = np.ascontiguousarray(inputs["L0_attn_in_features"])
    for cid in range(NCORE):
        b, h = cid // 2, cid % 2
        m = dict(host_constants(cid))
        m["x_own"] = np.ascontiguousarray(x[b, h * NT:(h + 1) * NT])
        m["x_ext"] = np.ascontiguousarray(x[b, (1 - h) * NT:(2 - h) * NT])
        for k in ("L0_attn_Wq", "L0_attn_Wk", "L0_attn_Wv", "L0_attn_Wo", "L0_ffn_wg",
                  "L0_ffn_fc1_w", "L0_ffn_fc1_b", "L0_ffn_fc2_w", "L0_ffn_fc2_b",
                  "L0_ffn_ln_gamma", "L0_ffn_ln_beta",
                  "L1_attn_W_hg", "L1_attn_W_out", "L1_ffn_gate_w", "L1_ffn_w1",
                  "L1_ffn_b1", "L1_ffn_w2", "L1_ffn_b2"):
            m[k] = inputs[k]
        maps.append(m)
    return maps


def shard_inputs_l1(inputs, x1_shards):
    maps = []
    for cid in range(NCORE):
        m = dict(host_constants(cid))
        m["x1_own"] = np.ascontiguousarray(x1_shards[cid])
        m["L0_ffn_ln_gamma"] = inputs["L0_ffn_ln_gamma"]
        m["L0_ffn_ln_beta"] = inputs["L0_ffn_ln_beta"]
        for k in ("L1_attn_W_hg", "L1_attn_W_out", "L1_ffn_gate_w", "L1_ffn_w1",
                  "L1_ffn_b1", "L1_ffn_w2", "L1_ffn_b2"):
            m[k] = inputs[k]
        maps.append(m)
    return maps


_CACHE = {}


def kernel(**inputs):
    if "fused" not in _CACHE:
        _CACHE["fused"] = build_fused()
    nc = _CACHE["fused"]
    maps = shard_inputs_fused(inputs)
    res = run_bass_kernel_spmd(nc, maps, core_ids=list(range(NCORE)))
    out = np.empty((4, 4096, 1024), dtype=np.float32)
    for cid in range(NCORE):
        b, h = cid // 2, cid % 2
        out[b, h * NT:(h + 1) * NT] = res.results[cid]["x2_out"]
    return out


# version 06 (final)
# speedup vs baseline: 1.2181x; 1.0057x over previous
import numpy as np
import ml_dtypes
from contextlib import ExitStack

import concourse.bass as bass
import concourse.mybir as mybir
from concourse.bass_utils import run_bass_kernel_spmd

F32 = mybir.dt.float32
BF16 = mybir.dt.bfloat16
U32 = mybir.dt.uint32
U8 = mybir.dt.uint8
AF = mybir.ActivationFunctionType
ALU = mybir.AluOpType

D = 1024
NT = 2048
TT = NT // 128
NHEAD = 16
DH = 64
NE = 8
HFF = 4096
CAP = 5120
DI = 2048
NCORE = 8

ENGINES = ("sync", "scalar", "vector", "tensor", "gpsimd")
NO_CC = False


class Sem:
    def __init__(self, ctx, nc, name):
        self.h = ctx.enter_context(nc.semaphore(name))
        self.n = 0

    def bump(self, k=1):
        self.n += k
        return self.n


class Rec:
    def __init__(self):
        self.steps = {e: [] for e in ENGINES}

    def on(self, e):
        def deco(fn):
            self.steps[e].append(fn)
            return None
        return deco

    def register(self, block):
        for e in ENGINES:
            fns = self.steps[e]
            getattr(block, e)(lambda eng, fns=fns: [f(eng) for f in fns])


class Ctx:
    pass



def build_l0(dbg=(), moe=True, no_attn=False):
    nc = bass.Bass("TRN2", target_bir_lowering=False)
    ctx = ExitStack()
    c = Ctx()
    c.nc = nc
    c.rec = Rec()
    c.dbg = set(dbg)
    c.ctx = ctx

    A = lambda name, shape, dt: nc.dram_tensor(name, shape, dt, kind="ExternalInput").ap()
    c.x_own = A("x_own", [NT, D], F32)
    c.x_ext = A("x_ext", [NT, D], F32)
    c.Wq = A("L0_attn_Wq", [D, D], F32)
    c.Wk = A("L0_attn_Wk", [D, D], F32)
    c.Wv = A("L0_attn_Wv", [D, D], F32)
    c.Wo = A("L0_attn_Wo", [D, D], F32)
    c.wg = A("L0_ffn_wg", [D, NE], F32)
    c.fc1_w = A("L0_ffn_fc1_w", [NE, HFF, D], F32)
    c.fc1_b = A("L0_ffn_fc1_b", [NE, HFF], F32)
    c.fc2_w = A("L0_ffn_fc2_w", [NE, HFF, D], F32)
    c.fc2_b = A("L0_ffn_fc2_b", [NE, D], F32)
    c.ln_g = A("L0_ffn_ln_gamma", [D], F32)
    c.ln_b = A("L0_ffn_ln_beta", [D], F32)
    c.ident_bf = A("ident_bf", [128, 128], BF16)
    c.ident_f32 = A("ident_f32", [128, 128], F32)
    c.triu_f32 = A("triu_f32", [128, 128], F32)
    c.iota8 = A("iota8", [128, 8], F32)
    c.cmask = A("cmask", [128, 4, 512], BF16)
    c.vmask = A("vmask", [128, 32], F32)
    c.onespfx = A("ones_prefix", [8, 1], F32)

    c.x1_out = nc.dram_tensor("x1_out", [NT, D], F32, kind="ExternalOutput").ap()

    c.fc1bf = nc.dram_tensor("fc1bf", [NE * HFF, D], BF16)
    c.acc_spill = nc.dram_tensor("acc_spill", [NT, D], F32)
    c.ag_cnt_in = nc.dram_tensor("ag_cnt_in", [1, 512], F32)
    c.ag_cnt_out = nc.dram_tensor("ag_cnt_out", [8, 512], F32)

    _alloc_common(c)
    _emit_phase0(c)
    _emit_keepw(c)
    c._p0_ev_done = c.v_ev.n
    if not no_attn:
        _emit_attn(c)
    else:
        c._spill_all = None
        c._attn_end_mm = None
    if moe:
        _emit_moe0(c, no_attn=no_attn)
    _emit_dbg(c)
    _emit_final_wait(c)

    with nc.Block() as block:
        c.rec.register(block)
    ctx.close()
    return nc


def _alloc_common(c):
    nc, ctx = c.nc, c.ctx
    E = ctx.enter_context

    c.arA = E(nc.sbuf_tensor("arA", [128, 32768], U8))
    c.arB = E(nc.sbuf_tensor("arB", [128, 32768], U8))
    c.arC = E(nc.sbuf_tensor("arC", [128, 32768], U8))
    c.arD = E(nc.sbuf_tensor("arD", [128, 65536], U8))

    c.xs = E(nc.sbuf_tensor("xs", [128, 2, 1024], F32))
    c.xnb = E(nc.sbuf_tensor("xnb", [128, 2, 1024], BF16))
    c.stats = E(nc.sbuf_tensor("stats", [128, 2, 12], F32))
    c.mv = E(nc.sbuf_tensor("mv", [128, 2, 2], F32))
    c.rstd = E(nc.sbuf_tensor("rstd", [128, 2, 1], F32))
    c.eps_t = E(nc.sbuf_tensor("eps_t", [128, 1], F32))
    c.m20 = E(nc.sbuf_tensor("m20", [128, 1], F32))
    c.ones128_f = E(nc.sbuf_tensor("ones128_f", [128, 1], F32))
    c.ones1_f = E(nc.sbuf_tensor("ones1_f", [1, 128], F32))
    c.ident_bf_sb = E(nc.sbuf_tensor("ident_bf_sb", [128, 128], BF16))
    c.ident_f_sb = E(nc.sbuf_tensor("ident_f_sb", [128, 128], F32))
    c.triu_sb = E(nc.sbuf_tensor("triu_sb", [128, 128], F32))
    c.iota8_sb = E(nc.sbuf_tensor("iota8_sb", [128, 8], F32))
    c.cmask_sb = E(nc.sbuf_tensor("cmask_sb", [128, 4, 512], BF16))
    c.vmask_sb = E(nc.sbuf_tensor("vmask_sb", [128, 32], F32))
    c.onespfx_sb = E(nc.sbuf_tensor("onespfx_sb", [8, 1], F32))
    c.ones8_sb = E(nc.sbuf_tensor("ones8_sb", [8, 1], F32))
    c.wg_sb = E(nc.sbuf_tensor("wg_sb", [128, 8, 8], BF16))
    c.p1_all = E(nc.sbuf_tensor("p1_all", [128, 16, 16], F32))
    c.mask16f = E(nc.sbuf_tensor("mask16f", [128, 16, 16], F32))
    c.g_all = E(nc.sbuf_tensor("g_all", [128, 16, 2], F32))
    c.bases_all = E(nc.sbuf_tensor("bases_all", [1, 16, 16], F32))
    c.base_acc = E(nc.sbuf_tensor("base_acc", [1, 16], F32))
    c.cnt8 = E(nc.sbuf_tensor("cnt8", [8, 16], F32))
    c.allbase = E(nc.sbuf_tensor("allbase", [1, 16], F32))
    c.w8_all = E(nc.sbuf_tensor("w8_all", [128, 16, 8], F32))
    c.wT_all = E(nc.sbuf_tensor("wT_all", [8, NT], BF16))
    c.scr16 = E(nc.sbuf_tensor("scr16", [128, 16], F32))
    c.scr8 = E(nc.sbuf_tensor("scr8", [128, 8], F32))
    c.scrrow = E(nc.sbuf_tensor("scrrow", [1, 16], F32))
    c.zrow = E(nc.sbuf_tensor("zrow", [1, 512], F32))
    c.selk = E(nc.sbuf_tensor("selk", [128, 2], F32))
    c.if2 = E(nc.sbuf_tensor("if2", [128, 2], F32))
    c.topv = E(nc.sbuf_tensor("topv", [128, 8], F32))
    c.topi = E(nc.sbuf_tensor("topi", [128, 8], U32))
    c.gam_t = E(nc.sbuf_tensor("gam_t", [128, 1024], F32))
    c.bet_t = E(nc.sbuf_tensor("bet_t", [128, 1024], F32))

    c.b1col = E(nc.sbuf_tensor("b1col", [128, 2, 8], F32))
    c.b1row = E(nc.sbuf_tensor("b1row", [8, 2, 128], F32))
    c.fc2b_sb = E(nc.sbuf_tensor("fc2b_sb", [8, 1024], BF16))
    c.D_all = E(nc.sbuf_tensor("D_all", [16, 2048], F32))
    c.ginv_sb = E(nc.sbuf_tensor("ginv_sb", [128, 16], F32))

    c.ps = [E(nc.psum_tensor(f"ps{i}", [128, 512], F32)) for i in range(8)]

    for name in ("sq", "gq", "tq", "lq0", "lq1", "t_mm", "t_tr", "v_mv", "v_xnb", "v_ev",
                 "s_rstd", "s_act", "v_r", "cc", "st", "v_aux", "t_aux", "g_aux",
                 "tw0", "tw1", "gw0", "gw1", "bw0", "bw1", "aq0", "aq1", "s_silu", "v_cmb", "x1q", "agq", "fbq", "x1s0", "x1s1",
                 "wsl0", "wsl1", "a_mm", "a_exp", "a_msk", "a_pv", "dq", "dq1", "spq", "spq1", "tl0", "tl1"):
        setattr(c, name, Sem(c.ctx, c.nc, name))


def vA(c):
    return c.arA[:, :].bitcast(BF16).rearrange("p (k t) -> p k t", k=8)


def vB(c):
    return c.arB[:, :].bitcast(BF16).rearrange("p (k t) -> p k t", k=8)


def vC(c):
    return c.arC[:, :].bitcast(BF16).rearrange("p (k t) -> p k t", k=8)


def vD_acc(c):
    return c.arD[:, :].bitcast(F32).rearrange("p (t d) -> p t d", t=16)


def vD_comb(c):
    return c.arD[:, 0:32768].bitcast(BF16).rearrange("p (t d) -> p t d", t=16)


def vD_hTq(c):
    return c.arD[:, 32768:65536].bitcast(BF16).rearrange("p (k t) -> p k t", k=8)


def vW1(c):
    return c.arB[:, :].bitcast(BF16).rearrange("p (q k h) -> p q k h", q=2, k=8)


def vW2(c):
    return c.arC[:, :].bitcast(BF16).rearrange("p (q k d) -> p q k d", q=2, k=8)


def vEP_xn0n(c):
    return c.arD[:, 32768:36864].bitcast(BF16).rearrange("p (s d) -> p s d", s=2)


def vEP_r2(c):
    return c.arD[:, 36864:45056].bitcast(F32).rearrange("p (s d) -> p s d", s=2)


def vEP_acc(c):
    return c.arD[:, 45056:53248].bitcast(F32).rearrange("p (s d) -> p s d", s=2)


def vEP_x1t(c):
    return c.arD[:, 53248:61440].bitcast(F32).rearrange("p (s d) -> p s d", s=2)




def _emit_consts(c):
    on = c.rec.on

    @on("vector")
    def _(v):
        v.memset(c.eps_t[:, :], 1e-5)
        v.memset(c.m20[:, :], -20.0)
        v.memset(c.ones128_f[:, :], 1.0)
        v.memset(c.ones1_f[:, :], 1.0)
        v.memset(c.ones8_sb[:, :], 1.0)
        v.memset(c.base_acc[:, :], 0.0)
        v.memset(c.zrow[:, :], 0.0)
        v.engine_nop().then_inc(c.v_aux.h, 1)

    t_c = c.sq.bump(16 * 9)
    ms_gate = c.v_aux.bump()
    c._ms_gate = ms_gate

    @c.rec.on("scalar")
    def _(s_, ms_gate=ms_gate):
        s_.wait_ge(c.v_aux.h, ms_gate)

    @on("sync")
    def _(s):
        s.dma_start(out=c.ident_bf_sb[:, :], in_=c.ident_bf).then_inc(c.sq.h, 16)
        s.dma_start(out=c.ident_f_sb[:, :], in_=c.ident_f32).then_inc(c.sq.h, 16)
        s.dma_start(out=c.triu_sb[:, :], in_=c.triu_f32).then_inc(c.sq.h, 16)
        s.dma_start(out=c.iota8_sb[:, :], in_=c.iota8).then_inc(c.sq.h, 16)
        s.dma_start(out=c.cmask_sb[:, :, :], in_=c.cmask).then_inc(c.sq.h, 16)
        s.dma_start(out=c.vmask_sb[:, :], in_=c.vmask).then_inc(c.sq.h, 16)
        s.dma_start(out=c.onespfx_sb[:, :], in_=c.onespfx).then_inc(c.sq.h, 16)
        s.dma_start(out=vGBROW(c)[0:1, 0:1024], in_=c.ln_g[None, :]).then_inc(c.sq.h, 16)
        s.dma_start(out=vGBROW(c)[0:1, 1024:2048], in_=c.ln_b[None, :]).then_inc(c.sq.h, 16)

    c.wg_ld = c.gq.bump(16)

    @on("gpsimd")
    def _(g):
        g.dma_start(out=c.wg_sb[:, :, :], in_=c.wg.rearrange("(k p) e -> p k e", p=128)).then_inc(c.gq.h, 16)

    mm_a = c.t_mm.bump()
    ev_a = c.v_ev.bump(2)
    mm_b = c.t_mm.bump()
    ev_b = c.v_ev.bump(2)

    @on("tensor")
    def _(t, t_c=t_c, mm_a=mm_a, ev_a=ev_a, mm_b=mm_b, ms_gate=ms_gate):
        t.wait_ge(c.v_aux.h, ms_gate)
        t.wait_ge(c.sq.h, t_c)
        t.matmul(c.ps[6][:, :], c.ones1_f[:, :], vGBROW(c)[0:1, 0:512], start=True, stop=True)
        t.matmul(c.ps[7][:, :], c.ones1_f[:, :], vGBROW(c)[0:1, 1024:1536], start=True, stop=True).then_inc(c.t_mm.h, 1)
        t.wait_ge(c.v_ev.h, ev_a)
        t.matmul(c.ps[6][:, :], c.ones1_f[:, :], vGBROW(c)[0:1, 512:1024], start=True, stop=True)
        t.matmul(c.ps[7][:, :], c.ones1_f[:, :], vGBROW(c)[0:1, 1536:2048], start=True, stop=True).then_inc(c.t_mm.h, 1)

    @on("vector")
    def _(v, mm_a=mm_a, mm_b=mm_b):
        v.wait_ge(c.t_mm.h, mm_a)
        v.tensor_copy(out=c.gam_t[:, 0:512], in_=c.ps[6][:, :]).then_inc(c.v_ev.h, 1)
        v.tensor_copy(out=c.bet_t[:, 0:512], in_=c.ps[7][:, :]).then_inc(c.v_ev.h, 1)
        v.wait_ge(c.t_mm.h, mm_b)
        v.tensor_copy(out=c.gam_t[:, 512:1024], in_=c.ps[6][:, :]).then_inc(c.v_ev.h, 1)
        v.tensor_copy(out=c.bet_t[:, 512:1024], in_=c.ps[7][:, :]).then_inc(c.v_ev.h, 1)

    c.gb_done = ev_b


def _emit_ln_tile(c, src_ap, tt, dstT, routing, extra_gates=None):
    on = c.rec.on
    p = c._lnidx % 2
    c._lnidx += 1

    need_xs_free = c._xnb_cons[p]
    need_xnb_free = c._xnb_tr[p]
    lq = c.lq0 if p == 0 else c.lq1
    ld_t = lq.bump(16)

    @on("sync")
    def _(s, need_xs_free=need_xs_free, lq=lq, extra_gates=extra_gates):
        if extra_gates:
            for sem_h, val in extra_gates:
                s.wait_ge(sem_h, val)
        if need_xs_free is not None:
            s.wait_ge(c.v_xnb.h, need_xs_free)
        s.dma_start(out=c.xs[:, p, :], in_=src_ap[tt * 128:(tt + 1) * 128, :]).then_inc(lq.h, 16)

    mv_t = c.v_mv.bump()

    @on("vector")
    def _(v, ld_t=ld_t, lq=lq):
        v.wait_ge(lq.h, ld_t)
        v.bn_stats(c.stats[:, p, 0:6], c.xs[:, p, 0:512])
        v.bn_stats(c.stats[:, p, 6:12], c.xs[:, p, 512:1024])
        v.drain()
        v.bn_aggr(c.mv[:, p, :], c.stats[:, p, :]).then_inc(c.v_mv.h, 1)

    rs_t = c.s_rstd.bump()

    @on("scalar")
    def _(s, mv_t=mv_t):
        s.wait_ge(c.v_mv.h, mv_t)
        s.activation(out=c.rstd[:, p, :], in_=c.mv[:, p, 1:2], func=AF.Ln, bias=c.eps_t[:, 0:1], scale=1.0)
        s.drain()
        s.activation(out=c.rstd[:, p, :], in_=c.rstd[:, p, :], func=AF.Exp, bias=0.0, scale=-0.5).then_inc(c.s_rstd.h, 1)

    xnb_t = c.v_xnb.bump()
    c._xnb_cons[p] = xnb_t

    @on("vector")
    def _(v, rs_t=rs_t, need_xnb_free=need_xnb_free):
        v.wait_ge(c.s_rstd.h, rs_t)
        if need_xnb_free is not None:
            v.wait_ge(c.t_tr.h, need_xnb_free)
        v.tensor_scalar(out=c.xnb[:, p, :], in0=c.xs[:, p, :], scalar1=c.mv[:, p, 0:1],
                        scalar2=c.rstd[:, p, 0:1], op0=ALU.subtract, op1=ALU.mult).then_inc(c.v_xnb.h, 1)

    trs = [c.t_tr.bump() for _ in range(8)]
    c._xnb_tr[p] = trs[-1]
    evs = [c.v_ev.bump() for _ in range(8)]
    pre = (c._tr_ev_pre[0], c._tr_ev_pre[1])
    c._tr_ev_pre = [evs[6], evs[7]]

    @on("tensor")
    def _(t, xnb_t=xnb_t, pre=pre, evs=evs):
        t.wait_ge(c.v_xnb.h, xnb_t)
        for j in range(8):
            bank = 4 + (j % 2)
            gate = pre[j % 2] if j < 2 else evs[j - 2]
            if gate is not None:
                t.wait_ge(c.v_ev.h, gate)
            pv = c.ps[bank][:, :].bitcast(BF16)
            t.transpose(pv[:, 0:128], c.xnb[:, p, j * 128:(j + 1) * 128], c.ident_bf_sb[:, :]).then_inc(c.t_tr.h, 1)

    @on("vector")
    def _(v, trs=trs, evs=evs):
        for j in range(8):
            bank = 4 + (j % 2)
            pv = c.ps[bank][:, :].bitcast(BF16)
            v.wait_ge(c.t_tr.h, trs[j])
            v.tensor_copy(out=dstT(c)[:, j, tt * 128:(tt + 1) * 128], in_=pv[:, 0:128]).then_inc(c.v_ev.h, 1)

    if not routing:
        return

    lastev = evs[-1]
    rt_gate = c._rt_prev
    mm_t = c.t_mm.bump()

    @on("tensor")
    def _(t, lastev=lastev, rt_gate=rt_gate, wgld=c.wg_ld):
        t.wait_ge(c.v_ev.h, lastev)
        t.wait_ge(c.gq.h, wgld)
        if rt_gate is not None:
            t.wait_ge(c.v_r.h, rt_gate)
        for k in range(8):
            mm = t.matmul(c.ps[3][:, 0:8], vA(c)[:, k, tt * 128:(tt + 1) * 128], c.wg_sb[:, k, :],
                          start=(k == 0), stop=(k == 7))
        mm.then_inc(c.t_mm.h, 1)

    vr1 = c.v_r.bump()

    @on("vector")
    def _(v, mm_t=mm_t):
        v.wait_ge(c.t_mm.h, mm_t)
        v.max(c.topv[:, :], c.ps[3][:, 0:8])
        v.drain()
        v.max_index(c.topi[:, :], c.topv[:, :], c.ps[3][:, 0:8])
        v.tensor_tensor(out=c.if2[:, 0:1], in0=c.topv[:, 0:1], in1=c.topv[:, 1:2], op=ALU.subtract)
        v.drain()
        v.tensor_copy(out=c.if2[:, 1:2], in_=c.topi[:, 1:2])
        v.tensor_copy(out=c.scr16[:, 0:1], in_=c.topi[:, 0:1])
        v.drain()
        v.tensor_single_scalar(out=c.mask16f[:, tt, 0:8], in_=c.iota8_sb[:, :], scalar=c.scr16[:, 0:1], op=ALU.is_equal)
        v.tensor_single_scalar(out=c.mask16f[:, tt, 8:16], in_=c.iota8_sb[:, :], scalar=c.if2[:, 1:2], op=ALU.is_equal).then_inc(c.v_r.h, 1)

    sg_t = c.s_act.bump()

    @on("scalar")
    def _(s, vr1=vr1):
        s.wait_ge(c.v_r.h, vr1)
        s.activation(out=c.g_all[:, tt, 0:1], in_=c.if2[:, 0:1], func=AF.Sigmoid, bias=0.0, scale=1.0)
        s.drain()
        s.activation(out=c.g_all[:, tt, 1:2], in_=c.if2[:, 0:1], func=AF.Sigmoid, bias=0.0, scale=-1.0).then_inc(c.s_act.h, 1)

    if routing == "l1":
        vr2 = c.v_r.bump()
        c._rt_prev = vr2

        @on("vector")
        def _(v, sg_t=sg_t):
            v.wait_ge(c.s_act.h, sg_t)
            v.engine_nop().then_inc(c.v_r.h, 1)
        return

    mm2 = c.t_mm.bump()

    @on("tensor")
    def _(t, vr1=vr1):
        t.wait_ge(c.v_r.h, vr1)
        t.matmul(c.ps[2][:, 0:16], c.triu_sb[:, :], c.mask16f[:, tt, :], start=True, stop=True)
        t.matmul(c.ps[1][0:1, 0:16], c.ones128_f[:, :], c.mask16f[:, tt, :], start=True, stop=True).then_inc(c.t_mm.h, 1)

    vr2 = c.v_r.bump()
    c._rt_prev = vr2

    @on("vector")
    def _(v, mm2=mm2, sg_t=sg_t):
        v.wait_ge(c.t_mm.h, mm2)
        v.wait_ge(c.s_act.h, sg_t)
        v.tensor_copy(out=c.p1_all[:, tt, :], in_=c.ps[2][:, 0:16])
        v.tensor_copy(out=c.bases_all[:, tt, :], in_=c.base_acc[:, :])
        v.drain()
        v.tensor_tensor(out=c.base_acc[:, :], in0=c.base_acc[:, :], in1=c.ps[1][0:1, 0:16], op=ALU.add).then_inc(c.v_r.h, 1)


def _emit_phase0(c):
    on = c.rec.on
    c._lnidx = 0
    c._xnb_cons = [None, None]
    c._xnb_tr = [None, None]
    c._tr_ev_pre = [None, None]
    c._rt_prev = None

    _emit_consts(c)

    for tt in range(TT):
        _emit_ln_tile(c, c.x_own, tt, vA, routing='l0')
    for tt in range(TT):
        _emit_ln_tile(c, c.x_ext, tt, vB, routing=False)

    last_r = c._rt_prev
    g1 = c.agq.bump(16)
    c.agq.bump(16)
    cc1 = c.cc.bump(16 * NCORE if NO_CC else 1)
    g2 = c.agq.bump(16)

    @on("gpsimd")
    def _(g, last_r=last_r, g1=g1, cc1=cc1):
        g.wait_ge(c.v_r.h, last_r)
        g.dma_start(out=c.ag_cnt_in.ap(), in_=c.zrow[:, :]).then_inc(c.agq.h, 16)
        g.wait_ge(c.agq.h, g1)
        g.dma_start(out=c.ag_cnt_in.ap()[0:1, 0:16], in_=c.base_acc[:, :]).then_inc(c.agq.h, 16)
        g.wait_ge(c.agq.h, g1 + 16)
        if NO_CC:
            for r in range(NCORE):
                g.dma_start(out=c.ag_cnt_out.ap()[r:r + 1, :], in_=c.ag_cnt_in.ap()).then_inc(c.cc.h, 16)
        else:
            g.collective_compute(
                "AllGather", ALU.bypass,
                replica_groups=[list(range(NCORE))],
                ins=[c.ag_cnt_in.ap().opt()],
                outs=[c.ag_cnt_out.ap().opt()],
            ).then_inc(c.cc.h, 1)
        g.wait_ge(c.cc.h, cc1)
        g.dma_start(out=c.cnt8[:, :], in_=c.ag_cnt_out.ap()[:, 0:16]).then_inc(c.agq.h, 16)

    mm3 = c.t_mm.bump()

    @on("tensor")
    def _(t, g2=g2):
        t.wait_ge(c.agq.h, g2)
        t.matmul(c.ps[2][0:1, 0:16], c.onespfx_sb[:, :], c.cnt8[:, :], start=True, stop=True)
        t.matmul(c.ps[1][0:1, 0:16], c.ones8_sb[:, :], c.cnt8[:, :], start=True, stop=True).then_inc(c.t_mm.h, 1)

    ab_t = c.v_r.bump()
    c._allbase_done = ab_t

    @on("vector")
    def _(v, mm3=mm3):
        v.wait_ge(c.t_mm.h, mm3)
        v.memset(c.scrrow[0:1, 0:16], 0.0)
        v.drain()
        v.tensor_copy(out=c.scrrow[0:1, 8:16], in_=c.ps[1][0:1, 0:8])
        v.drain()
        v.tensor_tensor(out=c.allbase[:, :], in0=c.ps[2][0:1, 0:16], in1=c.scrrow[0:1, 0:16], op=ALU.add).then_inc(c.v_r.h, 1)



def _emit_keepw(c):
    on = c.rec.on
    kw_prev = None
    ev_prev = None
    for tt in range(TT):
        va = c.v_aux.bump()

        @on("vector")
        def _(v, tt=tt, first=(tt == 0), kw_prev=kw_prev):
            if first:
                v.wait_ge(c.v_r.h, c._allbase_done)
            v.tensor_tensor(out=c.scrrow[0:1, 0:16], in0=c.allbase[0:1, :], in1=c.bases_all[0:1, tt, :], op=ALU.add).then_inc(c.v_aux.h, 1)

        mm = c.t_mm.bump()

        @on("tensor")
        def _(t, va=va, kw_prev=kw_prev):
            t.wait_ge(c.v_aux.h, va)
            if kw_prev is not None:
                t.wait_ge(c.v_r.h, kw_prev)
            t.matmul(c.ps[3][:, 0:16], c.ones1_f[:, :], c.scrrow[0:1, 0:16], start=True, stop=True).then_inc(c.t_mm.h, 1)

        kw = c.v_r.bump()
        kw_prev = kw

        @on("vector")
        def _(v, tt=tt, mm=mm):
            v.wait_ge(c.t_mm.h, mm)
            v.tensor_tensor(out=c.scr16[:, 0:16], in0=c.p1_all[:, tt, :], in1=c.ps[3][:, 0:16], op=ALU.add)
            v.drain()
            for k in range(2):
                v.tensor_tensor(out=c.scr8[:, 0:8], in0=c.scr16[:, k * 8:(k + 1) * 8],
                                in1=c.mask16f[:, tt, k * 8:(k + 1) * 8], op=ALU.mult)
                v.drain()
                v.reduce_sum(c.selk[:, k:k + 1], c.scr8[:, 0:8], axis=mybir.AxisListType.X)
                v.drain()
            v.tensor_single_scalar(out=c.selk[:, 0:2], in_=c.selk[:, 0:2], scalar=float(CAP) + 0.5, op=ALU.is_lt)
            v.drain()
            v.tensor_tensor(out=c.selk[:, 0:2], in0=c.selk[:, 0:2], in1=c.g_all[:, tt, :], op=ALU.mult)
            v.drain()
            v.tensor_single_scalar(out=c.scr16[:, 0:8], in_=c.mask16f[:, tt, 8:16], scalar=c.selk[:, 1:2], op=ALU.mult)
            v.drain()
            v.scalar_tensor_tensor(out=c.w8_all[:, tt, :], in0=c.mask16f[:, tt, 0:8], scalar=c.selk[:, 0:1],
                                   in1=c.scr16[:, 0:8], op0=ALU.mult, op1=ALU.add).then_inc(c.v_r.h, 1)

        trw = c.t_tr.bump()

        @on("tensor")
        def _(t, tt=tt, kw=kw, ev_prev=ev_prev):
            t.wait_ge(c.v_r.h, kw)
            if ev_prev is not None:
                t.wait_ge(c.v_ev.h, ev_prev)
            t.transpose(c.ps[2][0:8, 0:128], c.w8_all[:, tt, :], c.ident_f_sb[:, :]).then_inc(c.t_tr.h, 1)

        evw = c.v_ev.bump()
        ev_prev = evw

        @on("vector")
        def _(v, tt=tt, trw=trw):
            v.wait_ge(c.t_tr.h, trw)
            v.tensor_copy(out=c.wT_all[0:8, tt * 128:(tt + 1) * 128], in_=c.ps[2][0:8, 0:128]).then_inc(c.v_ev.h, 1)

    c._w8_done = kw_prev
    c._wT_done = ev_prev






def vAT_qT2(c):
    return c.arD[:, 0:4096].bitcast(BF16)


def vAT_kT2(c):
    return c.arD[:, 4096:12288].bitcast(BF16)


def vAT_vT2(c):
    return c.arD[:, 12288:20480].bitcast(BF16)


def vAT_vaug(c):
    return c.arD[:, 20480:28928].bitcast(BF16).rearrange("p (k a) -> p k a", k=32)

def vAT_pt(c):
    return c.arD[:, 61440:65536].bitcast(BF16).rearrange("p (s a) -> p s a", s=4)

def vWSL(c):
    return c.arD[:, 32768:45056].bitcast(BF16).rearrange("p (q w k a) -> p q w k a", q=2, w=3, k=8)


def vWROW(c):
    return c.arD[:, 45056:57344].bitcast(BF16).rearrange("p (q w a) -> p q w a", q=2, w=3)


def vGBROW(c):
    return c.arD[0:1, 57344:65536].bitcast(F32)


def vDROW(c):
    return c.arD[0:1, 57344:61440].bitcast(F32).rearrange("p (s a) -> p s a", s=2)


def vWoT(c):
    return c.arB[:, 0:16384].bitcast(BF16).rearrange("p (k a) -> p k a", k=8)





def _emit_attn_pair(c, hp):
    on = c.rec.on
    par = hp % 2
    wsl = c.wsl0 if par == 0 else c.wsl1

    wload = [wsl.bump(16) for _ in range(3)]
    slot_gate = c._wsl_gate[par]

    @on("gpsimd")
    def _(g, hp=hp, par=par, wsl=wsl, slot_gate=slot_gate):
        if slot_gate is not None:
            g.wait_ge(c.t_mm.h, slot_gate)
        for w, W in enumerate((c.Wq, c.Wk, c.Wv)):
            g.dma_start(out=vWROW(c)[:, par, w, :], in_=W[hp * 128:(hp + 1) * 128, :]).then_inc(wsl.h, 16)

    trs = [c.t_tr.bump() for _ in range(24)]
    evs = [c.v_ev.bump() for _ in range(24)]
    pre = tuple(c._tb_pre)
    c._tb_pre = [evs[22], evs[23]]

    @on("tensor")
    def _(t, par=par, wsl=wsl, wload=wload, evs=evs, pre=pre,
          exp_gate=(c.a_exp.n if hp > 0 else None)):
        t.wait_ge(wsl.h, wload[2])
        if exp_gate is not None:
            t.wait_ge(c.a_exp.h, exp_gate)
        n = 0
        for w in range(3):
            for dblk in range(8):
                bank = 4 + (n % 2)
                gate = pre[n % 2] if n < 2 else evs[n - 2]
                if gate is not None:
                    t.wait_ge(c.v_ev.h, gate)
                pv = c.ps[bank][:, :].bitcast(BF16)
                t.transpose(pv[:, 0:128], vWROW(c)[:, par, w, dblk * 128:(dblk + 1) * 128],
                            c.ident_bf_sb[:, :]).then_inc(c.t_tr.h, 1)
                n += 1

    @on("vector")
    def _(v, par=par, trs=trs):
        n = 0
        for w in range(3):
            for dblk in range(8):
                bank = 4 + (n % 2)
                pv = c.ps[bank][:, :].bitcast(BF16)
                v.wait_ge(c.t_tr.h, trs[n])
                v.tensor_copy(out=vWSL(c)[:, par, w, dblk, :], in_=pv[:, 0:128]).then_inc(c.v_ev.h, 1)
                n += 1

    wsl_ev = evs[-1]

    specs = []
    for w, nchunk in ((0, 4), (1, 8), (2, 8)):
        for ch in range(nchunk):
            specs.append((w, ch))
    gen_sv0 = c.s_act.n
    for n, (w, ch) in enumerate(specs):
        mm = c.t_mm.bump()
        sv = c.s_act.bump()

        @on("tensor")
        def _(t, w=w, ch=ch, n=n, par=par, wsl_ev=wsl_ev,
              pg=(c._gen_ps_gate[n % 2]), bank=(n % 2),
              eg=(c.a_exp.n if (n < 2 and hp > 0) else None)):
            if n == 0:
                t.wait_ge(c.v_ev.h, wsl_ev)
            if eg is not None:
                t.wait_ge(c.a_exp.h, eg)
            if pg is not None:
                t.wait_ge(c.s_act.h, pg)
            src = vA if (w == 0 or ch < 4) else vB
            cc_ = ch if (w == 0 or ch < 4) else ch - 4
            for k in range(8):
                mmh = t.matmul(c.ps[bank][:, :], vWSL(c)[:, par, w, k, :],
                               src(c)[:, k, cc_ * 512:(cc_ + 1) * 512], start=(k == 0), stop=(k == 7))
            mmh.then_inc(c.t_mm.h, 1)

        @on("scalar")
        def _(s_, w=w, ch=ch, mm=mm, bank=(n % 2)):
            s_.wait_ge(c.t_mm.h, mm)
            dst = (vAT_qT2, vAT_kT2, vAT_vT2)[w]
            s_.activation(out=dst(c)[:, ch * 512:(ch + 1) * 512], in_=c.ps[bank][:, :],
                          func=AF.Copy, bias=0.0, scale=(0.125 if w == 0 else 1.0)).then_inc(c.s_act.h, 1)

        c._gen_ps_gate[n % 2] = sv

    gen_end_sv = c.s_act.n
    c._wsl_gate[par] = c.t_mm.n

    vtr = [c.t_tr.bump() for _ in range(32)]
    vev = [c.v_aux.bump() for _ in range(32)]
    pre2 = tuple(c._tb_pre)

    @on("tensor")
    def _(t, vtr=vtr, vev=vev, gen_end_sv=gen_end_sv, pre2=pre2):
        t.wait_ge(c.s_act.h, gen_end_sv)
        for kt in range(32):
            bank = 4 + (kt % 2)
            if kt < 2:
                gate = pre2[kt % 2]
                if gate is not None:
                    t.wait_ge(c.v_ev.h, gate)
            else:
                t.wait_ge(c.v_aux.h, vev[kt - 2])
            pv = c.ps[bank][:, :].bitcast(BF16)
            t.transpose(pv[:, 0:128], vAT_vT2(c)[:, kt * 128:(kt + 1) * 128],
                        c.ident_bf_sb[:, :]).then_inc(c.t_tr.h, 1)

    c._tb_pre = [None, None]
    c._tb_pre_aux = [vev[30], vev[31]]

    @on("vector")
    def _(v, vtr=vtr):
        for kt in range(32):
            bank = 4 + (kt % 2)
            pv = c.ps[bank][:, :].bitcast(BF16)
            v.wait_ge(c.t_tr.h, vtr[kt])
            v.tensor_single_scalar(out=vAT_vaug(c)[:, kt, 0:64], in_=pv[:, 0:64],
                                   scalar=c.vmask_sb[:, kt:kt + 1], op=ALU.mult)
            v.tensor_single_scalar(out=vAT_vaug(c)[:, kt, 66:130], in_=pv[:, 64:128],
                                   scalar=c.vmask_sb[:, kt:kt + 1], op=ALU.mult)
            v.tensor_copy(out=vAT_vaug(c)[:, kt, 64:65], in_=c.vmask_sb[:, kt:kt + 1])
            v.tensor_copy(out=vAT_vaug(c)[:, kt, 130:131], in_=c.vmask_sb[:, kt:kt + 1]).then_inc(c.v_aux.h, 1)

    vaug_ev = c.v_aux.n

    for h01 in range(2):
        for i in range(4):
            kts = list(range(0, 4 * i + 4)) + list(range(16, 32))
            nk = len(kts)
            pv_gate = c._pv_evac_gate[h01]
            mms, exs, mks, ready = [], [], [], []
            STB = (0, 1, 6, 7)
            LAG = 2
            ex_t = []
            pv_t = []
            pend = []
            for idx, kt in enumerate(kts):
                diag = (4 * i <= kt <= 4 * i + 3)
                mm = c.a_mm.bump()
                mms.append(mm)

                @on("tensor")
                def _(t, kt=kt, idx=idx, i=i, h01=h01, b=STB[idx % 4], diag=diag,
                      vg=(vaug_ev if idx == 0 else None),
                      sg=(gen_end_sv if idx < 4 else None),
                      gg=(c.gb_done if idx < 4 else None),
                      eg=(ex_t[idx - 4] if idx >= 4 else None),
                      pveg=(pv_gate if idx == 0 else None)):
                    if vg is not None:
                        t.wait_ge(c.v_aux.h, vg)
                    if sg is not None:
                        t.wait_ge(c.s_act.h, sg)
                    if gg is not None:
                        t.wait_ge(c.v_ev.h, gg)
                    if pveg is not None:
                        t.wait_ge(c.s_act.h, pveg)
                    if eg is not None:
                        t.wait_ge(c.a_exp.h, eg)
                    t.matmul(c.ps[b][:, :], vAT_kT2(c)[h01 * 64:(h01 + 1) * 64, kt * 128:(kt + 1) * 128],
                             vAT_qT2(c)[h01 * 64:(h01 + 1) * 64, i * 512:(i + 1) * 512],
                             start=True, stop=True).then_inc(c.a_mm.h, 1)

                ex = c.a_exp.bump()
                exs.append(ex)
                ex_t.append(ex)

                @on("scalar")
                def _(s_, idx=idx, b=STB[idx % 4], mm=mm,
                      pvg=(pv_t[idx - 4] if idx >= 4 else (c._pt_gate[idx % 4]))):
                    s_.wait_ge(c.a_mm.h, mm)
                    if pvg is not None:
                        s_.wait_ge(c.a_pv.h, pvg)
                    s_.activation(out=vAT_pt(c)[:, idx % 4, 0:512], in_=c.ps[b][:, :],
                                  func=AF.Exp, bias=c.m20[:, 0:1], scale=1.0).then_inc(c.a_exp.h, 1)

                if diag:
                    mk = c.a_msk.bump()

                    @on("vector")
                    def _(v, kt=kt, i=i, idx=idx, ex=ex):
                        v.wait_ge(c.a_exp.h, ex)
                        v.tensor_tensor(out=vAT_pt(c)[:, idx % 4, 0:512], in0=vAT_pt(c)[:, idx % 4, 0:512],
                                        in1=c.cmask_sb[:, kt - 4 * i, :], op=ALU.mult).then_inc(c.a_msk.h, 1)
                    ready.append(("m", mk))
                else:
                    ready.append(("e", ex))

                pvn = c.a_pv.bump()
                pv_t.append(pvn)

                def mk_pv(kt=kt, idx=idx, h01=h01, b=(idx % 4), nk=nk, r=ready[idx]):
                    @on("tensor")
                    def _(t):
                        t.wait_ge((c.a_msk.h if r[0] == "m" else c.a_exp.h), r[1])
                        t.matmul(c.ps[2 + h01][0:65, :], vAT_vaug(c)[:, kt, h01 * 66:h01 * 66 + 65],
                                 vAT_pt(c)[:, b, 0:512], start=(idx == 0), stop=(idx == nk - 1)).then_inc(c.a_pv.h, 1)
                pend.append(mk_pv)
                if idx >= LAG:
                    pend[idx - LAG]()
            for j in range(max(0, nk - LAG), nk):
                pend[j]()

            pvend = c.a_pv.n
            c._pt_gate = [pvend, pvend, pvend, pvend]

            dslot = c._dslot % 2
            dqs = c.dq if dslot == 0 else c.dq1
            dgate = c._drow_gate[dslot]
            sa1 = c.s_act.bump()
            sa2 = c.s_act.bump()
            dq_t = dqs.bump(16)
            c._drow_gate[dslot] = dq_t
            c._dslot += 1

            @on("scalar")
            def _(s_, hp=hp, h01=h01, i=i, pvend=pvend, dslot=dslot, dgate=dgate, dqs=dqs):
                s_.wait_ge(c.a_pv.h, pvend)
                if dgate is not None:
                    s_.wait_ge(dqs.h, dgate)
                s_.activation(out=vC(c)[h01 * 64:(h01 + 1) * 64, hp, i * 512:(i + 1) * 512],
                              in_=c.ps[2 + h01][0:64, :], func=AF.Copy, bias=0.0, scale=1.0).then_inc(c.s_act.h, 1)
                s_.activation(out=vDROW(c)[0:1, dslot, :], in_=c.ps[2 + h01][64:65, :],
                              func=AF.Copy, bias=0.0, scale=1.0).then_inc(c.s_act.h, 1)

            c._pv_evac_gate[h01] = c.s_act.n

            @on("gpsimd")
            def _(g, hp=hp, h01=h01, i=i, sa2=sa2, dslot=dslot, dqs=dqs):
                g.wait_ge(c.s_act.h, sa2)
                g.dma_start(out=c.D_all[2 * hp + h01:2 * hp + h01 + 1, i * 512:(i + 1) * 512],
                            in_=vDROW(c)[0:1, dslot, :]).then_inc(dqs.h, 16)


def _emit_attn(c):
    on = c.rec.on
    c._wsl_gate = [None, None]
    c._tb_pre = [c._tr_ev_pre[0], c._tr_ev_pre[1]]
    c._tb_pre_aux = [None, None]
    c._gen_ps_gate = [None, None]
    c._pv_evac_gate = [None, None]
    c._pt_gate = [None, None, None, None]
    c._drow_gate = [None, None]
    c._dslot = 0

    for hp in range(8):
        _emit_attn_pair(c, hp)

    dq_all = c.dq.n
    dq1_all = c.dq1.n
    rec_v = c.v_aux.bump()

    @on("vector")
    def _(v, dq_all=dq_all, dq1_all=dq1_all):
        v.wait_ge(c.dq.h, dq_all)
        v.wait_ge(c.dq1.h, dq1_all)
        v.reciprocal(out=c.D_all[0:16, :], in_=c.D_all[0:16, :])
        v.drain()
        v.engine_nop().then_inc(c.v_aux.h, 1)

    attn_exp_all = c.a_exp.n
    attn_sact_all = c.s_act.n
    prev_ev = None
    for ob in range(8):
        ld = c.wsl0.bump(16)

        @on("gpsimd")
        def _(g, ob=ob, prev_ev=prev_ev, gate=(c._wsl_gate[1] if ob == 0 else None)):
            if gate is not None:
                g.wait_ge(c.t_mm.h, gate)
            if prev_ev is not None:
                g.wait_ge(c.v_ev.h, prev_ev)
            g.dma_start(out=vWROW(c)[:, ob % 2, 0, :], in_=c.Wo[ob * 128:(ob + 1) * 128, :]).then_inc(c.wsl0.h, 16)

        trs = [c.t_tr.bump() for _ in range(8)]
        evs = [c.v_ev.bump() for _ in range(8)]
        pre = tuple(c._tb_pre_aux) if ob == 0 else (None, None)

        @on("tensor")
        def _(t, ob=ob, ld=ld, trs=trs, evs=evs, pre=pre, prev_ev=prev_ev):
            t.wait_ge(c.wsl0.h, ld)
            for ab in range(8):
                bank = 4 + (ab % 2)
                if ab < 2:
                    if ob == 0:
                        if pre[ab] is not None:
                            t.wait_ge(c.v_aux.h, pre[ab])
                    elif prev_ev is not None:
                        t.wait_ge(c.v_ev.h, prev_ev)
                else:
                    t.wait_ge(c.v_ev.h, evs[ab - 2])
                pv = c.ps[bank][:, :].bitcast(BF16)
                t.transpose(pv[:, 0:128], vWROW(c)[:, ob % 2, 0, ab * 128:(ab + 1) * 128],
                            c.ident_bf_sb[:, :]).then_inc(c.t_tr.h, 1)

        @on("vector")
        def _(v, ob=ob, trs=trs):
            for ab in range(8):
                bank = 4 + (ab % 2)
                pv = c.ps[bank][:, :].bitcast(BF16)
                v.wait_ge(c.t_tr.h, trs[ab])
                v.tensor_copy(out=vWoT(c)[:, ab, ob * 128:(ob + 1) * 128], in_=pv[:, 0:128]).then_inc(c.v_ev.h, 1)

        prev_ev = evs[-1]

    wo_ev_all = c.v_ev.n

    c._ginv_pre = None
    c._wo_ps_gate = [None, None]
    c._xres_gate = [None, None]
    c._spill_gate = [None, None]
    xnb_all = c.v_xnb.n
    for tt in range(TT):
        p = tt % 2
        gtr = c.t_tr.bump()
        gev = c.v_aux.bump()

        @on("tensor")
        def _(t, tt=tt, gtr=gtr, rec_v=rec_v, first=(tt == 0), gpre=c._ginv_pre, wo_ev_all=wo_ev_all):
            if first:
                t.wait_ge(c.v_aux.h, rec_v)
                t.wait_ge(c.v_ev.h, wo_ev_all)
            if gpre is not None:
                t.wait_ge(c.v_aux.h, gpre)
            t.transpose(c.ps[5][0:128, 0:16], c.D_all[0:16, tt * 128:(tt + 1) * 128],
                        c.ident_f_sb[0:16, 0:16]).then_inc(c.t_tr.h, 1)

        c._ginv_pre = gev

        @on("vector")
        def _(v, gtr=gtr):
            v.wait_ge(c.t_tr.h, gtr)
            v.tensor_copy(out=c.ginv_sb[:, :], in_=c.ps[5][0:128, 0:16]).then_inc(c.v_aux.h, 1)

        lq = c.lq0 if p == 0 else c.lq1
        xld = lq.bump(16)

        @on("sync")
        def _(s_, tt=tt, p=p, lq=lq, gate=c._xres_gate[p], xg=(xnb_all if tt < 2 else None)):
            if xg is not None:
                s_.wait_ge(c.v_xnb.h, xg)
            if gate is not None:
                s_.wait_ge(c.v_aux.h, gate)
            s_.dma_start(out=c.xs[:, p, :], in_=c.x_own[tt * 128:(tt + 1) * 128, :]).then_inc(lq.h, 16)

        for oc in range(2):
            for hd in range(16):
                b = hd % 2
                mm = c.t_mm.bump()

                @on("tensor")
                def _(t, tt=tt, oc=oc, hd=hd, b=b, first=(tt == 0 and oc == 0 and hd < 2),
                      wo_ev_all=wo_ev_all, sact=attn_sact_all, aexp=attn_exp_all,
                      wog=c._wo_ps_gate[b]):
                    if first:
                        t.wait_ge(c.v_ev.h, wo_ev_all)
                        t.wait_ge(c.s_act.h, sact)
                        t.wait_ge(c.a_exp.h, aexp)
                    if wog is not None:
                        t.wait_ge(c.v_cmb.h, wog)
                    t.matmul(c.ps[b][:, :], vC(c)[(hd % 2) * 64:(hd % 2) * 64 + 64, hd // 2, tt * 128:(tt + 1) * 128],
                             vWoT(c)[(hd % 2) * 64:(hd % 2) * 64 + 64, hd // 2, oc * 512:(oc + 1) * 512],
                             start=True, stop=True).then_inc(c.t_mm.h, 1)

                cm = c.v_cmb.bump()
                c._wo_ps_gate[b] = cm

                @on("vector")
                def _(v, tt=tt, oc=oc, hd=hd, b=b, mm=mm, gev=gev):
                    v.wait_ge(c.t_mm.h, mm)
                    if hd == 0:
                        v.wait_ge(c.v_aux.h, gev)
                        v.tensor_single_scalar(out=vD_acc(c)[:, tt, oc * 512:(oc + 1) * 512], in_=c.ps[b][:, :],
                                               scalar=c.ginv_sb[:, 0:1], op=ALU.mult)
                        v.drain()
                        v.engine_nop().then_inc(c.v_cmb.h, 1)
                    else:
                        v.scalar_tensor_tensor(out=vD_acc(c)[:, tt, oc * 512:(oc + 1) * 512], in0=c.ps[b][:, :],
                                               scalar=c.ginv_sb[:, hd:hd + 1],
                                               in1=vD_acc(c)[:, tt, oc * 512:(oc + 1) * 512],
                                               op0=ALU.mult, op1=ALU.add)
                        v.drain()
                        v.engine_nop().then_inc(c.v_cmb.h, 1)

        va = c.v_aux.bump()
        c._xres_gate[p] = va
        spqs = c.spq if p == 0 else c.spq1
        sp = spqs.bump(16)

        @on("vector")
        def _(v, tt=tt, p=p, xld=xld, lq=lq, spg=c._spill_gate[p], spqs=spqs):
            v.wait_ge(lq.h, xld)
            if spg is not None:
                v.wait_ge(spqs.h, spg)
            v.tensor_tensor(out=vD_acc(c)[:, tt, :], in0=vD_acc(c)[:, tt, :], in1=c.xs[:, p, :],
                            op=ALU.add).then_inc(c.v_aux.h, 1)

        c._spill_gate[p] = sp

        @on("gpsimd")
        def _(g, tt=tt, va=va, spqs=spqs):
            g.wait_ge(c.v_aux.h, va)
            g.dma_start(out=c.acc_spill.ap()[tt * 128:(tt + 1) * 128, :], in_=vD_acc(c)[:, tt, :]).then_inc(spqs.h, 16)

    c._spill_all = (c.spq.n, c.spq1.n)
    c._attn_end_mm = c.t_mm.n



def _emit_moe0_prep(c):
    on = c.rec.on
    c._gcast = Sem(c.ctx, c.nc, "gcast")
    casts = [c._gcast.bump(16) for _ in range(NE)]
    fb = c.fbq.bump(16)
    c._fc2b_ld = fb

    @on("gpsimd")
    def _(g, casts=casts, mb2=c.m_b2):
        for e in range(NE):
            if e > 0:
                g.wait_ge(c._gcast.h, casts[e - 1])
            g.dma_start(out=c.fc1bf.ap()[e * HFF:(e + 1) * HFF, :], in_=c.fc1_w[e]).then_inc(c._gcast.h, 16)
        g.dma_start(out=c.fc2b_sb[:, :], in_=mb2).then_inc(c.fbq.h, 16)


def _emit_moe0_combinit(c):
    on = c.rec.on
    comb = vD_comb
    prev_ci = None
    wT_done = c._wT_done
    fc2b_ld = c._fc2b_ld
    for tt in range(TT):
        mm = c.t_mm.bump()

        @on("tensor")
        def _(t, tt=tt, first=(tt == 0), prev_ci=prev_ci, wT_done=wT_done, fc2b_ld=fc2b_ld,
              spill=(c._spill_all if tt == 0 else None)):
            if first:
                t.wait_ge(c.v_ev.h, wT_done)
                t.wait_ge(c.fbq.h, fc2b_ld)
                if spill is not None:
                    t.wait_ge(c.spq.h, spill[0])
                    t.wait_ge(c.spq1.h, spill[1])
            else:
                t.wait_ge(c.v_cmb.h, prev_ci)
            t.matmul(c.ps[4][:, :], c.wT_all[0:8, tt * 128:(tt + 1) * 128], c.fc2b_sb[0:8, 0:512],
                     start=True, stop=True)
            t.matmul(c.ps[5][:, :], c.wT_all[0:8, tt * 128:(tt + 1) * 128], c.fc2b_sb[0:8, 512:1024],
                     start=True, stop=True).then_inc(c.t_mm.h, 1)

        ci = c.v_cmb.bump()
        prev_ci = ci

        @on("vector")
        def _(v, tt=tt, mm=mm):
            v.wait_ge(c.t_mm.h, mm)
            v.tensor_copy(out=comb(c)[:, tt, 0:512], in_=c.ps[4][:, :])
            v.tensor_copy(out=comb(c)[:, tt, 512:1024], in_=c.ps[5][:, :]).then_inc(c.v_cmb.h, 1)

    c._comb_init_done = c.v_cmb.n


def _emit_moe0_quarter(c, e, q, qi):
    on = c.rec.on
    qp = qi % 2
    if c.mcfg["l1"]:
        tw = c.tl0 if qp == 0 else c.tl1
    else:
        tw = c.tw0 if qp == 0 else c.tw1
    gw = c.gw0 if qp == 0 else c.gw1
    bw = c.bw0 if qp == 0 else c.bw1

    slot_free = c._q_slot_free[qp]

    l1 = c.mcfg["l1"]
    if l1:
        tw_t = tw.bump(16 * 8)

    if not l1:
        wev_last = None
        for hch in range(8):
            sp = hch % 2
            tws = c.tw0 if sp == 0 else c.tw1
            tw_t = tws.bump(16)
            stg_gate = c._fstg_gate[sp]

            @on("gpsimd")
            def _(g, e=e, q=q, hch=hch, sp=sp, tws=tws, slot_free=slot_free, gate=c._gcast_gate[e],
                  ag=(c._arena_gate_ev if qi == 0 and hch == 0 else None),
                  agm=(c._attn_end_mm if qi == 0 and hch == 0 else None),
                  stg_gate=stg_gate, first=(hch == 0)):
                if first:
                    g.wait_ge(c._gcast.h, gate)
                    if ag is not None:
                        g.wait_ge(c.v_ev.h, ag)
                    if agm is not None:
                        g.wait_ge(c.t_mm.h, agm)
                    if slot_free is not None:
                        g.wait_ge(c.t_mm.h, slot_free)
                if stg_gate is not None:
                    g.wait_ge(c.v_ev.h, stg_gate)
                base = e * HFF + q * 1024 + hch * 128
                g.dma_start(out=c.xnb[:, sp, :], in_=c.fc1bf.ap()[base:base + 128, :]).then_inc(tws.h, 16)

            trs = [c.t_tr.bump() for _ in range(8)]
            evs = [c.v_ev.bump() for _ in range(8)]
            c._fstg_gate[sp] = evs[-1]
            wev_last = evs[-1]
            pre = tuple(c._mtr_pre)
            c._mtr_pre = [evs[6], evs[7]]

            @on("tensor")
            def _(t, sp=sp, tw_t=tw_t, tws=tws, evs=evs, pre=pre,
                  civ=(getattr(c, "_comb_init_done", None) if (qi == 0 and hch == 0) else None)):
                t.wait_ge(tws.h, tw_t)
                if civ is not None:
                    t.wait_ge(c.v_cmb.h, civ)
                for dblk in range(8):
                    bank = 4 + (dblk % 2)
                    gate = pre[dblk % 2] if dblk < 2 else evs[dblk - 2]
                    if gate is not None:
                        t.wait_ge(c.v_ev.h, gate)
                    pv = c.ps[bank][:, :].bitcast(BF16)
                    t.transpose(pv[:, 0:128], c.xnb[:, sp, dblk * 128:(dblk + 1) * 128],
                                c.ident_bf_sb[:, :]).then_inc(c.t_tr.h, 1)

            @on("vector")
            def _(v, sp=sp, hch=hch, trs=trs):
                for dblk in range(8):
                    bank = 4 + (dblk % 2)
                    pv = c.ps[bank][:, :].bitcast(BF16)
                    v.wait_ge(c.t_tr.h, trs[dblk])
                    v.tensor_copy(out=vW1(c)[:, qp, dblk, hch * 128:(hch + 1) * 128],
                                  in_=pv[:, 0:128]).then_inc(c.v_ev.h, 1)

        c._w1tr_done = wev_last
        tw_t = None
    else:
        @on("gpsimd")
        def _(g, e=e, q=q, slot_free=slot_free,
              ag=(c._arena_gate_ev if qi == 0 else None),
              agm=(c._attn_end_mm if qi == 0 else None), mw1=c.m_w1):
            if ag is not None:
                g.wait_ge(c.v_ev.h, ag)
            if agm is not None:
                g.wait_ge(c.t_mm.h, agm)
            if slot_free is not None:
                g.wait_ge(c.t_mm.h, slot_free)
            for dblk in range(8):
                g.dma_start(out=vW1(c)[:, qp, dblk, :],
                            in_=mw1[e, dblk * 128:(dblk + 1) * 128, q * 1024:(q + 1) * 1024]).then_inc(tw.h, 16)

    gw_t = gw.bump(16 * 8)
    bw_t = bw.bump(16)

    @on("gpsimd")
    def _(g, e=e, q=q, slot_free=slot_free, ag=(c._arena_gate_ev if qi == 0 else None),
          agm=(c._attn_end_mm if qi == 0 else None), mw2=c.m_w2, mb1=c.m_b1):
        if ag is not None:
            g.wait_ge(c.v_ev.h, ag)
        if agm is not None:
            g.wait_ge(c.t_mm.h, agm)
        if slot_free is not None:
            g.wait_ge(c.t_mm.h, slot_free)
        for hb in range(8):
            g.dma_start(out=vW2(c)[:, qp, hb, :],
                        in_=mw2[e, q * 1024 + hb * 128:q * 1024 + (hb + 1) * 128, :]).then_inc(gw.h, 16)
        g.dma_start(out=c.b1row[0:8, qp, :],
                    in_=mb1[e, q * 1024:(q + 1) * 1024].rearrange("(c p) -> c p", p=128)).then_inc(bw.h, 16)

    btr = c.t_tr.bump()
    bev = c.v_ev.bump()

    @on("tensor")
    def _(t, bw_t=bw_t, gate=c._b1_ev_prev, ci_done=(c._comb_init_done if c._b1_ev_prev is None else None),
          mpre=c._mtr_pre[0]):
        t.wait_ge(bw.h, bw_t)
        if ci_done is not None:
            t.wait_ge(c.v_cmb.h, ci_done)
        if gate is not None:
            t.wait_ge(c.v_ev.h, gate)
        if mpre is not None:
            t.wait_ge(c.v_ev.h, mpre)
        t.transpose(c.ps[4][0:128, 0:8], c.b1row[0:8, qp, :], c.ident_f_sb[0:8, 0:8]).then_inc(c.t_tr.h, 1)

    c._b1_ev_prev = bev
    c._mtr_pre = [bev, c._mtr_pre[1]]

    @on("vector")
    def _(v, btr=btr):
        v.wait_ge(c.t_tr.h, btr)
        v.tensor_copy(out=c.b1col[:, qp, :], in_=c.ps[4][0:128, 0:8]).then_inc(c.v_ev.h, 1)

    hT = vD_hTq
    silu_start = c.s_silu.n
    mm_groups = []
    for gidx in range(32):
        ht, tc = gidx // 4, gidx % 4
        mm = c.t_mm.bump()
        mm_groups.append(mm)

        @on("tensor")
        def _(t, ht=ht, tc=tc, gidx=gidx, tw_t=tw_t, bev=bev,
              first=(gidx == 0), silu_start=silu_start, fg=(c._moe_first_gate if qi == 0 else None),
              wtr=(c._w1tr_done if (not l1 and gidx == 0) else None)):
            bank = gidx % 2
            if first:
                if fg is not None:
                    t.wait_ge(c.v_r.h, fg)
                if tw_t is not None:
                    t.wait_ge(tw.h, tw_t)
                if wtr is not None:
                    t.wait_ge(c.v_ev.h, wtr)
                t.wait_ge(c.v_ev.h, bev)
            if gidx >= 2:
                t.wait_ge(c.s_silu.h, silu_start + gidx - 1)
            for k in range(8):
                mmh = t.matmul(c.ps[bank][:, :], vW1(c)[:, qp, k, ht * 128:(ht + 1) * 128],
                               vA(c)[:, k, tc * 512:(tc + 1) * 512], start=(k == 0), stop=(k == 7))
            mmh.then_inc(c.t_mm.h, 1)

        sl = c.s_silu.bump()

        @on("scalar")
        def _(s_, ht=ht, tc=tc, gidx=gidx, mm=mm, act=c.mcfg["act"]):
            bank = gidx % 2
            s_.wait_ge(c.t_mm.h, mm)
            s_.activation(out=hT(c)[:, ht, tc * 512:(tc + 1) * 512], in_=c.ps[bank][:, :],
                          func=act, bias=c.b1col[:, qp, ht:ht + 1], scale=1.0).then_inc(c.s_silu.h, 1)

    silu_all = c.s_silu.n

    for tt in range(TT):
        pair = (2, 3) if tt % 2 == 0 else (6, 7)
        mm = c.t_mm.bump()

        @on("tensor")
        def _(t, tt=tt, pair=pair, first=(tt == 0), silu_all=silu_all, gw_t=gw_t,
              ygate=c._y_gate[tt % 2], wtd=(c._wT_done if qi == 0 and tt <= 1 else None)):
            if first:
                t.wait_ge(c.s_silu.h, silu_all)
                t.wait_ge(gw.h, gw_t)
            if wtd is not None:
                t.wait_ge(c.v_ev.h, wtd)
            if ygate is not None:
                t.wait_ge(c.v_cmb.h, ygate)
            for half in range(2):
                for k in range(8):
                    mmh = t.matmul(c.ps[pair[half]][:, :], hT(c)[:, k, tt * 128:(tt + 1) * 128],
                                   vW2(c)[:, qp, k, half * 512:(half + 1) * 512],
                                   start=(k == 0), stop=(k == 7))
            mmh.then_inc(c.t_mm.h, 1)

        cm = c.v_cmb.bump()
        c._y_gate[tt % 2] = cm

        @on("vector")
        def _(v, tt=tt, pair=pair, mm=mm, e=e):
            v.wait_ge(c.t_mm.h, mm)
            v.scalar_tensor_tensor(out=vD_comb(c)[:, tt, 0:512], in0=c.ps[pair[0]][:, :],
                                   scalar=c.w8_all[:, tt, e:e + 1], in1=vD_comb(c)[:, tt, 0:512],
                                   op0=ALU.mult, op1=ALU.add)
            v.scalar_tensor_tensor(out=vD_comb(c)[:, tt, 512:1024], in0=c.ps[pair[1]][:, :],
                                   scalar=c.w8_all[:, tt, e:e + 1], in1=vD_comb(c)[:, tt, 512:1024],
                                   op0=ALU.mult, op1=ALU.add).then_inc(c.v_cmb.h, 1)

    c._q_slot_free[qp] = c.t_mm.n


def _emit_moe0_epilogue(c):
    on = c.rec.on
    comb_all = c.v_cmb.n
    for tt in range(TT):
        p = tt % 2
        trs = [c.t_tr.bump() for _ in range(8)]
        evs = [c.v_ev.bump() for _ in range(8)]
        pre = (c._ep_tr_pre[0], c._ep_tr_pre[1])
        c._ep_tr_pre = [evs[6], evs[7]]

        @on("tensor")
        def _(t, tt=tt, p=p, trs=trs, evs=evs, pre=pre, first=(tt == 0), comb_all=comb_all,
              xgate=c._xn0n_gate[p]):
            if first:
                t.wait_ge(c.v_cmb.h, comb_all)
            if xgate is not None:
                t.wait_ge(c.v_aux.h, xgate)
            for j in range(8):
                bank = 4 + (j % 2)
                gate = pre[j % 2] if j < 2 else evs[j - 2]
                if gate is not None:
                    t.wait_ge(c.v_ev.h, gate)
                pv = c.ps[bank][:, :].bitcast(BF16)
                t.transpose(pv[:, 0:128], vA(c)[:, j, tt * 128:(tt + 1) * 128],
                            c.ident_bf_sb[:, :]).then_inc(c.t_tr.h, 1)

        @on("vector")
        def _(v, tt=tt, p=p, trs=trs):
            for j in range(8):
                bank = 4 + (j % 2)
                pv = c.ps[bank][:, :].bitcast(BF16)
                v.wait_ge(c.t_tr.h, trs[j])
                v.tensor_copy(out=vEP_xn0n(c)[:, p, j * 128:(j + 1) * 128], in_=pv[:, 0:128]).then_inc(c.v_ev.h, 1)

        aq = c.aq0 if p == 0 else c.aq1
        aq_t = aq.bump(16)

        @on("sync")
        def _(s_, tt=tt, p=p, aq=aq, agate=c._acc_gate[p], ca=(comb_all if tt < 2 else None),
              spa=(c._spill_all if tt < 2 else None)):
            if c._acc_pre is not None:
                s_.wait_ge(c.g_aux.h, c._acc_pre)
                c._acc_pre = None
            if ca is not None:
                s_.wait_ge(c.v_cmb.h, ca)
            if spa is not None:
                s_.wait_ge(c.spq.h, spa[0])
                s_.wait_ge(c.spq1.h, spa[1])
            if agate is not None:
                s_.wait_ge(c.v_aux.h, agate)
            s_.dma_start(out=vEP_acc(c)[:, p, :], in_=c.acc_spill.ap()[tt * 128:(tt + 1) * 128, :]).then_inc(aq.h, 16)

        lastev = evs[-1]
        x1sg = c._x1_store_gate[p]
        mvt = c.v_mv.bump()

        @on("vector")
        def _(v, tt=tt, p=p, lastev=lastev, xgate=x1sg):
            v.wait_ge(c.v_ev.h, lastev)
            if xgate is not None:
                v.wait_ge((c.x1s0 if p == 0 else c.x1s1).h, xgate)
            v.tensor_tensor(out=vEP_r2(c)[:, p, :], in0=vEP_xn0n(c)[:, p, :], in1=vD_comb(c)[:, tt, :], op=ALU.add)
            v.drain()
            v.bn_stats(c.stats[:, p, 0:6], vEP_r2(c)[:, p, 0:512])
            v.bn_stats(c.stats[:, p, 6:12], vEP_r2(c)[:, p, 512:1024])
            v.drain()
            v.bn_aggr(c.mv[:, p, :], c.stats[:, p, :]).then_inc(c.v_mv.h, 1)

        rs = c.s_rstd.bump()

        @on("scalar")
        def _(s_, p=p, mvt=mvt):
            s_.wait_ge(c.v_mv.h, mvt)
            s_.activation(out=c.rstd[:, p, :], in_=c.mv[:, p, 1:2], func=AF.Ln, bias=c.eps_t[:, 0:1], scale=1.0)
            s_.drain()
            s_.activation(out=c.rstd[:, p, :], in_=c.rstd[:, p, :], func=AF.Exp, bias=0.0, scale=-0.5).then_inc(c.s_rstd.h, 1)

        xa = c.v_aux.bump()
        c._xn0n_gate[p] = xa
        c._acc_gate[p] = xa
        x1s = c.x1s0 if p == 0 else c.x1s1
        st_t = x1s.bump(16)

        @on("vector")
        def _(v, tt=tt, p=p, rs=rs, aq_t=aq_t, aq=aq):
            v.wait_ge(c.s_rstd.h, rs)
            v.tensor_scalar(out=vEP_x1t(c)[:, p, :], in0=vEP_r2(c)[:, p, :], scalar1=c.mv[:, p, 0:1],
                            scalar2=c.rstd[:, p, 0:1], op0=ALU.subtract, op1=ALU.mult)
            v.drain()
            v.tensor_tensor(out=vEP_r2(c)[:, p, :], in0=vEP_x1t(c)[:, p, :], in1=c.gam_t[:, :], op=ALU.mult)
            v.drain()
            v.tensor_tensor(out=vEP_x1t(c)[:, p, :], in0=vEP_r2(c)[:, p, :], in1=c.bet_t[:, :], op=ALU.add)
            v.drain()
            v.wait_ge(aq.h, aq_t)
            v.tensor_tensor(out=vEP_r2(c)[:, p, :], in0=vEP_x1t(c)[:, p, :], in1=vEP_acc(c)[:, p, :], op=ALU.add).then_inc(c.v_aux.h, 1)

        c._x1_store_gate[p] = st_t

        @on("gpsimd")
        def _(g, tt=tt, p=p, xa=xa, x1s=x1s):
            g.wait_ge(c.v_aux.h, xa)
            g.dma_start(out=c.x1_out[tt * 128:(tt + 1) * 128, :], in_=vEP_r2(c)[:, p, :]).then_inc(x1s.h, 16)


def _emit_moe0(c, no_attn=False):
    on = c.rec.on
    if no_attn:
        ga = c.g_aux.bump(16)
        c._acc_pre = ga

        @on("gpsimd")
        def _(g):
            g.dma_start(out=c.acc_spill.ap(), in_=c.x_own).then_inc(c.g_aux.h, 16)
    else:
        c._acc_pre = None

    c.mcfg = {"l1": False, "act": AF.Silu}
    c.m_w1 = None
    c.m_w2 = c.fc2_w
    c.m_b1 = c.fc1_b
    c.m_b2 = c.fc2_b
    _emit_moe0_prep(c)
    c._fstg_gate = [None, None]
    c._mtr_pre = [None, None]
    c._mtr_init = c.v_aux.n
    c._arena_gate_ev = c._p0_ev_done
    c._gcast_gate = [16 * (e + 1) for e in range(NE)]
    c._moe_first_gate = c._w8_done
    c._q_slot_free = [None, None]
    c._b1_ev_prev = None
    c._y_gate = [None, None]
    _emit_moe0_combinit(c)
    for e in range(NE):
        for q in range(4):
            _emit_moe0_quarter(c, e, q, e * 4 + q)
    c._ep_tr_pre = [None, None]
    c._xn0n_gate = [None, None]
    c._acc_gate = [None, None]
    c._x1_store_gate = [None, None]
    _emit_moe0_epilogue(c)





def vL1_hid(c, par=0):
    o = 24576 * par
    return c.arD[:, o:o + 8192].bitcast(F32)


def vL1_gate(c, par=0):
    o = 24576 * par
    return c.arD[:, o + 8192:o + 16384].bitcast(F32)


def vL1_c(c, par=0):
    o = 24576 * par
    return c.arD[:, o + 16384:o + 24576].bitcast(F32)


def vL1_scan(c, par=0):
    o = 24576 * par
    return c.arD[:, o + 16384:o + 24576].bitcast(F32)


def vACC2ST(c):
    return c.arD[:, 32768:40960].bitcast(F32).rearrange("p (s d) -> p s d", s=2)


def vSW_SL(c):
    return c.arD[:, 49152:57344].bitcast(BF16).rearrange("p (q w k a) -> p q w k a", q=2, w=2, k=8)


def vSW_ROW(c):
    return c.arD[:, 57344:65536].bitcast(BF16).rearrange("p (q w a) -> p q w a", q=2, w=2)


def vL1_wosl(c):
    return c.arD[:, 0:32768].bitcast(BF16).rearrange("p (cb d) -> p cb d", cb=16)


def vL1_car8(c):
    return c.arD[0:8, 32768:40960].bitcast(F32)


def vL1_carrow(c):
    return c.arD[0:1, 0:8192].bitcast(F32)


def vL1_acc2st(c):
    return c.arD[:, 49152:57344].bitcast(F32).rearrange("p (s d) -> p s d", s=2)


def hT_sc(c, sc):
    if sc < 8:
        return vB(c)[:, sc, :]
    return vC(c)[:, sc - 8, :]


def _emit_scan_pass(c, sc, passB):
    on = c.rec.on
    par = sc % 2
    wsl = c.wsl0 if par == 0 else c.wsl1
    wld = wsl.bump(32)
    slot_gate = c._wsl_gate[par]

    @on("gpsimd")
    def _(g, sc=sc, par=par, wsl=wsl, slot_gate=slot_gate, passB=passB,
          wg=(getattr(c, "_scan_wrow_gate", None) if (sc < 2 and not passB) else None)):
        if wg is not None:
            g.wait_ge(wg[0], wg[1])
        if slot_gate is not None:
            g.wait_ge(c.t_mm.h, slot_gate)
        g.dma_start(out=vSW_ROW(c)[:, par, 0, :], in_=c.W_hg[sc * 128:(sc + 1) * 128, :]).then_inc(wsl.h, 16)
        g.dma_start(out=vSW_ROW(c)[:, par, 1, :], in_=c.W_hg[DI + sc * 128:DI + (sc + 1) * 128, :]).then_inc(wsl.h, 16)

    trs = [c.t_tr.bump() for _ in range(16)]
    evs = [c.v_ev.bump() for _ in range(16)]
    pre = tuple(c._tb_pre)
    c._tb_pre = [evs[14], evs[15]]

    @on("tensor")
    def _(t, par=par, wsl=wsl, wld=wld, evs=evs, pre=pre):
        t.wait_ge(wsl.h, wld)
        n = 0
        for w in range(2):
            for dblk in range(8):
                bank = 4 + (n % 2)
                gate = pre[n % 2] if n < 2 else evs[n - 2]
                if gate is not None:
                    t.wait_ge(c.v_ev.h, gate)
                pv = c.ps[bank][:, :].bitcast(BF16)
                t.transpose(pv[:, 0:128], vSW_ROW(c)[:, par, w, dblk * 128:(dblk + 1) * 128],
                            c.ident_bf_sb[:, :]).then_inc(c.t_tr.h, 1)
                n += 1

    @on("vector")
    def _(v, par=par, trs=trs):
        n = 0
        for w in range(2):
            for dblk in range(8):
                bank = 4 + (n % 2)
                pv = c.ps[bank][:, :].bitcast(BF16)
                v.wait_ge(c.t_tr.h, trs[n])
                v.tensor_copy(out=vSW_SL(c)[:, par, w, dblk, :], in_=pv[:, 0:128]).then_inc(c.v_ev.h, 1)
                n += 1

    wsl_ev = evs[-1]

    for w in range(2):
        for ch in range(4):
            n = w * 4 + ch
            mm = c.t_mm.bump()
            sv = c.s_act.bump()

            @on("tensor")
            def _(t, w=w, ch=ch, n=n, par=par, mm=mm, wsl_ev=wsl_ev,
                  pg=c._gen_ps_gate[n % 2], bank=(n % 2)):
                if n == 0:
                    t.wait_ge(c.v_ev.h, wsl_ev)
                if pg is not None:
                    t.wait_ge(c.s_act.h, pg)
                for k in range(8):
                    mmh = t.matmul(c.ps[bank][:, :], vSW_SL(c)[:, par, w, k, :],
                                   vA(c)[:, k, ch * 512:(ch + 1) * 512], start=(k == 0), stop=(k == 7))
                mmh.then_inc(c.t_mm.h, 1)

            @on("scalar")
            def _(s_, w=w, ch=ch, par=par, mm=mm, bank=(n % 2),
                  scg=(c._scan_consumed2[par] if n == 0 else None),
                  crg=(getattr(c, "_carrow_ev", None) if (n == 0 and passB and sc == 0) else None)):
                s_.wait_ge(c.t_mm.h, mm)
                if scg is not None:
                    s_.wait_ge(c.v_aux.h, scg)
                if crg is not None:
                    s_.wait_ge(c.v_ev.h, crg)
                dst = vL1_hid if w == 0 else vL1_gate
                s_.activation(out=dst(c, par)[:, ch * 512:(ch + 1) * 512], in_=c.ps[bank][:, :],
                              func=AF.Copy, bias=0.0, scale=1.0).then_inc(c.s_act.h, 1)

            c._gen_ps_gate[n % 2] = sv

    hg_done = c.s_act.n

    s3 = c.s_act.bump(2)
    v5 = c.v_aux.bump()

    @on("scalar")
    def _(s_, par=par, hg_done=hg_done):
        s_.wait_ge(c.s_act.h, hg_done)
        s_.activation(out=vL1_c(c, par)[:, :], in_=vL1_hid(c, par)[:, :], func=AF.Sigmoid, bias=0.0, scale=1.0)
        s_.drain()
        s_.activation(out=vL1_hid(c, par)[:, :], in_=vL1_hid(c, par)[:, :], func=AF.Identity,
                      bias=c.half_t[:, 0:1], scale=1.0).then_inc(c.s_act.h, 1)
        s_.drain()
        s_.activation(out=vL1_gate(c, par)[:, :], in_=vL1_gate(c, par)[:, :], func=AF.Sigmoid,
                      bias=0.0, scale=1.0).then_inc(c.s_act.h, 1)

    @on("vector")
    def _(v, sc=sc, par=par, s3=s3, passB=passB,
          cd=(c._carry_done_ev if (passB and sc == 0) else None)):
        v.wait_ge(c.s_act.h, s3 - 1)
        if cd is not None:
            v.wait_ge(c.v_ev.h, cd)
        v.tensor_tensor(out=vL1_hid(c, par)[:, :], in0=vL1_hid(c, par)[:, :], in1=vL1_c(c, par)[:, :], op=ALU.max)
        v.drain()
        v.wait_ge(c.s_act.h, s3)
        v.tensor_tensor(out=vL1_hid(c, par)[:, :], in0=vL1_gate(c, par)[:, :], in1=vL1_hid(c, par)[:, :], op=ALU.mult)
        v.tensor_scalar(out=vL1_c(c, par)[:, :], in0=vL1_gate(c, par)[:, :], scalar1=-1.0, scalar2=1.0,
                        op0=ALU.mult, op1=ALU.add)
        v.drain()
        init = c.carry_sb[:, sc:sc + 1] if passB else 0.0
        v.tensor_tensor_scan(out=vL1_scan(c, par)[:, :], data0=vL1_c(c, par)[:, :], data1=vL1_hid(c, par)[:, :],
                             initial=init, op0=ALU.mult, op1=ALU.add)
        v.drain()
        v.tensor_copy(out=hT_sc(c, sc), in_=vL1_scan(c, par)[:, :])
        if not passB:
            v.tensor_copy(out=c.fin_sb[:, sc:sc + 1], in_=vL1_scan(c, par)[:, 2047:2048])
        v.drain()
        v.engine_nop().then_inc(c.v_aux.h, 1)

    c._scan_consumed2[par] = v5
    c._scan_consumed = v5
    c._wsl_gate[par] = c.t_mm.n


def _emit_l1_scan(c):
    on = c.rec.on
    c._scan_consumed = None
    if not hasattr(c, "_scan_consumed2") or c._scan_consumed2 is None:
        c._scan_consumed2 = [getattr(c, "_scan_init_gate", None), getattr(c, "_scan_init_gate", None)]
    c._gen_ps_gate = [c._gen_ps_gate[0], c._gen_ps_gate[1]] if hasattr(c, "_gen_ps_gate") else [None, None]

    for sc in range(16):
        _emit_scan_pass(c, sc, passB=False)

    last = c._scan_consumed
    ev_all = c.v_ev.n
    ftr = c.t_tr.bump()
    fev = c.v_ev.bump()

    @on("tensor")
    def _(t, last=last, ev_all=ev_all):
        t.wait_ge(c.v_aux.h, last)
        t.wait_ge(c.v_ev.h, ev_all)
        t.transpose(c.ps[4][0:16, 0:128], c.fin_sb[:, :], c.ident_f_sb[:, :]).then_inc(c.t_tr.h, 1)

    @on("vector")
    def _(v, ftr=ftr):
        v.wait_ge(c.t_tr.h, ftr)
        v.tensor_copy(out=c.D_all[0:16, 0:128], in_=c.ps[4][0:16, 0:128]).then_inc(c.v_ev.h, 1)

    g1 = c.agq.bump(16)
    cc1 = c.cc.bump(16 * NCORE if NO_CC else 1)
    g2 = c.agq.bump(16)

    @on("gpsimd")
    def _(g, fev=fev, g1=g1, cc1=cc1):
        g.wait_ge(c.v_ev.h, fev)
        g.dma_start(out=c.ag_car_in.ap(), in_=c.D_all[0:16, 0:128]).then_inc(c.agq.h, 16)
        g.wait_ge(c.agq.h, g1)
        if NO_CC:
            for r in range(NCORE):
                g.dma_start(out=c.ag_car_out.ap()[r * 16:(r + 1) * 16, :], in_=c.ag_car_in.ap()).then_inc(c.cc.h, 16)
        else:
            g.collective_compute(
                "AllGather", ALU.bypass,
                replica_groups=[list(range(NCORE))],
                ins=[c.ag_car_in.ap().opt()],
                outs=[c.ag_car_out.ap().opt()],
            ).then_inc(c.cc.h, 1)
        g.wait_ge(c.cc.h, cc1)
        g.dma_start(out=vL1_car8(c)[0:8, :], in_=c.ag_car_out.ap().rearrange("(r i) p -> r (i p)", r=8)).then_inc(c.agq.h, 16)

    sel_evs = []
    prev_se = None
    for j in range(4):
        smm = c.t_mm.bump()
        se = c.v_ev.bump()
        sel_evs.append(se)

        @on("tensor")
        def _(t, j=j, g2=g2, fev=fev, prev_se=prev_se):
            if j == 0:
                t.wait_ge(c.agq.h, g2)
                t.wait_ge(c.v_ev.h, fev)
            else:
                t.wait_ge(c.v_ev.h, prev_se)
            t.matmul(c.ps[4][0:1, 0:512], c.selcar_sb[0:8, :], vL1_car8(c)[0:8, j * 512:(j + 1) * 512],
                     start=True, stop=True).then_inc(c.t_mm.h, 1)

        smm_t = c.t_mm.n

        @on("vector")
        def _(v, j=j, smm_t=smm_t):
            v.wait_ge(c.t_mm.h, smm_t)
            v.tensor_copy(out=vL1_carrow(c)[0:1, j * 512:(j + 1) * 512], in_=c.ps[4][0:1, 0:512]).then_inc(c.v_ev.h, 1)

        prev_se = se

    ctr = [c.t_tr.bump() for _ in range(16)]
    cev = [c.v_ev.bump() for _ in range(16)]

    @on("tensor")
    def _(t, ctr=ctr, cev=cev, prev_se=prev_se):
        t.wait_ge(c.v_ev.h, prev_se)
        for sc in range(16):
            if sc >= 2:
                t.wait_ge(c.v_ev.h, cev[sc - 2])
            t.transpose(c.ps[4 + sc % 2][0:128, 0:1], vL1_carrow(c)[0:1, sc * 128:(sc + 1) * 128],
                        c.ident_f_sb[0:1, 0:1]).then_inc(c.t_tr.h, 1)

    @on("vector")
    def _(v, ctr=ctr):
        for sc in range(16):
            v.wait_ge(c.t_tr.h, ctr[sc])
            v.tensor_copy(out=c.carry_sb[:, sc:sc + 1], in_=c.ps[4 + sc % 2][0:128, 0:1]).then_inc(c.v_ev.h, 1)

    c._carry_done_ev = cev[-1]
    c._carrow_ev = cev[-1]
    c._carry_done = None
    c._tb_pre = [cev[14], cev[15]]

    for sc in range(16):
        _emit_scan_pass(c, sc, passB=True)


def _emit_l1_outproj(c):
    on = c.rec.on
    scan_done = c._scan_consumed
    c._wosl_tr = [None, None]
    for cb in range(16):
        par = cb % 2
        wsl = c.wsl0 if par == 0 else c.wsl1
        wld = wsl.bump(16 * 8)
        slot_gate = c._wsl_gate[par]
        tr_gate = c._wosl_tr[par]

        @on("gpsimd")
        def _(g, cb=cb, par=par, wsl=wsl, slot_gate=slot_gate, sd=(scan_done if cb < 2 else None),
              tr_gate=tr_gate):
            if sd is not None:
                g.wait_ge(c.v_aux.h, sd)
            if slot_gate is not None:
                g.wait_ge(c.t_mm.h, slot_gate)
            if tr_gate is not None:
                g.wait_ge(c.t_tr.h, tr_gate)
            for dblk in range(8):
                g.dma_start(out=vWROW(c)[:, par, 0, dblk * 128:(dblk + 1) * 128],
                            in_=c.W_out[dblk * 128:(dblk + 1) * 128, cb * 128:(cb + 1) * 128]).then_inc(wsl.h, 16)

        trs = [c.t_tr.bump() for _ in range(8)]
        evs = [c.v_ev.bump() for _ in range(8)]
        pre = tuple(c._tb_pre)
        c._tb_pre = [evs[6], evs[7]]

        @on("tensor")
        def _(t, cb=cb, par=par, wsl=wsl, wld=wld, evs=evs, pre=pre, sd=(scan_done if cb == 0 else None)):
            t.wait_ge(wsl.h, wld)
            if sd is not None:
                t.wait_ge(c.v_aux.h, sd)
            for dblk in range(8):
                bank = 4 + (dblk % 2)
                gate = pre[dblk % 2] if dblk < 2 else evs[dblk - 2]
                if gate is not None:
                    t.wait_ge(c.v_ev.h, gate)
                pv = c.ps[bank][:, :].bitcast(BF16)
                t.transpose(pv[:, 0:128], vWROW(c)[:, par, 0, dblk * 128:(dblk + 1) * 128],
                            c.ident_bf_sb[:, :]).then_inc(c.t_tr.h, 1)

        @on("vector")
        def _(v, cb=cb, trs=trs, sd=(scan_done if cb == 0 else None)):
            if sd is not None:
                v.wait_ge(c.v_aux.h, sd)
            for dblk in range(8):
                bank = 4 + (dblk % 2)
                pv = c.ps[bank][:, :].bitcast(BF16)
                v.wait_ge(c.t_tr.h, trs[dblk])
                v.tensor_copy(out=vL1_wosl(c)[:, cb, dblk * 128:(dblk + 1) * 128], in_=pv[:, 0:128]).then_inc(c.v_ev.h, 1)

        c._wsl_gate[par] = c.t_mm.n
        c._wosl_tr[par] = trs[-1]

    wosl_ev = c.v_ev.n

    c._op_ps_gate = [None, None]
    c._x1ld_gate = [None, None]
    c._a2spill_gate = [None, None]
    xnb_all = c.v_xnb.n
    for tt in range(TT):
        p = tt % 2
        lq = c.lq0 if p == 0 else c.lq1
        xld = lq.bump(16)

        @on("sync")
        def _(s_, tt=tt, p=p, lq=lq, gate=c._x1ld_gate[p], xg=(xnb_all if tt < 2 else None)):
            if xg is not None:
                s_.wait_ge(c.v_xnb.h, xg)
            if gate is not None:
                s_.wait_ge(c.v_aux.h, gate)
            s_.dma_start(out=c.xs[:, p, :], in_=c.x1_src[tt * 128:(tt + 1) * 128, :]).then_inc(lq.h, 16)

        va_parts = []
        for half in range(2):
            b = half
            mm = c.t_mm.bump()

            @on("tensor")
            def _(t, tt=tt, half=half, b=b, first=(tt == 0 and half == 0), wosl_ev=wosl_ev,
                  og=c._op_ps_gate[b]):
                if first:
                    t.wait_ge(c.v_ev.h, wosl_ev)
                if og is not None:
                    t.wait_ge(c.v_aux.h, og)
                for cb in range(16):
                    mmh = t.matmul(c.ps[b][:, :], hT_sc(c, cb)[:, tt * 128:(tt + 1) * 128],
                                   vL1_wosl(c)[:, cb, half * 512:(half + 1) * 512],
                                   start=(cb == 0), stop=(cb == 15))
                mmh.then_inc(c.t_mm.h, 1)

            va = c.v_aux.bump()
            va_parts.append(va)
            c._op_ps_gate[b] = va

            @on("vector")
            def _(v, tt=tt, p=p, half=half, b=b, mm=mm, xld=xld, lq=lq, spg=c._a2spill_gate[p]):
                v.wait_ge(c.t_mm.h, mm)
                v.wait_ge(lq.h, xld)
                if spg is not None and half == 0:
                    v.wait_ge((c.spq if p == 0 else c.spq1).h, spg)
                v.tensor_tensor(out=vL1_acc2st(c)[:, p, half * 512:(half + 1) * 512], in0=c.ps[b][:, :],
                                in1=c.xs[:, p, half * 512:(half + 1) * 512], op=ALU.add).then_inc(c.v_aux.h, 1)

        spqs = c.spq if p == 0 else c.spq1
        sp = spqs.bump(16)
        c._a2spill_gate[p] = sp
        va2 = va_parts[-1]
        c._x1ld_gate[p] = va2

        @on("gpsimd")
        def _(g, tt=tt, p=p, va2=va2, spqs=spqs):
            g.wait_ge(c.v_aux.h, va2)
            g.dma_start(out=c.acc2_spill.ap()[tt * 128:(tt + 1) * 128, :], in_=vL1_acc2st(c)[:, p, :]).then_inc(spqs.h, 16)

    c._spill_all = (c.spq.n, c.spq1.n)
    c._attn_end_mm = c.t_mm.n


def _emit_l1_w8(c):
    on = c.rec.on
    kw_prev = None
    ev_prev = getattr(c, "_l1w8_ev_init", None)
    w8gate_cmb = getattr(c, "_l1w8_cmb_gate", None)
    last_r = c._rt_prev
    for tt in range(TT):
        kw = c.v_r.bump()

        @on("vector")
        def _(v, tt=tt, first=(tt == 0), last_r=last_r, wg8=(w8gate_cmb if tt == 0 else None)):
            if first:
                v.wait_ge(c.v_r.h, last_r)
            if wg8 is not None:
                v.wait_ge(c.v_cmb.h, wg8)
            v.tensor_single_scalar(out=c.scr16[:, 0:8], in_=c.mask16f[:, tt, 8:16], scalar=c.g_all[:, tt, 1:2], op=ALU.mult)
            v.drain()
            v.scalar_tensor_tensor(out=c.w8_all[:, tt, :], in0=c.mask16f[:, tt, 0:8], scalar=c.g_all[:, tt, 0:1],
                                   in1=c.scr16[:, 0:8], op0=ALU.mult, op1=ALU.add).then_inc(c.v_r.h, 1)

        kw_prev = kw
        trw = c.t_tr.bump()

        @on("tensor")
        def _(t, tt=tt, kw=kw, ev_prev=ev_prev):
            t.wait_ge(c.v_r.h, kw)
            if ev_prev is not None:
                t.wait_ge(c.v_ev.h, ev_prev)
            t.transpose(c.ps[2][0:8, 0:128], c.w8_all[:, tt, :], c.ident_f_sb[:, :]).then_inc(c.t_tr.h, 1)

        evw = c.v_ev.bump()
        ev_prev = evw

        @on("vector")
        def _(v, tt=tt, trw=trw):
            v.wait_ge(c.t_tr.h, trw)
            v.tensor_copy(out=c.wT_all[0:8, tt * 128:(tt + 1) * 128], in_=c.ps[2][0:8, 0:128]).then_inc(c.v_ev.h, 1)

    c._w8_done = kw_prev
    c._wT_done = ev_prev
    c._moe_first_gate = kw_prev


def _emit_l1_final(c):
    on = c.rec.on
    comb_all = c.v_cmb.n
    g_acc2 = [None, None]
    x2s = [c.x1s0, c.x1s1]
    x2gate = [None, None]
    for tt in range(TT):
        p = tt % 2
        aq = c.aq0 if p == 0 else c.aq1
        aq_t = aq.bump(16)

        @on("sync")
        def _(s_, tt=tt, p=p, aq=aq, agate=g_acc2[p], ca=(comb_all if tt < 2 else None),
              spa=(c._spill_all if tt < 2 else None), xg=x2gate[p], x1s=x2s[p]):
            if ca is not None:
                s_.wait_ge(c.v_cmb.h, ca)
            if spa is not None:
                s_.wait_ge(c.spq.h, spa[0])
                s_.wait_ge(c.spq1.h, spa[1])
            if agate is not None:
                s_.wait_ge(c.v_aux.h, agate)
            if xg is not None:
                s_.wait_ge(x1s.h, xg)
            s_.dma_start(out=vACC2ST(c)[:, p, :], in_=c.acc2_spill.ap()[tt * 128:(tt + 1) * 128, :]).then_inc(aq.h, 16)

        va = c.v_aux.bump()
        g_acc2[p] = va
        x1s = x2s[p]
        st_t = x1s.bump(16)

        @on("vector")
        def _(v, tt=tt, p=p, aq_t=aq_t, aq=aq):
            v.wait_ge(aq.h, aq_t)
            v.tensor_tensor(out=vACC2ST(c)[:, p, :], in0=vACC2ST(c)[:, p, :], in1=vD_comb(c)[:, tt, :],
                            op=ALU.add).then_inc(c.v_aux.h, 1)

        x2gate[p] = st_t

        @on("gpsimd")
        def _(g, tt=tt, p=p, va=va, x1s=x1s):
            g.wait_ge(c.v_aux.h, va)
            g.dma_start(out=c.x2_out[tt * 128:(tt + 1) * 128, :], in_=vACC2ST(c)[:, p, :]).then_inc(x1s.h, 16)


def build_l1(dbg=(), moe=True):
    nc = bass.Bass("TRN2", target_bir_lowering=False)
    ctx = ExitStack()
    c = Ctx()
    c.nc = nc
    c.rec = Rec()
    c.dbg = set(dbg)
    c.ctx = ctx

    A = lambda name, shape, dt: nc.dram_tensor(name, shape, dt, kind="ExternalInput").ap()
    c.x1_src = A("x1_own", [NT, D], F32)
    c.W_hg = A("L1_attn_W_hg", [2 * DI, D], F32)
    c.W_out = A("L1_attn_W_out", [D, DI], F32)
    c.wg = A("L1_ffn_gate_w", [D, NE], F32)
    c.w1_in = A("L1_ffn_w1", [NE, D, HFF], F32)
    c.b1_in = A("L1_ffn_b1", [NE, HFF], F32)
    c.w2_in = A("L1_ffn_w2", [NE, HFF, D], F32)
    c.b2_in = A("L1_ffn_b2", [NE, D], F32)
    c.ident_bf = A("ident_bf", [128, 128], BF16)
    c.ident_f32 = A("ident_f32", [128, 128], F32)
    c.iota8 = A("iota8", [128, 8], F32)
    c.selcar = A("sel_carry", [8, 1], F32)
    c.triu_f32 = A("triu_f32", [128, 128], F32)
    c.cmask = A("cmask", [128, 4, 512], BF16)
    c.vmask = A("vmask", [128, 32], F32)
    c.onespfx = A("ones_prefix", [8, 1], F32)
    c.ln_g = A("L0_ffn_ln_gamma", [D], F32)
    c.ln_b = A("L0_ffn_ln_beta", [D], F32)

    c.x2_out = nc.dram_tensor("x2_out", [NT, D], F32, kind="ExternalOutput").ap()

    c.acc2_spill = nc.dram_tensor("acc2_spill", [NT, D], F32)
    c.ag_car_in = nc.dram_tensor("ag_car_in", [16, 128], F32)
    c.ag_car_out = nc.dram_tensor("ag_car_out", [128, 128], F32)

    _alloc_common(c)
    _alloc_l1(c)
    _emit_phase0_l1(c)
    _emit_l1_w8(c)
    _emit_l1_scan(c)
    if moe != "scanonly":
        _emit_l1_outproj(c)
    if moe is True:
        _emit_moe1(c)
        _emit_l1_final(c)
    _emit_dbg(c)
    _emit_final_wait(c)

    with nc.Block() as block:
        c.rec.register(block)
    ctx.close()
    return nc


def _alloc_l1(c):
    nc, ctx = c.nc, c.ctx
    E = ctx.enter_context
    c.half_t = E(nc.sbuf_tensor("half_t", [128, 1], F32))
    c.fin_sb = E(nc.sbuf_tensor("fin_sb", [128, 16], F32))
    c.carry_sb = E(nc.sbuf_tensor("carry_sb", [128, 16], F32))
    c.selcar_sb = E(nc.sbuf_tensor("selcar_sb", [8, 1], F32))


def _emit_phase0_l1(c):
    on = c.rec.on
    c._lnidx = 0
    c._xnb_cons = [None, None]
    c._xnb_tr = [None, None]
    c._tr_ev_pre = [None, None]
    c._rt_prev = None

    _emit_consts(c)

    @on("vector")
    def _(v):
        v.memset(c.half_t[:, :], 0.5)

    sld = c.agq.bump(16)

    @on("gpsimd")
    def _(s_):
        s_.dma_start(out=c.selcar_sb[0:8, :], in_=c.selcar).then_inc(c.agq.h, 16)

    c._selcar_ld = sld

    for tt in range(TT):
        _emit_ln_tile(c, c.x1_src, tt, vA, routing="l1")

    c._tb_pre = [c._tr_ev_pre[0], c._tr_ev_pre[1]]
    c._gen_ps_gate = [None, None]
    c._wsl_gate = [None, None]


def _emit_moe1(c):
    on = c.rec.on
    c.mcfg = {"l1": True, "act": AF.Gelu}
    c.m_w1 = c.w1_in
    c.m_w2 = c.w2_in
    c.m_b1 = c.b1_in
    c.m_b2 = c.b2_in
    c._arena_gate_ev = c.v_ev.n
    c._fstg_gate = [None, None]
    c._mtr_pre = [None, None]
    c._mtr_init = None
    c._gcast_gate = [None] * NE
    c._moe_first_gate = c._w8_done
    c._q_slot_free = [None, None]
    c._b1_ev_prev = None
    c._y_gate = [None, None]
    c._acc_pre = None
    fb = c.fbq.bump(16)
    c._fc2b_ld = fb

    @on("gpsimd")
    def _(g, fbg=getattr(c, "_l1fc2b_gate", None), mb2=c.m_b2):
        if fbg is not None:
            g.wait_ge(c.v_cmb.h, fbg)
        g.dma_start(out=c.fc2b_sb[:, :], in_=mb2).then_inc(c.fbq.h, 16)

    _emit_moe0_combinit(c)
    for e in range(NE):
        for q in range(4):
            _emit_moe0_quarter(c, e, q, e * 4 + q)



def _emit_dbg(c):
    on = c.rec.on
    nc = c.nc
    if "l1scan" in c.dbg:
        dh = nc.dram_tensor("dbg_hT", [16, 128, NT], BF16, kind="ExternalOutput").ap()
        dx = nc.dram_tensor("dbg_xn1T", [8, 128, NT], BF16, kind="ExternalOutput").ap()
        dc = nc.dram_tensor("dbg_carry", [128, 16], F32, kind="ExternalOutput").ap()
        dfin = nc.dram_tensor("dbg_fin", [128, 16], F32, kind="ExternalOutput").ap()
        dmid = nc.dram_tensor("dbg_mid", [4, 128, NT], F32, kind="ExternalOutput").ap()
        c.st.bump(16 * 30)
        va_all = c.v_aux.n
        ev_all = c.v_ev.n

        @on("gpsimd")
        def _(g, va_all=va_all, ev_all=ev_all):
            g.wait_ge(c.v_aux.h, va_all)
            g.wait_ge(c.v_ev.h, ev_all)
            for sc in range(16):
                g.dma_start(out=dh[sc], in_=hT_sc(c, sc)).then_inc(c.st.h, 16)
            for k in range(8):
                g.dma_start(out=dx[k], in_=vA(c)[:, k, :]).then_inc(c.st.h, 16)
            g.dma_start(out=dc, in_=c.carry_sb[:, :]).then_inc(c.st.h, 16)
            g.dma_start(out=dfin, in_=c.fin_sb[:, :]).then_inc(c.st.h, 16)
            g.dma_start(out=dmid[0], in_=vL1_hid(c)[:, :]).then_inc(c.st.h, 16)
            g.dma_start(out=dmid[1], in_=vL1_gate(c)[:, :]).then_inc(c.st.h, 16)
            g.dma_start(out=dmid[2], in_=vL1_c(c)[:, :]).then_inc(c.st.h, 16)
            g.dma_start(out=dmid[3], in_=vL1_v(c)[:, :]).then_inc(c.st.h, 16)

    if "xnT" in c.dbg:
        d = nc.dram_tensor("dbg_xnT", [8, 128, NT], BF16, kind="ExternalOutput").ap()
        d2 = nc.dram_tensor("dbg_xnTe", [8, 128, NT], BF16, kind="ExternalOutput").ap()
        c.st.bump(16 * 16)
        ev_all = c.v_ev.n

        @on("gpsimd")
        def _(g, ev_all=ev_all):
            g.wait_ge(c.v_ev.h, ev_all)
            for k in range(8):
                g.dma_start(out=d[k], in_=vA(c)[:, k, :]).then_inc(c.st.h, 16)
                g.dma_start(out=d2[k], in_=vB(c)[:, k, :]).then_inc(c.st.h, 16)

    if "route" in c.dbg:
        dw = nc.dram_tensor("dbg_w8", [128, 16, 8], F32, kind="ExternalOutput").ap()
        dm = nc.dram_tensor("dbg_mask", [128, 16, 16], F32, kind="ExternalOutput").ap()
        dp = nc.dram_tensor("dbg_p1", [128, 16, 16], F32, kind="ExternalOutput").ap()
        db = nc.dram_tensor("dbg_allbase", [1, 16], F32, kind="ExternalOutput").ap()
        dbs = nc.dram_tensor("dbg_bases", [1, 16, 16], F32, kind="ExternalOutput").ap()
        dwt = nc.dram_tensor("dbg_wT", [8, NT], BF16, kind="ExternalOutput").ap()
        c.st.bump(16 * 6)
        vr_all = c.v_r.n
        ev_all = c.v_ev.n

        @on("gpsimd")
        def _(g, vr_all=vr_all, ev_all=ev_all):
            g.wait_ge(c.v_r.h, vr_all)
            g.wait_ge(c.v_ev.h, ev_all)
            g.dma_start(out=dw, in_=c.w8_all[:, :, :]).then_inc(c.st.h, 16)
            g.dma_start(out=dm, in_=c.mask16f[:, :, :]).then_inc(c.st.h, 16)
            g.dma_start(out=dp, in_=c.p1_all[:, :, :]).then_inc(c.st.h, 16)
            g.dma_start(out=db, in_=c.allbase[:, :]).then_inc(c.st.h, 16)
            g.dma_start(out=dbs, in_=c.bases_all[:, :, :]).then_inc(c.st.h, 16)
            g.dma_start(out=dwt, in_=c.wT_all[:, :]).then_inc(c.st.h, 16)


def _emit_final_wait(c):
    on = c.rec.on
    target = c.st.n
    t0, t1 = c.x1s0.n, c.x1s1.n

    @on("gpsimd")
    def _(g, target=target, t0=t0, t1=t1):
        if target:
            g.wait_ge(c.st.h, target)
        if t0:
            g.wait_ge(c.x1s0.h, t0)
        if t1:
            g.wait_ge(c.x1s1.h, t1)



def host_constants(core_id):
    b, h = core_id // 2, core_id % 2
    ident = np.eye(128, dtype=np.float32)
    cm = np.zeros((128, 4, 512), dtype=np.float32)
    for j in range(4):
        for jj in range(4):
            blk = cm[:, j, jj * 128:(jj + 1) * 128]
            if jj > j:
                blk[:] = 1.0
            elif jj == j:
                blk[:] = np.tril(np.ones((128, 128), np.float32)).T
    vm = np.zeros((128, 32), dtype=np.float32)
    vm[:, 0:16] = 1.0
    if h == 1:
        vm[:, 16:32] = 1.0
    onespfx = np.zeros((8, 1), dtype=np.float32)
    onespfx[:core_id, 0] = 1.0
    sel = np.zeros((8, 1), dtype=np.float32)
    if h == 1:
        sel[core_id - 1, 0] = 1.0
    return {
        "ident_bf": ident.astype(ml_dtypes.bfloat16),
        "ident_f32": ident,
        "triu_f32": np.triu(np.ones((128, 128), np.float32)),
        "iota8": np.tile(np.arange(8, dtype=np.float32), (128, 1)),
        "cmask": cm.astype(ml_dtypes.bfloat16),
        "vmask": vm,
        "ones_prefix": onespfx,
        "sel_carry": sel,
    }


def shard_inputs(inputs):
    x = np.ascontiguousarray(inputs["L0_attn_in_features"])
    maps = []
    for cid in range(NCORE):
        b, h = cid // 2, cid % 2
        m = dict(host_constants(cid))
        m.pop("sel_carry")
        m["x_own"] = np.ascontiguousarray(x[b, h * NT:(h + 1) * NT])
        m["x_ext"] = np.ascontiguousarray(x[b, (1 - h) * NT:(2 - h) * NT])
        for k in ("L0_attn_Wq", "L0_attn_Wk", "L0_attn_Wv", "L0_attn_Wo", "L0_ffn_wg",
                  "L0_ffn_fc1_w", "L0_ffn_fc1_b", "L0_ffn_fc2_w", "L0_ffn_fc2_b",
                  "L0_ffn_ln_gamma", "L0_ffn_ln_beta"):
            m[k] = inputs[k]
        maps.append(m)
    return maps





def build_fused(dbg=()):
    nc = bass.Bass("TRN2", target_bir_lowering=False)
    ctx = ExitStack()
    c = Ctx()
    c.nc = nc
    c.rec = Rec()
    c.dbg = set(dbg)
    c.ctx = ctx

    A = lambda name, shape, dt: nc.dram_tensor(name, shape, dt, kind="ExternalInput").ap()
    c.x_own = A("x_own", [NT, D], F32)
    c.x_ext = A("x_ext", [NT, D], F32)
    c.Wq = A("L0_attn_Wq", [D, D], F32)
    c.Wk = A("L0_attn_Wk", [D, D], F32)
    c.Wv = A("L0_attn_Wv", [D, D], F32)
    c.Wo = A("L0_attn_Wo", [D, D], F32)
    c.wg = A("L0_ffn_wg", [D, NE], F32)
    c.fc1_w = A("L0_ffn_fc1_w", [NE, HFF, D], F32)
    c.fc1_b = A("L0_ffn_fc1_b", [NE, HFF], F32)
    c.fc2_w = A("L0_ffn_fc2_w", [NE, HFF, D], F32)
    c.fc2_b = A("L0_ffn_fc2_b", [NE, D], F32)
    c.ln_g = A("L0_ffn_ln_gamma", [D], F32)
    c.ln_b = A("L0_ffn_ln_beta", [D], F32)
    c.W_hg = A("L1_attn_W_hg", [2 * DI, D], F32)
    c.W_out = A("L1_attn_W_out", [D, DI], F32)
    c.gate_w = A("L1_ffn_gate_w", [D, NE], F32)
    c.w1_in = A("L1_ffn_w1", [NE, D, HFF], F32)
    c.b1_in = A("L1_ffn_b1", [NE, HFF], F32)
    c.w2_in = A("L1_ffn_w2", [NE, HFF, D], F32)
    c.b2_in = A("L1_ffn_b2", [NE, D], F32)
    c.ident_bf = A("ident_bf", [128, 128], BF16)
    c.ident_f32 = A("ident_f32", [128, 128], F32)
    c.triu_f32 = A("triu_f32", [128, 128], F32)
    c.iota8 = A("iota8", [128, 8], F32)
    c.cmask = A("cmask", [128, 4, 512], BF16)
    c.vmask = A("vmask", [128, 32], F32)
    c.onespfx = A("ones_prefix", [8, 1], F32)
    c.selcar = A("sel_carry", [8, 1], F32)

    c.x2_out = nc.dram_tensor("x2_out", [NT, D], F32, kind="ExternalOutput").ap()

    c.fc1bf = nc.dram_tensor("fc1bf", [NE * HFF, D], BF16)
    c.acc_spill = nc.dram_tensor("acc_spill", [NT, D], F32)
    c.x1_spill = nc.dram_tensor("x1_spill", [NT, D], F32)
    c.acc2_spill = nc.dram_tensor("acc2_spill", [NT, D], F32)
    c.ag_cnt_in = nc.dram_tensor("ag_cnt_in", [1, 512], F32)
    c.ag_cnt_out = nc.dram_tensor("ag_cnt_out", [8, 512], F32)
    c.ag_car_in = nc.dram_tensor("ag_car_in", [16, 128], F32)
    c.ag_car_out = nc.dram_tensor("ag_car_out", [128, 128], F32)
    c.x1_out = c.x1_spill.ap()
    c.x1_src = c.x1_spill.ap()

    _alloc_common(c)
    _alloc_l1(c)

    _emit_phase0(c)
    _emit_keepw(c)
    c._p0_ev_done = c.v_ev.n
    _emit_attn(c)
    _emit_moe0(c)

    ep_end_vaux = c.v_aux.n
    x1_tot = (c.x1s0.n, c.x1s1.n)
    l0_cmb = c.v_cmb.n

    @c.rec.on("vector")
    def _(v):
        v.memset(c.half_t[:, :], 0.5)

    c.agq.bump(16)

    @c.rec.on("gpsimd")
    def _(g):
        g.dma_start(out=c.selcar_sb[0:8, :], in_=c.selcar).then_inc(c.agq.h, 16)

    gfw = c.gq.bump(16)
    c.wg_ld = gfw

    @c.rec.on("gpsimd")
    def _(g, wr=c._w8_done):
        g.wait_ge(c.v_r.h, wr)
        g.dma_start(out=c.wg_sb[:, :, :], in_=c.gate_w.rearrange("(k p) e -> p k e", p=128)).then_inc(c.gq.h, 16)

    c._tr_ev_pre = [c._ep_tr_pre[0], c._ep_tr_pre[1]]
    c._rt_prev = c._w8_done
    eg0 = [(c.x1s0.h, x1_tot[0]), (c.x1s1.h, x1_tot[1]), (c.v_aux.h, ep_end_vaux)]
    for tt in range(TT):
        _emit_ln_tile(c, c.x1_src, tt, vA, routing="l1",
                      extra_gates=(eg0 if tt < 2 else None))

    c._l1w8_ev_init = c._wT_done
    c._l1w8_cmb_gate = l0_cmb
    _emit_l1_w8(c)

    c._tb_pre = [c._tr_ev_pre[0], c._tr_ev_pre[1]]
    c._gen_ps_gate = [None, None]
    c._wsl_gate = [None, None]
    c._scan_init_gate = ep_end_vaux
    c._scan_consumed2 = None
    c._scan_wrow_gate = (c.v_aux.h, ep_end_vaux)
    _emit_l1_scan(c)
    _emit_l1_outproj(c)

    c._l1fc2b_gate = l0_cmb
    _emit_moe1(c)
    _emit_l1_final(c)
    _emit_dbg(c)
    _emit_final_wait(c)

    with nc.Block() as block:
        c.rec.register(block)
    ctx.close()
    return nc


def shard_inputs_fused(inputs):
    maps = []
    x = np.ascontiguousarray(inputs["L0_attn_in_features"])
    for cid in range(NCORE):
        b, h = cid // 2, cid % 2
        m = dict(host_constants(cid))
        m["x_own"] = np.ascontiguousarray(x[b, h * NT:(h + 1) * NT])
        m["x_ext"] = np.ascontiguousarray(x[b, (1 - h) * NT:(2 - h) * NT])
        for k in ("L0_attn_Wq", "L0_attn_Wk", "L0_attn_Wv", "L0_attn_Wo", "L0_ffn_wg",
                  "L0_ffn_fc1_w", "L0_ffn_fc1_b", "L0_ffn_fc2_w", "L0_ffn_fc2_b",
                  "L0_ffn_ln_gamma", "L0_ffn_ln_beta",
                  "L1_attn_W_hg", "L1_attn_W_out", "L1_ffn_gate_w", "L1_ffn_w1",
                  "L1_ffn_b1", "L1_ffn_w2", "L1_ffn_b2"):
            m[k] = inputs[k]
        maps.append(m)
    return maps


def shard_inputs_l1(inputs, x1_shards):
    maps = []
    for cid in range(NCORE):
        m = dict(host_constants(cid))
        m["x1_own"] = np.ascontiguousarray(x1_shards[cid])
        m["L0_ffn_ln_gamma"] = inputs["L0_ffn_ln_gamma"]
        m["L0_ffn_ln_beta"] = inputs["L0_ffn_ln_beta"]
        for k in ("L1_attn_W_hg", "L1_attn_W_out", "L1_ffn_gate_w", "L1_ffn_w1",
                  "L1_ffn_b1", "L1_ffn_w2", "L1_ffn_b2"):
            m[k] = inputs[k]
        maps.append(m)
    return maps


_CACHE = {}


def kernel(**inputs):
    if "fused" not in _CACHE:
        _CACHE["fused"] = build_fused()
    nc = _CACHE["fused"]
    maps = shard_inputs_fused(inputs)
    res = run_bass_kernel_spmd(nc, maps, core_ids=list(range(NCORE)))
    out = np.empty((4, 4096, 1024), dtype=np.float32)
    for cid in range(NCORE):
        b, h = cid // 2, cid % 2
        out[b, h * NT:(h + 1) * NT] = res.results[cid]["x2_out"]
    return out
